# version 02
# baseline (speedup 1.0000x reference)
import math
import numpy as np
import concourse.bass as bass
import concourse.mybir as mybir
from concourse.bass_utils import run_bass_kernel_spmd

F32 = mybir.dt.float32
BF16 = mybir.dt.bfloat16
I32 = mybir.dt.int32
AF = mybir.ActivationFunctionType
ALU = mybir.AluOpType
AX = mybir.AxisListType.X

NCORES = 8
D = 1024
C = 8
TC = 2048
NS = 8
SUB = 512
NSUB = TC // SUB
NE = 8
JC = 16
OWN = NS - 1
TWO_PI = 2.0 * math.pi
EPS = 1e-5
ALPHA = 1.702
LIMIT = 7.0
D7 = ALPHA * LIMIT * (1.0 / (1.0 + math.exp(-ALPHA * LIMIT)))

(C_SINB, C_COSB, C_PI, C_NPI, C_HPI, C_ISQ2, C_ALPHA, C_NEG1, C_ONE,
 C_HALF, C_NEGH) = range(11)
NCST = 11

ENGINES = ("sync", "scalar", "vector", "tensor", "gpsimd")

TAPS = set()
LAST_RES = None

BOFF = {"bpv": (0, D), "bk": (D, D), "bkv": (2 * D, D), "bg1": (3 * D, D),
        "bo": (4 * D, D), "bb1": (5 * D, D // 2), "gate_b": (5 * D + 512, NE),
        "bb2": (5 * D + 520, 2), "bg2": (5 * D + 522, 1)}


class Build:
    def __init__(self, nc):
        self.nc = nc
        self.ops = {e: [] for e in ENGINES}
        self.cnt = {}
        self.waited = {}
        self.buf_w = {}
        self.buf_r = {}
        for e in ENGINES:
            self.cnt["p_" + e] = 0
        for e in ("sync", "gpsimd", "scalar"):
            self.cnt["d_" + e] = 0

    def _wait(self, eng, semname, val):
        if val <= 0:
            return
        if semname == "p_" + eng:
            return
        if self.waited.get((eng, semname), 0) >= val:
            return
        self.waited[(eng, semname)] = val
        self.ops[eng].append(("w", semname, val))

    def emit(self, eng, fn, reads=(), writes=(), dma=False):
        for bn in reads:
            m = self.buf_w.get(bn)
            if m:
                self._wait(eng, *m)
        for bn in writes:
            m = self.buf_w.get(bn)
            if m:
                self._wait(eng, *m)
            for sn, v in (self.buf_r.get(bn) or {}).items():
                self._wait(eng, sn, v)
        sem = ("d_" + eng) if dma else ("p_" + eng)
        inc = 16 if dma else 1
        self.cnt[sem] += inc
        marker = (sem, self.cnt[sem])
        self.ops[eng].append(("i", fn, sem, inc))
        for bn in writes:
            self.buf_w[bn] = marker
            self.buf_r[bn] = {}
        for bn in reads:
            d = self.buf_r.setdefault(bn, {})
            if d.get(marker[0], 0) < marker[1]:
                d[marker[0]] = marker[1]
        return marker

    def barrier(self):
        vals = dict(self.cnt)
        for e in ENGINES:
            for sn, v in vals.items():
                self._wait(e, sn, v)

    def finalize_and_emit(self):
        nc = self.nc
        import contextlib
        with contextlib.ExitStack() as st:
            handles = {}
            for name in self.cnt.keys():
                handles[name] = st.enter_context(nc.semaphore(name))
            block = st.enter_context(nc.Block())
            for eng_name in ENGINES:
                ops = self.ops[eng_name]

                def make(ops=ops):
                    def body(E):
                        for op in ops:
                            if op[0] == "w":
                                E.wait_ge(handles[op[1]], op[2])
                            else:
                                op[1](E).then_inc(handles[op[2]], op[3])
                    return body
                getattr(block, eng_name)(make())


def build_program():
    nc = bass.Bass("TRN2", target_bir_lowering=False)

    def din(name, shape, dt=F32):
        return nc.dram_tensor(name, shape, dt, kind="ExternalInput").ap()

    xT = din("xT", [D, NS, TC + 1])
    ppT = din("ppT", [D, NS, TC])
    Wpv = din("Wpv", [D, D]); Wk = din("Wk", [D, D]); Wkv = din("Wkv", [D, D])
    Wg1 = din("Wg1", [2 * D, D]); Wg2 = din("Wg2", [D, 1])
    Wb1 = din("Wb1", [D, D // 2]); Wb2 = din("Wb2", [D // 2, 2])
    Wo = din("Wo", [D, D])
    biasrow = din("biasrow", [1, 5664])
    ln_g = din("ln_g", [128, C]); ln_b = din("ln_b", [128, C])
    nsc = din("nsc", [128, C])
    gate_w = din("gate_w", [D, NE])
    m1w = din("m1w", [NE, D, 2 * JC * 128])
    m1b = din("m1b", [NE, 1, 2 * JC * 128])
    m2w = din("m2w", [NE, JC * 128, D])
    m2b = din("m2b", [NE, 1, D])
    selw = din("selw", [NE, NE * 128])
    cmask = din("cmask", [128, NS])
    pmask = din("pmask", [128, NS])
    cst = din("cst", [128, NCST])
    identw = din("identw", [128, 128])

    out = nc.dram_tensor("out", [128, C, TC], F32, kind="ExternalOutput").ap()

    sp_xn = nc.dram_tensor("sp_xn", [128, C, TC], BF16, kind="Internal").ap()
    sp_s = nc.dram_tensor("sp_s", [128, C, TC], BF16, kind="Internal").ap()
    sp_pos = nc.dram_tensor("sp_pos", [128, C, TC], F32, kind="Internal").ap()
    sp_kv = nc.dram_tensor("sp_kv", [128, C, TC], F32, kind="Internal").ap()
    sp_bl = nc.dram_tensor("sp_bl", [128, C, TC], F32, kind="Internal").ap()

    FREE0 = ((nc.sbuf_base + 31) // 32) * 32
    TOP = nc.sbuf_top
    off = [FREE0]

    def alloc(name, shape, dt, at=None):
        nbytes = int(np.prod(shape[1:])) * mybir.dt.size(dt)
        if at is None:
            at = off[0]
            off[0] = at + ((nbytes + 31) // 32) * 32
        t = nc.alloc_sbuf_tensor_at(name, list(shape), dt, offset=at)
        assert at + nbytes <= TOP, f"SBUF overflow {name}: {at + nbytes} > {TOP}"
        return t

    p_cst = alloc("p_cst", [128, NCST], F32)
    p_cmask = alloc("p_cmask", [128, NS], F32)
    p_pmask = alloc("p_pmask", [128, NS], F32)
    p_lng = alloc("p_lng", [128, C], F32)
    p_lnb = alloc("p_lnb", [128, C], F32)
    p_nsc = alloc("p_nsc", [128, C], F32)
    p_gw = alloc("p_gw", [128, C, NE], BF16)
    p_wg2 = alloc("p_wg2", [128, C, 1], BF16)
    p_wb2 = alloc("p_wb2", [128, 4, 2], BF16)
    p_identf = alloc("p_identf", [128, 128], F32)
    p_ones_r = alloc("p_ones_r", [1, SUB + 128], BF16)
    p_ones_rf = alloc("p_ones_rf", [1, 256], F32)
    p_ones_c = alloc("p_ones_c", [128, 1], F32)
    p_ones_sc = alloc("p_ones_sc", [128, SUB], F32)
    p_sum_pc = alloc("p_sum_pc", [128, C, NS], F32)
    p_sum_ps = alloc("p_sum_ps", [128, C, NS], F32)
    p_sum_kc = alloc("p_sum_kc", [128, C, NS], F32)
    p_sum_ks = alloc("p_sum_ks", [128, C, NS], F32)
    p_sum_vg = alloc("p_sum_vg", [128, NS * NSUB], F32)
    p_vgm = alloc("p_vgm", [128, NS * NSUB], F32)
    p_car_pc = alloc("p_car_pc", [128, C], F32)
    p_car_ps = alloc("p_car_ps", [128, C], F32)
    p_car_kc = alloc("p_car_kc", [128, C], F32)
    p_car_ks = alloc("p_car_ks", [128, C], F32)
    p_car_vg = alloc("p_car_vg", [128, 1], F32)
    p_scarry = alloc("p_scarry", [128, C, 4], F32)
    p_bnd = alloc("p_bnd", [128, C, 4], F32)
    p_acol = alloc("p_acol", [128, C], F32)
    p_acol2 = alloc("p_acol2", [128, C], F32)
    p_bias = alloc("p_bias", [1, 5664], BF16)
    NTT = TC // 128
    p_gl = alloc("p_gl", [128, NTT * NE], F32)
    p_gw1 = alloc("p_gw1", [128, NTT * NE], F32)
    p_t2a = alloc("p_t2a", [128, NTT * NE], F32)
    p_t2b = alloc("p_t2b", [128, NTT * NE], F32)
    p_m1c = alloc("p_m1c", [128, NTT], F32)
    p_m2c = alloc("p_m2c", [128, NTT], F32)
    p_nm1f = alloc("p_nm1f", [128, NTT], F32)
    p_nm2f = alloc("p_nm2f", [128, NTT], F32)
    p_nmh1 = alloc("p_nmh1", [128, NTT], F32)
    p_nmh2 = alloc("p_nmh2", [128, NTT], F32)
    p_th1 = alloc("p_th1", [128, NTT], F32)
    p_th2 = alloc("p_th2", [128, NTT], F32)
    p_p1k = alloc("p_p1k", [128, NTT], F32)
    p_p2k = alloc("p_p2k", [128, NTT], F32)
    p_wrow = alloc("p_wrow", [NE, TC], BF16)
    p_sel = alloc("p_sel", [NE, NE * 128], BF16)
    r_gcs = alloc("r_gcs", [1, TC], BF16)
    r_bw0 = alloc("r_bw0", [1, TC], BF16)
    r_bw1 = alloc("r_bw1", [1, TC], BF16)
    r_mu2 = alloc("r_mu2", [1, TC], BF16)
    r_r2 = alloc("r_r2", [1, TC], BF16)
    r_rho = alloc("r_rho", [1, TC], BF16)
    r_vg = alloc("r_vg", [1, TC], F32)
    rf = [alloc(f"rf{i}", [1, SUB], F32) for i in range(4)]
    rwf = alloc("rwf", [1, TC], F32)
    r_ones = alloc("r_ones", [1, TC], F32)
    SP16_OFF = off[0]
    p_sp16a = alloc("p_sp16a", [128, SUB], BF16)
    p_sp16b = alloc("p_sp16b", [128, SUB], BF16)
    p_junk = alloc("p_junk", [128, SUB], F32, at=SP16_OFF)
    p_spfa = alloc("p_spfa", [128, SUB], F32)
    p_spfb = alloc("p_spfb", [128, SUB], F32)
    p_xn = alloc("p_xn", [128, C, TC + 1], BF16)

    A0 = off[0]
    ARENA = TOP - A0
    assert ARENA >= 88000, f"arena too small: {ARENA}"

    a_wpv = alloc("a_wpv", [128, C, D], BF16, at=A0)
    a_wk = alloc("a_wk", [128, C, D], BF16, at=A0 + 16384)
    a_wkv = alloc("a_wkv", [128, C, D], BF16, at=A0 + 32768)
    a_wg1 = alloc("a_wg1", [128, 2 * C, D], BF16, at=A0)
    a_wo = alloc("a_wo", [128, C, D], BF16, at=A0)
    a_wb1 = alloc("a_wb1", [128, C, D // 2], BF16, at=A0 + 32768)
    a_xsub = alloc("a_xsub", [128, C, SUB], F32, at=A0 + 32768)
    a_pc = alloc("a_pc", [128, C, SUB], BF16, at=A0 + 49152)
    a_ps = alloc("a_ps", [128, C, SUB], BF16, at=A0 + 57344)
    a_gh = alloc("a_gh", [128, C, SUB], BF16, at=A0 + 49152)
    a_bh = alloc("a_bh", [128, 4, SUB], BF16, at=A0 + 57344)
    a_ln = alloc("a_ln", [128, C, SUB], BF16, at=A0 + 49152)
    ZB = A0 + 65536
    z_tg1 = alloc("z_tg1", [128, SUB], I32, at=ZB)
    z_tg2 = alloc("z_tg2", [128, SUB], F32, at=ZB + 2048)
    z_kt = alloc("z_kt", [128, SUB], F32, at=ZB + 4096)
    z_ka = alloc("z_ka", [128, SUB], F32, at=ZB + 6144)
    z_kc = alloc("z_kc", [128, SUB + 1], F32, at=ZB + 8192)
    z_ks = alloc("z_ks", [128, SUB + 1], F32, at=ZB + 10304)
    z_p1 = alloc("z_p1", [128, SUB], F32, at=ZB + 12416)
    z_p2 = alloc("z_p2", [128, SUB], F32, at=ZB + 14464)
    z_m1 = alloc("z_m1", [128, SUB], F32, at=ZB + 16512)
    z_m2 = alloc("z_m2", [128, SUB], F32, at=ZB + 18560)
    z_sq = alloc("z_sq", [128, SUB], F32, at=ZB + 20608)
    z_x1 = alloc("z_x1", [128, SUB], F32, at=ZB + 20608)
    assert ZB + 22656 <= TOP

    f_h = alloc("f_h", [128, JC, TC // 2], BF16, at=A0)
    f_acc = alloc("f_acc", [128, C, TC], BF16, at=A0 + 32768)
    f_w1 = alloc("f_w1", [128, C, 256], BF16, at=A0 + 65536)
    f_w2m = alloc("f_w2m", [128, JC, 128], BF16, at=A0 + 69632)
    f_b1 = alloc("f_b1", [1, 256], BF16, at=A0 + 73728)
    f_b2 = alloc("f_b2", [1, D], BF16, at=A0 + 74240)
    f_bwe = alloc("f_bwe", [128, 2, SUB], F32, at=A0 + 76288)
    f_swz = alloc("f_swz", [128, SUB], F32, at=A0 + 80384)
    f_swz2 = alloc("f_swz2", [128, SUB], F32, at=A0 + 82432)
    f_swz3 = alloc("f_swz3", [128, SUB], BF16, at=A0 + 84480)
    f_swz4 = alloc("f_swz4", [128, SUB], BF16, at=A0 + 85504)
    assert A0 + 86528 <= TOP

    g_x = alloc("g_x", [128, C, SUB], F32, at=A0)
    g_g = alloc("g_g", [128, C, SUB], BF16, at=A0 + 16384)
    g_s = alloc("g_s", [128, C, SUB], BF16, at=A0 + 24576)
    g_o = alloc("g_o", [128, C, SUB], F32, at=A0 + 65536)
    assert A0 + 65536 + 16384 <= TOP

    pA = nc.alloc_psum_tensor("pA", [128, SUB], F32)
    pB = nc.alloc_psum_tensor("pB", [128, SUB], F32)
    pC_ = nc.alloc_psum_tensor("pC", [128, SUB], F32)
    pD = nc.alloc_psum_tensor("pD", [128, SUB], F32)
    pE = nc.alloc_psum_tensor("pE", [128, SUB], F32)
    pF = nc.alloc_psum_tensor("pF", [128, SUB], F32)
    pR0 = nc.alloc_psum_tensor("pR0", [1, SUB], F32)
    pR1 = nc.alloc_psum_tensor("pR1", [1, SUB], F32)

    b = Build(nc)

    def dma(dst_ap, src_ap, reads, writes, eng="gpsimd", slow=False):
        b.emit(eng,
               lambda E, d=dst_ap, s=src_ap, sl=slow: E.dma_start(
                   out=d, in_=s, allow_slow_non_contiguous=sl),
               reads=reads, writes=writes, dma=True)

    def V(fn, reads, writes):
        b.emit("vector", fn, reads=reads, writes=writes)

    def S(fn, reads, writes):
        b.emit("scalar", fn, reads=reads, writes=writes)

    def T(fn, reads, writes):
        b.emit("tensor", fn, reads=reads, writes=writes)

    def col(i):
        return p_cst[:, i:i + 1]

    def tap(name, sbuf_ap, rtags):
        if name not in TAPS:
            return
        shp = list(sbuf_ap.shape)
        dt = sbuf_ap.dtype
        t_out = nc.dram_tensor("tap_" + name, shp, dt,
                               kind="ExternalOutput").ap()
        dma(t_out, sbuf_ap, rtags, [f"tap_{name}"], eng="sync")

    def tap_dram(name, dram_ap, shp, dt, rtags):
        if name not in TAPS:
            return
        t_out = nc.dram_tensor("tap_" + name, list(shp), dt,
                               kind="ExternalOutput").ap()
        dma(t_out, dram_ap, rtags, [f"tap_{name}"], eng="sync")

    def bias_lhsT(name, mchunk):
        o, n = BOFF[name]
        return p_bias[:, o + mchunk * 128:o + min((mchunk + 1) * 128, o + n)
                      if False else o + mchunk * 128 + 128] \
            if False else p_bias[:, o + mchunk * 128:o + mchunk * 128 + 128]

    dma(p_cst[:], cst, [], ["p_cst"], eng="sync")
    dma(p_cmask[:], cmask, [], ["p_cmask"], eng="sync")
    dma(p_pmask[:], pmask, [], ["p_pmask"], eng="sync")
    dma(p_lng[:], ln_g, [], ["p_lng"], eng="sync")
    dma(p_lnb[:], ln_b, [], ["p_lnb"], eng="sync")
    dma(p_nsc[:], nsc, [], ["p_nsc"], eng="sync")
    dma(p_identf[:], identw, [], ["p_identf"], eng="sync")
    dma(p_identf[:], identw, [], ["p_identf"])
    dma(p_bias[:], biasrow, [], ["p_bias"])
    dma(p_sel[:], selw, [], ["p_sel"])
    dma(p_wg2[:, :, 0], Wg2.rearrange("(c p) o -> p (c o)", p=128)
        if False else Wg2[0:D, 0:1], [], ["p_wg2"]) if False else None
    for c in range(C):
        dma(p_wg2[:, c, 0:1], Wg2[c * 128:(c + 1) * 128, 0:1], [], ["p_wg2"],
            slow=True)
    for c in range(4):
        dma(p_wb2[:, c, :], Wb2[c * 128:(c + 1) * 128, :], [], ["p_wb2"])
    for c in range(C):
        dma(p_gw[:, c, :], gate_w[c * 128:(c + 1) * 128, :], [], ["p_gw"])
    V(lambda E: E.memset(p_ones_r[0:1, 0:SUB], 1.0), [], ["ones"])
    V(lambda E: E.memset(p_ones_r[0:1, SUB:SUB + 128], 1.0 / 32.0), [],
      ["ones"])
    V(lambda E: E.memset(p_ones_rf[0:1, 0:128], 1.0), [], ["ones"])
    V(lambda E: E.memset(p_ones_rf[0:1, 128:256], 1.0 / 32.0), [], ["ones"])
    V(lambda E: E.memset(p_ones_c[:], 1.0), [], ["ones"])
    V(lambda E: E.memset(p_ones_sc[:], 1.0), [], ["ones"])

    def ln_sums(x_chunks_fn, rtags, nn=SUB):
        for kc in range(C):
            T(lambda E, kc=kc, f=(kc == 0), l=(kc == C - 1), nn=nn: E.matmul(
                pR0[0:1, 0:nn], p_ones_c[:, 0:1], x_chunks_fn(kc),
                start=f, stop=l), ["ones"] + rtags, ["pR0"])
        for kc in range(C):
            S(lambda E, kc=kc, nn=nn: E.activation(
                out=z_sq[:, 0:nn], in_=x_chunks_fn(kc), func=AF.Square),
              rtags, ["z_sq"])
            T(lambda E, kc=kc, f=(kc == 0), l=(kc == C - 1), nn=nn: E.matmul(
                pR1[0:1, 0:nn], p_ones_c[:, 0:1], z_sq[:, 0:nn],
                start=f, stop=l), ["ones", "z_sq"], ["pR1"])

    def ln_rows(nn=SUB):
        V(lambda E, nn=nn: E.tensor_scalar(rf[0][0:1, 0:nn], pR0[0:1, 0:nn],
                                           1.0 / D, None, ALU.mult),
          ["pR0"], ["rf0"])
        V(lambda E, nn=nn: E.tensor_scalar(rf[3][0:1, 0:nn], pR1[0:1, 0:nn],
                                           1.0 / D, None, ALU.mult),
          ["pR1"], ["rf3"])
        V(lambda E, nn=nn: E.tensor_mul(rf[1][0:1, 0:nn], rf[0][0:1, 0:nn],
                                        rf[0][0:1, 0:nn]), ["rf0"], ["rf1"])
        V(lambda E, nn=nn: E.tensor_sub(rf[2][0:1, 0:nn], rf[3][0:1, 0:nn],
                                        rf[1][0:1, 0:nn]),
          ["rf3", "rf1"], ["rf2"])
        V(lambda E, nn=nn: E.tensor_scalar(rf[2][0:1, 0:nn], rf[2][0:1, 0:nn],
                                           EPS, None, ALU.add),
          ["rf2"], ["rf2"])
        V(lambda E, nn=nn: E.reciprocal(rf[3][0:1, 0:nn], rf[2][0:1, 0:nn]),
          ["rf2"], ["rf3"])
        S(lambda E, nn=nn: E.activation(out=rf[1][0:1, 0:nn],
                                        in_=rf[3][0:1, 0:nn], func=AF.Sqrt),
          ["rf3"], ["rf1"])

    def bcast(psum, pname, row_ap, rtags, nn=SUB):
        T(lambda E, nn=nn: E.matmul(psum[:, 0:nn], p_ones_rf[0:1, 0:128],
                                    row_ap, start=True, stop=True),
          ["ones"] + rtags, [pname])

    def bcast16(psum, pname, row_ap, rtags, nn=SUB):
        T(lambda E, nn=nn: E.matmul(psum[:, 0:nn], p_ones_r[0:1, 0:128],
                                    row_ap, start=True, stop=True),
          ["ones"] + rtags, [pname])

    def trig(dst_ap, wtag, src_ap, rtag, kind, nn=SUB):
        add_c = 32.0 if kind == "sin" else 32.25
        bias_i = C_SINB if kind == "sin" else C_COSB
        V(lambda E: E.tensor_scalar(z_tg1[:, 0:nn], src_ap, 1.0 / TWO_PI,
                                    add_c, ALU.mult, ALU.add),
          [rtag], ["z_tg1"])
        V(lambda E: E.tensor_copy(z_tg2[:, 0:nn], z_tg1[:, 0:nn]),
          ["z_tg1"], ["z_tg2"])
        V(lambda E: E.scalar_tensor_tensor(z_tg2[:, 0:nn], z_tg2[:, 0:nn],
                                           -TWO_PI, src_ap, ALU.mult, ALU.add),
          ["z_tg2", rtag], ["z_tg2"])
        S(lambda E: E.activation(out=dst_ap, in_=z_tg2[:, 0:nn], func=AF.Sin,
                                 bias=col(bias_i)),
          ["z_tg2", "p_cst"], [wtag])

    def wmm(psum, pname, w_sb, wtag, mchunk, x_fn, xtags, kchunks, bias,
            extra_k=()):
        first = True
        for kc in kchunks:
            T(lambda E, kc=kc, f=first: E.matmul(
                psum[:, 0:SUB], w_sb[:, kc, mchunk * 128:(mchunk + 1) * 128],
                x_fn(kc), start=f, stop=False), [wtag] + xtags, [pname])
            first = False
        for (w2_sb, w2tag, kc2, x2_fn) in extra_k:
            T(lambda E, kc2=kc2, w2_sb=w2_sb, x2_fn=x2_fn: E.matmul(
                psum[:, 0:SUB], w2_sb[:, kc2, mchunk * 128:(mchunk + 1) * 128],
                x2_fn(kc2), start=False, stop=False),
              [w2tag] + xtags, [pname])
        o, _n = BOFF[bias]
        T(lambda E, o=o, m=mchunk: E.matmul(
            psum[:, 0:SUB], p_bias[:, o + m * 128:o + (m + 1) * 128],
            p_ones_r[:, 0:SUB], start=False, stop=True),
          ["p_bias", "ones"], [pname])

    def process_slot(j, own):
        S(lambda E: E.preload_act_table(AF.Sqrt), [], [])
        for n in range(NSUB):
            t0 = 1 + n * SUB
            for c in range(C):
                dma(a_xsub[:, c, :], xT[c * 128:(c + 1) * 128, j, t0:t0 + SUB],
                    ["a_xsub"], ["a_xsub"], eng="sync")
            ln_sums(lambda kc: a_xsub[:, kc, :], ["a_xsub"])
            ln_rows()
            bcast(pA, "pA", rf[0][0:1, 0:SUB], ["rf0"])
            bcast(pB, "pB", rf[1][0:1, 0:SUB], ["rf1"])
            for c in range(C):
                V(lambda E, c=c: E.tensor_sub(z_p1[:, :], a_xsub[:, c, :],
                                              pA[:, :]),
                  ["a_xsub", "pA"], ["z_p1"])
                V(lambda E, c=c, t0=t0: E.tensor_mul(
                    p_xn[:, c, t0:t0 + SUB], z_p1[:, :], pB[:, :]),
                  ["z_p1", "pB"], ["p_xn"])
            if own:
                V(lambda E: E.tensor_scalar(rf[3][0:1, :], rf[2][0:1, :], EPS,
                                            None, ALU.subtract),
                  ["rf2"], ["rf3"])
                V(lambda E: E.reciprocal(rf[0][0:1, :], rf[2][0:1, :]),
                  ["rf2"], ["rf0"])
                V(lambda E: E.tensor_mul(rf[2][0:1, :], rf[3][0:1, :],
                                         rf[0][0:1, :]), ["rf3", "rf0"],
                  ["rf2"])
                V(lambda E: E.tensor_scalar(rf[3][0:1, :], rf[2][0:1, :], EPS,
                                            None, ALU.add), ["rf2"], ["rf3"])
                V(lambda E: E.reciprocal(rf[0][0:1, :], rf[3][0:1, :]),
                  ["rf3"], ["rf0"])
                S(lambda E, n=n: E.activation(
                    out=r_rho[0:1, n * SUB:(n + 1) * SUB], in_=rf[0][0:1, :],
                    func=AF.Sqrt), ["rf0"], ["r_rho"])
        for c in range(C):
            dma(a_xsub[:, c, :], xT[c * 128:(c + 1) * 128, j, 0:SUB],
                ["a_xsub"], ["a_xsub"], eng="sync")
        ln_sums(lambda kc: a_xsub[:, kc, :], ["a_xsub"])
        ln_rows()
        bcast(pA, "pA", rf[0][0:1, 0:SUB], ["rf0"])
        bcast(pB, "pB", rf[1][0:1, 0:SUB], ["rf1"])
        for c in range(C):
            V(lambda E, c=c: E.tensor_sub(z_p1[:, 0:1], a_xsub[:, c, 0:1],
                                          pA[:, 0:1]), ["a_xsub", "pA"],
              ["z_p1"])
            V(lambda E, c=c: E.tensor_mul(p_xn[:, c, 0:1], z_p1[:, 0:1],
                                          pB[:, 0:1]), ["z_p1", "pB"],
              ["p_xn"])
        b.barrier()

        S(lambda E: E.preload_act_table(AF.Sigmoid), [], [])
        for c2 in range(2 * C):
            dma(a_wg1[:, c2, :], Wg1[c2 * 128:(c2 + 1) * 128, :],
                ["a_wg1"] if c2 == 0 else [], ["a_wg1"])
        if own:
            for c in range(C):
                dma(a_wb1[:, c, :], Wb1[c * 128:(c + 1) * 128, :],
                    ["a_wb1"] if c == 0 else [], ["a_wb1"])
        for n in range(NSUB):
            t0 = 1 + n * SUB
            for m in range(C):
                ps1, pn1 = (pA, "pA") if m % 2 == 0 else (pB, "pB")
                wmm(ps1, pn1, a_wg1, "a_wg1", m,
                    lambda kc, t0=t0: p_xn[:, kc, t0:t0 + SUB], ["p_xn"],
                    range(C), "bg1",
                    extra_k=[(a_wg1, "a_wg1", C + kc2,
                              (lambda kc2_, t0=t0: p_xn[:, kc2_ - C,
                                                        t0 - 1:t0 - 1 + SUB]))
                             for kc2 in range(C)])
                V(lambda E, ps1=ps1: E.tensor_copy(z_sq[:, :], ps1[:, :]),
                  [pn1], ["z_sq"])
                S(lambda E: E.activation(out=z_kt[:, :], in_=z_sq[:, :],
                                         func=AF.Erf, scale=col(C_ISQ2)),
                  ["z_sq", "p_cst"], ["z_kt"])
                V(lambda E: E.tensor_scalar(z_ka[:, :], z_kt[:, :], 0.5, 0.5,
                                            ALU.mult, ALU.add),
                  ["z_kt"], ["z_ka"])
                V(lambda E, m=m: E.tensor_mul(a_gh[:, m, :], z_sq[:, :],
                                              z_ka[:, :]),
                  ["z_sq", "z_ka"], ["a_gh"])
                if own and n == 0 and m == 0:
                    tap("z00", z_sq[:, :], ["z_sq"])
            if own and n == 0:
                tap("gh0", a_gh[:, :, :], ["a_gh"])
            for kc in range(C):
                T(lambda E, kc=kc, f=(kc == 0): E.matmul(
                    pR0[0:1, 0:SUB], p_wg2[:, kc, 0:1], a_gh[:, kc, :],
                    start=f, stop=False), ["p_wg2", "a_gh"], ["pR0"])
            o_bg2, _n2 = BOFF["bg2"]
            T(lambda E, o=o_bg2: E.matmul(pR0[0:1, 0:SUB],
                                          p_bias[:, o:o + 1],
                                          p_ones_r[:, 0:SUB],
                                          start=False, stop=True),
              ["p_bias", "ones"], ["pR0"])
            S(lambda E, n=n: E.activation(out=r_vg[0:1, n * SUB:(n + 1) * SUB],
                                          in_=pR0[0:1, :], func=AF.Sigmoid),
              ["pR0"], ["r_vg"])
            if own:
                for m in range(4):
                    ps1, pn1 = (pC_, "pC") if m % 2 == 0 else (pD, "pD")
                    wmm(ps1, pn1, a_wb1, "a_wb1", m,
                        lambda kc, t0=t0: p_xn[:, kc, t0:t0 + SUB], ["p_xn"],
                        range(C), "bb1")
                    V(lambda E, ps1=ps1: E.tensor_copy(z_sq[:, :], ps1[:, :]),
                      [pn1], ["z_sq"])
                    S(lambda E: E.activation(out=z_kt[:, :], in_=z_sq[:, :],
                                             func=AF.Erf, scale=col(C_ISQ2)),
                      ["z_sq", "p_cst"], ["z_kt"])
                    V(lambda E: E.tensor_scalar(z_ka[:, :], z_kt[:, :], 0.5,
                                                0.5, ALU.mult, ALU.add),
                      ["z_kt"], ["z_ka"])
                    V(lambda E, m=m: E.tensor_mul(a_bh[:, m, :], z_sq[:, :],
                                                  z_ka[:, :]),
                      ["z_sq", "z_ka"], ["a_bh"])
                o_bb2, _n2 = BOFF["bb2"]
                for zi in range(2):
                    for kc in range(4):
                        T(lambda E, kc=kc, zi=zi, f=(kc == 0):
                          E.matmul(pR1[0:1, 0:SUB], p_wb2[:, kc, zi:zi + 1],
                                   a_bh[:, kc, :], start=f, stop=False),
                          ["p_wb2", "a_bh"], ["pR1"])
                    T(lambda E, zi=zi, o=o_bb2: E.matmul(
                        pR1[0:1, 0:SUB], p_bias[:, o + zi:o + zi + 1],
                        p_ones_r[:, 0:SUB], start=False, stop=True),
                      ["p_bias", "ones"], ["pR1"])
                    V(lambda E, zi=zi: E.tensor_copy(
                        rf[0 if zi == 0 else 3][0:1, :], pR1[0:1, :]),
                      ["pR1"], ["rf0" if zi == 0 else "rf3"])
                V(lambda E: E.tensor_sub(rf[1][0:1, :], rf[0][0:1, :],
                                         rf[3][0:1, :]),
                  ["rf0", "rf3"], ["rf1"])
                S(lambda E, n=n: E.activation(
                    out=r_bw0[0:1, n * SUB:(n + 1) * SUB], in_=rf[1][0:1, :],
                    func=AF.Sigmoid), ["rf1"], ["r_bw0"])
                S(lambda E, n=n: E.activation(
                    out=r_bw1[0:1, n * SUB:(n + 1) * SUB], in_=rf[1][0:1, :],
                    func=AF.Sigmoid, scale=p_cst[0:1, C_NEG1:C_NEG1 + 1]),
                  ["rf1", "p_cst"], ["r_bw1"])
        b.barrier()

        S(lambda E: E.preload_act_table(AF.Silu), [], [])
        for c in range(C):
            dma(a_wpv[:, c, :], Wpv[c * 128:(c + 1) * 128, :],
                ["a_wpv"] if c == 0 else [], ["a_wpv"])
        for c in range(C):
            dma(a_wk[:, c, :], Wk[c * 128:(c + 1) * 128, :],
                ["a_wk"] if c == 0 else [], ["a_wk"])
        for c in range(C):
            dma(a_wkv[:, c, :], Wkv[c * 128:(c + 1) * 128, :],
                ["a_wkv"] if c == 0 else [], ["a_wkv"])
        for m in range(C):
            ps1, pn1 = (pA, "pA") if m % 2 == 0 else (pB, "pB")
            wmm(ps1, pn1, a_wk, "a_wk", m, lambda kc: p_xn[:, kc, 0:SUB],
                ["p_xn"], range(C), "bk")
            S(lambda E, ps1=ps1: E.activation(out=z_kt[:, :], in_=ps1[:, :],
                                              func=AF.Tanh), [pn1], ["z_kt"])
            V(lambda E, m=m: E.tensor_copy(p_bnd[:, m, 0:1], z_kt[:, 0:1]),
              ["z_kt"], ["p_bnd"])
        S(lambda E: E.activation(out=p_bnd[:, :, 3:4], in_=p_bnd[:, :, 0:1],
                                 func=AF.Abs), ["p_bnd"], ["p_bnd"])
        S(lambda E: E.activation(out=p_bnd[:, :, 2:3], in_=p_bnd[:, :, 0:1],
                                 func=AF.Sin, scale=col(C_PI)),
          ["p_bnd", "p_cst"], ["p_bnd"])
        S(lambda E: E.activation(out=p_bnd[:, :, 3:4], in_=p_bnd[:, :, 3:4],
                                 func=AF.Sin, scale=col(C_NPI),
                                 bias=col(C_HPI)), ["p_bnd", "p_cst"],
          ["p_bnd"]) if False else None
        S(lambda E: E.activation(out=z_ka[:, 0:C], in_=p_bnd[:, :, 3],
                                 func=AF.Sin, scale=col(C_NPI),
                                 bias=col(C_HPI)),
          ["p_bnd", "p_cst"], ["z_ka"])
        for m in range(C):
            V(lambda E, m=m, j=j: E.tensor_scalar(
                p_bnd[:, m, 0:1], p_bnd[:, m, 2:3], p_pmask[:, j:j + 1],
                None, ALU.mult), ["p_bnd", "p_pmask"], ["p_bnd"])
            V(lambda E, m=m, j=j: E.tensor_scalar(
                p_bnd[:, m, 1:2], z_ka[:, m:m + 1], p_pmask[:, j:j + 1],
                None, ALU.mult), ["z_ka", "p_pmask"], ["p_bnd"])
        if own:
            for dst, dtag, src, stag in (
                    (p_car_pc, "p_car_pc", p_sum_pc, "p_sum_pc"),
                    (p_car_ps, "p_car_ps", p_sum_ps, "p_sum_ps"),
                    (p_car_kc, "p_car_kc", p_sum_kc, "p_sum_kc"),
                    (p_car_ks, "p_car_ks", p_sum_ks, "p_sum_ks")):
                V(lambda E, dst=dst: E.memset(dst[:, :], 0.0), [], [dtag])
                for jj in range(NS - 1):
                    V(lambda E, dst=dst, src=src, jj=jj:
                      E.scalar_tensor_tensor(dst[:, :], src[:, :, jj],
                                             p_cmask[:, jj:jj + 1],
                                             dst[:, :], ALU.mult, ALU.add),
                      [stag, "p_cmask", dtag], [dtag])
            for jj in range(NS - 1):
                V(lambda E, jj=jj: E.tensor_scalar(
                    p_vgm[:, jj * NSUB:(jj + 1) * NSUB],
                    p_sum_vg[:, jj * NSUB:(jj + 1) * NSUB],
                    p_cmask[:, jj:jj + 1], None,
                    ALU.mult), ["p_sum_vg", "p_cmask"], ["p_vgm"])
            V(lambda E: E.memset(p_vgm[:, (NS - 1) * NSUB:NS * NSUB], 0.0),
              [], ["p_vgm"])
            S(lambda E: E.nop(), ["p_vgm"], ["rt_vg"])
            V(lambda E: E.reduce_sum(p_car_vg[:, 0:1], p_vgm[:, :], axis=AX),
              ["p_vgm", "rt_vg"], ["p_car_vg"])
            V(lambda E: E.memset(r_ones[0:1, :], 1.0), [], ["r_ones"])
            S(lambda E: E.nop(), ["r_ones"], ["rt_ro"])
            V(lambda E: E.tensor_tensor_scan(
                rwf[0:1, :], r_ones[0:1, :], r_vg[0:1, :], 0.0,
                ALU.mult, ALU.add), ["r_ones", "r_vg", "rt_ro"], ["rwf"])
            V(lambda E: E.tensor_scalar(r_gcs[0:1, :], rwf[0:1, :],
                                        p_car_vg[0:1, 0:1], 1.0, ALU.add,
                                        ALU.max),
              ["rwf", "p_car_vg"], ["r_gcs"])
            for i, (src, stag) in enumerate(((p_car_pc, "p_car_pc"),
                                             (p_car_ps, "p_car_ps"),
                                             (p_car_kc, "p_car_kc"),
                                             (p_car_ks, "p_car_ks"))):
                for c in range(C):
                    V(lambda E, i=i, c=c, src=src: E.tensor_copy(
                        p_scarry[:, c, i:i + 1], src[:, c:c + 1]),
                      [stag], ["p_scarry"])

        for n in range(NSUB):
            t0 = 1 + n * SUB
            for c in range(C):
                dma(z_x1[:, :], ppT[c * 128:(c + 1) * 128, j,
                                    n * SUB:(n + 1) * SUB],
                    ["z_x1"], ["z_x1"], eng="sync")
                trig(a_ps[:, c, :], "a_ps", z_x1[:, :], "z_x1", "sin")
                trig(a_pc[:, c, :], "a_pc", z_x1[:, :], "z_x1", "cos")
            for m in range(C):
                ps1, pn1 = (pA, "pA") if m % 2 == 0 else (pB, "pB")
                wmm(ps1, pn1, a_wpv, "a_wpv", m,
                    lambda kc, t0=t0: p_xn[:, kc, t0:t0 + SUB], ["p_xn"],
                    range(C), "bpv")
                V(lambda E, m=m, ps1=ps1: E.tensor_mul(
                    z_p1[:, :], a_pc[:, m, :], ps1[:, :]),
                  ["a_pc", pn1], ["z_p1"])
                V(lambda E, m=m, ps1=ps1: E.tensor_mul(
                    z_p2[:, :], a_ps[:, m, :], ps1[:, :]),
                  ["a_ps", pn1], ["z_p2"])
                if own:
                    V(lambda E, m=m: E.tensor_tensor_scan(
                        z_m1[:, :], p_ones_sc[:, :], z_p1[:, :],
                        p_scarry[:, m, 0:1], ALU.mult, ALU.add),
                      ["ones", "z_p1", "p_scarry"], ["z_m1"])
                    V(lambda E, m=m: E.tensor_tensor_scan(
                        z_m2[:, :], p_ones_sc[:, :], z_p2[:, :],
                        p_scarry[:, m, 1:2], ALU.mult, ALU.add),
                      ["ones", "z_p2", "p_scarry"], ["z_m2"])
                    V(lambda E, m=m: E.tensor_copy(p_scarry[:, m, 0:1],
                                                   z_m1[:, SUB - 1:SUB]),
                      ["z_m1"], ["p_scarry"])
                    V(lambda E, m=m: E.tensor_copy(p_scarry[:, m, 1:2],
                                                   z_m2[:, SUB - 1:SUB]),
                      ["z_m2"], ["p_scarry"])
                    spb, sptag = (p_spfa, "p_spfa") if m % 2 == 0 \
                        else (p_spfb, "p_spfb")
                    V(lambda E, m=m: E.tensor_mul(z_p1[:, :], a_pc[:, m, :],
                                                  z_m1[:, :]),
                      ["a_pc", "z_m1"], ["z_p1"])
                    V(lambda E, m=m: E.tensor_mul(z_p2[:, :], a_ps[:, m, :],
                                                  z_m2[:, :]),
                      ["a_ps", "z_m2"], ["z_p2"])
                    V(lambda E, spb=spb: E.tensor_add(spb[:, :], z_p1[:, :],
                                                      z_p2[:, :]),
                      ["z_p1", "z_p2"], [sptag])
                    dma(sp_pos[:, m, n * SUB:(n + 1) * SUB], spb[:, :],
                        [sptag], [f"sp_pos_{m}_{n}"])
                else:
                    V(lambda E, m=m: E.scalar_tensor_tensor(
                        p_junk[:, :], z_p1[:, :], 1.0, p_ones_sc[:, :],
                        ALU.mult, ALU.mult, accum_out=p_acol[:, m:m + 1]),
                      ["z_p1", "ones"], ["p_junk", "p_acol"])
                    V(lambda E, m=m: E.scalar_tensor_tensor(
                        p_junk[:, :], z_p2[:, :], 1.0, p_ones_sc[:, :],
                        ALU.mult, ALU.mult, accum_out=p_acol2[:, m:m + 1]),
                      ["z_p2", "ones"], ["p_junk", "p_acol2"])
                if not own and m == C - 1:
                    if n == 0:
                        V(lambda E, j=j: E.tensor_copy(p_sum_pc[:, :, j],
                                                       p_acol[:, :]),
                          ["p_acol"], ["p_sum_pc"])
                        V(lambda E, j=j: E.tensor_copy(p_sum_ps[:, :, j],
                                                       p_acol2[:, :]),
                          ["p_acol2"], ["p_sum_ps"])
                    else:
                        V(lambda E, j=j: E.tensor_add(p_sum_pc[:, :, j],
                                                      p_sum_pc[:, :, j],
                                                      p_acol[:, :]),
                          ["p_acol", "p_sum_pc"], ["p_sum_pc"])
                        V(lambda E, j=j: E.tensor_add(p_sum_ps[:, :, j],
                                                      p_sum_ps[:, :, j],
                                                      p_acol2[:, :]),
                          ["p_acol2", "p_sum_ps"], ["p_sum_ps"])
            bcast(pE, "pE", r_vg[0:1, n * SUB:(n + 1) * SUB], ["r_vg"])
            if not own:
                V(lambda E, j=j, n=n: E.scalar_tensor_tensor(
                    p_junk[:, :], p_ones_sc[:, :], 1.0, pE[:, :],
                    ALU.bypass, ALU.mult,
                    accum_out=p_sum_vg[:, j * NSUB + n:j * NSUB + n + 1]),
                  ["ones", "pE"], ["p_junk", "p_sum_vg"])
            for m in range(C):
                ps1, pn1 = (pC_, "pC") if m % 2 == 0 else (pD, "pD")
                wmm(ps1, pn1, a_wk, "a_wk", m,
                    lambda kc, t0=t0: p_xn[:, kc, t0:t0 + SUB], ["p_xn"],
                    range(C), "bk")
                S(lambda E, ps1=ps1: E.activation(out=z_kt[:, :],
                                                  in_=ps1[:, :],
                                                  func=AF.Tanh),
                  [pn1], ["z_kt"])
                S(lambda E: E.activation(out=z_ka[:, :], in_=z_kt[:, :],
                                         func=AF.Abs), ["z_kt"], ["z_ka"])
                S(lambda E: E.activation(out=z_ks[:, 1:SUB + 1],
                                         in_=z_kt[:, :], func=AF.Sin,
                                         scale=col(C_PI)),
                  ["z_kt", "p_cst"], ["z_ks"])
                S(lambda E: E.activation(out=z_kc[:, 1:SUB + 1],
                                         in_=z_ka[:, :], func=AF.Sin,
                                         scale=col(C_NPI), bias=col(C_HPI)),
                  ["z_ka", "p_cst"], ["z_kc"])
                V(lambda E, m=m: E.tensor_copy(z_kc[:, 0:1],
                                               p_bnd[:, m, 1:2]),
                  ["p_bnd"], ["z_kc"])
                V(lambda E, m=m: E.tensor_copy(z_ks[:, 0:1],
                                               p_bnd[:, m, 0:1]),
                  ["p_bnd"], ["z_ks"])
                V(lambda E, m=m: E.tensor_copy(p_bnd[:, m, 1:2],
                                               z_kc[:, SUB:SUB + 1]),
                  ["z_kc"], ["p_bnd"])
                V(lambda E, m=m: E.tensor_copy(p_bnd[:, m, 0:1],
                                               z_ks[:, SUB:SUB + 1]),
                  ["z_ks"], ["p_bnd"])
                ps2, pn2 = (pA, "pA") if m % 2 == 0 else (pB, "pB")
                wmm(ps2, pn2, a_wkv, "a_wkv", m,
                    lambda kc, t0=t0: p_xn[:, kc, t0:t0 + SUB], ["p_xn"],
                    range(C), "bkv")
                V(lambda E, ps2=ps2: E.tensor_mul(z_p1[:, :], z_kc[:, 0:SUB],
                                                  ps2[:, :]),
                  ["z_kc", pn2], ["z_p1"])
                V(lambda E, ps2=ps2: E.tensor_mul(z_p2[:, :], z_ks[:, 0:SUB],
                                                  ps2[:, :]),
                  ["z_ks", pn2], ["z_p2"])
                if own:
                    V(lambda E: E.scalar_tensor_tensor(
                        z_p1[:, :], z_p1[:, :], 1.0, pE[:, :], ALU.bypass,
                        ALU.mult), ["z_p1", "pE"], ["z_p1"])
                    V(lambda E: E.scalar_tensor_tensor(
                        z_p2[:, :], z_p2[:, :], 1.0, pE[:, :], ALU.bypass,
                        ALU.mult), ["z_p2", "pE"], ["z_p2"])
                    V(lambda E, m=m: E.tensor_tensor_scan(
                        z_m1[:, :], p_ones_sc[:, :], z_p1[:, :],
                        p_scarry[:, m, 2:3], ALU.mult, ALU.add),
                      ["ones", "z_p1", "p_scarry"], ["z_m1"])
                    V(lambda E, m=m: E.tensor_tensor_scan(
                        z_m2[:, :], p_ones_sc[:, :], z_p2[:, :],
                        p_scarry[:, m, 3:4], ALU.mult, ALU.add),
                      ["ones", "z_p2", "p_scarry"], ["z_m2"])
                    V(lambda E, m=m: E.tensor_copy(p_scarry[:, m, 2:3],
                                                   z_m1[:, SUB - 1:SUB]),
                      ["z_m1"], ["p_scarry"])
                    V(lambda E, m=m: E.tensor_copy(p_scarry[:, m, 3:4],
                                                   z_m2[:, SUB - 1:SUB]),
                      ["z_m2"], ["p_scarry"])
                    spb, sptag = (p_spfa, "p_spfa") if m % 2 == 0 \
                        else (p_spfb, "p_spfb")
                    V(lambda E: E.tensor_mul(z_p1[:, :], z_kc[:, 1:SUB + 1],
                                             z_m1[:, :]),
                      ["z_kc", "z_m1"], ["z_p1"])
                    V(lambda E: E.tensor_mul(z_p2[:, :], z_ks[:, 1:SUB + 1],
                                             z_m2[:, :]),
                      ["z_ks", "z_m2"], ["z_p2"])
                    V(lambda E, spb=spb: E.tensor_add(spb[:, :], z_p1[:, :],
                                                      z_p2[:, :]),
                      ["z_p1", "z_p2"], [sptag])
                    dma(sp_kv[:, m, n * SUB:(n + 1) * SUB], spb[:, :],
                        [sptag], [f"sp_kv_{m}_{n}"])
                else:
                    V(lambda E, m=m: E.scalar_tensor_tensor(
                        p_junk[:, :], z_p1[:, :], 1.0, pE[:, :], ALU.bypass,
                        ALU.mult, accum_out=p_acol[:, m:m + 1]),
                      ["z_p1", "pE"], ["p_junk", "p_acol"])
                    V(lambda E, m=m: E.scalar_tensor_tensor(
                        p_junk[:, :], z_p2[:, :], 1.0, pE[:, :], ALU.bypass,
                        ALU.mult, accum_out=p_acol2[:, m:m + 1]),
                      ["z_p2", "pE"], ["p_junk", "p_acol2"])
            if not own:
                if n == 0:
                    V(lambda E, j=j: E.tensor_copy(p_sum_kc[:, :, j],
                                                   p_acol[:, :]),
                      ["p_acol"], ["p_sum_kc"])
                    V(lambda E, j=j: E.tensor_copy(p_sum_ks[:, :, j],
                                                   p_acol2[:, :]),
                      ["p_acol2"], ["p_sum_ks"])
                else:
                    V(lambda E, j=j: E.tensor_add(p_sum_kc[:, :, j],
                                                  p_sum_kc[:, :, j],
                                                  p_acol[:, :]),
                      ["p_acol", "p_sum_kc"], ["p_sum_kc"])
                    V(lambda E, j=j: E.tensor_add(p_sum_ks[:, :, j],
                                                  p_sum_ks[:, :, j],
                                                  p_acol2[:, :]),
                      ["p_acol2", "p_sum_ks"], ["p_sum_ks"])
        b.barrier()

    for j in range(NS - 1):
        process_slot(j, own=False)
    process_slot(OWN, own=True)

    tap("xn", p_xn[:, :, :], ["p_xn"])
    tap("vg", r_vg[0:1, :], ["r_vg"])
    tap("bw0", r_bw0[0:1, :], ["r_bw0"])
    tap("bw1", r_bw1[0:1, :], ["r_bw1"])
    tap("rho", r_rho[0:1, :], ["r_rho"])
    tap("gcs", r_gcs[0:1, :], ["r_gcs"])
    tap("car_pc", p_car_pc[:, :], ["p_car_pc"])
    tap("car_ps", p_car_ps[:, :], ["p_car_ps"])
    tap("car_kc", p_car_kc[:, :], ["p_car_kc"])
    tap("car_ks", p_car_ks[:, :], ["p_car_ks"])
    tap("sum_pc", p_sum_pc[:, :, :], ["p_sum_pc"])
    tap("sum_kc", p_sum_kc[:, :, :], ["p_sum_kc"])
    tap("sum_vg", p_sum_vg[0:1, :], ["p_sum_vg"])

    S(lambda E: E.preload_act_table(AF.Sqrt), [], [])
    for c in range(C):
        dma(a_wo[:, c, :], Wo[c * 128:(c + 1) * 128, :],
            ["a_wo"] if c == 0 else [], ["a_wo"])
    V(lambda E: E.reciprocal(rwf[0:1, :], r_gcs[0:1, :]), ["r_gcs"], ["rwf"])
    S(lambda E: E.activation(out=r_gcs[0:1, :], in_=rwf[0:1, :],
                             func=AF.Sqrt), ["rwf"], ["r_gcs"])
    V(lambda E: E.tensor_mul(rwf[0:1, :], r_bw1[0:1, :], r_gcs[0:1, :]),
      ["r_bw1", "r_gcs"], ["rwf"])
    for n in range(NSUB):
        T(lambda E, n=n: E.matmul(pA[:, 0:SUB],
                                  p_ones_r[0:1, SUB:SUB + 128],
                                  r_bw0[0:1, n * SUB:(n + 1) * SUB],
                                  start=True, stop=True),
          ["ones", "r_bw0"], ["pA"])
        T(lambda E, n=n: E.matmul(pB[:, 0:SUB],
                                  p_ones_rf[0:1, 128:256],
                                  rwf[0:1, n * SUB:(n + 1) * SUB],
                                  start=True, stop=True),
          ["ones", "rwf"], ["pB"])
        for c in range(C):
            dma(z_kt[:, :], sp_pos[:, c, n * SUB:(n + 1) * SUB],
                [f"sp_pos_{c}_{n}", "z_kt"], ["z_kt"], eng="sync")
            dma(z_ka[:, :], sp_kv[:, c, n * SUB:(n + 1) * SUB],
                [f"sp_kv_{c}_{n}", "z_ka"], ["z_ka"], eng="sync")
            V(lambda E: E.tensor_mul(z_p1[:, :], z_kt[:, :], pA[:, :]),
              ["z_kt", "pA"], ["z_p1"])
            V(lambda E: E.tensor_mul(z_p2[:, :], z_ka[:, :], pB[:, :]),
              ["z_ka", "pB"], ["z_p2"])
            spb, sptag = (p_spfa, "p_spfa") if c % 2 == 0 \
                else (p_spfb, "p_spfb")
            V(lambda E, spb=spb: E.tensor_add(spb[:, :], z_p1[:, :],
                                              z_p2[:, :]),
              ["z_p1", "z_p2"], [sptag])
            T(lambda E, c=c, f=(c == 0), l=(c == C - 1), spb=spb: E.matmul(
                pR0[0:1, 0:SUB], p_ones_c[:, 0:1], spb[:, :],
                start=f, stop=l), ["ones", sptag], ["pR0"])
            S(lambda E, spb=spb: E.activation(out=z_sq[:, :], in_=spb[:, :],
                                              func=AF.Square),
              [sptag], ["z_sq"])
            T(lambda E, c=c, f=(c == 0), l=(c == C - 1): E.matmul(
                pR1[0:1, 0:SUB], p_ones_c[:, 0:1], z_sq[:, :],
                start=f, stop=l), ["ones", "z_sq"], ["pR1"])
            dma(sp_bl[:, c, n * SUB:(n + 1) * SUB], spb[:, :], [sptag],
                [f"sp_bl_{c}_{n}"])
        ln_rows()
        V(lambda E, n=n: E.tensor_copy(r_mu2[0:1, n * SUB:(n + 1) * SUB],
                                       rf[0][0:1, :]), ["rf0"], ["r_mu2"])
        V(lambda E, n=n: E.tensor_copy(r_r2[0:1, n * SUB:(n + 1) * SUB],
                                       rf[1][0:1, :]), ["rf1"], ["r_r2"])
    for n in range(NSUB):
        t0 = 1 + n * SUB
        bcast16(pA, "pA", r_mu2[0:1, n * SUB:(n + 1) * SUB], ["r_mu2"])
        bcast16(pB, "pB", r_r2[0:1, n * SUB:(n + 1) * SUB], ["r_r2"])
        for c in range(C):
            dma(z_kt[:, :], sp_bl[:, c, n * SUB:(n + 1) * SUB],
                [f"sp_bl_{c}_{n}", "z_kt"], ["z_kt"], eng="sync")
            V(lambda E: E.tensor_sub(z_p1[:, :], z_kt[:, :], pA[:, :]),
              ["z_kt", "pA"], ["z_p1"])
            V(lambda E: E.tensor_mul(z_p2[:, :], z_p1[:, :], pB[:, :]),
              ["z_p1", "pB"], ["z_p2"])
            S(lambda E, c=c: E.activation(out=a_ln[:, c, :], in_=z_p2[:, :],
                                          func=AF.Identity,
                                          scale=p_lng[:, c:c + 1],
                                          bias=p_lnb[:, c:c + 1]),
              ["z_p2", "p_lng", "p_lnb"], ["a_ln"])
        for m in range(C):
            ps1, pn1 = (pC_, "pC") if m % 2 == 0 else (pD, "pD")
            wmm(ps1, pn1, a_wo, "a_wo", m, lambda kc: a_ln[:, kc, :],
                ["a_ln"], range(C), "bo")
            spb, sptag = (p_sp16a, "p_sp16a") if m % 2 == 0 \
                else (p_sp16b, "p_sp16b")
            V(lambda E, m=m, ps1=ps1, t0=t0, spb=spb: E.tensor_add(
                spb[:, :], ps1[:, :], p_xn[:, m, t0:t0 + SUB]),
              [pn1, "p_xn"], [sptag])
            dma(sp_s[:, m, n * SUB:(n + 1) * SUB], spb[:, :], [sptag],
                [f"sp_s_{m}_{n}"])
    b.barrier()
    tap_dram("pos", sp_pos, [128, C, TC], F32,
             [f"sp_pos_{c}_{n}" for c in range(C) for n in range(NSUB)])
    tap_dram("kv", sp_kv, [128, C, TC], F32,
             [f"sp_kv_{c}_{n}" for c in range(C) for n in range(NSUB)])
    tap_dram("bl", sp_bl, [128, C, TC], F32,
             [f"sp_bl_{c}_{n}" for c in range(C) for n in range(NSUB)])
    tap_dram("s", sp_s, [128, C, TC], BF16,
             [f"sp_s_{c}_{n}" for c in range(C) for n in range(NSUB)])
    tap("mu2", r_mu2[0:1, :], ["r_mu2"])
    tap("r2", r_r2[0:1, :], ["r_r2"])
    tap("bw1b", r_bw1[0:1, :], ["r_bw1"])

    S(lambda E: E.preload_act_table(AF.Silu), [], [])
    for c in range(C):
        dma(sp_xn[:, c, :], p_xn[:, c, 1:TC + 1], ["p_xn"], [f"sp_xn_{c}"])
    for n in range(NSUB):
        t0 = 1 + n * SUB
        bcast16(pA, "pA", r_rho[0:1, n * SUB:(n + 1) * SUB], ["r_rho"])
        for c in range(C):
            V(lambda E, c=c, t0=t0: E.tensor_mul(
                z_p1[:, :], p_xn[:, c, t0:t0 + SUB], pA[:, :]),
              ["p_xn", "pA", f"sp_xn_{c}"], ["z_p1"])
            V(lambda E, c=c, t0=t0: E.tensor_scalar(
                p_xn[:, c, t0:t0 + SUB], z_p1[:, :], p_nsc[:, c:c + 1], None,
                ALU.mult), ["z_p1", "p_nsc"], ["p_xn"])
    for nt in range(NTT):
        t0 = 1 + nt * 128
        psg, psgn = (pE, "pE") if nt % 2 == 0 else (pF, "pF")
        for kc in range(C):
            T(lambda E, kc=kc, t0=t0, f=(kc == 0), psg=psg: E.matmul(
                psg[0:128, 0:NE], p_xn[:, kc, t0:t0 + 128],
                p_gw[:, kc, :], start=f, stop=False),
              ["p_xn", "p_gw"], [psgn])
        o_gb, _n2 = BOFF["gate_b"]
        T(lambda E, o=o_gb, psg=psg: E.matmul(
            psg[0:128, 0:NE], p_ones_r[0:1, 0:128],
            p_bias[:, o:o + NE], start=False, stop=True),
          ["ones", "p_bias"], [psgn])
        S(lambda E, nt=nt, psg=psg: E.activation(
            out=p_gl[:, nt * NE:(nt + 1) * NE], in_=psg[0:128, 0:NE],
            func=AF.Copy), [psgn], ["p_gl"])
    b.barrier()

    def sl_(nt):
        return slice(nt * NE, (nt + 1) * NE)
    for nt in range(NTT):
        V(lambda E, nt=nt: E.reduce_max(p_m1c[:, nt:nt + 1],
                                        p_gl[:, sl_(nt)], axis=AX),
          ["p_gl"], ["p_m1c"])
    for nt in range(NTT):
        S(lambda E, nt=nt: E.activation(out=p_nm1f[:, nt:nt + 1],
                                        in_=p_m1c[:, nt:nt + 1],
                                        func=AF.Identity, scale=col(C_NEG1)),
          ["p_m1c", "p_cst"], ["p_nm1f"])
    for nt in range(NTT):
        S(lambda E, nt=nt: E.activation(out=p_t2a[:, sl_(nt)],
                                        in_=p_gl[:, sl_(nt)],
                                        func=AF.Identity,
                                        bias=p_nm1f[:, nt:nt + 1]),
          ["p_gl", "p_nm1f"], ["p_t2a"])
    for nt in range(NTT):
        S(lambda E, nt=nt: E.activation(out=p_t2b[:, sl_(nt)],
                                        in_=p_t2a[:, sl_(nt)],
                                        func=AF.Sign), ["p_t2a"], ["p_t2b"])
    for nt in range(NTT):
        S(lambda E, nt=nt: E.activation(out=p_t2a[:, sl_(nt)],
                                        in_=p_t2b[:, sl_(nt)],
                                        func=AF.Square), ["p_t2b"], ["p_t2a"])
    for nt in range(NTT):
        S(lambda E, nt=nt: E.activation(out=p_gw1[:, sl_(nt)],
                                        in_=p_t2a[:, sl_(nt)],
                                        func=AF.Identity, scale=col(C_NEG1),
                                        bias=col(C_ONE)),
          ["p_t2a", "p_cst"], ["p_gw1"])
    for nt in range(NTT):
        V(lambda E, nt=nt: E.scalar_tensor_tensor(
            p_t2a[:, sl_(nt)], p_gw1[:, sl_(nt)], -30000.0,
            p_gl[:, sl_(nt)], ALU.mult, ALU.add),
          ["p_gw1", "p_gl"], ["p_t2a"])
    S(lambda E: E.nop(), ["p_t2a"], ["rt_t2"])
    for nt in range(NTT):
        V(lambda E, nt=nt: E.reduce_max(p_m2c[:, nt:nt + 1],
                                        p_t2a[:, sl_(nt)], axis=AX),
          ["p_t2a", "rt_t2"], ["p_m2c"])
    for nt in range(NTT):
        S(lambda E, nt=nt: E.activation(out=p_nm2f[:, nt:nt + 1],
                                        in_=p_m2c[:, nt:nt + 1],
                                        func=AF.Identity, scale=col(C_NEG1)),
          ["p_m2c", "p_cst"], ["p_nm2f"])
    for nt in range(NTT):
        S(lambda E, nt=nt: E.activation(out=p_t2a[:, sl_(nt)],
                                        in_=p_gl[:, sl_(nt)],
                                        func=AF.Identity,
                                        bias=p_nm2f[:, nt:nt + 1]),
          ["p_gl", "p_nm2f"], ["p_t2a"])
    for nt in range(NTT):
        S(lambda E, nt=nt: E.activation(out=p_t2b[:, sl_(nt)],
                                        in_=p_t2a[:, sl_(nt)],
                                        func=AF.Sign), ["p_t2a"], ["p_t2b"])
    for nt in range(NTT):
        S(lambda E, nt=nt: E.activation(out=p_t2a[:, sl_(nt)],
                                        in_=p_t2b[:, sl_(nt)],
                                        func=AF.Square), ["p_t2b"], ["p_t2a"])
    for nt in range(NTT):
        S(lambda E, nt=nt: E.activation(out=p_t2b[:, sl_(nt)],
                                        in_=p_t2a[:, sl_(nt)],
                                        func=AF.Identity, scale=col(C_NEG1),
                                        bias=col(C_ONE)),
          ["p_t2a", "p_cst"], ["p_t2b"])
    for nt in range(NTT):
        S(lambda E, nt=nt: E.activation(out=p_nmh2[:, nt:nt + 1],
                                        in_=p_m2c[:, nt:nt + 1],
                                        func=AF.Identity, scale=col(C_NEGH)),
          ["p_m2c", "p_cst"], ["p_nmh2"])
    for nt in range(NTT):
        S(lambda E, nt=nt: E.activation(out=p_nmh1[:, nt:nt + 1],
                                        in_=p_m1c[:, nt:nt + 1],
                                        func=AF.Identity, scale=col(C_NEGH)),
          ["p_m1c", "p_cst"], ["p_nmh1"])
    for nt in range(NTT):
        S(lambda E, nt=nt: E.activation(out=p_th1[:, nt:nt + 1],
                                        in_=p_m1c[:, nt:nt + 1],
                                        func=AF.Tanh, scale=col(C_HALF),
                                        bias=p_nmh2[:, nt:nt + 1]),
          ["p_m1c", "p_nmh2", "p_cst"], ["p_th1"])
    for nt in range(NTT):
        S(lambda E, nt=nt: E.activation(out=p_th2[:, nt:nt + 1],
                                        in_=p_m2c[:, nt:nt + 1],
                                        func=AF.Tanh, scale=col(C_HALF),
                                        bias=p_nmh1[:, nt:nt + 1]),
          ["p_m2c", "p_nmh1", "p_cst"], ["p_th2"])
    for nt in range(NTT):
        S(lambda E, nt=nt: E.activation(out=p_p1k[:, nt:nt + 1],
                                        in_=p_th1[:, nt:nt + 1],
                                        func=AF.Identity, scale=col(C_HALF),
                                        bias=col(C_HALF)),
          ["p_th1", "p_cst"], ["p_p1k"])
    for nt in range(NTT):
        S(lambda E, nt=nt: E.activation(out=p_p2k[:, nt:nt + 1],
                                        in_=p_th2[:, nt:nt + 1],
                                        func=AF.Identity, scale=col(C_HALF),
                                        bias=col(C_HALF)),
          ["p_th2", "p_cst"], ["p_p2k"])
    for nt in range(NTT):
        V(lambda E, nt=nt: E.tensor_scalar(p_t2a[:, sl_(nt)],
                                           p_gw1[:, sl_(nt)],
                                           p_p1k[:, nt:nt + 1], None,
                                           ALU.mult),
          ["p_gw1", "p_p1k"], ["p_t2a"])
    for nt in range(NTT):
        V(lambda E, nt=nt: E.tensor_scalar(p_gw1[:, sl_(nt)],
                                           p_t2b[:, sl_(nt)],
                                           p_p2k[:, nt:nt + 1], None,
                                           ALU.mult),
          ["p_t2b", "p_p2k"], ["p_gw1"])
    V(lambda E: E.tensor_add(p_gw1[:, :], p_gw1[:, :], p_t2a[:, :]),
      ["p_gw1", "p_t2a"], ["p_gw1"])
    tap("gl0", p_gl[:, 0:NE], ["p_gl"])
    tap("m1c", p_m1c[:, :], ["p_m1c"])
    tap("m2c", p_m2c[:, :], ["p_m2c"])
    tap("p1k", p_p1k[:, :], ["p_p1k"])
    tap("gw10", p_gw1[:, 0:NE], ["p_gw1"])
    b.barrier()
    for nt in range(NTT):
        psg, psgn = (pE, "pE") if nt % 2 == 0 else (pF, "pF")
        T(lambda E, nt=nt, psg=psg: E.transpose(
            psg[0:NE, 0:128], p_gw1[:, nt * NE:(nt + 1) * NE],
            p_identf[:, :]),
          ["p_gw1", "p_identf"], [psgn])
        S(lambda E, nt=nt, psg=psg: E.activation(
            out=p_wrow[:, nt * 128:(nt + 1) * 128], in_=psg[0:NE, 0:128],
            func=AF.Copy), [psgn], ["p_wrow"])
    tap("wrow", p_wrow[:, :], ["p_wrow"])
    b.barrier()
    V(lambda E: E.memset(f_acc[:, :, :], 0.0), [], ["f_acc"])
    for e in range(NE):
        for half in range(2):
            hbase = half * (TC // 2)
            for jj in range(JC):
                for c in range(C):
                    dma(f_w1[:, c, :],
                        m1w[e, c * 128:(c + 1) * 128,
                            jj * 256:(jj + 1) * 256],
                        ["f_w1"] if c == 0 else [], ["f_w1"])
                dma(f_b1[0:1, :], m1b[e, 0:1, jj * 256:(jj + 1) * 256],
                    ["f_b1"], ["f_b1"])
                for n2 in range(2):
                    t0 = 1 + hbase + n2 * SUB
                    for kc in range(C):
                        T(lambda E, kc=kc, t0=t0, f=(kc == 0): E.matmul(
                            pA[:, 0:SUB], f_w1[:, kc, 0:256:2],
                            p_xn[:, kc, t0:t0 + SUB], start=f, stop=False),
                          ["f_w1", "p_xn"], ["pA"])
                    T(lambda E: E.matmul(pA[:, 0:SUB], f_b1[0:1, 0:256:2],
                                         p_ones_r[:, 0:SUB], start=False,
                                         stop=True), ["f_b1", "ones"], ["pA"])
                    for kc in range(C):
                        T(lambda E, kc=kc, t0=t0, f=(kc == 0): E.matmul(
                            pB[:, 0:SUB], f_w1[:, kc, 1:256:2],
                            p_xn[:, kc, t0:t0 + SUB], start=f, stop=False),
                          ["f_w1", "p_xn"], ["pB"])
                    T(lambda E: E.matmul(pB[:, 0:SUB], f_b1[0:1, 1:256:2],
                                         p_ones_r[:, 0:SUB], start=False,
                                         stop=True), ["f_b1", "ones"], ["pB"])
                    S(lambda E: E.activation(out=f_swz3[:, :], in_=pA[:, :],
                                             func=AF.Silu,
                                             scale=col(C_ALPHA)),
                      ["pA", "p_cst"], ["f_swz3"])
                    V(lambda E: E.tensor_scalar(f_swz4[:, :], f_swz3[:, :],
                                                D7, None, ALU.min),
                      ["f_swz3"], ["f_swz4"])
                    V(lambda E: E.tensor_scalar(f_swz[:, :], pB[:, :], LIMIT,
                                                -LIMIT, ALU.min, ALU.max),
                      ["pB"], ["f_swz"])
                    V(lambda E: E.tensor_scalar(f_swz2[:, :], f_swz[:, :],
                                                1.0, 1.0 / ALPHA, ALU.add,
                                                ALU.mult),
                      ["f_swz"], ["f_swz2"])
                    V(lambda E, jj=jj, n2=n2: E.tensor_mul(
                        f_h[:, jj, n2 * SUB:(n2 + 1) * SUB], f_swz4[:, :],
                        f_swz2[:, :]), ["f_swz4", "f_swz2"], ["f_h"])
            dma(f_b2[0:1, :], m2b[e, 0:1, :], ["f_b2"], ["f_b2"])
            for n2 in range(2):
                n = half * 2 + n2
                T(lambda E, e=e, n=n, n2=n2: E.matmul(
                    pE[:, 0:SUB], p_sel[0:NE, e * 128:(e + 1) * 128],
                    p_wrow[0:NE, n * SUB:(n + 1) * SUB],
                    start=True, stop=True), ["p_sel", "p_wrow"], ["pE"])
                V(lambda E, n2=n2: E.tensor_copy(f_bwe[:, n2, :], pE[:, :]),
                  ["pE"], ["f_bwe"])
            for m in range(C):
                for kc in range(JC):
                    dma(f_w2m[:, kc, :],
                        m2w[e, kc * 128:(kc + 1) * 128,
                            m * 128:(m + 1) * 128],
                        ["f_w2m"] if kc == 0 else [], ["f_w2m"])
                for n2 in range(2):
                    n = half * 2 + n2
                    ps1, pn1 = (pC_, "pC") if n2 == 0 else (pD, "pD")
                    for kc in range(JC):
                        T(lambda E, kc=kc, n2=n2, ps1=ps1, f=(kc == 0):
                          E.matmul(ps1[:, 0:SUB], f_w2m[:, kc, :],
                                   f_h[:, kc, n2 * SUB:(n2 + 1) * SUB],
                                   start=f, stop=False),
                          ["f_w2m", "f_h"], [pn1])
                    T(lambda E, m=m, ps1=ps1: E.matmul(
                        ps1[:, 0:SUB], f_b2[:, m * 128:(m + 1) * 128],
                        p_ones_r[:, 0:SUB], start=False, stop=True),
                      ["f_b2", "ones"], [pn1])
                    V(lambda E, n2=n2, ps1=ps1: E.tensor_mul(
                        f_swz[:, :], ps1[:, :], f_bwe[:, n2, :]),
                      [pn1, "f_bwe"], ["f_swz"])
                    V(lambda E, m=m, n=n: E.tensor_add(
                        f_acc[:, m, n * SUB:(n + 1) * SUB],
                        f_acc[:, m, n * SUB:(n + 1) * SUB], f_swz[:, :]),
                      ["f_swz", "f_acc"], ["f_acc"])
    b.barrier()
    tap("acc", f_acc[:, :, :], ["f_acc"])
    tap("t_fm", p_xn[:, :, :], ["p_xn"])

    for c in range(C):
        dma(p_xn[:, c, 1:TC + 1], sp_xn[:, c, :], [f"sp_xn_{c}", "p_xn"],
            ["p_xn"])
    for n in range(NSUB):
        t0 = 1 + n * SUB
        for c in range(C):
            dma(g_x[:, c, :], xT[c * 128:(c + 1) * 128, OWN, t0:t0 + SUB],
                ["g_x"], ["g_x"], eng="sync")
            dma(g_s[:, c, :], sp_s[:, c, n * SUB:(n + 1) * SUB],
                [f"sp_s_{c}_{n}", "g_s"], ["g_s"], eng="sync")
        V(lambda E, t0=t0, n=n: E.tensor_add(
            g_o[:, :, :], p_xn[:, :, t0:t0 + SUB],
            f_acc[:, :, n * SUB:(n + 1) * SUB]),
          ["p_xn", "f_acc"], ["g_o"])
        S(lambda E: E.activation(out=g_g[:, :, :], in_=g_o[:, :, :],
                                 func=AF.Silu), ["g_o"], ["g_g"])
        V(lambda E: E.tensor_mul(g_o[:, :, :], g_s[:, :, :], g_g[:, :, :]),
          ["g_s", "g_g"], ["g_o"])
        V(lambda E: E.tensor_add(g_x[:, :, :], g_x[:, :, :], g_o[:, :, :]),
          ["g_x", "g_o"], ["g_x"])
        dma(out[:, :, n * SUB:(n + 1) * SUB], g_x[:, :, :], ["g_x"],
            [f"out_{n}"], eng="sync")

    for semname in ("d_sync", "d_gpsimd", "d_scalar"):
        if b.cnt.get(semname, 0) > 0:
            b.ops["sync"].append(("w", semname, b.cnt[semname]))
    b.finalize_and_emit()
    return nc


def kernel(**inputs):
    x = np.asarray(inputs["L0_attn_x"])
    pp = np.asarray(inputs["L0_attn_pos_phases"])
    L = x.shape[1]
    xT_full = np.ascontiguousarray(x[0].T)
    ppT_full = np.ascontiguousarray(pp.T)

    def W(name):
        return np.ascontiguousarray(np.asarray(inputs[name], np.float32))

    def colsfmt(v):
        return np.ascontiguousarray(np.asarray(v, np.float32)
                                    .reshape(C, 128).T)

    consts = np.zeros((128, NCST), np.float32)
    consts[:, C_SINB] = 64 * math.pi
    consts[:, C_COSB] = 64 * math.pi + math.pi / 2
    consts[:, C_PI] = math.pi
    consts[:, C_NPI] = -math.pi
    consts[:, C_HPI] = math.pi / 2
    consts[:, C_ISQ2] = 1.0 / math.sqrt(2.0)
    consts[:, C_ALPHA] = ALPHA
    consts[:, C_NEG1] = -1.0
    consts[:, C_ONE] = 1.0
    consts[:, C_HALF] = 0.5
    consts[:, C_NEGH] = -0.5

    sel = np.zeros((NE, NE * 128), np.float32)
    for e in range(NE):
        sel[e, e * 128:(e + 1) * 128] = 1.0

    brow = np.zeros((1, 5664), np.float32)

    def put(name, v):
        o, n = BOFF[name]
        brow[0, o:o + n] = np.asarray(v, np.float32).reshape(-1)

    put("bpv", inputs["L0_attn_bpv"]); put("bk", inputs["L0_attn_bk"])
    put("bkv", inputs["L0_attn_bkv"]); put("bg1", inputs["L0_attn_bg1"])
    put("bo", inputs["L0_attn_bo"]); put("bb1", inputs["L0_attn_bb1"])
    put("gate_b", inputs["L0_ffn_gate_b"]); put("bb2", inputs["L0_attn_bb2"])
    put("bg2", inputs["L0_attn_bg2"])

    shared = {
        "Wpv": W("L0_attn_Wpv"), "Wk": W("L0_attn_Wk"),
        "Wkv": W("L0_attn_Wkv"), "Wg1": W("L0_attn_Wg1"),
        "Wg2": W("L0_attn_Wg2"), "Wb1": W("L0_attn_Wb1"),
        "Wb2": W("L0_attn_Wb2"), "Wo": W("L0_attn_Wo"),
        "biasrow": brow,
        "ln_g": colsfmt(inputs["L0_attn_ln_g"]),
        "ln_b": colsfmt(inputs["L0_attn_ln_b"]),
        "nsc": colsfmt(inputs["L0_ffn_norm_scale"]),
        "gate_w": W("L0_ffn_gate_w"),
        "m1w": W("L0_ffn_mlp1_w"),
        "m1b": W("L0_ffn_mlp1_b").reshape(NE, 1, 2 * JC * 128),
        "m2w": W("L0_ffn_mlp2_w"),
        "m2b": W("L0_ffn_mlp2_b").reshape(NE, 1, D),
        "selw": sel, "cst": consts,
        "identw": np.eye(128, dtype=np.float32),
    }

    in_maps = []
    for c in range(NCORES):
        order = [k for k in range(NCORES) if k != c] + [c]
        xT_sl = np.zeros((D, NS, TC + 1), np.float32)
        ppT_sl = np.zeros((D, NS, TC), np.float32)
        cm = np.zeros((128, NS), np.float32)
        pm = np.zeros((128, NS), np.float32)
        for jslot, k in enumerate(order):
            t0 = k * TC
            if t0 > 0:
                xT_sl[:, jslot, 0] = xT_full[:, t0 - 1]
                pm[:, jslot] = 1.0
            xT_sl[:, jslot, 1:] = xT_full[:, t0:t0 + TC]
            ppT_sl[:, jslot, :] = ppT_full[:, t0:t0 + TC]
            if k < c:
                cm[:, jslot] = 1.0
        m = dict(shared)
        m["xT"] = xT_sl
        m["ppT"] = ppT_sl
        m["cmask"] = cm
        m["pmask"] = pm
        in_maps.append(m)

    nc = build_program()
    res = run_bass_kernel_spmd(nc, in_maps, core_ids=list(range(NCORES)))
    global LAST_RES
    LAST_RES = res
    outs = []
    for c in range(NCORES):
        o = np.asarray(res.results[c]["out"])
        outs.append(np.transpose(o, (2, 1, 0)).reshape(TC, D))
    return np.concatenate(outs, axis=0).reshape(1, L, D).astype(np.float32)


# version 03
# speedup vs baseline: 1.6944x; 1.6944x over previous
import math
import numpy as np
import concourse.bass as bass
import concourse.mybir as mybir
from concourse.bass_utils import run_bass_kernel_spmd

F32 = mybir.dt.float32
BF16 = mybir.dt.bfloat16
I32 = mybir.dt.int32
AF = mybir.ActivationFunctionType
ALU = mybir.AluOpType
AX = mybir.AxisListType.X

NCORES = 8
D = 1024
C = 8
TC = 2048
NS = 8
SUB = 512
NSUB = TC // SUB
NE = 8
JC = 16
OWN = NS - 1
TWO_PI = 2.0 * math.pi
EPS = 1e-5
ALPHA = 1.702
LIMIT = 7.0
D7 = ALPHA * LIMIT * (1.0 / (1.0 + math.exp(-ALPHA * LIMIT)))

(C_SINB, C_COSB, C_PI, C_NPI, C_HPI, C_ISQ2, C_ALPHA, C_NEG1, C_ONE,
 C_HALF, C_NEGH) = range(11)
NCST = 11

ENGINES = ("sync", "scalar", "vector", "tensor", "gpsimd")

TAPS = set()
LAST_RES = None

BOFF = {"bpv": (0, D), "bk": (D, D), "bkv": (2 * D, D), "bg1": (3 * D, D),
        "bo": (4 * D, D), "bb1": (5 * D, D // 2), "gate_b": (5 * D + 512, NE),
        "bb2": (5 * D + 520, 2), "bg2": (5 * D + 522, 1)}


class Build:
    def __init__(self, nc):
        self.nc = nc
        self.ops = {e: [] for e in ENGINES}
        self.cnt = {}
        self.waited = {}
        self.buf_w = {}
        self.buf_r = {}
        for e in ENGINES:
            self.cnt["p_" + e] = 0
        for e in ("sync", "gpsimd", "scalar"):
            self.cnt["d_" + e] = 0

    def _wait(self, eng, semname, val):
        if val <= 0:
            return
        if semname == "p_" + eng:
            return
        if self.waited.get((eng, semname), 0) >= val:
            return
        self.waited[(eng, semname)] = val
        self.ops[eng].append(("w", semname, val))

    def emit(self, eng, fn, reads=(), writes=(), dma=False, waw=True):
        for bn in reads:
            for sn, v in (self.buf_w.get(bn) or {}).items():
                self._wait(eng, sn, v)
        for bn in writes:
            if waw:
                for sn, v in (self.buf_w.get(bn) or {}).items():
                    self._wait(eng, sn, v)
            for sn, v in (self.buf_r.get(bn) or {}).items():
                self._wait(eng, sn, v)
        sem = ("d_" + eng) if dma else ("p_" + eng)
        inc = 16 if dma else 1
        self.cnt[sem] += inc
        marker = (sem, self.cnt[sem])
        self.ops[eng].append(("i", fn, sem, inc))
        for bn in writes:
            if waw:
                self.buf_w[bn] = {marker[0]: marker[1]}
                self.buf_r[bn] = {}
            else:
                d = self.buf_w.setdefault(bn, {})
                if d.get(marker[0], 0) < marker[1]:
                    d[marker[0]] = marker[1]
        for bn in reads:
            d = self.buf_r.setdefault(bn, {})
            if d.get(marker[0], 0) < marker[1]:
                d[marker[0]] = marker[1]
        return marker

    def barrier(self):
        vals = dict(self.cnt)
        for e in ENGINES:
            for sn, v in vals.items():
                self._wait(e, sn, v)

    def finalize_and_emit(self):
        nc = self.nc
        import contextlib
        with contextlib.ExitStack() as st:
            handles = {}
            for name in self.cnt.keys():
                handles[name] = st.enter_context(nc.semaphore(name))
            block = st.enter_context(nc.Block())
            for eng_name in ENGINES:
                ops = self.ops[eng_name]

                def make(ops=ops):
                    def body(E):
                        for op in ops:
                            if op[0] == "w":
                                E.wait_ge(handles[op[1]], op[2])
                            else:
                                op[1](E).then_inc(handles[op[2]], op[3])
                    return body
                getattr(block, eng_name)(make())


def build_program():
    nc = bass.Bass("TRN2", target_bir_lowering=False)

    def din(name, shape, dt=F32):
        return nc.dram_tensor(name, shape, dt, kind="ExternalInput").ap()

    xT = din("xT", [D, NS, TC + 1])
    ppT = din("ppT", [D, NS, TC])
    Wpv = din("Wpv", [D, D]); Wk = din("Wk", [D, D]); Wkv = din("Wkv", [D, D])
    Wg1 = din("Wg1", [2 * D, D]); Wg2 = din("Wg2", [D, 1])
    Wb1 = din("Wb1", [D, D // 2]); Wb2 = din("Wb2", [D // 2, 2])
    Wo = din("Wo", [D, D])
    biasrow = din("biasrow", [1, 5664])
    ln_g = din("ln_g", [128, C]); ln_b = din("ln_b", [128, C])
    nsc = din("nsc", [128, C])
    gate_w = din("gate_w", [D, NE])
    m1w = din("m1w", [NE, D, 2 * JC * 128])
    m1bT = din("m1bT", [NE, 2, 128, JC])
    m2w = din("m2w", [NE, JC * 128, D])
    m2bT = din("m2bT", [NE, 128, C])
    selw = din("selw", [NE, NE * 128])
    cmask = din("cmask", [128, NS])
    pmask = din("pmask", [128, NS])
    cst = din("cst", [128, NCST])
    identw = din("identw", [128, 128])

    out = nc.dram_tensor("out", [128, C, TC], F32, kind="ExternalOutput").ap()

    sp_xn = nc.dram_tensor("sp_xn", [128, C, TC], BF16, kind="Internal").ap()
    sp_s = nc.dram_tensor("sp_s", [128, C, TC], BF16, kind="Internal").ap()
    sp_pos = nc.dram_tensor("sp_pos", [128, C, TC], F32, kind="Internal").ap()
    sp_kv = nc.dram_tensor("sp_kv", [128, C, TC], F32, kind="Internal").ap()
    sp_bl = nc.dram_tensor("sp_bl", [128, C, TC], F32, kind="Internal").ap()
    m1wb = nc.dram_tensor("m1wb", [NE, D, 2 * JC * 128], BF16,
                          kind="Internal").ap()
    m2wb = nc.dram_tensor("m2wb", [NE, JC * 128, D], BF16,
                          kind="Internal").ap()

    FREE0 = ((nc.sbuf_base + 31) // 32) * 32
    TOP = nc.sbuf_top
    off = [FREE0]

    def alloc(name, shape, dt, at=None):
        nbytes = int(np.prod(shape[1:])) * mybir.dt.size(dt)
        if at is None:
            at = off[0]
            off[0] = at + ((nbytes + 31) // 32) * 32
        t = nc.alloc_sbuf_tensor_at(name, list(shape), dt, offset=at)
        assert at + nbytes <= TOP, f"SBUF overflow {name}: {at + nbytes} > {TOP}"
        return t

    p_cst = alloc("p_cst", [128, NCST], F32)
    p_cmask = alloc("p_cmask", [128, NS], F32)
    p_pmask = alloc("p_pmask", [128, NS], F32)
    p_lng = alloc("p_lng", [128, C], F32)
    p_lnb = alloc("p_lnb", [128, C], F32)
    p_nsc = alloc("p_nsc", [128, C], F32)
    p_gw = alloc("p_gw", [128, C, NE], BF16)
    p_wg2 = alloc("p_wg2", [128, C, 1], BF16)
    p_wb2 = alloc("p_wb2", [128, 4, 2], BF16)
    p_identf = alloc("p_identf", [128, 128], F32)
    p_ones_r = alloc("p_ones_r", [1, SUB + 128], BF16)
    p_ones_rf = alloc("p_ones_rf", [1, 256], F32)
    p_ones_c = alloc("p_ones_c", [128, 1], F32)
    p_ones_sc = alloc("p_ones_sc", [128, SUB], F32)
    p_sum_pc = alloc("p_sum_pc", [128, C, NS], F32)
    p_sum_ps = alloc("p_sum_ps", [128, C, NS], F32)
    p_sum_kc = alloc("p_sum_kc", [128, C, NS], F32)
    p_sum_ks = alloc("p_sum_ks", [128, C, NS], F32)
    p_sum_vg = alloc("p_sum_vg", [128, NS * NSUB], F32)
    p_vgm = alloc("p_vgm", [128, NS * NSUB], F32)
    p_car_pc = alloc("p_car_pc", [128, C], F32)
    p_car_ps = alloc("p_car_ps", [128, C], F32)
    p_car_kc = alloc("p_car_kc", [128, C], F32)
    p_car_ks = alloc("p_car_ks", [128, C], F32)
    p_car_vg = alloc("p_car_vg", [128, 1], F32)
    p_scarry = alloc("p_scarry", [128, C, 4], F32)
    p_bnd = alloc("p_bnd", [128, C, 4], F32)
    p_acol = alloc("p_acol", [128, C], F32)
    p_acol2 = alloc("p_acol2", [128, C], F32)
    p_bias = alloc("p_bias", [1, 5664], BF16)
    NTT = TC // 128
    p_gl = alloc("p_gl", [128, NTT * NE], F32)
    p_gw1 = alloc("p_gw1", [128, NTT * NE], F32)
    p_t2a = alloc("p_t2a", [128, NTT * NE], F32)
    p_t2b = alloc("p_t2b", [128, NTT * NE], F32)
    p_m1c = alloc("p_m1c", [128, NTT], F32)
    p_m2c = alloc("p_m2c", [128, NTT], F32)
    p_nm1f = alloc("p_nm1f", [128, NTT], F32)
    p_nm2f = alloc("p_nm2f", [128, NTT], F32)
    p_nmh1 = alloc("p_nmh1", [128, NTT], F32)
    p_nmh2 = alloc("p_nmh2", [128, NTT], F32)
    p_th1 = alloc("p_th1", [128, NTT], F32)
    p_th2 = alloc("p_th2", [128, NTT], F32)
    p_p1k = alloc("p_p1k", [128, NTT], F32)
    p_p2k = alloc("p_p2k", [128, NTT], F32)
    f_be = alloc("f_be", [128, 2, JC], F32)
    f_beE = alloc("f_beE", [128, JC], F32)
    f_b2c = alloc("f_b2c", [128, C], F32)
    p_wrow = alloc("p_wrow", [NE, TC], BF16)
    p_sel = alloc("p_sel", [NE, NE * 128], BF16)
    r_gcs = alloc("r_gcs", [1, TC], BF16)
    r_bw0 = alloc("r_bw0", [1, TC], BF16)
    r_bw1 = alloc("r_bw1", [1, TC], BF16)
    r_mu2 = alloc("r_mu2", [1, TC], BF16)
    r_r2 = alloc("r_r2", [1, TC], BF16)
    r_rho = alloc("r_rho", [1, TC], BF16)
    r_vg = alloc("r_vg", [1, TC], F32)
    rf = [alloc(f"rf{i}", [1, SUB], F32) for i in range(4)]
    rwf = alloc("rwf", [1, TC], F32)
    r_ones = alloc("r_ones", [1, TC], F32)
    SP16_OFF = off[0]
    p_sp16a = alloc("p_sp16a", [128, SUB], BF16)
    p_sp16b = alloc("p_sp16b", [128, SUB], BF16)
    p_junk = alloc("p_junk", [128, SUB], F32, at=SP16_OFF)
    p_spfa = alloc("p_spfa", [128, SUB], F32)
    p_spfb = alloc("p_spfb", [128, SUB], F32)
    p_xn = alloc("p_xn", [128, C, TC + 1], BF16)

    A0 = off[0]
    ARENA = TOP - A0
    assert ARENA >= 88000, f"arena too small: {ARENA}"

    a_wpv = alloc("a_wpv", [128, C, D], BF16, at=A0)
    a_wk = alloc("a_wk", [128, C, D], BF16, at=A0 + 16384)
    a_wkv = alloc("a_wkv", [128, C, D], BF16, at=A0 + 32768)
    a_wg1 = alloc("a_wg1", [128, 2 * C, D], BF16, at=A0)
    a_wo = alloc("a_wo", [128, C, D], BF16, at=A0)
    a_wb1 = alloc("a_wb1", [128, C, D // 2], BF16, at=A0 + 32768)
    a_xsub = alloc("a_xsub", [128, C, SUB], F32, at=A0 + 32768)
    a_pc = alloc("a_pc", [128, C, SUB], BF16, at=A0 + 49152)
    a_ps = alloc("a_ps", [128, C, SUB], BF16, at=A0 + 57344)
    a_gh = alloc("a_gh", [128, C, SUB], BF16, at=A0 + 49152)
    a_bh = alloc("a_bh", [128, 4, SUB], BF16, at=A0 + 57344)
    a_ln = alloc("a_ln", [128, C, SUB], BF16, at=A0 + 49152)
    ZB = A0 + 65536
    z_tg1 = alloc("z_tg1", [128, SUB], I32, at=ZB)
    z_tg2 = alloc("z_tg2", [128, SUB], F32, at=ZB + 2048)
    z_kt = alloc("z_kt", [128, SUB], F32, at=ZB + 4096)
    z_ka = alloc("z_ka", [128, SUB], F32, at=ZB + 6144)
    z_kc = alloc("z_kc", [128, SUB + 1], F32, at=ZB + 8192)
    z_ks = alloc("z_ks", [128, SUB + 1], F32, at=ZB + 10304)
    z_p1 = alloc("z_p1", [128, SUB], F32, at=ZB + 12416)
    z_p2 = alloc("z_p2", [128, SUB], F32, at=ZB + 14464)
    z_m1 = alloc("z_m1", [128, SUB], F32, at=ZB + 16512)
    z_m2 = alloc("z_m2", [128, SUB], F32, at=ZB + 18560)
    z_sq = alloc("z_sq", [128, SUB], F32, at=ZB + 20608)
    z_x1 = alloc("z_x1", [128, SUB], F32, at=ZB + 20608)
    assert ZB + 22656 <= TOP

    f_h = alloc("f_h", [128, JC, TC // 2], BF16, at=A0)
    f_acc = alloc("f_acc", [128, C, TC], BF16, at=A0 + 32768)
    f_w1a = alloc("f_w1a", [128, 4, 256], BF16, at=A0 + 65536)
    f_w1b = alloc("f_w1b", [128, 4, 256], BF16, at=A0 + 67584)
    f_w2ma = alloc("f_w2ma", [128, 8, 128], BF16, at=A0 + 69632)
    f_w2mb = alloc("f_w2mb", [128, 8, 128], BF16, at=A0 + 71680)
    f_b1 = alloc("f_b1", [1, 256], BF16, at=A0 + 73728)
    f_b2 = alloc("f_b2", [1, D], BF16, at=A0 + 74240)
    f_bwe = alloc("f_bwe", [128, 2, SUB], F32, at=A0 + 76288)
    f_swz = alloc("f_swz", [128, SUB], F32, at=A0 + 80384)
    f_swz2 = alloc("f_swz2", [128, SUB], F32, at=A0 + 82432)
    f_swz3 = alloc("f_swz3", [128, SUB], BF16, at=A0 + 84480)
    f_swz4 = alloc("f_swz4", [128, SUB], BF16, at=A0 + 85504)
    assert A0 + 86528 <= TOP

    g_x = alloc("g_x", [128, C, SUB], F32, at=A0)
    g_g = alloc("g_g", [128, C, SUB], BF16, at=A0 + 16384)
    g_s = alloc("g_s", [128, C, SUB], BF16, at=A0 + 24576)
    g_o = alloc("g_o", [128, C, SUB], F32, at=A0 + 65536)
    assert A0 + 65536 + 16384 <= TOP

    pA = nc.alloc_psum_tensor("pA", [128, SUB], F32)
    pB = nc.alloc_psum_tensor("pB", [128, SUB], F32)
    pC_ = nc.alloc_psum_tensor("pC", [128, SUB], F32)
    pD = nc.alloc_psum_tensor("pD", [128, SUB], F32)
    pE = nc.alloc_psum_tensor("pE", [128, SUB], F32)
    pF = nc.alloc_psum_tensor("pF", [128, SUB], F32)
    pR0 = nc.alloc_psum_tensor("pR0", [1, SUB], F32)
    pR1 = nc.alloc_psum_tensor("pR1", [1, SUB], F32)

    b = Build(nc)

    def dma(dst_ap, src_ap, reads, writes, eng="gpsimd", slow=False,
            waw=True):
        b.emit(eng,
               lambda E, d=dst_ap, s=src_ap, sl=slow: E.dma_start(
                   out=d, in_=s, allow_slow_non_contiguous=sl),
               reads=reads, writes=writes, dma=True, waw=waw)

    def V(fn, reads, writes):
        b.emit("vector", fn, reads=reads, writes=writes)

    def S(fn, reads, writes):
        b.emit("scalar", fn, reads=reads, writes=writes)

    def T(fn, reads, writes):
        b.emit("tensor", fn, reads=reads, writes=writes)

    def col(i):
        return p_cst[:, i:i + 1]

    def tap(name, sbuf_ap, rtags):
        if name not in TAPS:
            return
        shp = list(sbuf_ap.shape)
        dt = sbuf_ap.dtype
        t_out = nc.dram_tensor("tap_" + name, shp, dt,
                               kind="ExternalOutput").ap()
        dma(t_out, sbuf_ap, rtags, [f"tap_{name}"], eng="sync")

    def tap_dram(name, dram_ap, shp, dt, rtags):
        if name not in TAPS:
            return
        t_out = nc.dram_tensor("tap_" + name, list(shp), dt,
                               kind="ExternalOutput").ap()
        dma(t_out, dram_ap, rtags, [f"tap_{name}"], eng="sync")

    def bias_lhsT(name, mchunk):
        o, n = BOFF[name]
        return p_bias[:, o + mchunk * 128:o + min((mchunk + 1) * 128, o + n)
                      if False else o + mchunk * 128 + 128] \
            if False else p_bias[:, o + mchunk * 128:o + mchunk * 128 + 128]

    dma(p_cst[:], cst, [], ["p_cst"], eng="sync")
    dma(p_cmask[:], cmask, [], ["p_cmask"], eng="sync")
    dma(p_pmask[:], pmask, [], ["p_pmask"], eng="sync")
    dma(p_lng[:], ln_g, [], ["p_lng"], eng="sync")
    dma(p_lnb[:], ln_b, [], ["p_lnb"], eng="sync")
    dma(p_nsc[:], nsc, [], ["p_nsc"], eng="sync")
    dma(p_identf[:], identw, [], ["p_identf"], eng="sync")
    dma(p_identf[:], identw, [], ["p_identf"])
    dma(p_bias[:], biasrow, [], ["p_bias"])
    dma(p_sel[:], selw, [], ["p_sel"])
    dma(p_wg2[:, :, 0], Wg2.rearrange("(c p) o -> p (c o)", p=128)
        if False else Wg2[0:D, 0:1], [], ["p_wg2"]) if False else None
    for c in range(C):
        dma(p_wg2[:, c, 0:1], Wg2[c * 128:(c + 1) * 128, 0:1], [], ["p_wg2"],
            slow=True)
    for c in range(4):
        dma(p_wb2[:, c, :], Wb2[c * 128:(c + 1) * 128, :], [], ["p_wb2"])
    for c in range(C):
        dma(p_gw[:, c, :], gate_w[c * 128:(c + 1) * 128, :], [], ["p_gw"])
    V(lambda E: E.memset(p_ones_r[0:1, 0:SUB], 1.0), [], ["ones"])
    V(lambda E: E.memset(p_ones_r[0:1, SUB:SUB + 128], 1.0 / 32.0), [],
      ["ones"])
    V(lambda E: E.memset(p_ones_rf[0:1, 0:128], 1.0), [], ["ones"])
    V(lambda E: E.memset(p_ones_rf[0:1, 128:256], 1.0 / 32.0), [], ["ones"])
    V(lambda E: E.memset(p_ones_c[:], 1.0), [], ["ones"])
    V(lambda E: E.memset(p_ones_sc[:], 1.0), [], ["ones"])

    def ln_sums(x_chunks_fn, rtags, nn=SUB):
        for kc in range(C):
            T(lambda E, kc=kc, f=(kc == 0), l=(kc == C - 1), nn=nn: E.matmul(
                pR0[0:1, 0:nn], p_ones_c[:, 0:1], x_chunks_fn(kc),
                start=f, stop=l), ["ones"] + rtags, ["pR0"])
        for kc in range(C):
            S(lambda E, kc=kc, nn=nn: E.activation(
                out=z_sq[:, 0:nn], in_=x_chunks_fn(kc), func=AF.Square),
              rtags, ["z_sq"])
            T(lambda E, kc=kc, f=(kc == 0), l=(kc == C - 1), nn=nn: E.matmul(
                pR1[0:1, 0:nn], p_ones_c[:, 0:1], z_sq[:, 0:nn],
                start=f, stop=l), ["ones", "z_sq"], ["pR1"])

    def ln_rows(nn=SUB):
        V(lambda E, nn=nn: E.tensor_scalar(rf[0][0:1, 0:nn], pR0[0:1, 0:nn],
                                           1.0 / D, None, ALU.mult),
          ["pR0"], ["rf0"])
        V(lambda E, nn=nn: E.tensor_scalar(rf[3][0:1, 0:nn], pR1[0:1, 0:nn],
                                           1.0 / D, None, ALU.mult),
          ["pR1"], ["rf3"])
        V(lambda E, nn=nn: E.tensor_mul(rf[1][0:1, 0:nn], rf[0][0:1, 0:nn],
                                        rf[0][0:1, 0:nn]), ["rf0"], ["rf1"])
        V(lambda E, nn=nn: E.tensor_sub(rf[2][0:1, 0:nn], rf[3][0:1, 0:nn],
                                        rf[1][0:1, 0:nn]),
          ["rf3", "rf1"], ["rf2"])
        V(lambda E, nn=nn: E.tensor_scalar(rf[2][0:1, 0:nn], rf[2][0:1, 0:nn],
                                           EPS, None, ALU.add),
          ["rf2"], ["rf2"])
        V(lambda E, nn=nn: E.reciprocal(rf[3][0:1, 0:nn], rf[2][0:1, 0:nn]),
          ["rf2"], ["rf3"])
        S(lambda E, nn=nn: E.activation(out=rf[1][0:1, 0:nn],
                                        in_=rf[3][0:1, 0:nn], func=AF.Sqrt),
          ["rf3"], ["rf1"])

    def bcast(psum, pname, row_ap, rtags, nn=SUB):
        T(lambda E, nn=nn: E.matmul(psum[:, 0:nn], p_ones_rf[0:1, 0:128],
                                    row_ap, start=True, stop=True),
          ["ones"] + rtags, [pname])

    def bcast16(psum, pname, row_ap, rtags, nn=SUB):
        T(lambda E, nn=nn: E.matmul(psum[:, 0:nn], p_ones_r[0:1, 0:128],
                                    row_ap, start=True, stop=True),
          ["ones"] + rtags, [pname])

    def trig(dst_ap, wtag, src_ap, rtag, kind, nn=SUB):
        add_c = 32.0 if kind == "sin" else 32.25
        bias_i = C_SINB if kind == "sin" else C_COSB
        V(lambda E: E.tensor_scalar(z_tg1[:, 0:nn], src_ap, 1.0 / TWO_PI,
                                    add_c, ALU.mult, ALU.add),
          [rtag], ["z_tg1"])
        V(lambda E: E.tensor_copy(z_tg2[:, 0:nn], z_tg1[:, 0:nn]),
          ["z_tg1"], ["z_tg2"])
        V(lambda E: E.scalar_tensor_tensor(z_tg2[:, 0:nn], z_tg2[:, 0:nn],
                                           -TWO_PI, src_ap, ALU.mult, ALU.add),
          ["z_tg2", rtag], ["z_tg2"])
        S(lambda E: E.activation(out=dst_ap, in_=z_tg2[:, 0:nn], func=AF.Sin,
                                 bias=col(bias_i)),
          ["z_tg2", "p_cst"], [wtag])

    def wmm(psum, pname, w_sb, wtag, mchunk, x_fn, xtags, kchunks, bias,
            extra_k=()):
        first = True
        for kc in kchunks:
            T(lambda E, kc=kc, f=first: E.matmul(
                psum[:, 0:SUB], w_sb[:, kc, mchunk * 128:(mchunk + 1) * 128],
                x_fn(kc), start=f, stop=False), [wtag] + xtags, [pname])
            first = False
        for (w2_sb, w2tag, kc2, x2_fn) in extra_k:
            T(lambda E, kc2=kc2, w2_sb=w2_sb, x2_fn=x2_fn: E.matmul(
                psum[:, 0:SUB], w2_sb[:, kc2, mchunk * 128:(mchunk + 1) * 128],
                x2_fn(kc2), start=False, stop=False),
              [w2tag] + xtags, [pname])
        o, _n = BOFF[bias]
        T(lambda E, o=o, m=mchunk: E.matmul(
            psum[:, 0:SUB], p_bias[:, o + m * 128:o + (m + 1) * 128],
            p_ones_r[:, 0:SUB], start=False, stop=True),
          ["p_bias", "ones"], [pname])

    def process_slot(j, own):
        S(lambda E: E.preload_act_table(AF.Sqrt), [], [])
        for n in range(NSUB):
            t0 = 1 + n * SUB
            for c in range(C):
                dma(a_xsub[:, c, :], xT[c * 128:(c + 1) * 128, j, t0:t0 + SUB],
                    ["a_xsub"], ["a_xsub"], eng="sync")
            ln_sums(lambda kc: a_xsub[:, kc, :], ["a_xsub"])
            ln_rows()
            bcast(pA, "pA", rf[0][0:1, 0:SUB], ["rf0"])
            bcast(pB, "pB", rf[1][0:1, 0:SUB], ["rf1"])
            for c in range(C):
                V(lambda E, c=c: E.tensor_sub(z_p1[:, :], a_xsub[:, c, :],
                                              pA[:, :]),
                  ["a_xsub", "pA"], ["z_p1"])
                V(lambda E, c=c, t0=t0: E.tensor_mul(
                    p_xn[:, c, t0:t0 + SUB], z_p1[:, :], pB[:, :]),
                  ["z_p1", "pB"], ["p_xn"])
            if own:
                V(lambda E: E.tensor_scalar(rf[3][0:1, :], rf[2][0:1, :], EPS,
                                            None, ALU.subtract),
                  ["rf2"], ["rf3"])
                V(lambda E: E.reciprocal(rf[0][0:1, :], rf[2][0:1, :]),
                  ["rf2"], ["rf0"])
                V(lambda E: E.tensor_mul(rf[2][0:1, :], rf[3][0:1, :],
                                         rf[0][0:1, :]), ["rf3", "rf0"],
                  ["rf2"])
                V(lambda E: E.tensor_scalar(rf[3][0:1, :], rf[2][0:1, :], EPS,
                                            None, ALU.add), ["rf2"], ["rf3"])
                V(lambda E: E.reciprocal(rf[0][0:1, :], rf[3][0:1, :]),
                  ["rf3"], ["rf0"])
                S(lambda E, n=n: E.activation(
                    out=r_rho[0:1, n * SUB:(n + 1) * SUB], in_=rf[0][0:1, :],
                    func=AF.Sqrt), ["rf0"], ["r_rho"])
        for c in range(C):
            dma(a_xsub[:, c, :], xT[c * 128:(c + 1) * 128, j, 0:SUB],
                ["a_xsub"], ["a_xsub"], eng="sync")
        ln_sums(lambda kc: a_xsub[:, kc, :], ["a_xsub"])
        ln_rows()
        bcast(pA, "pA", rf[0][0:1, 0:SUB], ["rf0"])
        bcast(pB, "pB", rf[1][0:1, 0:SUB], ["rf1"])
        for c in range(C):
            V(lambda E, c=c: E.tensor_sub(z_p1[:, 0:1], a_xsub[:, c, 0:1],
                                          pA[:, 0:1]), ["a_xsub", "pA"],
              ["z_p1"])
            V(lambda E, c=c: E.tensor_mul(p_xn[:, c, 0:1], z_p1[:, 0:1],
                                          pB[:, 0:1]), ["z_p1", "pB"],
              ["p_xn"])
        b.barrier()

        S(lambda E: E.preload_act_table(AF.Sigmoid), [], [])
        for c2 in range(2 * C):
            dma(a_wg1[:, c2, :], Wg1[c2 * 128:(c2 + 1) * 128, :],
                ["a_wg1"] if c2 == 0 else [], ["a_wg1"])
        if own:
            for c in range(C):
                dma(a_wb1[:, c, :], Wb1[c * 128:(c + 1) * 128, :],
                    ["a_wb1"] if c == 0 else [], ["a_wb1"])
        for n in range(NSUB):
            t0 = 1 + n * SUB
            for m in range(C):
                ps1, pn1 = (pA, "pA") if m % 2 == 0 else (pB, "pB")
                wmm(ps1, pn1, a_wg1, "a_wg1", m,
                    lambda kc, t0=t0: p_xn[:, kc, t0:t0 + SUB], ["p_xn"],
                    range(C), "bg1",
                    extra_k=[(a_wg1, "a_wg1", C + kc2,
                              (lambda kc2_, t0=t0: p_xn[:, kc2_ - C,
                                                        t0 - 1:t0 - 1 + SUB]))
                             for kc2 in range(C)])
                V(lambda E, ps1=ps1: E.tensor_copy(z_sq[:, :], ps1[:, :]),
                  [pn1], ["z_sq"])
                S(lambda E: E.activation(out=z_kt[:, :], in_=z_sq[:, :],
                                         func=AF.Erf, scale=col(C_ISQ2)),
                  ["z_sq", "p_cst"], ["z_kt"])
                V(lambda E: E.tensor_scalar(z_ka[:, :], z_kt[:, :], 0.5, 0.5,
                                            ALU.mult, ALU.add),
                  ["z_kt"], ["z_ka"])
                V(lambda E, m=m: E.tensor_mul(a_gh[:, m, :], z_sq[:, :],
                                              z_ka[:, :]),
                  ["z_sq", "z_ka"], ["a_gh"])
                if own and n == 0 and m == 0:
                    tap("z00", z_sq[:, :], ["z_sq"])
            if own and n == 0:
                tap("gh0", a_gh[:, :, :], ["a_gh"])
            for kc in range(C):
                T(lambda E, kc=kc, f=(kc == 0): E.matmul(
                    pR0[0:1, 0:SUB], p_wg2[:, kc, 0:1], a_gh[:, kc, :],
                    start=f, stop=False), ["p_wg2", "a_gh"], ["pR0"])
            o_bg2, _n2 = BOFF["bg2"]
            T(lambda E, o=o_bg2: E.matmul(pR0[0:1, 0:SUB],
                                          p_bias[:, o:o + 1],
                                          p_ones_r[:, 0:SUB],
                                          start=False, stop=True),
              ["p_bias", "ones"], ["pR0"])
            S(lambda E, n=n: E.activation(out=r_vg[0:1, n * SUB:(n + 1) * SUB],
                                          in_=pR0[0:1, :], func=AF.Sigmoid),
              ["pR0"], ["r_vg"])
            if own:
                for m in range(4):
                    ps1, pn1 = (pC_, "pC") if m % 2 == 0 else (pD, "pD")
                    wmm(ps1, pn1, a_wb1, "a_wb1", m,
                        lambda kc, t0=t0: p_xn[:, kc, t0:t0 + SUB], ["p_xn"],
                        range(C), "bb1")
                    V(lambda E, ps1=ps1: E.tensor_copy(z_sq[:, :], ps1[:, :]),
                      [pn1], ["z_sq"])
                    S(lambda E: E.activation(out=z_kt[:, :], in_=z_sq[:, :],
                                             func=AF.Erf, scale=col(C_ISQ2)),
                      ["z_sq", "p_cst"], ["z_kt"])
                    V(lambda E: E.tensor_scalar(z_ka[:, :], z_kt[:, :], 0.5,
                                                0.5, ALU.mult, ALU.add),
                      ["z_kt"], ["z_ka"])
                    V(lambda E, m=m: E.tensor_mul(a_bh[:, m, :], z_sq[:, :],
                                                  z_ka[:, :]),
                      ["z_sq", "z_ka"], ["a_bh"])
                o_bb2, _n2 = BOFF["bb2"]
                for zi in range(2):
                    for kc in range(4):
                        T(lambda E, kc=kc, zi=zi, f=(kc == 0):
                          E.matmul(pR1[0:1, 0:SUB], p_wb2[:, kc, zi:zi + 1],
                                   a_bh[:, kc, :], start=f, stop=False),
                          ["p_wb2", "a_bh"], ["pR1"])
                    T(lambda E, zi=zi, o=o_bb2: E.matmul(
                        pR1[0:1, 0:SUB], p_bias[:, o + zi:o + zi + 1],
                        p_ones_r[:, 0:SUB], start=False, stop=True),
                      ["p_bias", "ones"], ["pR1"])
                    V(lambda E, zi=zi: E.tensor_copy(
                        rf[0 if zi == 0 else 3][0:1, :], pR1[0:1, :]),
                      ["pR1"], ["rf0" if zi == 0 else "rf3"])
                V(lambda E: E.tensor_sub(rf[1][0:1, :], rf[0][0:1, :],
                                         rf[3][0:1, :]),
                  ["rf0", "rf3"], ["rf1"])
                S(lambda E, n=n: E.activation(
                    out=r_bw0[0:1, n * SUB:(n + 1) * SUB], in_=rf[1][0:1, :],
                    func=AF.Sigmoid), ["rf1"], ["r_bw0"])
                S(lambda E, n=n: E.activation(
                    out=r_bw1[0:1, n * SUB:(n + 1) * SUB], in_=rf[1][0:1, :],
                    func=AF.Sigmoid, scale=p_cst[0:1, C_NEG1:C_NEG1 + 1]),
                  ["rf1", "p_cst"], ["r_bw1"])
        b.barrier()

        S(lambda E: E.preload_act_table(AF.Silu), [], [])
        for c in range(C):
            dma(a_wpv[:, c, :], Wpv[c * 128:(c + 1) * 128, :],
                ["a_wpv"] if c == 0 else [], ["a_wpv"])
        for c in range(C):
            dma(a_wk[:, c, :], Wk[c * 128:(c + 1) * 128, :],
                ["a_wk"] if c == 0 else [], ["a_wk"])
        for c in range(C):
            dma(a_wkv[:, c, :], Wkv[c * 128:(c + 1) * 128, :],
                ["a_wkv"] if c == 0 else [], ["a_wkv"])
        for m in range(C):
            ps1, pn1 = (pA, "pA") if m % 2 == 0 else (pB, "pB")
            wmm(ps1, pn1, a_wk, "a_wk", m, lambda kc: p_xn[:, kc, 0:SUB],
                ["p_xn"], range(C), "bk")
            S(lambda E, ps1=ps1: E.activation(out=z_kt[:, :], in_=ps1[:, :],
                                              func=AF.Tanh), [pn1], ["z_kt"])
            V(lambda E, m=m: E.tensor_copy(p_bnd[:, m, 0:1], z_kt[:, 0:1]),
              ["z_kt"], ["p_bnd"])
        S(lambda E: E.activation(out=p_bnd[:, :, 3:4], in_=p_bnd[:, :, 0:1],
                                 func=AF.Abs), ["p_bnd"], ["p_bnd"])
        S(lambda E: E.activation(out=p_bnd[:, :, 2:3], in_=p_bnd[:, :, 0:1],
                                 func=AF.Sin, scale=col(C_PI)),
          ["p_bnd", "p_cst"], ["p_bnd"])
        S(lambda E: E.activation(out=p_bnd[:, :, 3:4], in_=p_bnd[:, :, 3:4],
                                 func=AF.Sin, scale=col(C_NPI),
                                 bias=col(C_HPI)), ["p_bnd", "p_cst"],
          ["p_bnd"]) if False else None
        S(lambda E: E.activation(out=z_ka[:, 0:C], in_=p_bnd[:, :, 3],
                                 func=AF.Sin, scale=col(C_NPI),
                                 bias=col(C_HPI)),
          ["p_bnd", "p_cst"], ["z_ka"])
        for m in range(C):
            V(lambda E, m=m, j=j: E.tensor_scalar(
                p_bnd[:, m, 0:1], p_bnd[:, m, 2:3], p_pmask[:, j:j + 1],
                None, ALU.mult), ["p_bnd", "p_pmask"], ["p_bnd"])
            V(lambda E, m=m, j=j: E.tensor_scalar(
                p_bnd[:, m, 1:2], z_ka[:, m:m + 1], p_pmask[:, j:j + 1],
                None, ALU.mult), ["z_ka", "p_pmask"], ["p_bnd"])
        if own:
            for dst, dtag, src, stag in (
                    (p_car_pc, "p_car_pc", p_sum_pc, "p_sum_pc"),
                    (p_car_ps, "p_car_ps", p_sum_ps, "p_sum_ps"),
                    (p_car_kc, "p_car_kc", p_sum_kc, "p_sum_kc"),
                    (p_car_ks, "p_car_ks", p_sum_ks, "p_sum_ks")):
                V(lambda E, dst=dst: E.memset(dst[:, :], 0.0), [], [dtag])
                for jj in range(NS - 1):
                    V(lambda E, dst=dst, src=src, jj=jj:
                      E.scalar_tensor_tensor(dst[:, :], src[:, :, jj],
                                             p_cmask[:, jj:jj + 1],
                                             dst[:, :], ALU.mult, ALU.add),
                      [stag, "p_cmask", dtag], [dtag])
            for jj in range(NS - 1):
                V(lambda E, jj=jj: E.tensor_scalar(
                    p_vgm[:, jj * NSUB:(jj + 1) * NSUB],
                    p_sum_vg[:, jj * NSUB:(jj + 1) * NSUB],
                    p_cmask[:, jj:jj + 1], None,
                    ALU.mult), ["p_sum_vg", "p_cmask"], ["p_vgm"])
            V(lambda E: E.memset(p_vgm[:, (NS - 1) * NSUB:NS * NSUB], 0.0),
              [], ["p_vgm"])
            S(lambda E: E.nop(), ["p_vgm"], ["rt_vg"])
            V(lambda E: E.reduce_sum(p_car_vg[:, 0:1], p_vgm[:, :], axis=AX),
              ["p_vgm", "rt_vg"], ["p_car_vg"])
            V(lambda E: E.memset(r_ones[0:1, :], 1.0), [], ["r_ones"])
            S(lambda E: E.nop(), ["r_ones"], ["rt_ro"])
            V(lambda E: E.tensor_tensor_scan(
                rwf[0:1, :], r_ones[0:1, :], r_vg[0:1, :], 0.0,
                ALU.mult, ALU.add), ["r_ones", "r_vg", "rt_ro"], ["rwf"])
            V(lambda E: E.tensor_scalar(r_gcs[0:1, :], rwf[0:1, :],
                                        p_car_vg[0:1, 0:1], 1.0, ALU.add,
                                        ALU.max),
              ["rwf", "p_car_vg"], ["r_gcs"])
            for i, (src, stag) in enumerate(((p_car_pc, "p_car_pc"),
                                             (p_car_ps, "p_car_ps"),
                                             (p_car_kc, "p_car_kc"),
                                             (p_car_ks, "p_car_ks"))):
                for c in range(C):
                    V(lambda E, i=i, c=c, src=src: E.tensor_copy(
                        p_scarry[:, c, i:i + 1], src[:, c:c + 1]),
                      [stag], ["p_scarry"])

        for n in range(NSUB):
            t0 = 1 + n * SUB
            for c in range(C):
                dma(z_x1[:, :], ppT[c * 128:(c + 1) * 128, j,
                                    n * SUB:(n + 1) * SUB],
                    ["z_x1"], ["z_x1"], eng="sync")
                trig(a_ps[:, c, :], "a_ps", z_x1[:, :], "z_x1", "sin")
                trig(a_pc[:, c, :], "a_pc", z_x1[:, :], "z_x1", "cos")
            for m in range(C):
                ps1, pn1 = (pA, "pA") if m % 2 == 0 else (pB, "pB")
                wmm(ps1, pn1, a_wpv, "a_wpv", m,
                    lambda kc, t0=t0: p_xn[:, kc, t0:t0 + SUB], ["p_xn"],
                    range(C), "bpv")
                V(lambda E, m=m, ps1=ps1: E.tensor_mul(
                    z_p1[:, :], a_pc[:, m, :], ps1[:, :]),
                  ["a_pc", pn1], ["z_p1"])
                V(lambda E, m=m, ps1=ps1: E.tensor_mul(
                    z_p2[:, :], a_ps[:, m, :], ps1[:, :]),
                  ["a_ps", pn1], ["z_p2"])
                if own:
                    S(lambda E: E.nop(), ["z_p1", "z_p2", "p_scarry"],
                      ["rt_s2"])
                    V(lambda E, m=m: E.tensor_tensor_scan(
                        z_m1[:, :], p_ones_sc[:, :], z_p1[:, :],
                        p_scarry[:, m, 0:1], ALU.mult, ALU.add),
                      ["ones", "z_p1", "p_scarry", "rt_s2"], ["z_m1"])
                    V(lambda E, m=m: E.tensor_tensor_scan(
                        z_m2[:, :], p_ones_sc[:, :], z_p2[:, :],
                        p_scarry[:, m, 1:2], ALU.mult, ALU.add),
                      ["ones", "z_p2", "p_scarry"], ["z_m2"])
                    V(lambda E, m=m: E.tensor_copy(p_scarry[:, m, 0:1],
                                                   z_m1[:, SUB - 1:SUB]),
                      ["z_m1"], ["p_scarry"])
                    V(lambda E, m=m: E.tensor_copy(p_scarry[:, m, 1:2],
                                                   z_m2[:, SUB - 1:SUB]),
                      ["z_m2"], ["p_scarry"])
                    spb, sptag = (p_spfa, "p_spfa") if m % 2 == 0 \
                        else (p_spfb, "p_spfb")
                    V(lambda E, m=m: E.tensor_mul(z_p1[:, :], a_pc[:, m, :],
                                                  z_m1[:, :]),
                      ["a_pc", "z_m1"], ["z_p1"])
                    V(lambda E, m=m: E.tensor_mul(z_p2[:, :], a_ps[:, m, :],
                                                  z_m2[:, :]),
                      ["a_ps", "z_m2"], ["z_p2"])
                    V(lambda E, spb=spb: E.tensor_add(spb[:, :], z_p1[:, :],
                                                      z_p2[:, :]),
                      ["z_p1", "z_p2"], [sptag])
                    dma(sp_pos[:, m, n * SUB:(n + 1) * SUB], spb[:, :],
                        [sptag], [f"sp_pos_{m}_{n}"])
                else:
                    V(lambda E, m=m: E.scalar_tensor_tensor(
                        p_junk[:, :], z_p1[:, :], 1.0, p_ones_sc[:, :],
                        ALU.mult, ALU.mult, accum_out=p_acol[:, m:m + 1]),
                      ["z_p1", "ones"], ["p_junk", "p_acol"])
                    V(lambda E, m=m: E.scalar_tensor_tensor(
                        p_junk[:, :], z_p2[:, :], 1.0, p_ones_sc[:, :],
                        ALU.mult, ALU.mult, accum_out=p_acol2[:, m:m + 1]),
                      ["z_p2", "ones"], ["p_junk", "p_acol2"])
                if not own and m == C - 1:
                    if n == 0:
                        V(lambda E, j=j: E.tensor_copy(p_sum_pc[:, :, j],
                                                       p_acol[:, :]),
                          ["p_acol"], ["p_sum_pc"])
                        V(lambda E, j=j: E.tensor_copy(p_sum_ps[:, :, j],
                                                       p_acol2[:, :]),
                          ["p_acol2"], ["p_sum_ps"])
                    else:
                        V(lambda E, j=j: E.tensor_add(p_sum_pc[:, :, j],
                                                      p_sum_pc[:, :, j],
                                                      p_acol[:, :]),
                          ["p_acol", "p_sum_pc"], ["p_sum_pc"])
                        V(lambda E, j=j: E.tensor_add(p_sum_ps[:, :, j],
                                                      p_sum_ps[:, :, j],
                                                      p_acol2[:, :]),
                          ["p_acol2", "p_sum_ps"], ["p_sum_ps"])
            bcast(pE, "pE", r_vg[0:1, n * SUB:(n + 1) * SUB], ["r_vg"])
            if not own:
                V(lambda E, j=j, n=n: E.scalar_tensor_tensor(
                    p_junk[:, :], p_ones_sc[:, :], 1.0, pE[:, :],
                    ALU.bypass, ALU.mult,
                    accum_out=p_sum_vg[:, j * NSUB + n:j * NSUB + n + 1]),
                  ["ones", "pE"], ["p_junk", "p_sum_vg"])
            for m in range(C):
                ps1, pn1 = (pC_, "pC") if m % 2 == 0 else (pD, "pD")
                wmm(ps1, pn1, a_wk, "a_wk", m,
                    lambda kc, t0=t0: p_xn[:, kc, t0:t0 + SUB], ["p_xn"],
                    range(C), "bk")
                S(lambda E, ps1=ps1: E.activation(out=z_kt[:, :],
                                                  in_=ps1[:, :],
                                                  func=AF.Tanh),
                  [pn1], ["z_kt"])
                S(lambda E: E.activation(out=z_ka[:, :], in_=z_kt[:, :],
                                         func=AF.Abs), ["z_kt"], ["z_ka"])
                S(lambda E: E.activation(out=z_ks[:, 1:SUB + 1],
                                         in_=z_kt[:, :], func=AF.Sin,
                                         scale=col(C_PI)),
                  ["z_kt", "p_cst"], ["z_ks"])
                S(lambda E: E.activation(out=z_kc[:, 1:SUB + 1],
                                         in_=z_ka[:, :], func=AF.Sin,
                                         scale=col(C_NPI), bias=col(C_HPI)),
                  ["z_ka", "p_cst"], ["z_kc"])
                V(lambda E, m=m: E.tensor_copy(z_kc[:, 0:1],
                                               p_bnd[:, m, 1:2]),
                  ["p_bnd"], ["z_kc"])
                V(lambda E, m=m: E.tensor_copy(z_ks[:, 0:1],
                                               p_bnd[:, m, 0:1]),
                  ["p_bnd"], ["z_ks"])
                V(lambda E, m=m: E.tensor_copy(p_bnd[:, m, 1:2],
                                               z_kc[:, SUB:SUB + 1]),
                  ["z_kc"], ["p_bnd"])
                V(lambda E, m=m: E.tensor_copy(p_bnd[:, m, 0:1],
                                               z_ks[:, SUB:SUB + 1]),
                  ["z_ks"], ["p_bnd"])
                ps2, pn2 = (pA, "pA") if m % 2 == 0 else (pB, "pB")
                wmm(ps2, pn2, a_wkv, "a_wkv", m,
                    lambda kc, t0=t0: p_xn[:, kc, t0:t0 + SUB], ["p_xn"],
                    range(C), "bkv")
                V(lambda E, ps2=ps2: E.tensor_mul(z_p1[:, :], z_kc[:, 0:SUB],
                                                  ps2[:, :]),
                  ["z_kc", pn2], ["z_p1"])
                V(lambda E, ps2=ps2: E.tensor_mul(z_p2[:, :], z_ks[:, 0:SUB],
                                                  ps2[:, :]),
                  ["z_ks", pn2], ["z_p2"])
                if own:
                    V(lambda E: E.scalar_tensor_tensor(
                        z_p1[:, :], z_p1[:, :], 1.0, pE[:, :], ALU.bypass,
                        ALU.mult), ["z_p1", "pE"], ["z_p1"])
                    V(lambda E: E.scalar_tensor_tensor(
                        z_p2[:, :], z_p2[:, :], 1.0, pE[:, :], ALU.bypass,
                        ALU.mult), ["z_p2", "pE"], ["z_p2"])
                    S(lambda E: E.nop(), ["z_p1", "z_p2", "p_scarry"],
                      ["rt_s3"])
                    V(lambda E, m=m: E.tensor_tensor_scan(
                        z_m1[:, :], p_ones_sc[:, :], z_p1[:, :],
                        p_scarry[:, m, 2:3], ALU.mult, ALU.add),
                      ["ones", "z_p1", "p_scarry", "rt_s3"], ["z_m1"])
                    V(lambda E, m=m: E.tensor_tensor_scan(
                        z_m2[:, :], p_ones_sc[:, :], z_p2[:, :],
                        p_scarry[:, m, 3:4], ALU.mult, ALU.add),
                      ["ones", "z_p2", "p_scarry"], ["z_m2"])
                    V(lambda E, m=m: E.tensor_copy(p_scarry[:, m, 2:3],
                                                   z_m1[:, SUB - 1:SUB]),
                      ["z_m1"], ["p_scarry"])
                    V(lambda E, m=m: E.tensor_copy(p_scarry[:, m, 3:4],
                                                   z_m2[:, SUB - 1:SUB]),
                      ["z_m2"], ["p_scarry"])
                    spb, sptag = (p_spfa, "p_spfa") if m % 2 == 0 \
                        else (p_spfb, "p_spfb")
                    V(lambda E: E.tensor_mul(z_p1[:, :], z_kc[:, 1:SUB + 1],
                                             z_m1[:, :]),
                      ["z_kc", "z_m1"], ["z_p1"])
                    V(lambda E: E.tensor_mul(z_p2[:, :], z_ks[:, 1:SUB + 1],
                                             z_m2[:, :]),
                      ["z_ks", "z_m2"], ["z_p2"])
                    V(lambda E, spb=spb: E.tensor_add(spb[:, :], z_p1[:, :],
                                                      z_p2[:, :]),
                      ["z_p1", "z_p2"], [sptag])
                    dma(sp_kv[:, m, n * SUB:(n + 1) * SUB], spb[:, :],
                        [sptag], [f"sp_kv_{m}_{n}"])
                else:
                    V(lambda E, m=m: E.scalar_tensor_tensor(
                        p_junk[:, :], z_p1[:, :], 1.0, pE[:, :], ALU.bypass,
                        ALU.mult, accum_out=p_acol[:, m:m + 1]),
                      ["z_p1", "pE"], ["p_junk", "p_acol"])
                    V(lambda E, m=m: E.scalar_tensor_tensor(
                        p_junk[:, :], z_p2[:, :], 1.0, pE[:, :], ALU.bypass,
                        ALU.mult, accum_out=p_acol2[:, m:m + 1]),
                      ["z_p2", "pE"], ["p_junk", "p_acol2"])
            if not own:
                if n == 0:
                    V(lambda E, j=j: E.tensor_copy(p_sum_kc[:, :, j],
                                                   p_acol[:, :]),
                      ["p_acol"], ["p_sum_kc"])
                    V(lambda E, j=j: E.tensor_copy(p_sum_ks[:, :, j],
                                                   p_acol2[:, :]),
                      ["p_acol2"], ["p_sum_ks"])
                else:
                    V(lambda E, j=j: E.tensor_add(p_sum_kc[:, :, j],
                                                  p_sum_kc[:, :, j],
                                                  p_acol[:, :]),
                      ["p_acol", "p_sum_kc"], ["p_sum_kc"])
                    V(lambda E, j=j: E.tensor_add(p_sum_ks[:, :, j],
                                                  p_sum_ks[:, :, j],
                                                  p_acol2[:, :]),
                      ["p_acol2", "p_sum_ks"], ["p_sum_ks"])
        b.barrier()

    def precast(e):
        for r in range(0, D, 256):
            dma(m1wb[e, r:r + 256, :], m1w[e, r:r + 256, :], [], ["m1wb"],
                waw=False)
        dma(m2wb[e, :, :], m2w[e, :, :], [], ["m2wb"], waw=False)

    for j in range(NS - 1):
        process_slot(j, own=False)
        precast(j)
    process_slot(OWN, own=True)
    precast(OWN)

    tap("xn", p_xn[:, :, :], ["p_xn"])
    tap("vg", r_vg[0:1, :], ["r_vg"])
    tap("bw0", r_bw0[0:1, :], ["r_bw0"])
    tap("bw1", r_bw1[0:1, :], ["r_bw1"])
    tap("rho", r_rho[0:1, :], ["r_rho"])
    tap("gcs", r_gcs[0:1, :], ["r_gcs"])
    tap("car_pc", p_car_pc[:, :], ["p_car_pc"])
    tap("car_ps", p_car_ps[:, :], ["p_car_ps"])
    tap("car_kc", p_car_kc[:, :], ["p_car_kc"])
    tap("car_ks", p_car_ks[:, :], ["p_car_ks"])
    tap("sum_pc", p_sum_pc[:, :, :], ["p_sum_pc"])
    tap("sum_kc", p_sum_kc[:, :, :], ["p_sum_kc"])
    tap("sum_vg", p_sum_vg[0:1, :], ["p_sum_vg"])

    S(lambda E: E.preload_act_table(AF.Sqrt), [], [])
    for c in range(C):
        dma(a_wo[:, c, :], Wo[c * 128:(c + 1) * 128, :],
            ["a_wo"] if c == 0 else [], ["a_wo"])
    V(lambda E: E.reciprocal(rwf[0:1, :], r_gcs[0:1, :]), ["r_gcs"], ["rwf"])
    S(lambda E: E.activation(out=r_gcs[0:1, :], in_=rwf[0:1, :],
                             func=AF.Sqrt), ["rwf"], ["r_gcs"])
    V(lambda E: E.tensor_mul(rwf[0:1, :], r_bw1[0:1, :], r_gcs[0:1, :]),
      ["r_bw1", "r_gcs"], ["rwf"])
    for n in range(NSUB):
        T(lambda E, n=n: E.matmul(pA[:, 0:SUB],
                                  p_ones_r[0:1, SUB:SUB + 128],
                                  r_bw0[0:1, n * SUB:(n + 1) * SUB],
                                  start=True, stop=True),
          ["ones", "r_bw0"], ["pA"])
        T(lambda E, n=n: E.matmul(pB[:, 0:SUB],
                                  p_ones_rf[0:1, 128:256],
                                  rwf[0:1, n * SUB:(n + 1) * SUB],
                                  start=True, stop=True),
          ["ones", "rwf"], ["pB"])
        for c in range(C):
            dma(z_kt[:, :], sp_pos[:, c, n * SUB:(n + 1) * SUB],
                [f"sp_pos_{c}_{n}", "z_kt"], ["z_kt"], eng="sync")
            dma(z_ka[:, :], sp_kv[:, c, n * SUB:(n + 1) * SUB],
                [f"sp_kv_{c}_{n}", "z_ka"], ["z_ka"], eng="sync")
            V(lambda E: E.tensor_mul(z_p1[:, :], z_kt[:, :], pA[:, :]),
              ["z_kt", "pA"], ["z_p1"])
            V(lambda E: E.tensor_mul(z_p2[:, :], z_ka[:, :], pB[:, :]),
              ["z_ka", "pB"], ["z_p2"])
            spb, sptag = (p_spfa, "p_spfa") if c % 2 == 0 \
                else (p_spfb, "p_spfb")
            V(lambda E, spb=spb: E.tensor_add(spb[:, :], z_p1[:, :],
                                              z_p2[:, :]),
              ["z_p1", "z_p2"], [sptag])
            T(lambda E, c=c, f=(c == 0), l=(c == C - 1), spb=spb: E.matmul(
                pR0[0:1, 0:SUB], p_ones_c[:, 0:1], spb[:, :],
                start=f, stop=l), ["ones", sptag], ["pR0"])
            S(lambda E, spb=spb: E.activation(out=z_sq[:, :], in_=spb[:, :],
                                              func=AF.Square),
              [sptag], ["z_sq"])
            T(lambda E, c=c, f=(c == 0), l=(c == C - 1): E.matmul(
                pR1[0:1, 0:SUB], p_ones_c[:, 0:1], z_sq[:, :],
                start=f, stop=l), ["ones", "z_sq"], ["pR1"])
            dma(sp_bl[:, c, n * SUB:(n + 1) * SUB], spb[:, :], [sptag],
                [f"sp_bl_{c}_{n}"])
        ln_rows()
        V(lambda E, n=n: E.tensor_copy(r_mu2[0:1, n * SUB:(n + 1) * SUB],
                                       rf[0][0:1, :]), ["rf0"], ["r_mu2"])
        V(lambda E, n=n: E.tensor_copy(r_r2[0:1, n * SUB:(n + 1) * SUB],
                                       rf[1][0:1, :]), ["rf1"], ["r_r2"])
    for n in range(NSUB):
        t0 = 1 + n * SUB
        bcast16(pA, "pA", r_mu2[0:1, n * SUB:(n + 1) * SUB], ["r_mu2"])
        bcast16(pB, "pB", r_r2[0:1, n * SUB:(n + 1) * SUB], ["r_r2"])
        for c in range(C):
            dma(z_kt[:, :], sp_bl[:, c, n * SUB:(n + 1) * SUB],
                [f"sp_bl_{c}_{n}", "z_kt"], ["z_kt"], eng="sync")
            V(lambda E: E.tensor_sub(z_p1[:, :], z_kt[:, :], pA[:, :]),
              ["z_kt", "pA"], ["z_p1"])
            V(lambda E: E.tensor_mul(z_p2[:, :], z_p1[:, :], pB[:, :]),
              ["z_p1", "pB"], ["z_p2"])
            S(lambda E, c=c: E.activation(out=a_ln[:, c, :], in_=z_p2[:, :],
                                          func=AF.Identity,
                                          scale=p_lng[:, c:c + 1],
                                          bias=p_lnb[:, c:c + 1]),
              ["z_p2", "p_lng", "p_lnb"], ["a_ln"])
        for m in range(C):
            ps1, pn1 = (pC_, "pC") if m % 2 == 0 else (pD, "pD")
            wmm(ps1, pn1, a_wo, "a_wo", m, lambda kc: a_ln[:, kc, :],
                ["a_ln"], range(C), "bo")
            spb, sptag = (p_sp16a, "p_sp16a") if m % 2 == 0 \
                else (p_sp16b, "p_sp16b")
            V(lambda E, m=m, ps1=ps1, t0=t0, spb=spb: E.tensor_add(
                spb[:, :], ps1[:, :], p_xn[:, m, t0:t0 + SUB]),
              [pn1, "p_xn"], [sptag])
            dma(sp_s[:, m, n * SUB:(n + 1) * SUB], spb[:, :], [sptag],
                [f"sp_s_{m}_{n}"])
    b.barrier()
    tap_dram("pos", sp_pos, [128, C, TC], F32,
             [f"sp_pos_{c}_{n}" for c in range(C) for n in range(NSUB)])
    tap_dram("kv", sp_kv, [128, C, TC], F32,
             [f"sp_kv_{c}_{n}" for c in range(C) for n in range(NSUB)])
    tap_dram("bl", sp_bl, [128, C, TC], F32,
             [f"sp_bl_{c}_{n}" for c in range(C) for n in range(NSUB)])
    tap_dram("s", sp_s, [128, C, TC], BF16,
             [f"sp_s_{c}_{n}" for c in range(C) for n in range(NSUB)])
    tap("mu2", r_mu2[0:1, :], ["r_mu2"])
    tap("r2", r_r2[0:1, :], ["r_r2"])
    tap("bw1b", r_bw1[0:1, :], ["r_bw1"])

    S(lambda E: E.preload_act_table(AF.Silu), [], [])
    for c in range(C):
        dma(sp_xn[:, c, :], p_xn[:, c, 1:TC + 1], ["p_xn"], [f"sp_xn_{c}"])
    for n in range(NSUB):
        t0 = 1 + n * SUB
        bcast16(pA, "pA", r_rho[0:1, n * SUB:(n + 1) * SUB], ["r_rho"])
        for c in range(C):
            V(lambda E, c=c, t0=t0: E.tensor_mul(
                z_p1[:, :], p_xn[:, c, t0:t0 + SUB], pA[:, :]),
              ["p_xn", "pA", f"sp_xn_{c}"], ["z_p1"])
            V(lambda E, c=c, t0=t0: E.tensor_scalar(
                p_xn[:, c, t0:t0 + SUB], z_p1[:, :], p_nsc[:, c:c + 1], None,
                ALU.mult), ["z_p1", "p_nsc"], ["p_xn"])
    for nt in range(NTT):
        t0 = 1 + nt * 128
        psg, psgn = (pE, "pE") if nt % 2 == 0 else (pF, "pF")
        for kc in range(C):
            T(lambda E, kc=kc, t0=t0, f=(kc == 0), psg=psg: E.matmul(
                psg[0:128, 0:NE], p_xn[:, kc, t0:t0 + 128],
                p_gw[:, kc, :], start=f, stop=False),
              ["p_xn", "p_gw"], [psgn])
        o_gb, _n2 = BOFF["gate_b"]
        T(lambda E, o=o_gb, psg=psg: E.matmul(
            psg[0:128, 0:NE], p_ones_r[0:1, 0:128],
            p_bias[:, o:o + NE], start=False, stop=True),
          ["ones", "p_bias"], [psgn])
        S(lambda E, nt=nt, psg=psg: E.activation(
            out=p_gl[:, nt * NE:(nt + 1) * NE], in_=psg[0:128, 0:NE],
            func=AF.Copy), [psgn], ["p_gl"])
    b.barrier()

    def sl_(nt):
        return slice(nt * NE, (nt + 1) * NE)
    for nt in range(NTT):
        V(lambda E, nt=nt: E.reduce_max(p_m1c[:, nt:nt + 1],
                                        p_gl[:, sl_(nt)], axis=AX),
          ["p_gl"], ["p_m1c"])
    for nt in range(NTT):
        S(lambda E, nt=nt: E.activation(out=p_nm1f[:, nt:nt + 1],
                                        in_=p_m1c[:, nt:nt + 1],
                                        func=AF.Identity, scale=col(C_NEG1)),
          ["p_m1c", "p_cst"], ["p_nm1f"])
    for nt in range(NTT):
        S(lambda E, nt=nt: E.activation(out=p_t2a[:, sl_(nt)],
                                        in_=p_gl[:, sl_(nt)],
                                        func=AF.Identity,
                                        bias=p_nm1f[:, nt:nt + 1]),
          ["p_gl", "p_nm1f"], ["p_t2a"])
    for nt in range(NTT):
        S(lambda E, nt=nt: E.activation(out=p_t2b[:, sl_(nt)],
                                        in_=p_t2a[:, sl_(nt)],
                                        func=AF.Sign), ["p_t2a"], ["p_t2b"])
    for nt in range(NTT):
        S(lambda E, nt=nt: E.activation(out=p_t2a[:, sl_(nt)],
                                        in_=p_t2b[:, sl_(nt)],
                                        func=AF.Square), ["p_t2b"], ["p_t2a"])
    for nt in range(NTT):
        S(lambda E, nt=nt: E.activation(out=p_gw1[:, sl_(nt)],
                                        in_=p_t2a[:, sl_(nt)],
                                        func=AF.Identity, scale=col(C_NEG1),
                                        bias=col(C_ONE)),
          ["p_t2a", "p_cst"], ["p_gw1"])
    for nt in range(NTT):
        V(lambda E, nt=nt: E.scalar_tensor_tensor(
            p_t2a[:, sl_(nt)], p_gw1[:, sl_(nt)], -30000.0,
            p_gl[:, sl_(nt)], ALU.mult, ALU.add),
          ["p_gw1", "p_gl"], ["p_t2a"])
    S(lambda E: E.nop(), ["p_t2a"], ["rt_t2"])
    for nt in range(NTT):
        V(lambda E, nt=nt: E.reduce_max(p_m2c[:, nt:nt + 1],
                                        p_t2a[:, sl_(nt)], axis=AX),
          ["p_t2a", "rt_t2"], ["p_m2c"])
    for nt in range(NTT):
        S(lambda E, nt=nt: E.activation(out=p_nm2f[:, nt:nt + 1],
                                        in_=p_m2c[:, nt:nt + 1],
                                        func=AF.Identity, scale=col(C_NEG1)),
          ["p_m2c", "p_cst"], ["p_nm2f"])
    for nt in range(NTT):
        S(lambda E, nt=nt: E.activation(out=p_t2a[:, sl_(nt)],
                                        in_=p_gl[:, sl_(nt)],
                                        func=AF.Identity,
                                        bias=p_nm2f[:, nt:nt + 1]),
          ["p_gl", "p_nm2f"], ["p_t2a"])
    for nt in range(NTT):
        S(lambda E, nt=nt: E.activation(out=p_t2b[:, sl_(nt)],
                                        in_=p_t2a[:, sl_(nt)],
                                        func=AF.Sign), ["p_t2a"], ["p_t2b"])
    for nt in range(NTT):
        S(lambda E, nt=nt: E.activation(out=p_t2a[:, sl_(nt)],
                                        in_=p_t2b[:, sl_(nt)],
                                        func=AF.Square), ["p_t2b"], ["p_t2a"])
    for nt in range(NTT):
        S(lambda E, nt=nt: E.activation(out=p_t2b[:, sl_(nt)],
                                        in_=p_t2a[:, sl_(nt)],
                                        func=AF.Identity, scale=col(C_NEG1),
                                        bias=col(C_ONE)),
          ["p_t2a", "p_cst"], ["p_t2b"])
    for nt in range(NTT):
        S(lambda E, nt=nt: E.activation(out=p_nmh2[:, nt:nt + 1],
                                        in_=p_m2c[:, nt:nt + 1],
                                        func=AF.Identity, scale=col(C_NEGH)),
          ["p_m2c", "p_cst"], ["p_nmh2"])
    for nt in range(NTT):
        S(lambda E, nt=nt: E.activation(out=p_nmh1[:, nt:nt + 1],
                                        in_=p_m1c[:, nt:nt + 1],
                                        func=AF.Identity, scale=col(C_NEGH)),
          ["p_m1c", "p_cst"], ["p_nmh1"])
    for nt in range(NTT):
        S(lambda E, nt=nt: E.activation(out=p_th1[:, nt:nt + 1],
                                        in_=p_m1c[:, nt:nt + 1],
                                        func=AF.Tanh, scale=col(C_HALF),
                                        bias=p_nmh2[:, nt:nt + 1]),
          ["p_m1c", "p_nmh2", "p_cst"], ["p_th1"])
    for nt in range(NTT):
        S(lambda E, nt=nt: E.activation(out=p_th2[:, nt:nt + 1],
                                        in_=p_m2c[:, nt:nt + 1],
                                        func=AF.Tanh, scale=col(C_HALF),
                                        bias=p_nmh1[:, nt:nt + 1]),
          ["p_m2c", "p_nmh1", "p_cst"], ["p_th2"])
    for nt in range(NTT):
        S(lambda E, nt=nt: E.activation(out=p_p1k[:, nt:nt + 1],
                                        in_=p_th1[:, nt:nt + 1],
                                        func=AF.Identity, scale=col(C_HALF),
                                        bias=col(C_HALF)),
          ["p_th1", "p_cst"], ["p_p1k"])
    for nt in range(NTT):
        S(lambda E, nt=nt: E.activation(out=p_p2k[:, nt:nt + 1],
                                        in_=p_th2[:, nt:nt + 1],
                                        func=AF.Identity, scale=col(C_HALF),
                                        bias=col(C_HALF)),
          ["p_th2", "p_cst"], ["p_p2k"])
    for nt in range(NTT):
        V(lambda E, nt=nt: E.tensor_scalar(p_t2a[:, sl_(nt)],
                                           p_gw1[:, sl_(nt)],
                                           p_p1k[:, nt:nt + 1], None,
                                           ALU.mult),
          ["p_gw1", "p_p1k"], ["p_t2a"])
    for nt in range(NTT):
        V(lambda E, nt=nt: E.tensor_scalar(p_gw1[:, sl_(nt)],
                                           p_t2b[:, sl_(nt)],
                                           p_p2k[:, nt:nt + 1], None,
                                           ALU.mult),
          ["p_t2b", "p_p2k"], ["p_gw1"])
    V(lambda E: E.tensor_add(p_gw1[:, :], p_gw1[:, :], p_t2a[:, :]),
      ["p_gw1", "p_t2a"], ["p_gw1"])
    tap("gl0", p_gl[:, 0:NE], ["p_gl"])
    tap("m1c", p_m1c[:, :], ["p_m1c"])
    tap("m2c", p_m2c[:, :], ["p_m2c"])
    tap("p1k", p_p1k[:, :], ["p_p1k"])
    tap("gw10", p_gw1[:, 0:NE], ["p_gw1"])
    b.barrier()
    for nt in range(NTT):
        psg, psgn = (pE, "pE") if nt % 2 == 0 else (pF, "pF")
        T(lambda E, nt=nt, psg=psg: E.transpose(
            psg[0:NE, 0:128], p_gw1[:, nt * NE:(nt + 1) * NE],
            p_identf[:, :]),
          ["p_gw1", "p_identf"], [psgn])
        S(lambda E, nt=nt, psg=psg: E.activation(
            out=p_wrow[:, nt * 128:(nt + 1) * 128], in_=psg[0:NE, 0:128],
            func=AF.Copy), [psgn], ["p_wrow"])
    tap("wrow", p_wrow[:, :], ["p_wrow"])
    b.barrier()
    V(lambda E: E.memset(f_acc[:, :, :], 0.0), [], ["f_acc"])
    tap_dram("m2wb0", m2wb[0, 0:128, 0:128], [128, 128], BF16, ["m2wb"])
    tap_dram("m2wb1", m2wb[0, 1024:1152, 0:128], [128, 128], BF16, ["m2wb"])
    tap_dram("m2wb2", m2wb[0, 1536:1664, 0:128], [128, 128], BF16, ["m2wb"])
    tap_dram("m1wb0", m1wb[0, 512:640, 0:128], [128, 128], BF16, ["m1wb"])
    qrr = [0]

    def wdma(dst, src_ap, rtags, wtags, waw=True):
        eng = ("gpsimd", "sync")[qrr[0] % 2]
        qrr[0] += 1
        dma(dst, src_ap, rtags, wtags, eng=eng, waw=waw)

    for e in range(NE):
        wdma(f_be[:, 0, :], m1bT[e, 0], ["f_be"], ["f_be"])
        wdma(f_be[:, 1, :], m1bT[e, 1], [], ["f_be"])
        wdma(f_b2c[:, :], m2bT[e], ["f_b2c"], ["f_b2c"])
        S(lambda E: E.activation(out=f_beE[:, :], in_=f_be[:, 0, :],
                                 func=AF.Identity, scale=col(C_ALPHA)),
          ["f_be", "p_cst"], ["f_beE"])
        for half in range(2):
            hbase = half * (TC // 2)
            for jj in range(JC):
                for c in range(4):
                    wdma(f_w1a[:, c, :],
                         m1wb[e, c * 128:(c + 1) * 128,
                              jj * 256:(jj + 1) * 256],
                         [], ["f_w1a"], waw=False)
                for c in range(4):
                    wdma(f_w1b[:, c, :],
                         m1wb[e, (c + 4) * 128:(c + 5) * 128,
                              jj * 256:(jj + 1) * 256],
                         [], ["f_w1b"], waw=False)
                for n2 in range(2):
                    t0 = 1 + hbase + n2 * SUB
                    q = jj * 2 + n2
                    psE, pnE, psO, pnO = (
                        (pA, "pA", pB, "pB") if q % 2 == 0
                        else (pC_, "pC", pD, "pD"))
                    for kc in range(C):
                        wt, wtag = (f_w1a, "f_w1a") if kc < 4 \
                            else (f_w1b, "f_w1b")
                        T(lambda E, kc=kc, t0=t0, f=(kc == 0), wt=wt,
                          psE=psE, l=(kc == C - 1): E.matmul(
                            psE[:, 0:SUB], wt[:, kc % 4, 0:256:2],
                            p_xn[:, kc, t0:t0 + SUB], start=f, stop=l),
                          [wtag, "p_xn"], [pnE])
                    for kc in range(C):
                        wt, wtag = (f_w1a, "f_w1a") if kc < 4 \
                            else (f_w1b, "f_w1b")
                        T(lambda E, kc=kc, t0=t0, f=(kc == 0), wt=wt,
                          psO=psO, l=(kc == C - 1): E.matmul(
                            psO[:, 0:SUB], wt[:, kc % 4, 1:256:2],
                            p_xn[:, kc, t0:t0 + SUB], start=f, stop=l),
                          [wtag, "p_xn"], [pnO])
                    S(lambda E, jj=jj, psE=psE: E.activation(
                        out=f_swz3[:, :], in_=psE[:, :], func=AF.Silu,
                        scale=col(C_ALPHA), bias=f_beE[:, jj:jj + 1]),
                      [pnE, "p_cst", "f_beE"], ["f_swz3"])
                    V(lambda E: E.tensor_scalar(f_swz4[:, :], f_swz3[:, :],
                                                D7, None, ALU.min),
                      ["f_swz3"], ["f_swz4"])
                    S(lambda E, jj=jj, psO=psO: E.activation(
                        out=f_swz[:, :], in_=psO[:, :], func=AF.Identity,
                        bias=f_be[:, 1, jj:jj + 1]),
                      [pnO, "f_be"], ["f_swz"])
                    V(lambda E: E.tensor_scalar(f_swz2[:, :], f_swz[:, :],
                                                LIMIT, -LIMIT, ALU.min,
                                                ALU.max),
                      ["f_swz"], ["f_swz2"])
                    V(lambda E: E.tensor_scalar(f_swz[:, :], f_swz2[:, :],
                                                1.0, 1.0 / ALPHA, ALU.add,
                                                ALU.mult),
                      ["f_swz2"], ["f_swz"])
                    V(lambda E, jj=jj, n2=n2: E.tensor_mul(
                        f_h[:, jj, n2 * SUB:(n2 + 1) * SUB], f_swz4[:, :],
                        f_swz[:, :]), ["f_swz4", "f_swz"], ["f_h"])
                    if e == 0 and half == 0 and jj == 0 and n2 == 0:
                        tap("fh0", f_h[:, 0, 0:SUB], ["f_h"])
                        tap("fbeE", f_beE[:, :], ["f_beE"])
                        tap("fbeO", f_be[:, 1, :], ["f_be"])
            if e == 0 and half == 0:
                tap("fhall", f_h[:, :, :], ["f_h"])
            for n2 in range(2):
                n = half * 2 + n2
                T(lambda E, e=e, n=n: E.matmul(
                    pE[:, 0:SUB], p_sel[0:NE, e * 128:(e + 1) * 128],
                    p_wrow[0:NE, n * SUB:(n + 1) * SUB],
                    start=True, stop=True), ["p_sel", "p_wrow"], ["pE"])
                V(lambda E, n2=n2: E.tensor_copy(f_bwe[:, n2, :], pE[:, :]),
                  ["pE"], ["f_bwe"])
            for m in range(C):
                for k in range(8):
                    wdma(f_w2ma[:, k, :],
                         m2wb[e, k * 128:(k + 1) * 128,
                              m * 128:(m + 1) * 128],
                         [], ["f_w2ma"], waw=False)
                for k in range(8):
                    wdma(f_w2mb[:, k, :],
                         m2wb[e, (k + 8) * 128:(k + 9) * 128,
                              m * 128:(m + 1) * 128],
                         [], ["f_w2mb"], waw=False)
                if e == 0 and half == 0 and m == 0:
                    tap("fw2a", f_w2ma[:, :, :], ["f_w2ma"])
                    tap("fw2b", f_w2mb[:, :, :], ["f_w2mb"])
                for n2 in range(2):
                    n = half * 2 + n2
                    ps1, pn1 = ((pC_, "pC") if n2 == 0 else (pD, "pD")) \
                        if m % 2 == 0 else ((pF, "pF") if n2 == 0
                                            else (pA, "pA"))
                    for kc in range(JC):
                        wt, wtag = (f_w2ma, "f_w2ma") if kc < 8 \
                            else (f_w2mb, "f_w2mb")
                        T(lambda E, kc=kc, n2=n2, ps1=ps1, wt=wt,
                          f=(kc == 0), l=(kc == JC - 1):
                          E.matmul(ps1[:, 0:SUB], wt[:, kc % 8, :],
                                   f_h[:, kc, n2 * SUB:(n2 + 1) * SUB],
                                   start=f, stop=l),
                          [wtag, "f_h"], [pn1])
                    S(lambda E, m=m, ps1=ps1: E.activation(
                        out=f_swz3[:, :], in_=ps1[:, :], func=AF.Identity,
                        bias=f_b2c[:, m:m + 1]),
                      [pn1, "f_b2c"], ["f_swz3"])
                    if e == 0 and half == 0 and m == 0 and n2 == 0:
                        tap("fy00", f_swz3[:, :], ["f_swz3"])
                    V(lambda E, n2=n2: E.tensor_mul(
                        f_swz[:, :], f_swz3[:, :], f_bwe[:, n2, :]),
                      ["f_swz3", "f_bwe"], ["f_swz"])
                    V(lambda E, m=m, n=n: E.tensor_add(
                        f_acc[:, m, n * SUB:(n + 1) * SUB],
                        f_acc[:, m, n * SUB:(n + 1) * SUB], f_swz[:, :]),
                      ["f_swz", "f_acc"], ["f_acc"])
                    if e == 0 and m == 0 and n == 3:
                        tap("facc00", f_acc[:, 0, :], ["f_acc"])
                        tap("fbwe0", f_bwe[:, :, :], ["f_bwe"])
    b.barrier()
    tap("acc", f_acc[:, :, :], ["f_acc"])
    tap("t_fm", p_xn[:, :, :], ["p_xn"])

    for c in range(C):
        dma(p_xn[:, c, 1:TC + 1], sp_xn[:, c, :], [f"sp_xn_{c}", "p_xn"],
            ["p_xn"])
    for n in range(NSUB):
        t0 = 1 + n * SUB
        for c in range(C):
            dma(g_x[:, c, :], xT[c * 128:(c + 1) * 128, OWN, t0:t0 + SUB],
                ["g_x"], ["g_x"], eng="sync")
            dma(g_s[:, c, :], sp_s[:, c, n * SUB:(n + 1) * SUB],
                [f"sp_s_{c}_{n}", "g_s"], ["g_s"], eng="sync")
        V(lambda E, t0=t0, n=n: E.tensor_add(
            g_o[:, :, :], p_xn[:, :, t0:t0 + SUB],
            f_acc[:, :, n * SUB:(n + 1) * SUB]),
          ["p_xn", "f_acc"], ["g_o"])
        S(lambda E: E.activation(out=g_g[:, :, :], in_=g_o[:, :, :],
                                 func=AF.Silu), ["g_o"], ["g_g"])
        V(lambda E: E.tensor_mul(g_o[:, :, :], g_s[:, :, :], g_g[:, :, :]),
          ["g_s", "g_g"], ["g_o"])
        V(lambda E: E.tensor_add(g_x[:, :, :], g_x[:, :, :], g_o[:, :, :]),
          ["g_x", "g_o"], ["g_x"])
        dma(out[:, :, n * SUB:(n + 1) * SUB], g_x[:, :, :], ["g_x"],
            [f"out_{n}"], eng="sync")

    for semname in ("d_sync", "d_gpsimd", "d_scalar"):
        if b.cnt.get(semname, 0) > 0:
            b.ops["sync"].append(("w", semname, b.cnt[semname]))
    b.finalize_and_emit()
    return nc


def kernel(**inputs):
    x = np.asarray(inputs["L0_attn_x"])
    pp = np.asarray(inputs["L0_attn_pos_phases"])
    L = x.shape[1]
    xT_full = np.ascontiguousarray(x[0].T)
    ppT_full = np.ascontiguousarray(pp.T)

    def W(name):
        return np.ascontiguousarray(np.asarray(inputs[name], np.float32))

    def colsfmt(v):
        return np.ascontiguousarray(np.asarray(v, np.float32)
                                    .reshape(C, 128).T)

    consts = np.zeros((128, NCST), np.float32)
    consts[:, C_SINB] = 64 * math.pi
    consts[:, C_COSB] = 64 * math.pi + math.pi / 2
    consts[:, C_PI] = math.pi
    consts[:, C_NPI] = -math.pi
    consts[:, C_HPI] = math.pi / 2
    consts[:, C_ISQ2] = 1.0 / math.sqrt(2.0)
    consts[:, C_ALPHA] = ALPHA
    consts[:, C_NEG1] = -1.0
    consts[:, C_ONE] = 1.0
    consts[:, C_HALF] = 0.5
    consts[:, C_NEGH] = -0.5

    sel = np.zeros((NE, NE * 128), np.float32)
    for e in range(NE):
        sel[e, e * 128:(e + 1) * 128] = 1.0

    brow = np.zeros((1, 5664), np.float32)

    def put(name, v):
        o, n = BOFF[name]
        brow[0, o:o + n] = np.asarray(v, np.float32).reshape(-1)

    put("bpv", inputs["L0_attn_bpv"]); put("bk", inputs["L0_attn_bk"])
    put("bkv", inputs["L0_attn_bkv"]); put("bg1", inputs["L0_attn_bg1"])
    put("bo", inputs["L0_attn_bo"]); put("bb1", inputs["L0_attn_bb1"])
    put("gate_b", inputs["L0_ffn_gate_b"]); put("bb2", inputs["L0_attn_bb2"])
    put("bg2", inputs["L0_attn_bg2"])

    shared = {
        "Wpv": W("L0_attn_Wpv"), "Wk": W("L0_attn_Wk"),
        "Wkv": W("L0_attn_Wkv"), "Wg1": W("L0_attn_Wg1"),
        "Wg2": W("L0_attn_Wg2"), "Wb1": W("L0_attn_Wb1"),
        "Wb2": W("L0_attn_Wb2"), "Wo": W("L0_attn_Wo"),
        "biasrow": brow,
        "ln_g": colsfmt(inputs["L0_attn_ln_g"]),
        "ln_b": colsfmt(inputs["L0_attn_ln_b"]),
        "nsc": colsfmt(inputs["L0_ffn_norm_scale"]),
        "gate_w": W("L0_ffn_gate_w"),
        "m1w": W("L0_ffn_mlp1_w"),
        "m1bT": np.ascontiguousarray(
            W("L0_ffn_mlp1_b").reshape(NE, JC, 128, 2)
            .transpose(0, 3, 2, 1)),
        "m2w": W("L0_ffn_mlp2_w"),
        "m2bT": np.ascontiguousarray(
            W("L0_ffn_mlp2_b").reshape(NE, C, 128).transpose(0, 2, 1)),
        "selw": sel, "cst": consts,
        "identw": np.eye(128, dtype=np.float32),
    }

    in_maps = []
    for c in range(NCORES):
        order = [k for k in range(NCORES) if k != c] + [c]
        xT_sl = np.zeros((D, NS, TC + 1), np.float32)
        ppT_sl = np.zeros((D, NS, TC), np.float32)
        cm = np.zeros((128, NS), np.float32)
        pm = np.zeros((128, NS), np.float32)
        for jslot, k in enumerate(order):
            t0 = k * TC
            if t0 > 0:
                xT_sl[:, jslot, 0] = xT_full[:, t0 - 1]
                pm[:, jslot] = 1.0
            xT_sl[:, jslot, 1:] = xT_full[:, t0:t0 + TC]
            ppT_sl[:, jslot, :] = ppT_full[:, t0:t0 + TC]
            if k < c:
                cm[:, jslot] = 1.0
        m = dict(shared)
        m["xT"] = xT_sl
        m["ppT"] = ppT_sl
        m["cmask"] = cm
        m["pmask"] = pm
        in_maps.append(m)

    nc = build_program()
    res = run_bass_kernel_spmd(nc, in_maps, core_ids=list(range(NCORES)))
    global LAST_RES
    LAST_RES = res
    outs = []
    for c in range(NCORES):
        o = np.asarray(res.results[c]["out"])
        outs.append(np.transpose(o, (2, 1, 0)).reshape(TC, D))
    return np.concatenate(outs, axis=0).reshape(1, L, D).astype(np.float32)


# version 04
# speedup vs baseline: 1.7856x; 1.0539x over previous
import math
import numpy as np
import concourse.bass as bass
import concourse.mybir as mybir
from concourse.bass_utils import run_bass_kernel_spmd

F32 = mybir.dt.float32
BF16 = mybir.dt.bfloat16
I32 = mybir.dt.int32
AF = mybir.ActivationFunctionType
ALU = mybir.AluOpType
AX = mybir.AxisListType.X

NCORES = 8
D = 1024
C = 8
TC = 2048
NS = 8
SUB = 512
NSUB = TC // SUB
NE = 8
JC = 16
OWN = NS - 1
TWO_PI = 2.0 * math.pi
EPS = 1e-5
ALPHA = 1.702
LIMIT = 7.0
D7 = ALPHA * LIMIT * (1.0 / (1.0 + math.exp(-ALPHA * LIMIT)))

(C_SINB, C_COSB, C_PI, C_NPI, C_HPI, C_ISQ2, C_ALPHA, C_NEG1, C_ONE,
 C_HALF, C_NEGH) = range(11)
NCST = 11

ENGINES = ("sync", "scalar", "vector", "tensor", "gpsimd")

TAPS = set()
LAST_RES = None

BOFF = {"bpv": (0, D), "bk": (D, D), "bkv": (2 * D, D), "bg1": (3 * D, D),
        "bo": (4 * D, D), "bb1": (5 * D, D // 2), "gate_b": (5 * D + 512, NE),
        "bb2": (5 * D + 520, 2), "bg2": (5 * D + 522, 1)}


class Build:
    def __init__(self, nc):
        self.nc = nc
        self.ops = {e: [] for e in ENGINES}
        self.cnt = {}
        self.waited = {}
        self.buf_w = {}
        self.buf_r = {}
        self.alias = {}
        for e in ENGINES:
            self.cnt["p_" + e] = 0
        for e in ("sync", "gpsimd", "scalar"):
            self.cnt["d_" + e] = 0

    def _wait(self, eng, semname, val):
        if val <= 0:
            return
        if semname == "p_" + eng:
            return
        if self.waited.get((eng, semname), 0) >= val:
            return
        self.waited[(eng, semname)] = val
        self.ops[eng].append(("w", semname, val))

    def set_alias(self, ivmap):
        items = list(ivmap.items())
        ov = {}
        for n1, (l1, h1) in items:
            for n2, (l2, h2) in items:
                if n2 != n1 and l2 < h1 and l1 < h2:
                    ov.setdefault(n1, []).append(n2)
        self.alias = ov

    def emit(self, eng, fn, reads=(), writes=(), dma=False, waw=True):
        for bn in reads:
            for sn, v in (self.buf_w.get(bn) or {}).items():
                self._wait(eng, sn, v)
            for an in self.alias.get(bn, ()):
                for sn, v in (self.buf_w.get(an) or {}).items():
                    self._wait(eng, sn, v)
        for bn in writes:
            if waw:
                for sn, v in (self.buf_w.get(bn) or {}).items():
                    self._wait(eng, sn, v)
            for an in self.alias.get(bn, ()):
                for sn, v in (self.buf_w.get(an) or {}).items():
                    self._wait(eng, sn, v)
                for sn, v in (self.buf_r.get(an) or {}).items():
                    self._wait(eng, sn, v)
            for sn, v in (self.buf_r.get(bn) or {}).items():
                self._wait(eng, sn, v)
        sem = ("d_" + eng) if dma else ("p_" + eng)
        inc = 16 if dma else 1
        self.cnt[sem] += inc
        marker = (sem, self.cnt[sem])
        self.ops[eng].append(("i", fn, sem, inc))
        for bn in writes:
            if waw:
                self.buf_w[bn] = {marker[0]: marker[1]}
                self.buf_r[bn] = {}
            else:
                d = self.buf_w.setdefault(bn, {})
                if d.get(marker[0], 0) < marker[1]:
                    d[marker[0]] = marker[1]
        for bn in reads:
            d = self.buf_r.setdefault(bn, {})
            if d.get(marker[0], 0) < marker[1]:
                d[marker[0]] = marker[1]
        return marker

    def barrier(self):
        vals = dict(self.cnt)
        for e in ENGINES:
            for sn, v in vals.items():
                self._wait(e, sn, v)

    def finalize_and_emit(self):
        nc = self.nc
        import contextlib
        with contextlib.ExitStack() as st:
            handles = {}
            for name in self.cnt.keys():
                handles[name] = st.enter_context(nc.semaphore(name))
            block = st.enter_context(nc.Block())
            for eng_name in ENGINES:
                ops = self.ops[eng_name]

                def make(ops=ops):
                    def body(E):
                        for op in ops:
                            if op[0] == "w":
                                E.wait_ge(handles[op[1]], op[2])
                            else:
                                op[1](E).then_inc(handles[op[2]], op[3])
                    return body
                getattr(block, eng_name)(make())


def build_program():
    nc = bass.Bass("TRN2", target_bir_lowering=False)

    def din(name, shape, dt=F32):
        return nc.dram_tensor(name, shape, dt, kind="ExternalInput").ap()

    xT = din("xT", [D, NS, TC + 1])
    ppT = din("ppT", [D, NS, TC])
    Wpv = din("Wpv", [D, D]); Wk = din("Wk", [D, D]); Wkv = din("Wkv", [D, D])
    Wg1 = din("Wg1", [2 * D, D]); Wg2 = din("Wg2", [D, 1])
    Wb1 = din("Wb1", [D, D // 2]); Wb2 = din("Wb2", [D // 2, 2])
    Wo = din("Wo", [D, D])
    biasrow = din("biasrow", [1, 5664])
    ln_g = din("ln_g", [128, C]); ln_b = din("ln_b", [128, C])
    nsc = din("nsc", [128, C])
    gate_w = din("gate_w", [D, NE])
    m1w = din("m1w", [NE, D, 2 * JC * 128])
    m1bT = din("m1bT", [NE, 2, 128, JC])
    m2w = din("m2w", [NE, JC * 128, D])
    m2bT = din("m2bT", [NE, 128, C])
    selw = din("selw", [NE, NE * 128])
    cmask = din("cmask", [128, NS])
    pmask = din("pmask", [128, NS])
    cst = din("cst", [128, NCST])
    identw = din("identw", [128, 128])

    out = nc.dram_tensor("out", [128, C, TC], F32, kind="ExternalOutput").ap()

    sp_xn = nc.dram_tensor("sp_xn", [128, C, TC], BF16, kind="Internal").ap()
    sp_s = nc.dram_tensor("sp_s", [128, C, TC], BF16, kind="Internal").ap()
    sp_pos = nc.dram_tensor("sp_pos", [128, C, TC], F32, kind="Internal").ap()
    sp_kv = nc.dram_tensor("sp_kv", [128, C, TC], F32, kind="Internal").ap()
    sp_bl = nc.dram_tensor("sp_bl", [128, C, TC], F32, kind="Internal").ap()
    m1wb = nc.dram_tensor("m1wb", [NE, D, 2 * JC * 128], BF16,
                          kind="Internal").ap()
    m2wb = nc.dram_tensor("m2wb", [NE, JC * 128, D], BF16,
                          kind="Internal").ap()

    FREE0 = ((nc.sbuf_base + 31) // 32) * 32
    TOP = nc.sbuf_top
    off = [FREE0]

    ALLOC_IV = {}

    def alloc(name, shape, dt, at=None):
        nbytes = int(np.prod(shape[1:])) * mybir.dt.size(dt)
        if at is None:
            at = off[0]
            off[0] = at + ((nbytes + 31) // 32) * 32
        t = nc.alloc_sbuf_tensor_at(name, list(shape), dt, offset=at)
        assert at + nbytes <= TOP, f"SBUF overflow {name}: {at + nbytes} > {TOP}"
        ALLOC_IV[name] = (at, at + nbytes)
        return t

    p_cst = alloc("p_cst", [128, NCST], F32)
    p_cmask = alloc("p_cmask", [128, NS], F32)
    p_pmask = alloc("p_pmask", [128, NS], F32)
    p_lng = alloc("p_lng", [128, C], F32)
    p_lnb = alloc("p_lnb", [128, C], F32)
    p_nsc = alloc("p_nsc", [128, C], F32)
    p_gw = alloc("p_gw", [128, C, NE], BF16)
    p_wg2 = alloc("p_wg2", [128, C, 1], BF16)
    p_wb2 = alloc("p_wb2", [128, 4, 2], BF16)
    p_identf = alloc("p_identf", [128, 128], F32)
    p_ones_r = alloc("p_ones_r", [1, SUB + 128], BF16)
    p_ones_rf = alloc("p_ones_rf", [1, 256], F32)
    p_ones_c = alloc("p_ones_c", [128, 1], F32)
    p_ones_sc = alloc("p_ones_sc", [128, SUB], F32)
    p_sum_pc = alloc("p_sum_pc", [128, C, NS], F32)
    p_sum_ps = alloc("p_sum_ps", [128, C, NS], F32)
    p_sum_kc = alloc("p_sum_kc", [128, C, NS], F32)
    p_sum_ks = alloc("p_sum_ks", [128, C, NS], F32)
    p_sum_vg = alloc("p_sum_vg", [128, NS * NSUB], F32)
    p_vgm = alloc("p_vgm", [128, NS * NSUB], F32)
    p_car_pc = alloc("p_car_pc", [128, C], F32)
    p_car_ps = alloc("p_car_ps", [128, C], F32)
    p_car_kc = alloc("p_car_kc", [128, C], F32)
    p_car_ks = alloc("p_car_ks", [128, C], F32)
    p_car_vg = alloc("p_car_vg", [128, 1], F32)
    p_scarry = alloc("p_scarry", [128, C, 4], F32)
    p_bnd = alloc("p_bnd", [128, C, 4], F32)
    p_acol = alloc("p_acol", [128, C], F32)
    p_acol2 = alloc("p_acol2", [128, C], F32)
    p_bias = alloc("p_bias", [1, 5664], BF16)
    NTT = TC // 128
    p_gl = alloc("p_gl", [128, NTT * NE], F32)
    p_gw1 = alloc("p_gw1", [128, NTT * NE], F32)
    p_t2a = alloc("p_t2a", [128, NTT * NE], F32)
    p_t2b = alloc("p_t2b", [128, NTT * NE], F32)
    p_m1c = alloc("p_m1c", [128, NTT], F32)
    p_m2c = alloc("p_m2c", [128, NTT], F32)
    p_nm1f = alloc("p_nm1f", [128, NTT], F32)
    p_nm2f = alloc("p_nm2f", [128, NTT], F32)
    p_nmh1 = alloc("p_nmh1", [128, NTT], F32)
    p_nmh2 = alloc("p_nmh2", [128, NTT], F32)
    p_th1 = alloc("p_th1", [128, NTT], F32)
    p_th2 = alloc("p_th2", [128, NTT], F32)
    p_p1k = alloc("p_p1k", [128, NTT], F32)
    p_p2k = alloc("p_p2k", [128, NTT], F32)
    f_be = alloc("f_be", [128, 2, JC], F32)
    f_beE = alloc("f_beE", [128, JC], F32)
    f_b2c = alloc("f_b2c", [128, C], F32)
    p_wrow = alloc("p_wrow", [NE, TC], BF16)
    p_sel = alloc("p_sel", [NE, NE * 128], BF16)
    r_gcs = alloc("r_gcs", [1, TC], BF16)
    r_bw0 = alloc("r_bw0", [1, TC], BF16)
    r_bw1 = alloc("r_bw1", [1, TC], BF16)
    r_mu2 = alloc("r_mu2", [1, TC], BF16)
    r_r2 = alloc("r_r2", [1, TC], BF16)
    r_rho = alloc("r_rho", [1, TC], BF16)
    r_vg = alloc("r_vg", [1, TC], F32)
    rf = [alloc(f"rf{i}", [1, SUB], F32) for i in range(4)]
    rwf = alloc("rwf", [1, TC], F32)
    r_ones = alloc("r_ones", [1, TC], F32)
    SP16_OFF = off[0]
    p_sp16a = alloc("p_sp16a", [128, SUB], BF16)
    p_sp16b = alloc("p_sp16b", [128, SUB], BF16)
    p_junk = alloc("p_junk", [128, SUB], F32, at=SP16_OFF)
    p_spfa = alloc("p_spfa", [128, SUB], F32)
    p_spfb = alloc("p_spfb", [128, SUB], F32)
    p_xn = alloc("p_xn", [128, C, TC + 1], BF16)

    A0 = off[0]
    ARENA = TOP - A0
    assert ARENA >= 88000, f"arena too small: {ARENA}"

    a_wpv = alloc("a_wpv", [128, C, D], BF16, at=A0)
    a_wk = alloc("a_wk", [128, C, D], BF16, at=A0 + 16384)
    a_wkv = alloc("a_wkv", [128, C, D], BF16, at=A0 + 32768)
    a_wg1 = alloc("a_wg1", [128, 2 * C, D], BF16, at=A0)
    a_wo = alloc("a_wo", [128, C, D], BF16, at=A0)
    a_wb1 = alloc("a_wb1", [128, C, D // 2], BF16, at=A0 + 32768)
    a_xsub = alloc("a_xsub", [128, C, SUB], F32, at=A0 + 32768)
    a_pc = alloc("a_pc", [128, C, SUB], BF16, at=A0 + 49152)
    a_ps = alloc("a_ps", [128, C, SUB], BF16, at=A0 + 57344)
    a_gh = alloc("a_gh", [128, C, SUB], BF16, at=A0 + 49152)
    a_bh = alloc("a_bh", [128, 4, SUB], BF16, at=A0 + 57344)
    a_ln = alloc("a_ln", [128, C, SUB], BF16, at=A0 + 49152)
    ZB = A0 + 65536
    z_tg1 = alloc("z_tg1", [128, SUB], I32, at=ZB)
    z_tg2 = alloc("z_tg2", [128, SUB], F32, at=ZB + 2048)
    z_kt = alloc("z_kt", [128, SUB], F32, at=ZB + 4096)
    z_ka = alloc("z_ka", [128, SUB], F32, at=ZB + 6144)
    z_kc = alloc("z_kc", [128, SUB + 1], F32, at=ZB + 8192)
    z_ks = alloc("z_ks", [128, SUB + 1], F32, at=ZB + 10304)
    z_p1 = alloc("z_p1", [128, SUB], F32, at=ZB + 12416)
    z_p2 = alloc("z_p2", [128, SUB], F32, at=ZB + 14464)
    z_m1 = alloc("z_m1", [128, SUB], F32, at=ZB + 16512)
    z_m2 = alloc("z_m2", [128, SUB], F32, at=ZB + 18560)
    z_sq = alloc("z_sq", [128, SUB], F32, at=ZB + 20608)
    z_x1 = alloc("z_x1", [128, SUB], F32, at=ZB + 20608)
    assert ZB + 22656 <= TOP

    f_h = alloc("f_h", [128, JC, TC // 2], BF16, at=A0)
    f_acc = alloc("f_acc", [128, C, TC], BF16, at=A0 + 32768)
    f_w1a = alloc("f_w1a", [128, 4, 256], BF16, at=A0 + 65536)
    f_w1b = alloc("f_w1b", [128, 4, 256], BF16, at=A0 + 67584)
    f_w2ma = alloc("f_w2ma", [128, 8, 128], BF16, at=A0 + 69632)
    f_w2mb = alloc("f_w2mb", [128, 8, 128], BF16, at=A0 + 71680)
    f_b1 = alloc("f_b1", [1, 256], BF16, at=A0 + 73728)
    f_b2 = alloc("f_b2", [1, D], BF16, at=A0 + 74240)
    f_bwe = alloc("f_bwe", [128, 2, SUB], F32, at=A0 + 76288)
    f_swz = alloc("f_swz", [128, SUB], F32, at=A0 + 80384)
    f_swz2 = alloc("f_swz2", [128, SUB], F32, at=A0 + 82432)
    f_swz3 = alloc("f_swz3", [128, SUB], BF16, at=A0 + 84480)
    f_swz4 = alloc("f_swz4", [128, SUB], BF16, at=A0 + 85504)
    assert A0 + 86528 <= TOP

    g_x = alloc("g_x", [128, C, SUB], F32, at=A0)
    g_g = alloc("g_g", [128, C, SUB], BF16, at=A0 + 16384)
    g_s = alloc("g_s", [128, C, SUB], BF16, at=A0 + 24576)
    g_o = alloc("g_o", [128, C, SUB], F32, at=A0 + 65536)
    assert A0 + 65536 + 16384 <= TOP

    pA = nc.alloc_psum_tensor("pA", [128, SUB], F32)
    pB = nc.alloc_psum_tensor("pB", [128, SUB], F32)
    pC_ = nc.alloc_psum_tensor("pC", [128, SUB], F32)
    pD = nc.alloc_psum_tensor("pD", [128, SUB], F32)
    pE = nc.alloc_psum_tensor("pE", [128, SUB], F32)
    pF = nc.alloc_psum_tensor("pF", [128, SUB], F32)
    pR0 = nc.alloc_psum_tensor("pR0", [1, SUB], F32)
    pR1 = nc.alloc_psum_tensor("pR1", [1, SUB], F32)

    b = Build(nc)
    b.set_alias(ALLOC_IV)

    def dma(dst_ap, src_ap, reads, writes, eng="gpsimd", slow=False,
            waw=True):
        b.emit(eng,
               lambda E, d=dst_ap, s=src_ap, sl=slow: E.dma_start(
                   out=d, in_=s, allow_slow_non_contiguous=sl),
               reads=reads, writes=writes, dma=True, waw=waw)

    def V(fn, reads, writes):
        b.emit("vector", fn, reads=reads, writes=writes)

    def S(fn, reads, writes):
        b.emit("scalar", fn, reads=reads, writes=writes)

    def T(fn, reads, writes):
        b.emit("tensor", fn, reads=reads, writes=writes)

    def col(i):
        return p_cst[:, i:i + 1]

    def tap(name, sbuf_ap, rtags):
        if name not in TAPS:
            return
        shp = list(sbuf_ap.shape)
        dt = sbuf_ap.dtype
        t_out = nc.dram_tensor("tap_" + name, shp, dt,
                               kind="ExternalOutput").ap()
        dma(t_out, sbuf_ap, rtags, [f"tap_{name}"], eng="sync")

    def tap_dram(name, dram_ap, shp, dt, rtags):
        if name not in TAPS:
            return
        t_out = nc.dram_tensor("tap_" + name, list(shp), dt,
                               kind="ExternalOutput").ap()
        dma(t_out, dram_ap, rtags, [f"tap_{name}"], eng="sync")

    def bias_lhsT(name, mchunk):
        o, n = BOFF[name]
        return p_bias[:, o + mchunk * 128:o + min((mchunk + 1) * 128, o + n)
                      if False else o + mchunk * 128 + 128] \
            if False else p_bias[:, o + mchunk * 128:o + mchunk * 128 + 128]

    dma(p_cst[:], cst, [], ["p_cst"], eng="sync")
    dma(p_cmask[:], cmask, [], ["p_cmask"], eng="sync")
    dma(p_pmask[:], pmask, [], ["p_pmask"], eng="sync")
    dma(p_lng[:], ln_g, [], ["p_lng"], eng="sync")
    dma(p_lnb[:], ln_b, [], ["p_lnb"], eng="sync")
    dma(p_nsc[:], nsc, [], ["p_nsc"], eng="sync")
    dma(p_identf[:], identw, [], ["p_identf"], eng="sync")
    dma(p_identf[:], identw, [], ["p_identf"])
    dma(p_bias[:], biasrow, [], ["p_bias"])
    dma(p_sel[:], selw, [], ["p_sel"])
    dma(p_wg2[:, :, 0], Wg2.rearrange("(c p) o -> p (c o)", p=128)
        if False else Wg2[0:D, 0:1], [], ["p_wg2"]) if False else None
    for c in range(C):
        dma(p_wg2[:, c, 0:1], Wg2[c * 128:(c + 1) * 128, 0:1], [], ["p_wg2"],
            slow=True)
    for c in range(4):
        dma(p_wb2[:, c, :], Wb2[c * 128:(c + 1) * 128, :], [], ["p_wb2"])
    for c in range(C):
        dma(p_gw[:, c, :], gate_w[c * 128:(c + 1) * 128, :], [], ["p_gw"])
    V(lambda E: E.memset(p_ones_r[0:1, 0:SUB], 1.0), [], ["ones"])
    V(lambda E: E.memset(p_ones_r[0:1, SUB:SUB + 128], 1.0 / 32.0), [],
      ["ones"])
    V(lambda E: E.memset(p_ones_rf[0:1, 0:128], 1.0), [], ["ones"])
    V(lambda E: E.memset(p_ones_rf[0:1, 128:256], 1.0 / 32.0), [], ["ones"])
    V(lambda E: E.memset(p_ones_c[:], 1.0), [], ["ones"])
    V(lambda E: E.memset(p_ones_sc[:], 1.0), [], ["ones"])

    def ln_sums(x_chunks_fn, rtags, nn=SUB):
        for kc in range(C):
            T(lambda E, kc=kc, f=(kc == 0), l=(kc == C - 1), nn=nn: E.matmul(
                pR0[0:1, 0:nn], p_ones_c[:, 0:1], x_chunks_fn(kc),
                start=f, stop=l), ["ones"] + rtags, ["pR0"])
        for kc in range(C):
            S(lambda E, kc=kc, nn=nn: E.activation(
                out=z_sq[:, 0:nn], in_=x_chunks_fn(kc), func=AF.Square),
              rtags, ["z_sq"])
            T(lambda E, kc=kc, f=(kc == 0), l=(kc == C - 1), nn=nn: E.matmul(
                pR1[0:1, 0:nn], p_ones_c[:, 0:1], z_sq[:, 0:nn],
                start=f, stop=l), ["ones", "z_sq"], ["pR1"])

    def ln_rows(nn=SUB):
        V(lambda E, nn=nn: E.tensor_scalar(rf[0][0:1, 0:nn], pR0[0:1, 0:nn],
                                           1.0 / D, None, ALU.mult),
          ["pR0"], ["rf0"])
        V(lambda E, nn=nn: E.tensor_scalar(rf[3][0:1, 0:nn], pR1[0:1, 0:nn],
                                           1.0 / D, None, ALU.mult),
          ["pR1"], ["rf3"])
        V(lambda E, nn=nn: E.tensor_mul(rf[1][0:1, 0:nn], rf[0][0:1, 0:nn],
                                        rf[0][0:1, 0:nn]), ["rf0"], ["rf1"])
        V(lambda E, nn=nn: E.tensor_sub(rf[2][0:1, 0:nn], rf[3][0:1, 0:nn],
                                        rf[1][0:1, 0:nn]),
          ["rf3", "rf1"], ["rf2"])
        V(lambda E, nn=nn: E.tensor_scalar(rf[2][0:1, 0:nn], rf[2][0:1, 0:nn],
                                           EPS, None, ALU.add),
          ["rf2"], ["rf2"])
        V(lambda E, nn=nn: E.reciprocal(rf[3][0:1, 0:nn], rf[2][0:1, 0:nn]),
          ["rf2"], ["rf3"])
        S(lambda E, nn=nn: E.activation(out=rf[1][0:1, 0:nn],
                                        in_=rf[3][0:1, 0:nn], func=AF.Sqrt),
          ["rf3"], ["rf1"])

    def bcast(psum, pname, row_ap, rtags, nn=SUB):
        T(lambda E, nn=nn: E.matmul(psum[:, 0:nn], p_ones_rf[0:1, 0:128],
                                    row_ap, start=True, stop=True),
          ["ones"] + rtags, [pname])

    def bcast16(psum, pname, row_ap, rtags, nn=SUB):
        T(lambda E, nn=nn: E.matmul(psum[:, 0:nn], p_ones_r[0:1, 0:128],
                                    row_ap, start=True, stop=True),
          ["ones"] + rtags, [pname])

    def trig(dst_ap, wtag, src_ap, rtag, kind, nn=SUB):
        add_c = 32.0 if kind == "sin" else 32.25
        bias_i = C_SINB if kind == "sin" else C_COSB
        V(lambda E: E.tensor_scalar(z_tg1[:, 0:nn], src_ap, 1.0 / TWO_PI,
                                    add_c, ALU.mult, ALU.add),
          [rtag], ["z_tg1"])
        V(lambda E: E.tensor_copy(z_tg2[:, 0:nn], z_tg1[:, 0:nn]),
          ["z_tg1"], ["z_tg2"])
        V(lambda E: E.scalar_tensor_tensor(z_tg2[:, 0:nn], z_tg2[:, 0:nn],
                                           -TWO_PI, src_ap, ALU.mult, ALU.add),
          ["z_tg2", rtag], ["z_tg2"])
        S(lambda E: E.activation(out=dst_ap, in_=z_tg2[:, 0:nn], func=AF.Sin,
                                 bias=col(bias_i)),
          ["z_tg2", "p_cst"], [wtag])

    def wmm(psum, pname, w_sb, wtag, mchunk, x_fn, xtags, kchunks, bias,
            extra_k=()):
        first = True
        for kc in kchunks:
            T(lambda E, kc=kc, f=first: E.matmul(
                psum[:, 0:SUB], w_sb[:, kc, mchunk * 128:(mchunk + 1) * 128],
                x_fn(kc), start=f, stop=False), [wtag] + xtags, [pname])
            first = False
        for (w2_sb, w2tag, kc2, x2_fn) in extra_k:
            T(lambda E, kc2=kc2, w2_sb=w2_sb, x2_fn=x2_fn: E.matmul(
                psum[:, 0:SUB], w2_sb[:, kc2, mchunk * 128:(mchunk + 1) * 128],
                x2_fn(kc2), start=False, stop=False),
              [w2tag] + xtags, [pname])
        o, _n = BOFF[bias]
        T(lambda E, o=o, m=mchunk: E.matmul(
            psum[:, 0:SUB], p_bias[:, o + m * 128:o + (m + 1) * 128],
            p_ones_r[:, 0:SUB], start=False, stop=True),
          ["p_bias", "ones"], [pname])

    def process_slot(j, own):
        S(lambda E: E.preload_act_table(AF.Sqrt), [], [])
        for n in range(NSUB):
            t0 = 1 + n * SUB
            for c in range(C):
                dma(a_xsub[:, c, :], xT[c * 128:(c + 1) * 128, j, t0:t0 + SUB],
                    ["a_xsub"], ["a_xsub"], eng="sync")
            ln_sums(lambda kc: a_xsub[:, kc, :], ["a_xsub"])
            ln_rows()
            bcast(pA, "pA", rf[0][0:1, 0:SUB], ["rf0"])
            bcast(pB, "pB", rf[1][0:1, 0:SUB], ["rf1"])
            for c in range(C):
                V(lambda E, c=c: E.tensor_sub(z_p1[:, :], a_xsub[:, c, :],
                                              pA[:, :]),
                  ["a_xsub", "pA"], ["z_p1"])
                V(lambda E, c=c, t0=t0: E.tensor_mul(
                    p_xn[:, c, t0:t0 + SUB], z_p1[:, :], pB[:, :]),
                  ["z_p1", "pB"], ["p_xn"])
            if own:
                V(lambda E: E.tensor_scalar(rf[3][0:1, :], rf[2][0:1, :], EPS,
                                            None, ALU.subtract),
                  ["rf2"], ["rf3"])
                V(lambda E: E.reciprocal(rf[0][0:1, :], rf[2][0:1, :]),
                  ["rf2"], ["rf0"])
                V(lambda E: E.tensor_mul(rf[2][0:1, :], rf[3][0:1, :],
                                         rf[0][0:1, :]), ["rf3", "rf0"],
                  ["rf2"])
                V(lambda E: E.tensor_scalar(rf[3][0:1, :], rf[2][0:1, :], EPS,
                                            None, ALU.add), ["rf2"], ["rf3"])
                V(lambda E: E.reciprocal(rf[0][0:1, :], rf[3][0:1, :]),
                  ["rf3"], ["rf0"])
                S(lambda E, n=n: E.activation(
                    out=r_rho[0:1, n * SUB:(n + 1) * SUB], in_=rf[0][0:1, :],
                    func=AF.Sqrt), ["rf0"], ["r_rho"])
        for c in range(C):
            dma(a_xsub[:, c, :], xT[c * 128:(c + 1) * 128, j, 0:SUB],
                ["a_xsub"], ["a_xsub"], eng="sync")
        ln_sums(lambda kc: a_xsub[:, kc, :], ["a_xsub"])
        ln_rows()
        bcast(pA, "pA", rf[0][0:1, 0:SUB], ["rf0"])
        bcast(pB, "pB", rf[1][0:1, 0:SUB], ["rf1"])
        for c in range(C):
            V(lambda E, c=c: E.tensor_sub(z_p1[:, 0:1], a_xsub[:, c, 0:1],
                                          pA[:, 0:1]), ["a_xsub", "pA"],
              ["z_p1"])
            V(lambda E, c=c: E.tensor_mul(p_xn[:, c, 0:1], z_p1[:, 0:1],
                                          pB[:, 0:1]), ["z_p1", "pB"],
              ["p_xn"])
        pass

        S(lambda E: E.preload_act_table(AF.Sigmoid), [], [])
        for c2 in range(2 * C):
            dma(a_wg1[:, c2, :], Wg1[c2 * 128:(c2 + 1) * 128, :],
                ["a_wg1"] if c2 == 0 else [], ["a_wg1"])
        if own:
            for c in range(C):
                dma(a_wb1[:, c, :], Wb1[c * 128:(c + 1) * 128, :],
                    ["a_wb1"] if c == 0 else [], ["a_wb1"])
        for n in range(NSUB):
            t0 = 1 + n * SUB
            for m in range(C):
                ps1, pn1 = (pA, "pA") if m % 2 == 0 else (pB, "pB")
                wmm(ps1, pn1, a_wg1, "a_wg1", m,
                    lambda kc, t0=t0: p_xn[:, kc, t0:t0 + SUB], ["p_xn"],
                    range(C), "bg1",
                    extra_k=[(a_wg1, "a_wg1", C + kc2,
                              (lambda kc2_, t0=t0: p_xn[:, kc2_ - C,
                                                        t0 - 1:t0 - 1 + SUB]))
                             for kc2 in range(C)])
                V(lambda E, ps1=ps1: E.tensor_copy(z_sq[:, :], ps1[:, :]),
                  [pn1], ["z_sq"])
                S(lambda E: E.activation(out=z_kt[:, :], in_=z_sq[:, :],
                                         func=AF.Erf, scale=col(C_ISQ2)),
                  ["z_sq", "p_cst"], ["z_kt"])
                V(lambda E: E.tensor_scalar(z_ka[:, :], z_kt[:, :], 0.5, 0.5,
                                            ALU.mult, ALU.add),
                  ["z_kt"], ["z_ka"])
                V(lambda E, m=m: E.tensor_mul(a_gh[:, m, :], z_sq[:, :],
                                              z_ka[:, :]),
                  ["z_sq", "z_ka"], ["a_gh"])
                if own and n == 0 and m == 0:
                    tap("z00", z_sq[:, :], ["z_sq"])
            if own and n == 0:
                tap("gh0", a_gh[:, :, :], ["a_gh"])
            for kc in range(C):
                T(lambda E, kc=kc, f=(kc == 0): E.matmul(
                    pR0[0:1, 0:SUB], p_wg2[:, kc, 0:1], a_gh[:, kc, :],
                    start=f, stop=False), ["p_wg2", "a_gh"], ["pR0"])
            o_bg2, _n2 = BOFF["bg2"]
            T(lambda E, o=o_bg2: E.matmul(pR0[0:1, 0:SUB],
                                          p_bias[:, o:o + 1],
                                          p_ones_r[:, 0:SUB],
                                          start=False, stop=True),
              ["p_bias", "ones"], ["pR0"])
            S(lambda E, n=n: E.activation(out=r_vg[0:1, n * SUB:(n + 1) * SUB],
                                          in_=pR0[0:1, :], func=AF.Sigmoid),
              ["pR0"], ["r_vg"])
            if own:
                for m in range(4):
                    ps1, pn1 = (pC_, "pC") if m % 2 == 0 else (pD, "pD")
                    wmm(ps1, pn1, a_wb1, "a_wb1", m,
                        lambda kc, t0=t0: p_xn[:, kc, t0:t0 + SUB], ["p_xn"],
                        range(C), "bb1")
                    V(lambda E, ps1=ps1: E.tensor_copy(z_sq[:, :], ps1[:, :]),
                      [pn1], ["z_sq"])
                    S(lambda E: E.activation(out=z_kt[:, :], in_=z_sq[:, :],
                                             func=AF.Erf, scale=col(C_ISQ2)),
                      ["z_sq", "p_cst"], ["z_kt"])
                    V(lambda E: E.tensor_scalar(z_ka[:, :], z_kt[:, :], 0.5,
                                                0.5, ALU.mult, ALU.add),
                      ["z_kt"], ["z_ka"])
                    V(lambda E, m=m: E.tensor_mul(a_bh[:, m, :], z_sq[:, :],
                                                  z_ka[:, :]),
                      ["z_sq", "z_ka"], ["a_bh"])
                o_bb2, _n2 = BOFF["bb2"]
                for zi in range(2):
                    for kc in range(4):
                        T(lambda E, kc=kc, zi=zi, f=(kc == 0):
                          E.matmul(pR1[0:1, 0:SUB], p_wb2[:, kc, zi:zi + 1],
                                   a_bh[:, kc, :], start=f, stop=False),
                          ["p_wb2", "a_bh"], ["pR1"])
                    T(lambda E, zi=zi, o=o_bb2: E.matmul(
                        pR1[0:1, 0:SUB], p_bias[:, o + zi:o + zi + 1],
                        p_ones_r[:, 0:SUB], start=False, stop=True),
                      ["p_bias", "ones"], ["pR1"])
                    V(lambda E, zi=zi: E.tensor_copy(
                        rf[0 if zi == 0 else 3][0:1, :], pR1[0:1, :]),
                      ["pR1"], ["rf0" if zi == 0 else "rf3"])
                V(lambda E: E.tensor_sub(rf[1][0:1, :], rf[0][0:1, :],
                                         rf[3][0:1, :]),
                  ["rf0", "rf3"], ["rf1"])
                S(lambda E, n=n: E.activation(
                    out=r_bw0[0:1, n * SUB:(n + 1) * SUB], in_=rf[1][0:1, :],
                    func=AF.Sigmoid), ["rf1"], ["r_bw0"])
                S(lambda E, n=n: E.activation(
                    out=r_bw1[0:1, n * SUB:(n + 1) * SUB], in_=rf[1][0:1, :],
                    func=AF.Sigmoid, scale=p_cst[0:1, C_NEG1:C_NEG1 + 1]),
                  ["rf1", "p_cst"], ["r_bw1"])
        pass

        S(lambda E: E.preload_act_table(AF.Silu), [], [])
        for c in range(C):
            dma(a_wpv[:, c, :], Wpv[c * 128:(c + 1) * 128, :],
                ["a_wpv"] if c == 0 else [], ["a_wpv"])
        for c in range(C):
            dma(a_wk[:, c, :], Wk[c * 128:(c + 1) * 128, :],
                ["a_wk"] if c == 0 else [], ["a_wk"])
        for c in range(C):
            dma(a_wkv[:, c, :], Wkv[c * 128:(c + 1) * 128, :],
                ["a_wkv"] if c == 0 else [], ["a_wkv"])
        for m in range(C):
            ps1, pn1 = (pA, "pA") if m % 2 == 0 else (pB, "pB")
            wmm(ps1, pn1, a_wk, "a_wk", m, lambda kc: p_xn[:, kc, 0:SUB],
                ["p_xn"], range(C), "bk")
            S(lambda E, ps1=ps1: E.activation(out=z_kt[:, :], in_=ps1[:, :],
                                              func=AF.Tanh), [pn1], ["z_kt"])
            V(lambda E, m=m: E.tensor_copy(p_bnd[:, m, 0:1], z_kt[:, 0:1]),
              ["z_kt"], ["p_bnd"])
        S(lambda E: E.activation(out=p_bnd[:, :, 3:4], in_=p_bnd[:, :, 0:1],
                                 func=AF.Abs), ["p_bnd"], ["p_bnd"])
        S(lambda E: E.activation(out=p_bnd[:, :, 2:3], in_=p_bnd[:, :, 0:1],
                                 func=AF.Sin, scale=col(C_PI)),
          ["p_bnd", "p_cst"], ["p_bnd"])
        S(lambda E: E.activation(out=p_bnd[:, :, 3:4], in_=p_bnd[:, :, 3:4],
                                 func=AF.Sin, scale=col(C_NPI),
                                 bias=col(C_HPI)), ["p_bnd", "p_cst"],
          ["p_bnd"]) if False else None
        S(lambda E: E.activation(out=z_ka[:, 0:C], in_=p_bnd[:, :, 3],
                                 func=AF.Sin, scale=col(C_NPI),
                                 bias=col(C_HPI)),
          ["p_bnd", "p_cst"], ["z_ka"])
        for m in range(C):
            V(lambda E, m=m, j=j: E.tensor_scalar(
                p_bnd[:, m, 0:1], p_bnd[:, m, 2:3], p_pmask[:, j:j + 1],
                None, ALU.mult), ["p_bnd", "p_pmask"], ["p_bnd"])
            V(lambda E, m=m, j=j: E.tensor_scalar(
                p_bnd[:, m, 1:2], z_ka[:, m:m + 1], p_pmask[:, j:j + 1],
                None, ALU.mult), ["z_ka", "p_pmask"], ["p_bnd"])
        if own:
            for dst, dtag, src, stag in (
                    (p_car_pc, "p_car_pc", p_sum_pc, "p_sum_pc"),
                    (p_car_ps, "p_car_ps", p_sum_ps, "p_sum_ps"),
                    (p_car_kc, "p_car_kc", p_sum_kc, "p_sum_kc"),
                    (p_car_ks, "p_car_ks", p_sum_ks, "p_sum_ks")):
                V(lambda E, dst=dst: E.memset(dst[:, :], 0.0), [], [dtag])
                for jj in range(NS - 1):
                    V(lambda E, dst=dst, src=src, jj=jj:
                      E.scalar_tensor_tensor(dst[:, :], src[:, :, jj],
                                             p_cmask[:, jj:jj + 1],
                                             dst[:, :], ALU.mult, ALU.add),
                      [stag, "p_cmask", dtag], [dtag])
            for jj in range(NS - 1):
                V(lambda E, jj=jj: E.tensor_scalar(
                    p_vgm[:, jj * NSUB:(jj + 1) * NSUB],
                    p_sum_vg[:, jj * NSUB:(jj + 1) * NSUB],
                    p_cmask[:, jj:jj + 1], None,
                    ALU.mult), ["p_sum_vg", "p_cmask"], ["p_vgm"])
            V(lambda E: E.memset(p_vgm[:, (NS - 1) * NSUB:NS * NSUB], 0.0),
              [], ["p_vgm"])
            S(lambda E: E.nop(), ["p_vgm"], ["rt_vg"])
            V(lambda E: E.reduce_sum(p_car_vg[:, 0:1], p_vgm[:, :], axis=AX),
              ["p_vgm", "rt_vg"], ["p_car_vg"])
            V(lambda E: E.memset(r_ones[0:1, :], 1.0), [], ["r_ones"])
            S(lambda E: E.nop(), ["r_ones"], ["rt_ro"])
            V(lambda E: E.tensor_tensor_scan(
                rwf[0:1, :], r_ones[0:1, :], r_vg[0:1, :], 0.0,
                ALU.mult, ALU.add), ["r_ones", "r_vg", "rt_ro"], ["rwf"])
            V(lambda E: E.tensor_scalar(r_gcs[0:1, :], rwf[0:1, :],
                                        p_car_vg[0:1, 0:1], 1.0, ALU.add,
                                        ALU.max),
              ["rwf", "p_car_vg"], ["r_gcs"])
            for i, (src, stag) in enumerate(((p_car_pc, "p_car_pc"),
                                             (p_car_ps, "p_car_ps"),
                                             (p_car_kc, "p_car_kc"),
                                             (p_car_ks, "p_car_ks"))):
                for c in range(C):
                    V(lambda E, i=i, c=c, src=src: E.tensor_copy(
                        p_scarry[:, c, i:i + 1], src[:, c:c + 1]),
                      [stag], ["p_scarry"])

        for n in range(NSUB):
            t0 = 1 + n * SUB
            for c in range(C):
                dma(z_x1[:, :], ppT[c * 128:(c + 1) * 128, j,
                                    n * SUB:(n + 1) * SUB],
                    ["z_x1"], ["z_x1"], eng="sync")
                trig(a_ps[:, c, :], "a_ps", z_x1[:, :], "z_x1", "sin")
                trig(a_pc[:, c, :], "a_pc", z_x1[:, :], "z_x1", "cos")
            for m in range(C):
                ps1, pn1 = (pA, "pA") if m % 2 == 0 else (pB, "pB")
                wmm(ps1, pn1, a_wpv, "a_wpv", m,
                    lambda kc, t0=t0: p_xn[:, kc, t0:t0 + SUB], ["p_xn"],
                    range(C), "bpv")
                V(lambda E, m=m, ps1=ps1: E.tensor_mul(
                    z_p1[:, :], a_pc[:, m, :], ps1[:, :]),
                  ["a_pc", pn1], ["z_p1"])
                V(lambda E, m=m, ps1=ps1: E.tensor_mul(
                    z_p2[:, :], a_ps[:, m, :], ps1[:, :]),
                  ["a_ps", pn1], ["z_p2"])
                if own:
                    S(lambda E: E.nop(), ["z_p1", "z_p2", "p_scarry"],
                      ["rt_s2"])
                    V(lambda E, m=m: E.tensor_tensor_scan(
                        z_m1[:, :], p_ones_sc[:, :], z_p1[:, :],
                        p_scarry[:, m, 0:1], ALU.mult, ALU.add),
                      ["ones", "z_p1", "p_scarry", "rt_s2"], ["z_m1"])
                    V(lambda E, m=m: E.tensor_tensor_scan(
                        z_m2[:, :], p_ones_sc[:, :], z_p2[:, :],
                        p_scarry[:, m, 1:2], ALU.mult, ALU.add),
                      ["ones", "z_p2", "p_scarry"], ["z_m2"])
                    V(lambda E, m=m: E.tensor_copy(p_scarry[:, m, 0:1],
                                                   z_m1[:, SUB - 1:SUB]),
                      ["z_m1"], ["p_scarry"])
                    V(lambda E, m=m: E.tensor_copy(p_scarry[:, m, 1:2],
                                                   z_m2[:, SUB - 1:SUB]),
                      ["z_m2"], ["p_scarry"])
                    spb, sptag = (p_spfa, "p_spfa") if m % 2 == 0 \
                        else (p_spfb, "p_spfb")
                    V(lambda E, m=m: E.tensor_mul(z_p1[:, :], a_pc[:, m, :],
                                                  z_m1[:, :]),
                      ["a_pc", "z_m1"], ["z_p1"])
                    V(lambda E, m=m: E.tensor_mul(z_p2[:, :], a_ps[:, m, :],
                                                  z_m2[:, :]),
                      ["a_ps", "z_m2"], ["z_p2"])
                    V(lambda E, spb=spb: E.tensor_add(spb[:, :], z_p1[:, :],
                                                      z_p2[:, :]),
                      ["z_p1", "z_p2"], [sptag])
                    dma(sp_pos[:, m, n * SUB:(n + 1) * SUB], spb[:, :],
                        [sptag], [f"sp_pos_{m}_{n}"])
                else:
                    V(lambda E, m=m: E.scalar_tensor_tensor(
                        p_junk[:, :], z_p1[:, :], 1.0, p_ones_sc[:, :],
                        ALU.mult, ALU.mult, accum_out=p_acol[:, m:m + 1]),
                      ["z_p1", "ones"], ["p_junk", "p_acol"])
                    V(lambda E, m=m: E.scalar_tensor_tensor(
                        p_junk[:, :], z_p2[:, :], 1.0, p_ones_sc[:, :],
                        ALU.mult, ALU.mult, accum_out=p_acol2[:, m:m + 1]),
                      ["z_p2", "ones"], ["p_junk", "p_acol2"])
                if not own and m == C - 1:
                    if n == 0:
                        V(lambda E, j=j: E.tensor_copy(p_sum_pc[:, :, j],
                                                       p_acol[:, :]),
                          ["p_acol"], ["p_sum_pc"])
                        V(lambda E, j=j: E.tensor_copy(p_sum_ps[:, :, j],
                                                       p_acol2[:, :]),
                          ["p_acol2"], ["p_sum_ps"])
                    else:
                        V(lambda E, j=j: E.tensor_add(p_sum_pc[:, :, j],
                                                      p_sum_pc[:, :, j],
                                                      p_acol[:, :]),
                          ["p_acol", "p_sum_pc"], ["p_sum_pc"])
                        V(lambda E, j=j: E.tensor_add(p_sum_ps[:, :, j],
                                                      p_sum_ps[:, :, j],
                                                      p_acol2[:, :]),
                          ["p_acol2", "p_sum_ps"], ["p_sum_ps"])
            bcast(pE, "pE", r_vg[0:1, n * SUB:(n + 1) * SUB], ["r_vg"])
            if not own:
                V(lambda E, j=j, n=n: E.scalar_tensor_tensor(
                    p_junk[:, :], p_ones_sc[:, :], 1.0, pE[:, :],
                    ALU.bypass, ALU.mult,
                    accum_out=p_sum_vg[:, j * NSUB + n:j * NSUB + n + 1]),
                  ["ones", "pE"], ["p_junk", "p_sum_vg"])
            for m in range(C):
                ps1, pn1 = (pC_, "pC") if m % 2 == 0 else (pD, "pD")
                wmm(ps1, pn1, a_wk, "a_wk", m,
                    lambda kc, t0=t0: p_xn[:, kc, t0:t0 + SUB], ["p_xn"],
                    range(C), "bk")
                S(lambda E, ps1=ps1: E.activation(out=z_kt[:, :],
                                                  in_=ps1[:, :],
                                                  func=AF.Tanh),
                  [pn1], ["z_kt"])
                S(lambda E: E.activation(out=z_ka[:, :], in_=z_kt[:, :],
                                         func=AF.Abs), ["z_kt"], ["z_ka"])
                S(lambda E: E.activation(out=z_ks[:, 1:SUB + 1],
                                         in_=z_kt[:, :], func=AF.Sin,
                                         scale=col(C_PI)),
                  ["z_kt", "p_cst"], ["z_ks"])
                S(lambda E: E.activation(out=z_kc[:, 1:SUB + 1],
                                         in_=z_ka[:, :], func=AF.Sin,
                                         scale=col(C_NPI), bias=col(C_HPI)),
                  ["z_ka", "p_cst"], ["z_kc"])
                V(lambda E, m=m: E.tensor_copy(z_kc[:, 0:1],
                                               p_bnd[:, m, 1:2]),
                  ["p_bnd"], ["z_kc"])
                V(lambda E, m=m: E.tensor_copy(z_ks[:, 0:1],
                                               p_bnd[:, m, 0:1]),
                  ["p_bnd"], ["z_ks"])
                V(lambda E, m=m: E.tensor_copy(p_bnd[:, m, 1:2],
                                               z_kc[:, SUB:SUB + 1]),
                  ["z_kc"], ["p_bnd"])
                V(lambda E, m=m: E.tensor_copy(p_bnd[:, m, 0:1],
                                               z_ks[:, SUB:SUB + 1]),
                  ["z_ks"], ["p_bnd"])
                ps2, pn2 = (pA, "pA") if m % 2 == 0 else (pB, "pB")
                wmm(ps2, pn2, a_wkv, "a_wkv", m,
                    lambda kc, t0=t0: p_xn[:, kc, t0:t0 + SUB], ["p_xn"],
                    range(C), "bkv")
                V(lambda E, ps2=ps2: E.tensor_mul(z_p1[:, :], z_kc[:, 0:SUB],
                                                  ps2[:, :]),
                  ["z_kc", pn2], ["z_p1"])
                V(lambda E, ps2=ps2: E.tensor_mul(z_p2[:, :], z_ks[:, 0:SUB],
                                                  ps2[:, :]),
                  ["z_ks", pn2], ["z_p2"])
                if own:
                    V(lambda E: E.scalar_tensor_tensor(
                        z_p1[:, :], z_p1[:, :], 1.0, pE[:, :], ALU.bypass,
                        ALU.mult), ["z_p1", "pE"], ["z_p1"])
                    V(lambda E: E.scalar_tensor_tensor(
                        z_p2[:, :], z_p2[:, :], 1.0, pE[:, :], ALU.bypass,
                        ALU.mult), ["z_p2", "pE"], ["z_p2"])
                    S(lambda E: E.nop(), ["z_p1", "z_p2", "p_scarry"],
                      ["rt_s3"])
                    V(lambda E, m=m: E.tensor_tensor_scan(
                        z_m1[:, :], p_ones_sc[:, :], z_p1[:, :],
                        p_scarry[:, m, 2:3], ALU.mult, ALU.add),
                      ["ones", "z_p1", "p_scarry", "rt_s3"], ["z_m1"])
                    V(lambda E, m=m: E.tensor_tensor_scan(
                        z_m2[:, :], p_ones_sc[:, :], z_p2[:, :],
                        p_scarry[:, m, 3:4], ALU.mult, ALU.add),
                      ["ones", "z_p2", "p_scarry"], ["z_m2"])
                    V(lambda E, m=m: E.tensor_copy(p_scarry[:, m, 2:3],
                                                   z_m1[:, SUB - 1:SUB]),
                      ["z_m1"], ["p_scarry"])
                    V(lambda E, m=m: E.tensor_copy(p_scarry[:, m, 3:4],
                                                   z_m2[:, SUB - 1:SUB]),
                      ["z_m2"], ["p_scarry"])
                    spb, sptag = (p_spfa, "p_spfa") if m % 2 == 0 \
                        else (p_spfb, "p_spfb")
                    V(lambda E: E.tensor_mul(z_p1[:, :], z_kc[:, 1:SUB + 1],
                                             z_m1[:, :]),
                      ["z_kc", "z_m1"], ["z_p1"])
                    V(lambda E: E.tensor_mul(z_p2[:, :], z_ks[:, 1:SUB + 1],
                                             z_m2[:, :]),
                      ["z_ks", "z_m2"], ["z_p2"])
                    V(lambda E, spb=spb: E.tensor_add(spb[:, :], z_p1[:, :],
                                                      z_p2[:, :]),
                      ["z_p1", "z_p2"], [sptag])
                    dma(sp_kv[:, m, n * SUB:(n + 1) * SUB], spb[:, :],
                        [sptag], [f"sp_kv_{m}_{n}"])
                else:
                    V(lambda E, m=m: E.scalar_tensor_tensor(
                        p_junk[:, :], z_p1[:, :], 1.0, pE[:, :], ALU.bypass,
                        ALU.mult, accum_out=p_acol[:, m:m + 1]),
                      ["z_p1", "pE"], ["p_junk", "p_acol"])
                    V(lambda E, m=m: E.scalar_tensor_tensor(
                        p_junk[:, :], z_p2[:, :], 1.0, pE[:, :], ALU.bypass,
                        ALU.mult, accum_out=p_acol2[:, m:m + 1]),
                      ["z_p2", "pE"], ["p_junk", "p_acol2"])
            if not own:
                if n == 0:
                    V(lambda E, j=j: E.tensor_copy(p_sum_kc[:, :, j],
                                                   p_acol[:, :]),
                      ["p_acol"], ["p_sum_kc"])
                    V(lambda E, j=j: E.tensor_copy(p_sum_ks[:, :, j],
                                                   p_acol2[:, :]),
                      ["p_acol2"], ["p_sum_ks"])
                else:
                    V(lambda E, j=j: E.tensor_add(p_sum_kc[:, :, j],
                                                  p_sum_kc[:, :, j],
                                                  p_acol[:, :]),
                      ["p_acol", "p_sum_kc"], ["p_sum_kc"])
                    V(lambda E, j=j: E.tensor_add(p_sum_ks[:, :, j],
                                                  p_sum_ks[:, :, j],
                                                  p_acol2[:, :]),
                      ["p_acol2", "p_sum_ks"], ["p_sum_ks"])
        pass

    def precast(e):
        for r in range(0, D, 256):
            dma(m1wb[e, r:r + 256, :], m1w[e, r:r + 256, :], [], ["m1wb"],
                waw=False)
        dma(m2wb[e, :, :], m2w[e, :, :], [], ["m2wb"], waw=False)

    for j in range(NS - 1):
        process_slot(j, own=False)
        precast(j)
    process_slot(OWN, own=True)
    precast(OWN)

    tap("xn", p_xn[:, :, :], ["p_xn"])
    tap("vg", r_vg[0:1, :], ["r_vg"])
    tap("bw0", r_bw0[0:1, :], ["r_bw0"])
    tap("bw1", r_bw1[0:1, :], ["r_bw1"])
    tap("rho", r_rho[0:1, :], ["r_rho"])
    tap("gcs", r_gcs[0:1, :], ["r_gcs"])
    tap("car_pc", p_car_pc[:, :], ["p_car_pc"])
    tap("car_ps", p_car_ps[:, :], ["p_car_ps"])
    tap("car_kc", p_car_kc[:, :], ["p_car_kc"])
    tap("car_ks", p_car_ks[:, :], ["p_car_ks"])
    tap("sum_pc", p_sum_pc[:, :, :], ["p_sum_pc"])
    tap("sum_kc", p_sum_kc[:, :, :], ["p_sum_kc"])
    tap("sum_vg", p_sum_vg[0:1, :], ["p_sum_vg"])

    S(lambda E: E.preload_act_table(AF.Sqrt), [], [])
    for c in range(C):
        dma(a_wo[:, c, :], Wo[c * 128:(c + 1) * 128, :],
            ["a_wo"] if c == 0 else [], ["a_wo"])
    V(lambda E: E.reciprocal(rwf[0:1, :], r_gcs[0:1, :]), ["r_gcs"], ["rwf"])
    S(lambda E: E.activation(out=r_gcs[0:1, :], in_=rwf[0:1, :],
                             func=AF.Sqrt), ["rwf"], ["r_gcs"])
    V(lambda E: E.tensor_mul(rwf[0:1, :], r_bw1[0:1, :], r_gcs[0:1, :]),
      ["r_bw1", "r_gcs"], ["rwf"])
    for n in range(NSUB):
        T(lambda E, n=n: E.matmul(pA[:, 0:SUB],
                                  p_ones_r[0:1, SUB:SUB + 128],
                                  r_bw0[0:1, n * SUB:(n + 1) * SUB],
                                  start=True, stop=True),
          ["ones", "r_bw0"], ["pA"])
        T(lambda E, n=n: E.matmul(pB[:, 0:SUB],
                                  p_ones_rf[0:1, 128:256],
                                  rwf[0:1, n * SUB:(n + 1) * SUB],
                                  start=True, stop=True),
          ["ones", "rwf"], ["pB"])
        for c in range(C):
            dma(z_kt[:, :], sp_pos[:, c, n * SUB:(n + 1) * SUB],
                [f"sp_pos_{c}_{n}", "z_kt"], ["z_kt"], eng="sync")
            dma(z_ka[:, :], sp_kv[:, c, n * SUB:(n + 1) * SUB],
                [f"sp_kv_{c}_{n}", "z_ka"], ["z_ka"], eng="sync")
            V(lambda E: E.tensor_mul(z_p1[:, :], z_kt[:, :], pA[:, :]),
              ["z_kt", "pA"], ["z_p1"])
            V(lambda E: E.tensor_mul(z_p2[:, :], z_ka[:, :], pB[:, :]),
              ["z_ka", "pB"], ["z_p2"])
            spb, sptag = (p_spfa, "p_spfa") if c % 2 == 0 \
                else (p_spfb, "p_spfb")
            V(lambda E, spb=spb: E.tensor_add(spb[:, :], z_p1[:, :],
                                              z_p2[:, :]),
              ["z_p1", "z_p2"], [sptag])
            T(lambda E, c=c, f=(c == 0), l=(c == C - 1), spb=spb: E.matmul(
                pR0[0:1, 0:SUB], p_ones_c[:, 0:1], spb[:, :],
                start=f, stop=l), ["ones", sptag], ["pR0"])
            S(lambda E, spb=spb: E.activation(out=z_sq[:, :], in_=spb[:, :],
                                              func=AF.Square),
              [sptag], ["z_sq"])
            T(lambda E, c=c, f=(c == 0), l=(c == C - 1): E.matmul(
                pR1[0:1, 0:SUB], p_ones_c[:, 0:1], z_sq[:, :],
                start=f, stop=l), ["ones", "z_sq"], ["pR1"])
            dma(sp_bl[:, c, n * SUB:(n + 1) * SUB], spb[:, :], [sptag],
                [f"sp_bl_{c}_{n}"])
        ln_rows()
        V(lambda E, n=n: E.tensor_copy(r_mu2[0:1, n * SUB:(n + 1) * SUB],
                                       rf[0][0:1, :]), ["rf0"], ["r_mu2"])
        V(lambda E, n=n: E.tensor_copy(r_r2[0:1, n * SUB:(n + 1) * SUB],
                                       rf[1][0:1, :]), ["rf1"], ["r_r2"])
    for n in range(NSUB):
        t0 = 1 + n * SUB
        bcast16(pA, "pA", r_mu2[0:1, n * SUB:(n + 1) * SUB], ["r_mu2"])
        bcast16(pB, "pB", r_r2[0:1, n * SUB:(n + 1) * SUB], ["r_r2"])
        for c in range(C):
            dma(z_kt[:, :], sp_bl[:, c, n * SUB:(n + 1) * SUB],
                [f"sp_bl_{c}_{n}", "z_kt"], ["z_kt"], eng="sync")
            V(lambda E: E.tensor_sub(z_p1[:, :], z_kt[:, :], pA[:, :]),
              ["z_kt", "pA"], ["z_p1"])
            V(lambda E: E.tensor_mul(z_p2[:, :], z_p1[:, :], pB[:, :]),
              ["z_p1", "pB"], ["z_p2"])
            S(lambda E, c=c: E.activation(out=a_ln[:, c, :], in_=z_p2[:, :],
                                          func=AF.Identity,
                                          scale=p_lng[:, c:c + 1],
                                          bias=p_lnb[:, c:c + 1]),
              ["z_p2", "p_lng", "p_lnb"], ["a_ln"])
        for m in range(C):
            ps1, pn1 = (pC_, "pC") if m % 2 == 0 else (pD, "pD")
            wmm(ps1, pn1, a_wo, "a_wo", m, lambda kc: a_ln[:, kc, :],
                ["a_ln"], range(C), "bo")
            spb, sptag = (p_sp16a, "p_sp16a") if m % 2 == 0 \
                else (p_sp16b, "p_sp16b")
            V(lambda E, m=m, ps1=ps1, t0=t0, spb=spb: E.tensor_add(
                spb[:, :], ps1[:, :], p_xn[:, m, t0:t0 + SUB]),
              [pn1, "p_xn"], [sptag])
            dma(sp_s[:, m, n * SUB:(n + 1) * SUB], spb[:, :], [sptag],
                [f"sp_s_{m}_{n}"])
    pass
    tap_dram("pos", sp_pos, [128, C, TC], F32,
             [f"sp_pos_{c}_{n}" for c in range(C) for n in range(NSUB)])
    tap_dram("kv", sp_kv, [128, C, TC], F32,
             [f"sp_kv_{c}_{n}" for c in range(C) for n in range(NSUB)])
    tap_dram("bl", sp_bl, [128, C, TC], F32,
             [f"sp_bl_{c}_{n}" for c in range(C) for n in range(NSUB)])
    tap_dram("s", sp_s, [128, C, TC], BF16,
             [f"sp_s_{c}_{n}" for c in range(C) for n in range(NSUB)])
    tap("mu2", r_mu2[0:1, :], ["r_mu2"])
    tap("r2", r_r2[0:1, :], ["r_r2"])
    tap("bw1b", r_bw1[0:1, :], ["r_bw1"])

    S(lambda E: E.preload_act_table(AF.Silu), [], [])
    for c in range(C):
        dma(sp_xn[:, c, :], p_xn[:, c, 1:TC + 1], ["p_xn"], [f"sp_xn_{c}"])
    for n in range(NSUB):
        t0 = 1 + n * SUB
        bcast16(pA, "pA", r_rho[0:1, n * SUB:(n + 1) * SUB], ["r_rho"])
        for c in range(C):
            V(lambda E, c=c, t0=t0: E.tensor_mul(
                z_p1[:, :], p_xn[:, c, t0:t0 + SUB], pA[:, :]),
              ["p_xn", "pA", f"sp_xn_{c}"], ["z_p1"])
            V(lambda E, c=c, t0=t0: E.tensor_scalar(
                p_xn[:, c, t0:t0 + SUB], z_p1[:, :], p_nsc[:, c:c + 1], None,
                ALU.mult), ["z_p1", "p_nsc"], ["p_xn"])
    for nt in range(NTT):
        t0 = 1 + nt * 128
        psg, psgn = (pE, "pE") if nt % 2 == 0 else (pF, "pF")
        for kc in range(C):
            T(lambda E, kc=kc, t0=t0, f=(kc == 0), psg=psg: E.matmul(
                psg[0:128, 0:NE], p_xn[:, kc, t0:t0 + 128],
                p_gw[:, kc, :], start=f, stop=False),
              ["p_xn", "p_gw"], [psgn])
        o_gb, _n2 = BOFF["gate_b"]
        T(lambda E, o=o_gb, psg=psg: E.matmul(
            psg[0:128, 0:NE], p_ones_r[0:1, 0:128],
            p_bias[:, o:o + NE], start=False, stop=True),
          ["ones", "p_bias"], [psgn])
        S(lambda E, nt=nt, psg=psg: E.activation(
            out=p_gl[:, nt * NE:(nt + 1) * NE], in_=psg[0:128, 0:NE],
            func=AF.Copy), [psgn], ["p_gl"])
    pass

    def sl_(nt):
        return slice(nt * NE, (nt + 1) * NE)
    for nt in range(NTT):
        V(lambda E, nt=nt: E.reduce_max(p_m1c[:, nt:nt + 1],
                                        p_gl[:, sl_(nt)], axis=AX),
          ["p_gl"], ["p_m1c"])
    for nt in range(NTT):
        S(lambda E, nt=nt: E.activation(out=p_nm1f[:, nt:nt + 1],
                                        in_=p_m1c[:, nt:nt + 1],
                                        func=AF.Identity, scale=col(C_NEG1)),
          ["p_m1c", "p_cst"], ["p_nm1f"])
    for nt in range(NTT):
        S(lambda E, nt=nt: E.activation(out=p_t2a[:, sl_(nt)],
                                        in_=p_gl[:, sl_(nt)],
                                        func=AF.Identity,
                                        bias=p_nm1f[:, nt:nt + 1]),
          ["p_gl", "p_nm1f"], ["p_t2a"])
    for nt in range(NTT):
        S(lambda E, nt=nt: E.activation(out=p_t2b[:, sl_(nt)],
                                        in_=p_t2a[:, sl_(nt)],
                                        func=AF.Sign), ["p_t2a"], ["p_t2b"])
    for nt in range(NTT):
        S(lambda E, nt=nt: E.activation(out=p_t2a[:, sl_(nt)],
                                        in_=p_t2b[:, sl_(nt)],
                                        func=AF.Square), ["p_t2b"], ["p_t2a"])
    for nt in range(NTT):
        S(lambda E, nt=nt: E.activation(out=p_gw1[:, sl_(nt)],
                                        in_=p_t2a[:, sl_(nt)],
                                        func=AF.Identity, scale=col(C_NEG1),
                                        bias=col(C_ONE)),
          ["p_t2a", "p_cst"], ["p_gw1"])
    for nt in range(NTT):
        V(lambda E, nt=nt: E.scalar_tensor_tensor(
            p_t2a[:, sl_(nt)], p_gw1[:, sl_(nt)], -30000.0,
            p_gl[:, sl_(nt)], ALU.mult, ALU.add),
          ["p_gw1", "p_gl"], ["p_t2a"])
    S(lambda E: E.nop(), ["p_t2a"], ["rt_t2"])
    for nt in range(NTT):
        V(lambda E, nt=nt: E.reduce_max(p_m2c[:, nt:nt + 1],
                                        p_t2a[:, sl_(nt)], axis=AX),
          ["p_t2a", "rt_t2"], ["p_m2c"])
    for nt in range(NTT):
        S(lambda E, nt=nt: E.activation(out=p_nm2f[:, nt:nt + 1],
                                        in_=p_m2c[:, nt:nt + 1],
                                        func=AF.Identity, scale=col(C_NEG1)),
          ["p_m2c", "p_cst"], ["p_nm2f"])
    for nt in range(NTT):
        S(lambda E, nt=nt: E.activation(out=p_t2a[:, sl_(nt)],
                                        in_=p_gl[:, sl_(nt)],
                                        func=AF.Identity,
                                        bias=p_nm2f[:, nt:nt + 1]),
          ["p_gl", "p_nm2f"], ["p_t2a"])
    for nt in range(NTT):
        S(lambda E, nt=nt: E.activation(out=p_t2b[:, sl_(nt)],
                                        in_=p_t2a[:, sl_(nt)],
                                        func=AF.Sign), ["p_t2a"], ["p_t2b"])
    for nt in range(NTT):
        S(lambda E, nt=nt: E.activation(out=p_t2a[:, sl_(nt)],
                                        in_=p_t2b[:, sl_(nt)],
                                        func=AF.Square), ["p_t2b"], ["p_t2a"])
    for nt in range(NTT):
        S(lambda E, nt=nt: E.activation(out=p_t2b[:, sl_(nt)],
                                        in_=p_t2a[:, sl_(nt)],
                                        func=AF.Identity, scale=col(C_NEG1),
                                        bias=col(C_ONE)),
          ["p_t2a", "p_cst"], ["p_t2b"])
    for nt in range(NTT):
        S(lambda E, nt=nt: E.activation(out=p_nmh2[:, nt:nt + 1],
                                        in_=p_m2c[:, nt:nt + 1],
                                        func=AF.Identity, scale=col(C_NEGH)),
          ["p_m2c", "p_cst"], ["p_nmh2"])
    for nt in range(NTT):
        S(lambda E, nt=nt: E.activation(out=p_nmh1[:, nt:nt + 1],
                                        in_=p_m1c[:, nt:nt + 1],
                                        func=AF.Identity, scale=col(C_NEGH)),
          ["p_m1c", "p_cst"], ["p_nmh1"])
    for nt in range(NTT):
        S(lambda E, nt=nt: E.activation(out=p_th1[:, nt:nt + 1],
                                        in_=p_m1c[:, nt:nt + 1],
                                        func=AF.Tanh, scale=col(C_HALF),
                                        bias=p_nmh2[:, nt:nt + 1]),
          ["p_m1c", "p_nmh2", "p_cst"], ["p_th1"])
    for nt in range(NTT):
        S(lambda E, nt=nt: E.activation(out=p_th2[:, nt:nt + 1],
                                        in_=p_m2c[:, nt:nt + 1],
                                        func=AF.Tanh, scale=col(C_HALF),
                                        bias=p_nmh1[:, nt:nt + 1]),
          ["p_m2c", "p_nmh1", "p_cst"], ["p_th2"])
    for nt in range(NTT):
        S(lambda E, nt=nt: E.activation(out=p_p1k[:, nt:nt + 1],
                                        in_=p_th1[:, nt:nt + 1],
                                        func=AF.Identity, scale=col(C_HALF),
                                        bias=col(C_HALF)),
          ["p_th1", "p_cst"], ["p_p1k"])
    for nt in range(NTT):
        S(lambda E, nt=nt: E.activation(out=p_p2k[:, nt:nt + 1],
                                        in_=p_th2[:, nt:nt + 1],
                                        func=AF.Identity, scale=col(C_HALF),
                                        bias=col(C_HALF)),
          ["p_th2", "p_cst"], ["p_p2k"])
    for nt in range(NTT):
        V(lambda E, nt=nt: E.tensor_scalar(p_t2a[:, sl_(nt)],
                                           p_gw1[:, sl_(nt)],
                                           p_p1k[:, nt:nt + 1], None,
                                           ALU.mult),
          ["p_gw1", "p_p1k"], ["p_t2a"])
    for nt in range(NTT):
        V(lambda E, nt=nt: E.tensor_scalar(p_gw1[:, sl_(nt)],
                                           p_t2b[:, sl_(nt)],
                                           p_p2k[:, nt:nt + 1], None,
                                           ALU.mult),
          ["p_t2b", "p_p2k"], ["p_gw1"])
    V(lambda E: E.tensor_add(p_gw1[:, :], p_gw1[:, :], p_t2a[:, :]),
      ["p_gw1", "p_t2a"], ["p_gw1"])
    tap("gl0", p_gl[:, 0:NE], ["p_gl"])
    tap("m1c", p_m1c[:, :], ["p_m1c"])
    tap("m2c", p_m2c[:, :], ["p_m2c"])
    tap("p1k", p_p1k[:, :], ["p_p1k"])
    tap("gw10", p_gw1[:, 0:NE], ["p_gw1"])
    pass
    for nt in range(NTT):
        psg, psgn = (pE, "pE") if nt % 2 == 0 else (pF, "pF")
        T(lambda E, nt=nt, psg=psg: E.transpose(
            psg[0:NE, 0:128], p_gw1[:, nt * NE:(nt + 1) * NE],
            p_identf[:, :]),
          ["p_gw1", "p_identf"], [psgn])
        S(lambda E, nt=nt, psg=psg: E.activation(
            out=p_wrow[:, nt * 128:(nt + 1) * 128], in_=psg[0:NE, 0:128],
            func=AF.Copy), [psgn], ["p_wrow"])
    tap("wrow", p_wrow[:, :], ["p_wrow"])
    pass
    V(lambda E: E.memset(f_acc[:, :, :], 0.0), [], ["f_acc"])
    tap_dram("m2wb0", m2wb[0, 0:128, 0:128], [128, 128], BF16, ["m2wb"])
    tap_dram("m2wb1", m2wb[0, 1024:1152, 0:128], [128, 128], BF16, ["m2wb"])
    tap_dram("m2wb2", m2wb[0, 1536:1664, 0:128], [128, 128], BF16, ["m2wb"])
    tap_dram("m1wb0", m1wb[0, 512:640, 0:128], [128, 128], BF16, ["m1wb"])
    qrr = [0]

    def wdma(dst, src_ap, rtags, wtags, waw=True):
        eng = ("gpsimd", "sync")[qrr[0] % 2]
        qrr[0] += 1
        dma(dst, src_ap, rtags, wtags, eng=eng, waw=waw)

    for e in range(NE):
        wdma(f_be[:, 0, :], m1bT[e, 0], ["f_be"], ["f_be"])
        wdma(f_be[:, 1, :], m1bT[e, 1], [], ["f_be"])
        wdma(f_b2c[:, :], m2bT[e], ["f_b2c"], ["f_b2c"])
        S(lambda E: E.activation(out=f_beE[:, :], in_=f_be[:, 0, :],
                                 func=AF.Identity, scale=col(C_ALPHA)),
          ["f_be", "p_cst"], ["f_beE"])
        for half in range(2):
            hbase = half * (TC // 2)
            for jj in range(JC):
                for c in range(4):
                    wdma(f_w1a[:, c, :],
                         m1wb[e, c * 128:(c + 1) * 128,
                              jj * 256:(jj + 1) * 256],
                         [], ["f_w1a"], waw=False)
                for c in range(4):
                    wdma(f_w1b[:, c, :],
                         m1wb[e, (c + 4) * 128:(c + 5) * 128,
                              jj * 256:(jj + 1) * 256],
                         [], ["f_w1b"], waw=False)
                for n2 in range(2):
                    t0 = 1 + hbase + n2 * SUB
                    q = jj * 2 + n2
                    psE, pnE, psO, pnO = (
                        (pA, "pA", pB, "pB") if q % 2 == 0
                        else (pC_, "pC", pD, "pD"))
                    for kc in range(C):
                        wt, wtag = (f_w1a, "f_w1a") if kc < 4 \
                            else (f_w1b, "f_w1b")
                        T(lambda E, kc=kc, t0=t0, f=(kc == 0), wt=wt,
                          psE=psE, l=(kc == C - 1): E.matmul(
                            psE[:, 0:SUB], wt[:, kc % 4, 0:256:2],
                            p_xn[:, kc, t0:t0 + SUB], start=f, stop=l),
                          [wtag, "p_xn"], [pnE])
                    for kc in range(C):
                        wt, wtag = (f_w1a, "f_w1a") if kc < 4 \
                            else (f_w1b, "f_w1b")
                        T(lambda E, kc=kc, t0=t0, f=(kc == 0), wt=wt,
                          psO=psO, l=(kc == C - 1): E.matmul(
                            psO[:, 0:SUB], wt[:, kc % 4, 1:256:2],
                            p_xn[:, kc, t0:t0 + SUB], start=f, stop=l),
                          [wtag, "p_xn"], [pnO])
                    S(lambda E, jj=jj, psE=psE: E.activation(
                        out=f_swz3[:, :], in_=psE[:, :], func=AF.Silu,
                        scale=col(C_ALPHA), bias=f_beE[:, jj:jj + 1]),
                      [pnE, "p_cst", "f_beE"], ["f_swz3"])
                    V(lambda E: E.tensor_scalar(f_swz4[:, :], f_swz3[:, :],
                                                D7, None, ALU.min),
                      ["f_swz3"], ["f_swz4"])
                    S(lambda E, jj=jj, psO=psO: E.activation(
                        out=f_swz[:, :], in_=psO[:, :], func=AF.Identity,
                        bias=f_be[:, 1, jj:jj + 1]),
                      [pnO, "f_be"], ["f_swz"])
                    V(lambda E: E.tensor_scalar(f_swz2[:, :], f_swz[:, :],
                                                LIMIT, -LIMIT, ALU.min,
                                                ALU.max),
                      ["f_swz"], ["f_swz2"])
                    V(lambda E: E.tensor_scalar(f_swz[:, :], f_swz2[:, :],
                                                1.0, 1.0 / ALPHA, ALU.add,
                                                ALU.mult),
                      ["f_swz2"], ["f_swz"])
                    V(lambda E, jj=jj, n2=n2: E.tensor_mul(
                        f_h[:, jj, n2 * SUB:(n2 + 1) * SUB], f_swz4[:, :],
                        f_swz[:, :]), ["f_swz4", "f_swz"], ["f_h"])
                    if e == 0 and half == 0 and jj == 0 and n2 == 0:
                        tap("fh0", f_h[:, 0, 0:SUB], ["f_h"])
                        tap("fbeE", f_beE[:, :], ["f_beE"])
                        tap("fbeO", f_be[:, 1, :], ["f_be"])
            if e == 0 and half == 0:
                tap("fhall", f_h[:, :, :], ["f_h"])
            for n2 in range(2):
                n = half * 2 + n2
                T(lambda E, e=e, n=n: E.matmul(
                    pE[:, 0:SUB], p_sel[0:NE, e * 128:(e + 1) * 128],
                    p_wrow[0:NE, n * SUB:(n + 1) * SUB],
                    start=True, stop=True), ["p_sel", "p_wrow"], ["pE"])
                V(lambda E, n2=n2: E.tensor_copy(f_bwe[:, n2, :], pE[:, :]),
                  ["pE"], ["f_bwe"])
            for m in range(C):
                for k in range(8):
                    wdma(f_w2ma[:, k, :],
                         m2wb[e, k * 128:(k + 1) * 128,
                              m * 128:(m + 1) * 128],
                         [], ["f_w2ma"], waw=False)
                for k in range(8):
                    wdma(f_w2mb[:, k, :],
                         m2wb[e, (k + 8) * 128:(k + 9) * 128,
                              m * 128:(m + 1) * 128],
                         [], ["f_w2mb"], waw=False)
                if e == 0 and half == 0 and m == 0:
                    tap("fw2a", f_w2ma[:, :, :], ["f_w2ma"])
                    tap("fw2b", f_w2mb[:, :, :], ["f_w2mb"])
                for n2 in range(2):
                    n = half * 2 + n2
                    ps1, pn1 = ((pC_, "pC") if n2 == 0 else (pD, "pD")) \
                        if m % 2 == 0 else ((pF, "pF") if n2 == 0
                                            else (pA, "pA"))
                    for kc in range(JC):
                        wt, wtag = (f_w2ma, "f_w2ma") if kc < 8 \
                            else (f_w2mb, "f_w2mb")
                        T(lambda E, kc=kc, n2=n2, ps1=ps1, wt=wt,
                          f=(kc == 0), l=(kc == JC - 1):
                          E.matmul(ps1[:, 0:SUB], wt[:, kc % 8, :],
                                   f_h[:, kc, n2 * SUB:(n2 + 1) * SUB],
                                   start=f, stop=l),
                          [wtag, "f_h"], [pn1])
                    S(lambda E, m=m, ps1=ps1: E.activation(
                        out=f_swz3[:, :], in_=ps1[:, :], func=AF.Identity,
                        bias=f_b2c[:, m:m + 1]),
                      [pn1, "f_b2c"], ["f_swz3"])
                    if e == 0 and half == 0 and m == 0 and n2 == 0:
                        tap("fy00", f_swz3[:, :], ["f_swz3"])
                    V(lambda E, n2=n2: E.tensor_mul(
                        f_swz[:, :], f_swz3[:, :], f_bwe[:, n2, :]),
                      ["f_swz3", "f_bwe"], ["f_swz"])
                    V(lambda E, m=m, n=n: E.tensor_add(
                        f_acc[:, m, n * SUB:(n + 1) * SUB],
                        f_acc[:, m, n * SUB:(n + 1) * SUB], f_swz[:, :]),
                      ["f_swz", "f_acc"], ["f_acc"])
                    if e == 0 and m == 0 and n == 3:
                        tap("facc00", f_acc[:, 0, :], ["f_acc"])
                        tap("fbwe0", f_bwe[:, :, :], ["f_bwe"])
    b.barrier()
    tap("acc", f_acc[:, :, :], ["f_acc"])
    tap("t_fm", p_xn[:, :, :], ["p_xn"])

    for c in range(C):
        dma(p_xn[:, c, 1:TC + 1], sp_xn[:, c, :], [f"sp_xn_{c}", "p_xn"],
            ["p_xn"])
    for n in range(NSUB):
        t0 = 1 + n * SUB
        for c in range(C):
            dma(g_x[:, c, :], xT[c * 128:(c + 1) * 128, OWN, t0:t0 + SUB],
                ["g_x"], ["g_x"], eng="sync")
            dma(g_s[:, c, :], sp_s[:, c, n * SUB:(n + 1) * SUB],
                [f"sp_s_{c}_{n}", "g_s"], ["g_s"], eng="sync")
        V(lambda E, t0=t0, n=n: E.tensor_add(
            g_o[:, :, :], p_xn[:, :, t0:t0 + SUB],
            f_acc[:, :, n * SUB:(n + 1) * SUB]),
          ["p_xn", "f_acc"], ["g_o"])
        S(lambda E: E.activation(out=g_g[:, :, :], in_=g_o[:, :, :],
                                 func=AF.Silu), ["g_o"], ["g_g"])
        V(lambda E: E.tensor_mul(g_o[:, :, :], g_s[:, :, :], g_g[:, :, :]),
          ["g_s", "g_g"], ["g_o"])
        V(lambda E: E.tensor_add(g_x[:, :, :], g_x[:, :, :], g_o[:, :, :]),
          ["g_x", "g_o"], ["g_x"])
        dma(out[:, :, n * SUB:(n + 1) * SUB], g_x[:, :, :], ["g_x"],
            [f"out_{n}"], eng="sync")

    for semname in ("d_sync", "d_gpsimd", "d_scalar"):
        if b.cnt.get(semname, 0) > 0:
            b.ops["sync"].append(("w", semname, b.cnt[semname]))
    b.finalize_and_emit()
    return nc


def kernel(**inputs):
    x = np.asarray(inputs["L0_attn_x"])
    pp = np.asarray(inputs["L0_attn_pos_phases"])
    L = x.shape[1]
    xT_full = np.ascontiguousarray(x[0].T)
    ppT_full = np.ascontiguousarray(pp.T)

    def W(name):
        return np.ascontiguousarray(np.asarray(inputs[name], np.float32))

    def colsfmt(v):
        return np.ascontiguousarray(np.asarray(v, np.float32)
                                    .reshape(C, 128).T)

    consts = np.zeros((128, NCST), np.float32)
    consts[:, C_SINB] = 64 * math.pi
    consts[:, C_COSB] = 64 * math.pi + math.pi / 2
    consts[:, C_PI] = math.pi
    consts[:, C_NPI] = -math.pi
    consts[:, C_HPI] = math.pi / 2
    consts[:, C_ISQ2] = 1.0 / math.sqrt(2.0)
    consts[:, C_ALPHA] = ALPHA
    consts[:, C_NEG1] = -1.0
    consts[:, C_ONE] = 1.0
    consts[:, C_HALF] = 0.5
    consts[:, C_NEGH] = -0.5

    sel = np.zeros((NE, NE * 128), np.float32)
    for e in range(NE):
        sel[e, e * 128:(e + 1) * 128] = 1.0

    brow = np.zeros((1, 5664), np.float32)

    def put(name, v):
        o, n = BOFF[name]
        brow[0, o:o + n] = np.asarray(v, np.float32).reshape(-1)

    put("bpv", inputs["L0_attn_bpv"]); put("bk", inputs["L0_attn_bk"])
    put("bkv", inputs["L0_attn_bkv"]); put("bg1", inputs["L0_attn_bg1"])
    put("bo", inputs["L0_attn_bo"]); put("bb1", inputs["L0_attn_bb1"])
    put("gate_b", inputs["L0_ffn_gate_b"]); put("bb2", inputs["L0_attn_bb2"])
    put("bg2", inputs["L0_attn_bg2"])

    shared = {
        "Wpv": W("L0_attn_Wpv"), "Wk": W("L0_attn_Wk"),
        "Wkv": W("L0_attn_Wkv"), "Wg1": W("L0_attn_Wg1"),
        "Wg2": W("L0_attn_Wg2"), "Wb1": W("L0_attn_Wb1"),
        "Wb2": W("L0_attn_Wb2"), "Wo": W("L0_attn_Wo"),
        "biasrow": brow,
        "ln_g": colsfmt(inputs["L0_attn_ln_g"]),
        "ln_b": colsfmt(inputs["L0_attn_ln_b"]),
        "nsc": colsfmt(inputs["L0_ffn_norm_scale"]),
        "gate_w": W("L0_ffn_gate_w"),
        "m1w": W("L0_ffn_mlp1_w"),
        "m1bT": np.ascontiguousarray(
            W("L0_ffn_mlp1_b").reshape(NE, JC, 128, 2)
            .transpose(0, 3, 2, 1)),
        "m2w": W("L0_ffn_mlp2_w"),
        "m2bT": np.ascontiguousarray(
            W("L0_ffn_mlp2_b").reshape(NE, C, 128).transpose(0, 2, 1)),
        "selw": sel, "cst": consts,
        "identw": np.eye(128, dtype=np.float32),
    }

    in_maps = []
    for c in range(NCORES):
        order = [k for k in range(NCORES) if k != c] + [c]
        xT_sl = np.zeros((D, NS, TC + 1), np.float32)
        ppT_sl = np.zeros((D, NS, TC), np.float32)
        cm = np.zeros((128, NS), np.float32)
        pm = np.zeros((128, NS), np.float32)
        for jslot, k in enumerate(order):
            t0 = k * TC
            if t0 > 0:
                xT_sl[:, jslot, 0] = xT_full[:, t0 - 1]
                pm[:, jslot] = 1.0
            xT_sl[:, jslot, 1:] = xT_full[:, t0:t0 + TC]
            ppT_sl[:, jslot, :] = ppT_full[:, t0:t0 + TC]
            if k < c:
                cm[:, jslot] = 1.0
        m = dict(shared)
        m["xT"] = xT_sl
        m["ppT"] = ppT_sl
        m["cmask"] = cm
        m["pmask"] = pm
        in_maps.append(m)

    nc = build_program()
    res = run_bass_kernel_spmd(nc, in_maps, core_ids=list(range(NCORES)))
    global LAST_RES
    LAST_RES = res
    outs = []
    for c in range(NCORES):
        o = np.asarray(res.results[c]["out"])
        outs.append(np.transpose(o, (2, 1, 0)).reshape(TC, D))
    return np.concatenate(outs, axis=0).reshape(1, L, D).astype(np.float32)


# version 05
# speedup vs baseline: 1.7947x; 1.0051x over previous
import math
import numpy as np
import concourse.bass as bass
import concourse.mybir as mybir
from concourse.bass_utils import run_bass_kernel_spmd

F32 = mybir.dt.float32
BF16 = mybir.dt.bfloat16
I32 = mybir.dt.int32
AF = mybir.ActivationFunctionType
ALU = mybir.AluOpType
AX = mybir.AxisListType.X

NCORES = 8
D = 1024
C = 8
TC = 2048
NS = 8
SUB = 512
NSUB = TC // SUB
NE = 8
JC = 16
OWN = NS - 1
TWO_PI = 2.0 * math.pi
EPS = 1e-5
ALPHA = 1.702
LIMIT = 7.0
D7 = ALPHA * LIMIT * (1.0 / (1.0 + math.exp(-ALPHA * LIMIT)))

(C_SINB, C_COSB, C_PI, C_NPI, C_HPI, C_ISQ2, C_ALPHA, C_NEG1, C_ONE,
 C_HALF, C_NEGH) = range(11)
NCST = 11

ENGINES = ("sync", "scalar", "vector", "tensor", "gpsimd")

TAPS = set()
LAST_RES = None

BOFF = {"bpv": (0, D), "bk": (D, D), "bkv": (2 * D, D), "bg1": (3 * D, D),
        "bo": (4 * D, D), "bb1": (5 * D, D // 2), "gate_b": (5 * D + 512, NE),
        "bb2": (5 * D + 520, 2), "bg2": (5 * D + 522, 1)}


class Build:
    def __init__(self, nc):
        self.nc = nc
        self.ops = {e: [] for e in ENGINES}
        self.cnt = {}
        self.waited = {}
        self.buf_w = {}
        self.buf_r = {}
        self.alias = {}
        for e in ENGINES:
            self.cnt["p_" + e] = 0
        for e in ("sync", "gpsimd", "scalar"):
            self.cnt["d_" + e] = 0

    def _wait(self, eng, semname, val):
        if val <= 0:
            return
        if semname == "p_" + eng:
            return
        if self.waited.get((eng, semname), 0) >= val:
            return
        self.waited[(eng, semname)] = val
        self.ops[eng].append(("w", semname, val))

    def set_alias(self, ivmap):
        items = list(ivmap.items())
        ov = {}
        for n1, (l1, h1) in items:
            for n2, (l2, h2) in items:
                if n2 != n1 and l2 < h1 and l1 < h2:
                    ov.setdefault(n1, []).append(n2)
        self.alias = ov

    def emit(self, eng, fn, reads=(), writes=(), dma=False, waw=True):
        for bn in reads:
            for sn, v in (self.buf_w.get(bn) or {}).items():
                self._wait(eng, sn, v)
            for an in self.alias.get(bn, ()):
                for sn, v in (self.buf_w.get(an) or {}).items():
                    self._wait(eng, sn, v)
        for bn in writes:
            if waw:
                for sn, v in (self.buf_w.get(bn) or {}).items():
                    self._wait(eng, sn, v)
            for an in self.alias.get(bn, ()):
                for sn, v in (self.buf_w.get(an) or {}).items():
                    self._wait(eng, sn, v)
                for sn, v in (self.buf_r.get(an) or {}).items():
                    self._wait(eng, sn, v)
            for sn, v in (self.buf_r.get(bn) or {}).items():
                self._wait(eng, sn, v)
        sem = ("d_" + eng) if dma else ("p_" + eng)
        inc = 16 if dma else 1
        self.cnt[sem] += inc
        marker = (sem, self.cnt[sem])
        self.ops[eng].append(("i", fn, sem, inc))
        for bn in writes:
            if waw:
                self.buf_w[bn] = {marker[0]: marker[1]}
                self.buf_r[bn] = {}
            else:
                d = self.buf_w.setdefault(bn, {})
                if d.get(marker[0], 0) < marker[1]:
                    d[marker[0]] = marker[1]
        for bn in reads:
            d = self.buf_r.setdefault(bn, {})
            if d.get(marker[0], 0) < marker[1]:
                d[marker[0]] = marker[1]
        return marker

    def barrier(self):
        vals = dict(self.cnt)
        for e in ENGINES:
            for sn, v in vals.items():
                self._wait(e, sn, v)

    def finalize_and_emit(self):
        nc = self.nc
        import contextlib
        with contextlib.ExitStack() as st:
            handles = {}
            for name in self.cnt.keys():
                handles[name] = st.enter_context(nc.semaphore(name))
            block = st.enter_context(nc.Block())
            for eng_name in ENGINES:
                ops = self.ops[eng_name]

                def make(ops=ops):
                    def body(E):
                        for op in ops:
                            if op[0] == "w":
                                E.wait_ge(handles[op[1]], op[2])
                            else:
                                op[1](E).then_inc(handles[op[2]], op[3])
                    return body
                getattr(block, eng_name)(make())


def build_program():
    nc = bass.Bass("TRN2", target_bir_lowering=False)

    def din(name, shape, dt=F32):
        return nc.dram_tensor(name, shape, dt, kind="ExternalInput").ap()

    xT = din("xT", [D, NS, TC + 1])
    ppT = din("ppT", [D, NS, TC])
    Wpv = din("Wpv", [D, D]); Wk = din("Wk", [D, D]); Wkv = din("Wkv", [D, D])
    Wg1 = din("Wg1", [2 * D, D]); Wg2 = din("Wg2", [D, 1])
    Wb1 = din("Wb1", [D, D // 2]); Wb2 = din("Wb2", [D // 2, 2])
    Wo = din("Wo", [D, D])
    biasrow = din("biasrow", [1, 5664])
    ln_g = din("ln_g", [128, C]); ln_b = din("ln_b", [128, C])
    nsc = din("nsc", [128, C])
    gate_w = din("gate_w", [D, NE])
    m1w = din("m1w", [NE, D, 2 * JC * 128])
    m1bT = din("m1bT", [NE, 2, 128, JC])
    m2w = din("m2w", [NE, JC * 128, D])
    m2bT = din("m2bT", [NE, 128, C])
    selw = din("selw", [NE, NE * 128])
    cmask = din("cmask", [128, NS])
    pmask = din("pmask", [128, NS])
    cst = din("cst", [128, NCST])
    identw = din("identw", [128, 128])

    out = nc.dram_tensor("out", [128, C, TC], F32, kind="ExternalOutput").ap()

    sp_xn = nc.dram_tensor("sp_xn", [128, C, TC], BF16, kind="Internal").ap()
    sp_s = nc.dram_tensor("sp_s", [128, C, TC], BF16, kind="Internal").ap()
    sp_pos = nc.dram_tensor("sp_pos", [128, C, TC], F32, kind="Internal").ap()
    sp_kv = nc.dram_tensor("sp_kv", [128, C, TC], F32, kind="Internal").ap()
    sp_bl = nc.dram_tensor("sp_bl", [128, C, TC], F32, kind="Internal").ap()
    m1wb = nc.dram_tensor("m1wb", [NE, D, 2 * JC * 128], BF16,
                          kind="Internal").ap()
    m2wb = nc.dram_tensor("m2wb", [NE, JC * 128, D], BF16,
                          kind="Internal").ap()

    FREE0 = ((nc.sbuf_base + 31) // 32) * 32
    TOP = nc.sbuf_top
    off = [FREE0]

    ALLOC_IV = {}

    def alloc(name, shape, dt, at=None):
        nbytes = int(np.prod(shape[1:])) * mybir.dt.size(dt)
        if at is None:
            at = off[0]
            off[0] = at + ((nbytes + 31) // 32) * 32
        t = nc.alloc_sbuf_tensor_at(name, list(shape), dt, offset=at)
        assert at + nbytes <= TOP, f"SBUF overflow {name}: {at + nbytes} > {TOP}"
        ALLOC_IV[name] = (at, at + nbytes)
        return t

    p_cst = alloc("p_cst", [128, NCST], F32)
    p_cmask = alloc("p_cmask", [128, NS], F32)
    p_pmask = alloc("p_pmask", [128, NS], F32)
    p_lng = alloc("p_lng", [128, C], F32)
    p_lnb = alloc("p_lnb", [128, C], F32)
    p_nsc = alloc("p_nsc", [128, C], F32)
    p_gw = alloc("p_gw", [128, C, NE], BF16)
    p_wg2 = alloc("p_wg2", [128, C, 1], BF16)
    p_wb2 = alloc("p_wb2", [128, 4, 2], BF16)
    p_identf = alloc("p_identf", [128, 128], F32)
    p_ones_r = alloc("p_ones_r", [1, SUB + 128], BF16)
    p_ones_rf = alloc("p_ones_rf", [1, 256], F32)
    p_ones_c = alloc("p_ones_c", [128, 1], F32)
    p_ones_sc = alloc("p_ones_sc", [128, SUB], F32)
    p_sum_pc = alloc("p_sum_pc", [128, C, NS], F32)
    p_sum_ps = alloc("p_sum_ps", [128, C, NS], F32)
    p_sum_kc = alloc("p_sum_kc", [128, C, NS], F32)
    p_sum_ks = alloc("p_sum_ks", [128, C, NS], F32)
    p_sum_vg = alloc("p_sum_vg", [128, NS * NSUB], F32)
    p_vgm = alloc("p_vgm", [128, NS * NSUB], F32)
    p_car_pc = alloc("p_car_pc", [128, C], F32)
    p_car_ps = alloc("p_car_ps", [128, C], F32)
    p_car_kc = alloc("p_car_kc", [128, C], F32)
    p_car_ks = alloc("p_car_ks", [128, C], F32)
    p_car_vg = alloc("p_car_vg", [128, 1], F32)
    p_scarry = alloc("p_scarry", [128, C, 4], F32)
    p_bnd = alloc("p_bnd", [128, C, 4], F32)
    p_acol = alloc("p_acol", [128, C], F32)
    p_acol2 = alloc("p_acol2", [128, C], F32)
    p_bias = alloc("p_bias", [1, 5664], BF16)
    NTT = TC // 128
    p_gl = alloc("p_gl", [128, NTT * NE], F32)
    p_gw1 = alloc("p_gw1", [128, NTT * NE], F32)
    p_t2a = alloc("p_t2a", [128, NTT * NE], F32)
    p_t2b = alloc("p_t2b", [128, NTT * NE], F32)
    p_m1c = alloc("p_m1c", [128, NTT], F32)
    p_m2c = alloc("p_m2c", [128, NTT], F32)
    p_nm1f = alloc("p_nm1f", [128, NTT], F32)
    p_nm2f = alloc("p_nm2f", [128, NTT], F32)
    p_nmh1 = alloc("p_nmh1", [128, NTT], F32)
    p_nmh2 = alloc("p_nmh2", [128, NTT], F32)
    p_th1 = alloc("p_th1", [128, NTT], F32)
    p_th2 = alloc("p_th2", [128, NTT], F32)
    p_p1k = alloc("p_p1k", [128, NTT], F32)
    p_p2k = alloc("p_p2k", [128, NTT], F32)
    f_be = alloc("f_be", [128, 2, JC], F32)
    f_beE = alloc("f_beE", [128, JC], F32)
    f_b2c = alloc("f_b2c", [128, C], F32)
    p_wrow = alloc("p_wrow", [NE, TC], BF16)
    p_sel = alloc("p_sel", [NE, NE * 128], BF16)
    r_gcs = alloc("r_gcs", [1, TC], BF16)
    r_bw0 = alloc("r_bw0", [1, TC], BF16)
    r_bw1 = alloc("r_bw1", [1, TC], BF16)
    r_mu2 = alloc("r_mu2", [1, TC], BF16)
    r_r2 = alloc("r_r2", [1, TC], BF16)
    r_rho = alloc("r_rho", [1, TC], BF16)
    r_vg = alloc("r_vg", [1, TC], F32)
    rf = [alloc(f"rf{i}", [1, SUB], F32) for i in range(4)]
    rwf = alloc("rwf", [1, TC], F32)
    r_ones = alloc("r_ones", [1, TC], F32)
    SP16_OFF = off[0]
    p_sp16a = alloc("p_sp16a", [128, SUB], BF16)
    p_sp16b = alloc("p_sp16b", [128, SUB], BF16)
    p_junk = alloc("p_junk", [128, SUB], F32, at=SP16_OFF)
    p_spfa = alloc("p_spfa", [128, SUB], F32)
    p_spfb = alloc("p_spfb", [128, SUB], F32)
    p_xn = alloc("p_xn", [128, C, TC + 1], BF16)

    A0 = off[0]
    ARENA = TOP - A0
    assert ARENA >= 88000, f"arena too small: {ARENA}"

    a_wpv = alloc("a_wpv", [128, C, D], BF16, at=A0)
    a_wk = alloc("a_wk", [128, C, D], BF16, at=A0 + 16384)
    a_wkv = alloc("a_wkv", [128, C, D], BF16, at=A0 + 32768)
    a_wg1 = alloc("a_wg1", [128, 2 * C, D], BF16, at=A0)
    a_wo = alloc("a_wo", [128, C, D], BF16, at=A0)
    a_wb1 = alloc("a_wb1", [128, C, D // 2], BF16, at=A0 + 32768)
    a_xsub = alloc("a_xsub", [128, C, SUB], F32, at=A0 + 32768)
    a_pc = alloc("a_pc", [128, C, SUB], BF16, at=A0 + 49152)
    a_ps = alloc("a_ps", [128, C, SUB], BF16, at=A0 + 57344)
    a_gh = alloc("a_gh", [128, C, SUB], BF16, at=A0 + 49152)
    a_bh = alloc("a_bh", [128, 4, SUB], BF16, at=A0 + 57344)
    a_ln = alloc("a_ln", [128, C, SUB], BF16, at=A0 + 49152)
    ZB = A0 + 65536
    z_tg1 = alloc("z_tg1", [128, SUB], I32, at=ZB)
    z_tg2 = alloc("z_tg2", [128, SUB], F32, at=ZB + 2048)
    z_kt = alloc("z_kt", [128, SUB], F32, at=ZB + 4096)
    z_ka = alloc("z_ka", [128, SUB], F32, at=ZB + 6144)
    z_kc = alloc("z_kc", [128, SUB + 1], F32, at=ZB + 8192)
    z_ks = alloc("z_ks", [128, SUB + 1], F32, at=ZB + 10304)
    z_p1 = alloc("z_p1", [128, SUB], F32, at=ZB + 12416)
    z_p2 = alloc("z_p2", [128, SUB], F32, at=ZB + 14464)
    z_m1 = alloc("z_m1", [128, SUB], F32, at=ZB + 16512)
    z_m2 = alloc("z_m2", [128, SUB], F32, at=ZB + 18560)
    z_sq = alloc("z_sq", [128, SUB], F32, at=ZB + 20608)
    z_x1 = alloc("z_x1", [128, SUB], F32, at=ZB + 20608)
    assert ZB + 22656 <= TOP

    f_h = alloc("f_h", [128, JC, TC // 2], BF16, at=A0)
    f_acc = alloc("f_acc", [128, C, TC], BF16, at=A0 + 32768)
    f_w1a = alloc("f_w1a", [128, 4, 256], BF16, at=A0 + 65536)
    f_w1b = alloc("f_w1b", [128, 4, 256], BF16, at=A0 + 67584)
    f_w2ma = alloc("f_w2ma", [128, 8, 128], BF16, at=A0 + 69632)
    f_w2mb = alloc("f_w2mb", [128, 8, 128], BF16, at=A0 + 71680)
    f_b1 = alloc("f_b1", [1, 256], BF16, at=A0 + 73728)
    f_b2 = alloc("f_b2", [1, D], BF16, at=A0 + 74240)
    f_bwe = alloc("f_bwe", [128, 2, SUB], F32, at=A0 + 76288)
    f_swz = alloc("f_swz", [128, SUB], F32, at=A0 + 80384)
    f_swz2 = alloc("f_swz2", [128, SUB], F32, at=A0 + 82432)
    f_swz3 = alloc("f_swz3", [128, SUB], BF16, at=A0 + 84480)
    f_swz4 = alloc("f_swz4", [128, SUB], BF16, at=A0 + 85504)
    assert A0 + 86528 <= TOP

    g_x = alloc("g_x", [128, C, SUB], F32, at=A0)
    g_g = alloc("g_g", [128, C, SUB], BF16, at=A0 + 16384)
    g_s = alloc("g_s", [128, C, SUB], BF16, at=A0 + 24576)
    g_o = alloc("g_o", [128, C, SUB], F32, at=A0 + 65536)
    assert A0 + 65536 + 16384 <= TOP

    pA = nc.alloc_psum_tensor("pA", [128, SUB], F32)
    pB = nc.alloc_psum_tensor("pB", [128, SUB], F32)
    pC_ = nc.alloc_psum_tensor("pC", [128, SUB], F32)
    pD = nc.alloc_psum_tensor("pD", [128, SUB], F32)
    pE = nc.alloc_psum_tensor("pE", [128, SUB], F32)
    pF = nc.alloc_psum_tensor("pF", [128, SUB], F32)
    pR0 = nc.alloc_psum_tensor("pR0", [1, SUB], F32)
    pR1 = nc.alloc_psum_tensor("pR1", [1, SUB], F32)

    b = Build(nc)
    b.set_alias(ALLOC_IV)

    def dma(dst_ap, src_ap, reads, writes, eng="gpsimd", slow=False,
            waw=True):
        b.emit(eng,
               lambda E, d=dst_ap, s=src_ap, sl=slow: E.dma_start(
                   out=d, in_=s, allow_slow_non_contiguous=sl),
               reads=reads, writes=writes, dma=True, waw=waw)

    def V(fn, reads, writes):
        b.emit("vector", fn, reads=reads, writes=writes)

    def S(fn, reads, writes):
        b.emit("scalar", fn, reads=reads, writes=writes)

    def T(fn, reads, writes):
        b.emit("tensor", fn, reads=reads, writes=writes)

    def col(i):
        return p_cst[:, i:i + 1]

    def tap(name, sbuf_ap, rtags):
        if name not in TAPS:
            return
        shp = list(sbuf_ap.shape)
        dt = sbuf_ap.dtype
        t_out = nc.dram_tensor("tap_" + name, shp, dt,
                               kind="ExternalOutput").ap()
        dma(t_out, sbuf_ap, rtags, [f"tap_{name}"], eng="sync")

    def tap_dram(name, dram_ap, shp, dt, rtags):
        if name not in TAPS:
            return
        t_out = nc.dram_tensor("tap_" + name, list(shp), dt,
                               kind="ExternalOutput").ap()
        dma(t_out, dram_ap, rtags, [f"tap_{name}"], eng="sync")

    def bias_lhsT(name, mchunk):
        o, n = BOFF[name]
        return p_bias[:, o + mchunk * 128:o + min((mchunk + 1) * 128, o + n)
                      if False else o + mchunk * 128 + 128] \
            if False else p_bias[:, o + mchunk * 128:o + mchunk * 128 + 128]

    dma(p_cst[:], cst, [], ["p_cst"], eng="sync")
    dma(p_cmask[:], cmask, [], ["p_cmask"], eng="sync")
    dma(p_pmask[:], pmask, [], ["p_pmask"], eng="sync")
    dma(p_lng[:], ln_g, [], ["p_lng"], eng="sync")
    dma(p_lnb[:], ln_b, [], ["p_lnb"], eng="sync")
    dma(p_nsc[:], nsc, [], ["p_nsc"], eng="sync")
    dma(p_identf[:], identw, [], ["p_identf"], eng="sync")
    dma(p_identf[:], identw, [], ["p_identf"])
    dma(p_bias[:], biasrow, [], ["p_bias"])
    dma(p_sel[:], selw, [], ["p_sel"])
    dma(p_wg2[:, :, 0], Wg2.rearrange("(c p) o -> p (c o)", p=128)
        if False else Wg2[0:D, 0:1], [], ["p_wg2"]) if False else None
    for c in range(C):
        dma(p_wg2[:, c, 0:1], Wg2[c * 128:(c + 1) * 128, 0:1], [], ["p_wg2"],
            slow=True)
    for c in range(4):
        dma(p_wb2[:, c, :], Wb2[c * 128:(c + 1) * 128, :], [], ["p_wb2"])
    for c in range(C):
        dma(p_gw[:, c, :], gate_w[c * 128:(c + 1) * 128, :], [], ["p_gw"])
    V(lambda E: E.memset(p_ones_r[0:1, 0:SUB], 1.0), [], ["ones"])
    V(lambda E: E.memset(p_ones_r[0:1, SUB:SUB + 128], 1.0 / 32.0), [],
      ["ones"])
    V(lambda E: E.memset(p_ones_rf[0:1, 0:128], 1.0), [], ["ones"])
    V(lambda E: E.memset(p_ones_rf[0:1, 128:256], 1.0 / 32.0), [], ["ones"])
    V(lambda E: E.memset(p_ones_c[:], 1.0), [], ["ones"])
    V(lambda E: E.memset(p_ones_sc[:], 1.0), [], ["ones"])

    def ln_sums(x_chunks_fn, rtags, nn=SUB):
        for kc in range(C):
            T(lambda E, kc=kc, f=(kc == 0), l=(kc == C - 1), nn=nn: E.matmul(
                pR0[0:1, 0:nn], p_ones_c[:, 0:1], x_chunks_fn(kc),
                start=f, stop=l), ["ones"] + rtags, ["pR0"])
        for kc in range(C):
            S(lambda E, kc=kc, nn=nn: E.activation(
                out=z_sq[:, 0:nn], in_=x_chunks_fn(kc), func=AF.Square),
              rtags, ["z_sq"])
            T(lambda E, kc=kc, f=(kc == 0), l=(kc == C - 1), nn=nn: E.matmul(
                pR1[0:1, 0:nn], p_ones_c[:, 0:1], z_sq[:, 0:nn],
                start=f, stop=l), ["ones", "z_sq"], ["pR1"])

    def ln_rows(nn=SUB):
        V(lambda E, nn=nn: E.tensor_scalar(rf[0][0:1, 0:nn], pR0[0:1, 0:nn],
                                           1.0 / D, None, ALU.mult),
          ["pR0"], ["rf0"])
        V(lambda E, nn=nn: E.tensor_scalar(rf[3][0:1, 0:nn], pR1[0:1, 0:nn],
                                           1.0 / D, None, ALU.mult),
          ["pR1"], ["rf3"])
        V(lambda E, nn=nn: E.tensor_mul(rf[1][0:1, 0:nn], rf[0][0:1, 0:nn],
                                        rf[0][0:1, 0:nn]), ["rf0"], ["rf1"])
        V(lambda E, nn=nn: E.tensor_sub(rf[2][0:1, 0:nn], rf[3][0:1, 0:nn],
                                        rf[1][0:1, 0:nn]),
          ["rf3", "rf1"], ["rf2"])
        V(lambda E, nn=nn: E.tensor_scalar(rf[2][0:1, 0:nn], rf[2][0:1, 0:nn],
                                           EPS, None, ALU.add),
          ["rf2"], ["rf2"])
        V(lambda E, nn=nn: E.reciprocal(rf[3][0:1, 0:nn], rf[2][0:1, 0:nn]),
          ["rf2"], ["rf3"])
        S(lambda E, nn=nn: E.activation(out=rf[1][0:1, 0:nn],
                                        in_=rf[3][0:1, 0:nn], func=AF.Sqrt),
          ["rf3"], ["rf1"])

    def bcast(psum, pname, row_ap, rtags, nn=SUB):
        T(lambda E, nn=nn: E.matmul(psum[:, 0:nn], p_ones_rf[0:1, 0:128],
                                    row_ap, start=True, stop=True),
          ["ones"] + rtags, [pname])

    def bcast16(psum, pname, row_ap, rtags, nn=SUB):
        T(lambda E, nn=nn: E.matmul(psum[:, 0:nn], p_ones_r[0:1, 0:128],
                                    row_ap, start=True, stop=True),
          ["ones"] + rtags, [pname])

    def trig(dst_ap, wtag, src_ap, rtag, kind, nn=SUB):
        add_c = 32.0 if kind == "sin" else 32.25
        bias_i = C_SINB if kind == "sin" else C_COSB
        V(lambda E: E.tensor_scalar(z_tg1[:, 0:nn], src_ap, 1.0 / TWO_PI,
                                    add_c, ALU.mult, ALU.add),
          [rtag], ["z_tg1"])
        V(lambda E: E.tensor_copy(z_tg2[:, 0:nn], z_tg1[:, 0:nn]),
          ["z_tg1"], ["z_tg2"])
        V(lambda E: E.scalar_tensor_tensor(z_tg2[:, 0:nn], z_tg2[:, 0:nn],
                                           -TWO_PI, src_ap, ALU.mult, ALU.add),
          ["z_tg2", rtag], ["z_tg2"])
        S(lambda E: E.activation(out=dst_ap, in_=z_tg2[:, 0:nn], func=AF.Sin,
                                 bias=col(bias_i)),
          ["z_tg2", "p_cst"], [wtag])

    def wmm(psum, pname, w_sb, wtag, mchunk, x_fn, xtags, kchunks, bias,
            extra_k=()):
        first = True
        for kc in kchunks:
            T(lambda E, kc=kc, f=first: E.matmul(
                psum[:, 0:SUB], w_sb[:, kc, mchunk * 128:(mchunk + 1) * 128],
                x_fn(kc), start=f, stop=False), [wtag] + xtags, [pname])
            first = False
        for (w2_sb, w2tag, kc2, x2_fn) in extra_k:
            T(lambda E, kc2=kc2, w2_sb=w2_sb, x2_fn=x2_fn: E.matmul(
                psum[:, 0:SUB], w2_sb[:, kc2, mchunk * 128:(mchunk + 1) * 128],
                x2_fn(kc2), start=False, stop=False),
              [w2tag] + xtags, [pname])
        o, _n = BOFF[bias]
        T(lambda E, o=o, m=mchunk: E.matmul(
            psum[:, 0:SUB], p_bias[:, o + m * 128:o + (m + 1) * 128],
            p_ones_r[:, 0:SUB], start=False, stop=True),
          ["p_bias", "ones"], [pname])

    def process_slot(j, own):
        S(lambda E: E.preload_act_table(AF.Sqrt), [], [])
        for n in range(NSUB):
            t0 = 1 + n * SUB
            for c in range(C):
                dma(a_xsub[:, c, :], xT[c * 128:(c + 1) * 128, j, t0:t0 + SUB],
                    ["a_xsub"], ["a_xsub"], eng="sync")
            ln_sums(lambda kc: a_xsub[:, kc, :], ["a_xsub"])
            ln_rows()
            bcast(pA, "pA", rf[0][0:1, 0:SUB], ["rf0"])
            bcast(pB, "pB", rf[1][0:1, 0:SUB], ["rf1"])
            for c in range(C):
                V(lambda E, c=c: E.tensor_sub(z_p1[:, :], a_xsub[:, c, :],
                                              pA[:, :]),
                  ["a_xsub", "pA"], ["z_p1"])
                V(lambda E, c=c, t0=t0: E.tensor_mul(
                    p_xn[:, c, t0:t0 + SUB], z_p1[:, :], pB[:, :]),
                  ["z_p1", "pB"], ["p_xn"])
            if own:
                V(lambda E: E.tensor_scalar(rf[3][0:1, :], rf[2][0:1, :], EPS,
                                            None, ALU.subtract),
                  ["rf2"], ["rf3"])
                V(lambda E: E.reciprocal(rf[0][0:1, :], rf[2][0:1, :]),
                  ["rf2"], ["rf0"])
                V(lambda E: E.tensor_mul(rf[2][0:1, :], rf[3][0:1, :],
                                         rf[0][0:1, :]), ["rf3", "rf0"],
                  ["rf2"])
                V(lambda E: E.tensor_scalar(rf[3][0:1, :], rf[2][0:1, :], EPS,
                                            None, ALU.add), ["rf2"], ["rf3"])
                V(lambda E: E.reciprocal(rf[0][0:1, :], rf[3][0:1, :]),
                  ["rf3"], ["rf0"])
                S(lambda E, n=n: E.activation(
                    out=r_rho[0:1, n * SUB:(n + 1) * SUB], in_=rf[0][0:1, :],
                    func=AF.Sqrt), ["rf0"], ["r_rho"])
        for c in range(C):
            dma(a_xsub[:, c, :], xT[c * 128:(c + 1) * 128, j, 0:SUB],
                ["a_xsub"], ["a_xsub"], eng="sync")
        ln_sums(lambda kc: a_xsub[:, kc, :], ["a_xsub"])
        ln_rows()
        bcast(pA, "pA", rf[0][0:1, 0:SUB], ["rf0"])
        bcast(pB, "pB", rf[1][0:1, 0:SUB], ["rf1"])
        for c in range(C):
            V(lambda E, c=c: E.tensor_sub(z_p1[:, 0:1], a_xsub[:, c, 0:1],
                                          pA[:, 0:1]), ["a_xsub", "pA"],
              ["z_p1"])
            V(lambda E, c=c: E.tensor_mul(p_xn[:, c, 0:1], z_p1[:, 0:1],
                                          pB[:, 0:1]), ["z_p1", "pB"],
              ["p_xn"])
        pass

        S(lambda E: E.preload_act_table(AF.Sigmoid), [], [])
        for c2 in range(2 * C):
            dma(a_wg1[:, c2, :], Wg1[c2 * 128:(c2 + 1) * 128, :],
                ["a_wg1"] if c2 == 0 else [], ["a_wg1"])
        if own:
            for c in range(C):
                dma(a_wb1[:, c, :], Wb1[c * 128:(c + 1) * 128, :],
                    ["a_wb1"] if c == 0 else [], ["a_wb1"])
        rot4 = [(pA, "pA"), (pB, "pB"), (pC_, "pC"), (pD, "pD")]
        for n in range(NSUB):
            t0 = 1 + n * SUB
            for m in range(C):
                ps1, pn1 = rot4[m % (2 if own else 4)]
                wmm(ps1, pn1, a_wg1, "a_wg1", m,
                    lambda kc, t0=t0: p_xn[:, kc, t0:t0 + SUB], ["p_xn"],
                    range(C), "bg1",
                    extra_k=[(a_wg1, "a_wg1", C + kc2,
                              (lambda kc2_, t0=t0: p_xn[:, kc2_ - C,
                                                        t0 - 1:t0 - 1 + SUB]))
                             for kc2 in range(C)])
                V(lambda E, ps1=ps1: E.tensor_copy(z_sq[:, :], ps1[:, :]),
                  [pn1], ["z_sq"])
                S(lambda E: E.activation(out=z_kt[:, :], in_=z_sq[:, :],
                                         func=AF.Erf, scale=col(C_ISQ2)),
                  ["z_sq", "p_cst"], ["z_kt"])
                V(lambda E: E.tensor_scalar(z_ka[:, :], z_kt[:, :], 0.5, 0.5,
                                            ALU.mult, ALU.add),
                  ["z_kt"], ["z_ka"])
                V(lambda E, m=m: E.tensor_mul(a_gh[:, m, :], z_sq[:, :],
                                              z_ka[:, :]),
                  ["z_sq", "z_ka"], ["a_gh"])
                if own and n == 0 and m == 0:
                    tap("z00", z_sq[:, :], ["z_sq"])
            if own and n == 0:
                tap("gh0", a_gh[:, :, :], ["a_gh"])
            for kc in range(C):
                T(lambda E, kc=kc, f=(kc == 0): E.matmul(
                    pR0[0:1, 0:SUB], p_wg2[:, kc, 0:1], a_gh[:, kc, :],
                    start=f, stop=False), ["p_wg2", "a_gh"], ["pR0"])
            o_bg2, _n2 = BOFF["bg2"]
            T(lambda E, o=o_bg2: E.matmul(pR0[0:1, 0:SUB],
                                          p_bias[:, o:o + 1],
                                          p_ones_r[:, 0:SUB],
                                          start=False, stop=True),
              ["p_bias", "ones"], ["pR0"])
            S(lambda E, n=n: E.activation(out=r_vg[0:1, n * SUB:(n + 1) * SUB],
                                          in_=pR0[0:1, :], func=AF.Sigmoid),
              ["pR0"], ["r_vg"])
            if own:
                for m in range(4):
                    ps1, pn1 = (pC_, "pC") if m % 2 == 0 else (pD, "pD")
                    wmm(ps1, pn1, a_wb1, "a_wb1", m,
                        lambda kc, t0=t0: p_xn[:, kc, t0:t0 + SUB], ["p_xn"],
                        range(C), "bb1")
                    V(lambda E, ps1=ps1: E.tensor_copy(z_sq[:, :], ps1[:, :]),
                      [pn1], ["z_sq"])
                    S(lambda E: E.activation(out=z_kt[:, :], in_=z_sq[:, :],
                                             func=AF.Erf, scale=col(C_ISQ2)),
                      ["z_sq", "p_cst"], ["z_kt"])
                    V(lambda E: E.tensor_scalar(z_ka[:, :], z_kt[:, :], 0.5,
                                                0.5, ALU.mult, ALU.add),
                      ["z_kt"], ["z_ka"])
                    V(lambda E, m=m: E.tensor_mul(a_bh[:, m, :], z_sq[:, :],
                                                  z_ka[:, :]),
                      ["z_sq", "z_ka"], ["a_bh"])
                o_bb2, _n2 = BOFF["bb2"]
                for zi in range(2):
                    for kc in range(4):
                        T(lambda E, kc=kc, zi=zi, f=(kc == 0):
                          E.matmul(pR1[0:1, 0:SUB], p_wb2[:, kc, zi:zi + 1],
                                   a_bh[:, kc, :], start=f, stop=False),
                          ["p_wb2", "a_bh"], ["pR1"])
                    T(lambda E, zi=zi, o=o_bb2: E.matmul(
                        pR1[0:1, 0:SUB], p_bias[:, o + zi:o + zi + 1],
                        p_ones_r[:, 0:SUB], start=False, stop=True),
                      ["p_bias", "ones"], ["pR1"])
                    V(lambda E, zi=zi: E.tensor_copy(
                        rf[0 if zi == 0 else 3][0:1, :], pR1[0:1, :]),
                      ["pR1"], ["rf0" if zi == 0 else "rf3"])
                V(lambda E: E.tensor_sub(rf[1][0:1, :], rf[0][0:1, :],
                                         rf[3][0:1, :]),
                  ["rf0", "rf3"], ["rf1"])
                S(lambda E, n=n: E.activation(
                    out=r_bw0[0:1, n * SUB:(n + 1) * SUB], in_=rf[1][0:1, :],
                    func=AF.Sigmoid), ["rf1"], ["r_bw0"])
                S(lambda E, n=n: E.activation(
                    out=r_bw1[0:1, n * SUB:(n + 1) * SUB], in_=rf[1][0:1, :],
                    func=AF.Sigmoid, scale=p_cst[0:1, C_NEG1:C_NEG1 + 1]),
                  ["rf1", "p_cst"], ["r_bw1"])
        pass

        S(lambda E: E.preload_act_table(AF.Silu), [], [])
        for c in range(C):
            dma(a_wpv[:, c, :], Wpv[c * 128:(c + 1) * 128, :],
                ["a_wpv"] if c == 0 else [], ["a_wpv"])
        for c in range(C):
            dma(a_wk[:, c, :], Wk[c * 128:(c + 1) * 128, :],
                ["a_wk"] if c == 0 else [], ["a_wk"])
        for c in range(C):
            dma(a_wkv[:, c, :], Wkv[c * 128:(c + 1) * 128, :],
                ["a_wkv"] if c == 0 else [], ["a_wkv"])
        rot4b = [(pA, "pA"), (pB, "pB"), (pC_, "pC"), (pD, "pD")]
        for m in range(C):
            ps1, pn1 = rot4b[m % 4]
            wmm(ps1, pn1, a_wk, "a_wk", m, lambda kc: p_xn[:, kc, 0:SUB],
                ["p_xn"], range(C), "bk")
            S(lambda E, ps1=ps1: E.activation(out=z_kt[:, :], in_=ps1[:, :],
                                              func=AF.Tanh), [pn1], ["z_kt"])
            V(lambda E, m=m: E.tensor_copy(p_bnd[:, m, 0:1], z_kt[:, 0:1]),
              ["z_kt"], ["p_bnd"])
        S(lambda E: E.activation(out=p_bnd[:, :, 3:4], in_=p_bnd[:, :, 0:1],
                                 func=AF.Abs), ["p_bnd"], ["p_bnd"])
        S(lambda E: E.activation(out=p_bnd[:, :, 2:3], in_=p_bnd[:, :, 0:1],
                                 func=AF.Sin, scale=col(C_PI)),
          ["p_bnd", "p_cst"], ["p_bnd"])
        S(lambda E: E.activation(out=p_bnd[:, :, 3:4], in_=p_bnd[:, :, 3:4],
                                 func=AF.Sin, scale=col(C_NPI),
                                 bias=col(C_HPI)), ["p_bnd", "p_cst"],
          ["p_bnd"]) if False else None
        S(lambda E: E.activation(out=z_ka[:, 0:C], in_=p_bnd[:, :, 3],
                                 func=AF.Sin, scale=col(C_NPI),
                                 bias=col(C_HPI)),
          ["p_bnd", "p_cst"], ["z_ka"])
        for m in range(C):
            V(lambda E, m=m, j=j: E.tensor_scalar(
                p_bnd[:, m, 0:1], p_bnd[:, m, 2:3], p_pmask[:, j:j + 1],
                None, ALU.mult), ["p_bnd", "p_pmask"], ["p_bnd"])
            V(lambda E, m=m, j=j: E.tensor_scalar(
                p_bnd[:, m, 1:2], z_ka[:, m:m + 1], p_pmask[:, j:j + 1],
                None, ALU.mult), ["z_ka", "p_pmask"], ["p_bnd"])
        if own:
            for dst, dtag, src, stag in (
                    (p_car_pc, "p_car_pc", p_sum_pc, "p_sum_pc"),
                    (p_car_ps, "p_car_ps", p_sum_ps, "p_sum_ps"),
                    (p_car_kc, "p_car_kc", p_sum_kc, "p_sum_kc"),
                    (p_car_ks, "p_car_ks", p_sum_ks, "p_sum_ks")):
                V(lambda E, dst=dst: E.memset(dst[:, :], 0.0), [], [dtag])
                for jj in range(NS - 1):
                    V(lambda E, dst=dst, src=src, jj=jj:
                      E.scalar_tensor_tensor(dst[:, :], src[:, :, jj],
                                             p_cmask[:, jj:jj + 1],
                                             dst[:, :], ALU.mult, ALU.add),
                      [stag, "p_cmask", dtag], [dtag])
            for jj in range(NS - 1):
                V(lambda E, jj=jj: E.tensor_scalar(
                    p_vgm[:, jj * NSUB:(jj + 1) * NSUB],
                    p_sum_vg[:, jj * NSUB:(jj + 1) * NSUB],
                    p_cmask[:, jj:jj + 1], None,
                    ALU.mult), ["p_sum_vg", "p_cmask"], ["p_vgm"])
            V(lambda E: E.memset(p_vgm[:, (NS - 1) * NSUB:NS * NSUB], 0.0),
              [], ["p_vgm"])
            S(lambda E: E.nop(), ["p_vgm"], ["rt_vg"])
            V(lambda E: E.reduce_sum(p_car_vg[:, 0:1], p_vgm[:, :], axis=AX),
              ["p_vgm", "rt_vg"], ["p_car_vg"])
            V(lambda E: E.memset(r_ones[0:1, :], 1.0), [], ["r_ones"])
            S(lambda E: E.nop(), ["r_ones"], ["rt_ro"])
            V(lambda E: E.tensor_tensor_scan(
                rwf[0:1, :], r_ones[0:1, :], r_vg[0:1, :], 0.0,
                ALU.mult, ALU.add), ["r_ones", "r_vg", "rt_ro"], ["rwf"])
            V(lambda E: E.tensor_scalar(r_gcs[0:1, :], rwf[0:1, :],
                                        p_car_vg[0:1, 0:1], 1.0, ALU.add,
                                        ALU.max),
              ["rwf", "p_car_vg"], ["r_gcs"])
            for i, (src, stag) in enumerate(((p_car_pc, "p_car_pc"),
                                             (p_car_ps, "p_car_ps"),
                                             (p_car_kc, "p_car_kc"),
                                             (p_car_ks, "p_car_ks"))):
                for c in range(C):
                    V(lambda E, i=i, c=c, src=src: E.tensor_copy(
                        p_scarry[:, c, i:i + 1], src[:, c:c + 1]),
                      [stag], ["p_scarry"])

        for n in range(NSUB):
            t0 = 1 + n * SUB
            for c in range(C):
                dma(z_x1[:, :], ppT[c * 128:(c + 1) * 128, j,
                                    n * SUB:(n + 1) * SUB],
                    ["z_x1"], ["z_x1"], eng="sync")
                trig(a_ps[:, c, :], "a_ps", z_x1[:, :], "z_x1", "sin")
                trig(a_pc[:, c, :], "a_pc", z_x1[:, :], "z_x1", "cos")
            for m in range(C):
                ps1, pn1 = rot4b[m % 4]
                wmm(ps1, pn1, a_wpv, "a_wpv", m,
                    lambda kc, t0=t0: p_xn[:, kc, t0:t0 + SUB], ["p_xn"],
                    range(C), "bpv")
                V(lambda E, m=m, ps1=ps1: E.tensor_mul(
                    z_p1[:, :], a_pc[:, m, :], ps1[:, :]),
                  ["a_pc", pn1], ["z_p1"])
                V(lambda E, m=m, ps1=ps1: E.tensor_mul(
                    z_p2[:, :], a_ps[:, m, :], ps1[:, :]),
                  ["a_ps", pn1], ["z_p2"])
                if own:
                    S(lambda E: E.nop(), ["z_p1", "z_p2", "p_scarry"],
                      ["rt_s2"])
                    V(lambda E, m=m: E.tensor_tensor_scan(
                        z_m1[:, :], p_ones_sc[:, :], z_p1[:, :],
                        p_scarry[:, m, 0:1], ALU.mult, ALU.add),
                      ["ones", "z_p1", "p_scarry", "rt_s2"], ["z_m1"])
                    V(lambda E, m=m: E.tensor_tensor_scan(
                        z_m2[:, :], p_ones_sc[:, :], z_p2[:, :],
                        p_scarry[:, m, 1:2], ALU.mult, ALU.add),
                      ["ones", "z_p2", "p_scarry"], ["z_m2"])
                    V(lambda E, m=m: E.tensor_copy(p_scarry[:, m, 0:1],
                                                   z_m1[:, SUB - 1:SUB]),
                      ["z_m1"], ["p_scarry"])
                    V(lambda E, m=m: E.tensor_copy(p_scarry[:, m, 1:2],
                                                   z_m2[:, SUB - 1:SUB]),
                      ["z_m2"], ["p_scarry"])
                    spb, sptag = (p_spfa, "p_spfa") if m % 2 == 0 \
                        else (p_spfb, "p_spfb")
                    V(lambda E, m=m: E.tensor_mul(z_p1[:, :], a_pc[:, m, :],
                                                  z_m1[:, :]),
                      ["a_pc", "z_m1"], ["z_p1"])
                    V(lambda E, m=m: E.tensor_mul(z_p2[:, :], a_ps[:, m, :],
                                                  z_m2[:, :]),
                      ["a_ps", "z_m2"], ["z_p2"])
                    V(lambda E, spb=spb: E.tensor_add(spb[:, :], z_p1[:, :],
                                                      z_p2[:, :]),
                      ["z_p1", "z_p2"], [sptag])
                    dma(sp_pos[:, m, n * SUB:(n + 1) * SUB], spb[:, :],
                        [sptag], [f"sp_pos_{m}_{n}"])
                else:
                    V(lambda E, m=m: E.scalar_tensor_tensor(
                        p_junk[:, :], z_p1[:, :], 1.0, p_ones_sc[:, :],
                        ALU.mult, ALU.mult, accum_out=p_acol[:, m:m + 1]),
                      ["z_p1", "ones"], ["p_junk", "p_acol"])
                    V(lambda E, m=m: E.scalar_tensor_tensor(
                        p_junk[:, :], z_p2[:, :], 1.0, p_ones_sc[:, :],
                        ALU.mult, ALU.mult, accum_out=p_acol2[:, m:m + 1]),
                      ["z_p2", "ones"], ["p_junk", "p_acol2"])
                if not own and m == C - 1:
                    if n == 0:
                        V(lambda E, j=j: E.tensor_copy(p_sum_pc[:, :, j],
                                                       p_acol[:, :]),
                          ["p_acol"], ["p_sum_pc"])
                        V(lambda E, j=j: E.tensor_copy(p_sum_ps[:, :, j],
                                                       p_acol2[:, :]),
                          ["p_acol2"], ["p_sum_ps"])
                    else:
                        V(lambda E, j=j: E.tensor_add(p_sum_pc[:, :, j],
                                                      p_sum_pc[:, :, j],
                                                      p_acol[:, :]),
                          ["p_acol", "p_sum_pc"], ["p_sum_pc"])
                        V(lambda E, j=j: E.tensor_add(p_sum_ps[:, :, j],
                                                      p_sum_ps[:, :, j],
                                                      p_acol2[:, :]),
                          ["p_acol2", "p_sum_ps"], ["p_sum_ps"])
            bcast(pE, "pE", r_vg[0:1, n * SUB:(n + 1) * SUB], ["r_vg"])
            if not own:
                V(lambda E, j=j, n=n: E.scalar_tensor_tensor(
                    p_junk[:, :], p_ones_sc[:, :], 1.0, pE[:, :],
                    ALU.bypass, ALU.mult,
                    accum_out=p_sum_vg[:, j * NSUB + n:j * NSUB + n + 1]),
                  ["ones", "pE"], ["p_junk", "p_sum_vg"])
            for m in range(C):
                ps1, pn1 = (pC_, "pC") if m % 2 == 0 else (pD, "pD")
                wmm(ps1, pn1, a_wk, "a_wk", m,
                    lambda kc, t0=t0: p_xn[:, kc, t0:t0 + SUB], ["p_xn"],
                    range(C), "bk")
                S(lambda E, ps1=ps1: E.activation(out=z_kt[:, :],
                                                  in_=ps1[:, :],
                                                  func=AF.Tanh),
                  [pn1], ["z_kt"])
                S(lambda E: E.activation(out=z_ka[:, :], in_=z_kt[:, :],
                                         func=AF.Abs), ["z_kt"], ["z_ka"])
                S(lambda E: E.activation(out=z_ks[:, 1:SUB + 1],
                                         in_=z_kt[:, :], func=AF.Sin,
                                         scale=col(C_PI)),
                  ["z_kt", "p_cst"], ["z_ks"])
                S(lambda E: E.activation(out=z_kc[:, 1:SUB + 1],
                                         in_=z_ka[:, :], func=AF.Sin,
                                         scale=col(C_NPI), bias=col(C_HPI)),
                  ["z_ka", "p_cst"], ["z_kc"])
                V(lambda E, m=m: E.tensor_copy(z_kc[:, 0:1],
                                               p_bnd[:, m, 1:2]),
                  ["p_bnd"], ["z_kc"])
                V(lambda E, m=m: E.tensor_copy(z_ks[:, 0:1],
                                               p_bnd[:, m, 0:1]),
                  ["p_bnd"], ["z_ks"])
                V(lambda E, m=m: E.tensor_copy(p_bnd[:, m, 1:2],
                                               z_kc[:, SUB:SUB + 1]),
                  ["z_kc"], ["p_bnd"])
                V(lambda E, m=m: E.tensor_copy(p_bnd[:, m, 0:1],
                                               z_ks[:, SUB:SUB + 1]),
                  ["z_ks"], ["p_bnd"])
                ps2, pn2 = [(pA, "pA"), (pB, "pB"),
                            (pF, "pF")][m % 3]
                wmm(ps2, pn2, a_wkv, "a_wkv", m,
                    lambda kc, t0=t0: p_xn[:, kc, t0:t0 + SUB], ["p_xn"],
                    range(C), "bkv")
                V(lambda E, ps2=ps2: E.tensor_mul(z_p1[:, :], z_kc[:, 0:SUB],
                                                  ps2[:, :]),
                  ["z_kc", pn2], ["z_p1"])
                V(lambda E, ps2=ps2: E.tensor_mul(z_p2[:, :], z_ks[:, 0:SUB],
                                                  ps2[:, :]),
                  ["z_ks", pn2], ["z_p2"])
                if own:
                    V(lambda E: E.scalar_tensor_tensor(
                        z_p1[:, :], z_p1[:, :], 1.0, pE[:, :], ALU.bypass,
                        ALU.mult), ["z_p1", "pE"], ["z_p1"])
                    V(lambda E: E.scalar_tensor_tensor(
                        z_p2[:, :], z_p2[:, :], 1.0, pE[:, :], ALU.bypass,
                        ALU.mult), ["z_p2", "pE"], ["z_p2"])
                    S(lambda E: E.nop(), ["z_p1", "z_p2", "p_scarry"],
                      ["rt_s3"])
                    V(lambda E, m=m: E.tensor_tensor_scan(
                        z_m1[:, :], p_ones_sc[:, :], z_p1[:, :],
                        p_scarry[:, m, 2:3], ALU.mult, ALU.add),
                      ["ones", "z_p1", "p_scarry", "rt_s3"], ["z_m1"])
                    V(lambda E, m=m: E.tensor_tensor_scan(
                        z_m2[:, :], p_ones_sc[:, :], z_p2[:, :],
                        p_scarry[:, m, 3:4], ALU.mult, ALU.add),
                      ["ones", "z_p2", "p_scarry"], ["z_m2"])
                    V(lambda E, m=m: E.tensor_copy(p_scarry[:, m, 2:3],
                                                   z_m1[:, SUB - 1:SUB]),
                      ["z_m1"], ["p_scarry"])
                    V(lambda E, m=m: E.tensor_copy(p_scarry[:, m, 3:4],
                                                   z_m2[:, SUB - 1:SUB]),
                      ["z_m2"], ["p_scarry"])
                    spb, sptag = (p_spfa, "p_spfa") if m % 2 == 0 \
                        else (p_spfb, "p_spfb")
                    V(lambda E: E.tensor_mul(z_p1[:, :], z_kc[:, 1:SUB + 1],
                                             z_m1[:, :]),
                      ["z_kc", "z_m1"], ["z_p1"])
                    V(lambda E: E.tensor_mul(z_p2[:, :], z_ks[:, 1:SUB + 1],
                                             z_m2[:, :]),
                      ["z_ks", "z_m2"], ["z_p2"])
                    V(lambda E, spb=spb: E.tensor_add(spb[:, :], z_p1[:, :],
                                                      z_p2[:, :]),
                      ["z_p1", "z_p2"], [sptag])
                    dma(sp_kv[:, m, n * SUB:(n + 1) * SUB], spb[:, :],
                        [sptag], [f"sp_kv_{m}_{n}"])
                else:
                    V(lambda E, m=m: E.scalar_tensor_tensor(
                        p_junk[:, :], z_p1[:, :], 1.0, pE[:, :], ALU.bypass,
                        ALU.mult, accum_out=p_acol[:, m:m + 1]),
                      ["z_p1", "pE"], ["p_junk", "p_acol"])
                    V(lambda E, m=m: E.scalar_tensor_tensor(
                        p_junk[:, :], z_p2[:, :], 1.0, pE[:, :], ALU.bypass,
                        ALU.mult, accum_out=p_acol2[:, m:m + 1]),
                      ["z_p2", "pE"], ["p_junk", "p_acol2"])
            if not own:
                if n == 0:
                    V(lambda E, j=j: E.tensor_copy(p_sum_kc[:, :, j],
                                                   p_acol[:, :]),
                      ["p_acol"], ["p_sum_kc"])
                    V(lambda E, j=j: E.tensor_copy(p_sum_ks[:, :, j],
                                                   p_acol2[:, :]),
                      ["p_acol2"], ["p_sum_ks"])
                else:
                    V(lambda E, j=j: E.tensor_add(p_sum_kc[:, :, j],
                                                  p_sum_kc[:, :, j],
                                                  p_acol[:, :]),
                      ["p_acol", "p_sum_kc"], ["p_sum_kc"])
                    V(lambda E, j=j: E.tensor_add(p_sum_ks[:, :, j],
                                                  p_sum_ks[:, :, j],
                                                  p_acol2[:, :]),
                      ["p_acol2", "p_sum_ks"], ["p_sum_ks"])
        pass

    def precast(e):
        for r in range(0, D, 256):
            dma(m1wb[e, r:r + 256, :], m1w[e, r:r + 256, :], [], ["m1wb"],
                waw=False)
        dma(m2wb[e, :, :], m2w[e, :, :], [], ["m2wb"], waw=False)

    for j in range(NS - 1):
        process_slot(j, own=False)
        precast(j)
    process_slot(OWN, own=True)
    precast(OWN)

    tap("xn", p_xn[:, :, :], ["p_xn"])
    tap("vg", r_vg[0:1, :], ["r_vg"])
    tap("bw0", r_bw0[0:1, :], ["r_bw0"])
    tap("bw1", r_bw1[0:1, :], ["r_bw1"])
    tap("rho", r_rho[0:1, :], ["r_rho"])
    tap("gcs", r_gcs[0:1, :], ["r_gcs"])
    tap("car_pc", p_car_pc[:, :], ["p_car_pc"])
    tap("car_ps", p_car_ps[:, :], ["p_car_ps"])
    tap("car_kc", p_car_kc[:, :], ["p_car_kc"])
    tap("car_ks", p_car_ks[:, :], ["p_car_ks"])
    tap("sum_pc", p_sum_pc[:, :, :], ["p_sum_pc"])
    tap("sum_kc", p_sum_kc[:, :, :], ["p_sum_kc"])
    tap("sum_vg", p_sum_vg[0:1, :], ["p_sum_vg"])

    S(lambda E: E.preload_act_table(AF.Sqrt), [], [])
    for c in range(C):
        dma(a_wo[:, c, :], Wo[c * 128:(c + 1) * 128, :],
            ["a_wo"] if c == 0 else [], ["a_wo"])
    V(lambda E: E.reciprocal(rwf[0:1, :], r_gcs[0:1, :]), ["r_gcs"], ["rwf"])
    S(lambda E: E.activation(out=r_gcs[0:1, :], in_=rwf[0:1, :],
                             func=AF.Sqrt), ["rwf"], ["r_gcs"])
    V(lambda E: E.tensor_mul(rwf[0:1, :], r_bw1[0:1, :], r_gcs[0:1, :]),
      ["r_bw1", "r_gcs"], ["rwf"])
    for n in range(NSUB):
        T(lambda E, n=n: E.matmul(pA[:, 0:SUB],
                                  p_ones_r[0:1, SUB:SUB + 128],
                                  r_bw0[0:1, n * SUB:(n + 1) * SUB],
                                  start=True, stop=True),
          ["ones", "r_bw0"], ["pA"])
        T(lambda E, n=n: E.matmul(pB[:, 0:SUB],
                                  p_ones_rf[0:1, 128:256],
                                  rwf[0:1, n * SUB:(n + 1) * SUB],
                                  start=True, stop=True),
          ["ones", "rwf"], ["pB"])
        for c in range(C):
            dma(z_kt[:, :], sp_pos[:, c, n * SUB:(n + 1) * SUB],
                [f"sp_pos_{c}_{n}", "z_kt"], ["z_kt"], eng="sync")
            dma(z_ka[:, :], sp_kv[:, c, n * SUB:(n + 1) * SUB],
                [f"sp_kv_{c}_{n}", "z_ka"], ["z_ka"], eng="sync")
            V(lambda E: E.tensor_mul(z_p1[:, :], z_kt[:, :], pA[:, :]),
              ["z_kt", "pA"], ["z_p1"])
            V(lambda E: E.tensor_mul(z_p2[:, :], z_ka[:, :], pB[:, :]),
              ["z_ka", "pB"], ["z_p2"])
            spb, sptag = (p_spfa, "p_spfa") if c % 2 == 0 \
                else (p_spfb, "p_spfb")
            V(lambda E, spb=spb: E.tensor_add(spb[:, :], z_p1[:, :],
                                              z_p2[:, :]),
              ["z_p1", "z_p2"], [sptag])
            T(lambda E, c=c, f=(c == 0), l=(c == C - 1), spb=spb: E.matmul(
                pR0[0:1, 0:SUB], p_ones_c[:, 0:1], spb[:, :],
                start=f, stop=l), ["ones", sptag], ["pR0"])
            S(lambda E, spb=spb: E.activation(out=z_sq[:, :], in_=spb[:, :],
                                              func=AF.Square),
              [sptag], ["z_sq"])
            T(lambda E, c=c, f=(c == 0), l=(c == C - 1): E.matmul(
                pR1[0:1, 0:SUB], p_ones_c[:, 0:1], z_sq[:, :],
                start=f, stop=l), ["ones", "z_sq"], ["pR1"])
            dma(sp_bl[:, c, n * SUB:(n + 1) * SUB], spb[:, :], [sptag],
                [f"sp_bl_{c}_{n}"])
        ln_rows()
        V(lambda E, n=n: E.tensor_copy(r_mu2[0:1, n * SUB:(n + 1) * SUB],
                                       rf[0][0:1, :]), ["rf0"], ["r_mu2"])
        V(lambda E, n=n: E.tensor_copy(r_r2[0:1, n * SUB:(n + 1) * SUB],
                                       rf[1][0:1, :]), ["rf1"], ["r_r2"])
    for n in range(NSUB):
        t0 = 1 + n * SUB
        bcast16(pA, "pA", r_mu2[0:1, n * SUB:(n + 1) * SUB], ["r_mu2"])
        bcast16(pB, "pB", r_r2[0:1, n * SUB:(n + 1) * SUB], ["r_r2"])
        for c in range(C):
            dma(z_kt[:, :], sp_bl[:, c, n * SUB:(n + 1) * SUB],
                [f"sp_bl_{c}_{n}", "z_kt"], ["z_kt"], eng="sync")
            V(lambda E: E.tensor_sub(z_p1[:, :], z_kt[:, :], pA[:, :]),
              ["z_kt", "pA"], ["z_p1"])
            V(lambda E: E.tensor_mul(z_p2[:, :], z_p1[:, :], pB[:, :]),
              ["z_p1", "pB"], ["z_p2"])
            S(lambda E, c=c: E.activation(out=a_ln[:, c, :], in_=z_p2[:, :],
                                          func=AF.Identity,
                                          scale=p_lng[:, c:c + 1],
                                          bias=p_lnb[:, c:c + 1]),
              ["z_p2", "p_lng", "p_lnb"], ["a_ln"])
        for m in range(C):
            ps1, pn1 = (pC_, "pC") if m % 2 == 0 else (pD, "pD")
            wmm(ps1, pn1, a_wo, "a_wo", m, lambda kc: a_ln[:, kc, :],
                ["a_ln"], range(C), "bo")
            spb, sptag = (p_sp16a, "p_sp16a") if m % 2 == 0 \
                else (p_sp16b, "p_sp16b")
            V(lambda E, m=m, ps1=ps1, t0=t0, spb=spb: E.tensor_add(
                spb[:, :], ps1[:, :], p_xn[:, m, t0:t0 + SUB]),
              [pn1, "p_xn"], [sptag])
            dma(sp_s[:, m, n * SUB:(n + 1) * SUB], spb[:, :], [sptag],
                [f"sp_s_{m}_{n}"])
    pass
    tap_dram("pos", sp_pos, [128, C, TC], F32,
             [f"sp_pos_{c}_{n}" for c in range(C) for n in range(NSUB)])
    tap_dram("kv", sp_kv, [128, C, TC], F32,
             [f"sp_kv_{c}_{n}" for c in range(C) for n in range(NSUB)])
    tap_dram("bl", sp_bl, [128, C, TC], F32,
             [f"sp_bl_{c}_{n}" for c in range(C) for n in range(NSUB)])
    tap_dram("s", sp_s, [128, C, TC], BF16,
             [f"sp_s_{c}_{n}" for c in range(C) for n in range(NSUB)])
    tap("mu2", r_mu2[0:1, :], ["r_mu2"])
    tap("r2", r_r2[0:1, :], ["r_r2"])
    tap("bw1b", r_bw1[0:1, :], ["r_bw1"])

    S(lambda E: E.preload_act_table(AF.Silu), [], [])
    for c in range(C):
        dma(sp_xn[:, c, :], p_xn[:, c, 1:TC + 1], ["p_xn"], [f"sp_xn_{c}"])
    for n in range(NSUB):
        t0 = 1 + n * SUB
        bcast16(pA, "pA", r_rho[0:1, n * SUB:(n + 1) * SUB], ["r_rho"])
        for c in range(C):
            V(lambda E, c=c, t0=t0: E.tensor_mul(
                z_p1[:, :], p_xn[:, c, t0:t0 + SUB], pA[:, :]),
              ["p_xn", "pA", f"sp_xn_{c}"], ["z_p1"])
            V(lambda E, c=c, t0=t0: E.tensor_scalar(
                p_xn[:, c, t0:t0 + SUB], z_p1[:, :], p_nsc[:, c:c + 1], None,
                ALU.mult), ["z_p1", "p_nsc"], ["p_xn"])
    for nt in range(NTT):
        t0 = 1 + nt * 128
        psg, psgn = (pE, "pE") if nt % 2 == 0 else (pF, "pF")
        for kc in range(C):
            T(lambda E, kc=kc, t0=t0, f=(kc == 0), psg=psg: E.matmul(
                psg[0:128, 0:NE], p_xn[:, kc, t0:t0 + 128],
                p_gw[:, kc, :], start=f, stop=False),
              ["p_xn", "p_gw"], [psgn])
        o_gb, _n2 = BOFF["gate_b"]
        T(lambda E, o=o_gb, psg=psg: E.matmul(
            psg[0:128, 0:NE], p_ones_r[0:1, 0:128],
            p_bias[:, o:o + NE], start=False, stop=True),
          ["ones", "p_bias"], [psgn])
        S(lambda E, nt=nt, psg=psg: E.activation(
            out=p_gl[:, nt * NE:(nt + 1) * NE], in_=psg[0:128, 0:NE],
            func=AF.Copy), [psgn], ["p_gl"])
    pass

    def sl_(nt):
        return slice(nt * NE, (nt + 1) * NE)
    for nt in range(NTT):
        V(lambda E, nt=nt: E.reduce_max(p_m1c[:, nt:nt + 1],
                                        p_gl[:, sl_(nt)], axis=AX),
          ["p_gl"], ["p_m1c"])
    for nt in range(NTT):
        S(lambda E, nt=nt: E.activation(out=p_nm1f[:, nt:nt + 1],
                                        in_=p_m1c[:, nt:nt + 1],
                                        func=AF.Identity, scale=col(C_NEG1)),
          ["p_m1c", "p_cst"], ["p_nm1f"])
    for nt in range(NTT):
        S(lambda E, nt=nt: E.activation(out=p_t2a[:, sl_(nt)],
                                        in_=p_gl[:, sl_(nt)],
                                        func=AF.Identity,
                                        bias=p_nm1f[:, nt:nt + 1]),
          ["p_gl", "p_nm1f"], ["p_t2a"])
    for nt in range(NTT):
        S(lambda E, nt=nt: E.activation(out=p_t2b[:, sl_(nt)],
                                        in_=p_t2a[:, sl_(nt)],
                                        func=AF.Sign), ["p_t2a"], ["p_t2b"])
    for nt in range(NTT):
        S(lambda E, nt=nt: E.activation(out=p_t2a[:, sl_(nt)],
                                        in_=p_t2b[:, sl_(nt)],
                                        func=AF.Square), ["p_t2b"], ["p_t2a"])
    for nt in range(NTT):
        S(lambda E, nt=nt: E.activation(out=p_gw1[:, sl_(nt)],
                                        in_=p_t2a[:, sl_(nt)],
                                        func=AF.Identity, scale=col(C_NEG1),
                                        bias=col(C_ONE)),
          ["p_t2a", "p_cst"], ["p_gw1"])
    for nt in range(NTT):
        V(lambda E, nt=nt: E.scalar_tensor_tensor(
            p_t2a[:, sl_(nt)], p_gw1[:, sl_(nt)], -30000.0,
            p_gl[:, sl_(nt)], ALU.mult, ALU.add),
          ["p_gw1", "p_gl"], ["p_t2a"])
    S(lambda E: E.nop(), ["p_t2a"], ["rt_t2"])
    for nt in range(NTT):
        V(lambda E, nt=nt: E.reduce_max(p_m2c[:, nt:nt + 1],
                                        p_t2a[:, sl_(nt)], axis=AX),
          ["p_t2a", "rt_t2"], ["p_m2c"])
    for nt in range(NTT):
        S(lambda E, nt=nt: E.activation(out=p_nm2f[:, nt:nt + 1],
                                        in_=p_m2c[:, nt:nt + 1],
                                        func=AF.Identity, scale=col(C_NEG1)),
          ["p_m2c", "p_cst"], ["p_nm2f"])
    for nt in range(NTT):
        S(lambda E, nt=nt: E.activation(out=p_t2a[:, sl_(nt)],
                                        in_=p_gl[:, sl_(nt)],
                                        func=AF.Identity,
                                        bias=p_nm2f[:, nt:nt + 1]),
          ["p_gl", "p_nm2f"], ["p_t2a"])
    for nt in range(NTT):
        S(lambda E, nt=nt: E.activation(out=p_t2b[:, sl_(nt)],
                                        in_=p_t2a[:, sl_(nt)],
                                        func=AF.Sign), ["p_t2a"], ["p_t2b"])
    for nt in range(NTT):
        S(lambda E, nt=nt: E.activation(out=p_t2a[:, sl_(nt)],
                                        in_=p_t2b[:, sl_(nt)],
                                        func=AF.Square), ["p_t2b"], ["p_t2a"])
    for nt in range(NTT):
        S(lambda E, nt=nt: E.activation(out=p_t2b[:, sl_(nt)],
                                        in_=p_t2a[:, sl_(nt)],
                                        func=AF.Identity, scale=col(C_NEG1),
                                        bias=col(C_ONE)),
          ["p_t2a", "p_cst"], ["p_t2b"])
    for nt in range(NTT):
        S(lambda E, nt=nt: E.activation(out=p_nmh2[:, nt:nt + 1],
                                        in_=p_m2c[:, nt:nt + 1],
                                        func=AF.Identity, scale=col(C_NEGH)),
          ["p_m2c", "p_cst"], ["p_nmh2"])
    for nt in range(NTT):
        S(lambda E, nt=nt: E.activation(out=p_nmh1[:, nt:nt + 1],
                                        in_=p_m1c[:, nt:nt + 1],
                                        func=AF.Identity, scale=col(C_NEGH)),
          ["p_m1c", "p_cst"], ["p_nmh1"])
    for nt in range(NTT):
        S(lambda E, nt=nt: E.activation(out=p_th1[:, nt:nt + 1],
                                        in_=p_m1c[:, nt:nt + 1],
                                        func=AF.Tanh, scale=col(C_HALF),
                                        bias=p_nmh2[:, nt:nt + 1]),
          ["p_m1c", "p_nmh2", "p_cst"], ["p_th1"])
    for nt in range(NTT):
        S(lambda E, nt=nt: E.activation(out=p_th2[:, nt:nt + 1],
                                        in_=p_m2c[:, nt:nt + 1],
                                        func=AF.Tanh, scale=col(C_HALF),
                                        bias=p_nmh1[:, nt:nt + 1]),
          ["p_m2c", "p_nmh1", "p_cst"], ["p_th2"])
    for nt in range(NTT):
        S(lambda E, nt=nt: E.activation(out=p_p1k[:, nt:nt + 1],
                                        in_=p_th1[:, nt:nt + 1],
                                        func=AF.Identity, scale=col(C_HALF),
                                        bias=col(C_HALF)),
          ["p_th1", "p_cst"], ["p_p1k"])
    for nt in range(NTT):
        S(lambda E, nt=nt: E.activation(out=p_p2k[:, nt:nt + 1],
                                        in_=p_th2[:, nt:nt + 1],
                                        func=AF.Identity, scale=col(C_HALF),
                                        bias=col(C_HALF)),
          ["p_th2", "p_cst"], ["p_p2k"])
    for nt in range(NTT):
        V(lambda E, nt=nt: E.tensor_scalar(p_t2a[:, sl_(nt)],
                                           p_gw1[:, sl_(nt)],
                                           p_p1k[:, nt:nt + 1], None,
                                           ALU.mult),
          ["p_gw1", "p_p1k"], ["p_t2a"])
    for nt in range(NTT):
        V(lambda E, nt=nt: E.tensor_scalar(p_gw1[:, sl_(nt)],
                                           p_t2b[:, sl_(nt)],
                                           p_p2k[:, nt:nt + 1], None,
                                           ALU.mult),
          ["p_t2b", "p_p2k"], ["p_gw1"])
    V(lambda E: E.tensor_add(p_gw1[:, :], p_gw1[:, :], p_t2a[:, :]),
      ["p_gw1", "p_t2a"], ["p_gw1"])
    tap("gl0", p_gl[:, 0:NE], ["p_gl"])
    tap("m1c", p_m1c[:, :], ["p_m1c"])
    tap("m2c", p_m2c[:, :], ["p_m2c"])
    tap("p1k", p_p1k[:, :], ["p_p1k"])
    tap("gw10", p_gw1[:, 0:NE], ["p_gw1"])
    pass
    for nt in range(NTT):
        psg, psgn = (pE, "pE") if nt % 2 == 0 else (pF, "pF")
        T(lambda E, nt=nt, psg=psg: E.transpose(
            psg[0:NE, 0:128], p_gw1[:, nt * NE:(nt + 1) * NE],
            p_identf[:, :]),
          ["p_gw1", "p_identf"], [psgn])
        S(lambda E, nt=nt, psg=psg: E.activation(
            out=p_wrow[:, nt * 128:(nt + 1) * 128], in_=psg[0:NE, 0:128],
            func=AF.Copy), [psgn], ["p_wrow"])
    tap("wrow", p_wrow[:, :], ["p_wrow"])
    pass
    V(lambda E: E.memset(f_acc[:, :, :], 0.0), [], ["f_acc"])
    tap_dram("m2wb0", m2wb[0, 0:128, 0:128], [128, 128], BF16, ["m2wb"])
    tap_dram("m2wb1", m2wb[0, 1024:1152, 0:128], [128, 128], BF16, ["m2wb"])
    tap_dram("m2wb2", m2wb[0, 1536:1664, 0:128], [128, 128], BF16, ["m2wb"])
    tap_dram("m1wb0", m1wb[0, 512:640, 0:128], [128, 128], BF16, ["m1wb"])
    qrr = [0]

    def wdma(dst, src_ap, rtags, wtags, waw=True):
        eng = ("gpsimd", "sync")[qrr[0] % 2]
        qrr[0] += 1
        dma(dst, src_ap, rtags, wtags, eng=eng, waw=waw)

    for e in range(NE):
        wdma(f_be[:, 0, :], m1bT[e, 0], ["f_be"], ["f_be"])
        wdma(f_be[:, 1, :], m1bT[e, 1], [], ["f_be"])
        wdma(f_b2c[:, :], m2bT[e], ["f_b2c"], ["f_b2c"])
        S(lambda E: E.activation(out=f_beE[:, :], in_=f_be[:, 0, :],
                                 func=AF.Identity, scale=col(C_ALPHA)),
          ["f_be", "p_cst"], ["f_beE"])
        for half in range(2):
            hbase = half * (TC // 2)
            for jj in range(JC):
                for c in range(4):
                    wdma(f_w1a[:, c, :],
                         m1wb[e, c * 128:(c + 1) * 128,
                              jj * 256:(jj + 1) * 256],
                         [], ["f_w1a"], waw=False)
                for c in range(4):
                    wdma(f_w1b[:, c, :],
                         m1wb[e, (c + 4) * 128:(c + 5) * 128,
                              jj * 256:(jj + 1) * 256],
                         [], ["f_w1b"], waw=False)
                for n2 in range(2):
                    t0 = 1 + hbase + n2 * SUB
                    q = jj * 2 + n2
                    psE, pnE, psO, pnO = (
                        (pA, "pA", pB, "pB") if q % 2 == 0
                        else (pC_, "pC", pD, "pD"))
                    for kc in range(C):
                        wt, wtag = (f_w1a, "f_w1a") if kc < 4 \
                            else (f_w1b, "f_w1b")
                        T(lambda E, kc=kc, t0=t0, f=(kc == 0), wt=wt,
                          psE=psE, l=(kc == C - 1): E.matmul(
                            psE[:, 0:SUB], wt[:, kc % 4, 0:256:2],
                            p_xn[:, kc, t0:t0 + SUB], start=f, stop=l),
                          [wtag, "p_xn"], [pnE])
                    for kc in range(C):
                        wt, wtag = (f_w1a, "f_w1a") if kc < 4 \
                            else (f_w1b, "f_w1b")
                        T(lambda E, kc=kc, t0=t0, f=(kc == 0), wt=wt,
                          psO=psO, l=(kc == C - 1): E.matmul(
                            psO[:, 0:SUB], wt[:, kc % 4, 1:256:2],
                            p_xn[:, kc, t0:t0 + SUB], start=f, stop=l),
                          [wtag, "p_xn"], [pnO])
                    S(lambda E, jj=jj, psE=psE: E.activation(
                        out=f_swz3[:, :], in_=psE[:, :], func=AF.Silu,
                        scale=col(C_ALPHA), bias=f_beE[:, jj:jj + 1]),
                      [pnE, "p_cst", "f_beE"], ["f_swz3"])
                    V(lambda E: E.tensor_scalar(f_swz4[:, :], f_swz3[:, :],
                                                D7, None, ALU.min),
                      ["f_swz3"], ["f_swz4"])
                    S(lambda E, jj=jj, psO=psO: E.activation(
                        out=f_swz[:, :], in_=psO[:, :], func=AF.Identity,
                        bias=f_be[:, 1, jj:jj + 1]),
                      [pnO, "f_be"], ["f_swz"])
                    V(lambda E: E.tensor_scalar(f_swz2[:, :], f_swz[:, :],
                                                LIMIT, -LIMIT, ALU.min,
                                                ALU.max),
                      ["f_swz"], ["f_swz2"])
                    V(lambda E: E.tensor_scalar(f_swz[:, :], f_swz2[:, :],
                                                1.0, 1.0 / ALPHA, ALU.add,
                                                ALU.mult),
                      ["f_swz2"], ["f_swz"])
                    V(lambda E, jj=jj, n2=n2: E.tensor_mul(
                        f_h[:, jj, n2 * SUB:(n2 + 1) * SUB], f_swz4[:, :],
                        f_swz[:, :]), ["f_swz4", "f_swz"], ["f_h"])
                    if e == 0 and half == 0 and jj == 0 and n2 == 0:
                        tap("fh0", f_h[:, 0, 0:SUB], ["f_h"])
                        tap("fbeE", f_beE[:, :], ["f_beE"])
                        tap("fbeO", f_be[:, 1, :], ["f_be"])
            if e == 0 and half == 0:
                tap("fhall", f_h[:, :, :], ["f_h"])
            for n2 in range(2):
                n = half * 2 + n2
                T(lambda E, e=e, n=n: E.matmul(
                    pE[:, 0:SUB], p_sel[0:NE, e * 128:(e + 1) * 128],
                    p_wrow[0:NE, n * SUB:(n + 1) * SUB],
                    start=True, stop=True), ["p_sel", "p_wrow"], ["pE"])
                V(lambda E, n2=n2: E.tensor_copy(f_bwe[:, n2, :], pE[:, :]),
                  ["pE"], ["f_bwe"])
            for m in range(C):
                for k in range(8):
                    wdma(f_w2ma[:, k, :],
                         m2wb[e, k * 128:(k + 1) * 128,
                              m * 128:(m + 1) * 128],
                         [], ["f_w2ma"], waw=False)
                for k in range(8):
                    wdma(f_w2mb[:, k, :],
                         m2wb[e, (k + 8) * 128:(k + 9) * 128,
                              m * 128:(m + 1) * 128],
                         [], ["f_w2mb"], waw=False)
                if e == 0 and half == 0 and m == 0:
                    tap("fw2a", f_w2ma[:, :, :], ["f_w2ma"])
                    tap("fw2b", f_w2mb[:, :, :], ["f_w2mb"])
                for n2 in range(2):
                    n = half * 2 + n2
                    ps1, pn1 = ((pC_, "pC") if n2 == 0 else (pD, "pD")) \
                        if m % 2 == 0 else ((pF, "pF") if n2 == 0
                                            else (pA, "pA"))
                    for kc in range(JC):
                        wt, wtag = (f_w2ma, "f_w2ma") if kc < 8 \
                            else (f_w2mb, "f_w2mb")
                        T(lambda E, kc=kc, n2=n2, ps1=ps1, wt=wt,
                          f=(kc == 0), l=(kc == JC - 1):
                          E.matmul(ps1[:, 0:SUB], wt[:, kc % 8, :],
                                   f_h[:, kc, n2 * SUB:(n2 + 1) * SUB],
                                   start=f, stop=l),
                          [wtag, "f_h"], [pn1])
                    S(lambda E, m=m, ps1=ps1: E.activation(
                        out=f_swz3[:, :], in_=ps1[:, :], func=AF.Identity,
                        bias=f_b2c[:, m:m + 1]),
                      [pn1, "f_b2c"], ["f_swz3"])
                    if e == 0 and half == 0 and m == 0 and n2 == 0:
                        tap("fy00", f_swz3[:, :], ["f_swz3"])
                    V(lambda E, n2=n2: E.tensor_mul(
                        f_swz[:, :], f_swz3[:, :], f_bwe[:, n2, :]),
                      ["f_swz3", "f_bwe"], ["f_swz"])
                    V(lambda E, m=m, n=n: E.tensor_add(
                        f_acc[:, m, n * SUB:(n + 1) * SUB],
                        f_acc[:, m, n * SUB:(n + 1) * SUB], f_swz[:, :]),
                      ["f_swz", "f_acc"], ["f_acc"])
                    if e == 0 and m == 0 and n == 3:
                        tap("facc00", f_acc[:, 0, :], ["f_acc"])
                        tap("fbwe0", f_bwe[:, :, :], ["f_bwe"])
    b.barrier()
    tap("acc", f_acc[:, :, :], ["f_acc"])
    tap("t_fm", p_xn[:, :, :], ["p_xn"])

    for c in range(C):
        dma(p_xn[:, c, 1:TC + 1], sp_xn[:, c, :], [f"sp_xn_{c}", "p_xn"],
            ["p_xn"])
    for n in range(NSUB):
        t0 = 1 + n * SUB
        for c in range(C):
            dma(g_x[:, c, :], xT[c * 128:(c + 1) * 128, OWN, t0:t0 + SUB],
                ["g_x"], ["g_x"], eng="sync")
            dma(g_s[:, c, :], sp_s[:, c, n * SUB:(n + 1) * SUB],
                [f"sp_s_{c}_{n}", "g_s"], ["g_s"], eng="sync")
        V(lambda E, t0=t0, n=n: E.tensor_add(
            g_o[:, :, :], p_xn[:, :, t0:t0 + SUB],
            f_acc[:, :, n * SUB:(n + 1) * SUB]),
          ["p_xn", "f_acc"], ["g_o"])
        S(lambda E: E.activation(out=g_g[:, :, :], in_=g_o[:, :, :],
                                 func=AF.Silu), ["g_o"], ["g_g"])
        V(lambda E: E.tensor_mul(g_o[:, :, :], g_s[:, :, :], g_g[:, :, :]),
          ["g_s", "g_g"], ["g_o"])
        V(lambda E: E.tensor_add(g_x[:, :, :], g_x[:, :, :], g_o[:, :, :]),
          ["g_x", "g_o"], ["g_x"])
        dma(out[:, :, n * SUB:(n + 1) * SUB], g_x[:, :, :], ["g_x"],
            [f"out_{n}"], eng="sync")

    for semname in ("d_sync", "d_gpsimd", "d_scalar"):
        if b.cnt.get(semname, 0) > 0:
            b.ops["sync"].append(("w", semname, b.cnt[semname]))
    b.finalize_and_emit()
    return nc


def kernel(**inputs):
    x = np.asarray(inputs["L0_attn_x"])
    pp = np.asarray(inputs["L0_attn_pos_phases"])
    L = x.shape[1]
    xT_full = np.ascontiguousarray(x[0].T)
    ppT_full = np.ascontiguousarray(pp.T)

    def W(name):
        return np.ascontiguousarray(np.asarray(inputs[name], np.float32))

    def colsfmt(v):
        return np.ascontiguousarray(np.asarray(v, np.float32)
                                    .reshape(C, 128).T)

    consts = np.zeros((128, NCST), np.float32)
    consts[:, C_SINB] = 64 * math.pi
    consts[:, C_COSB] = 64 * math.pi + math.pi / 2
    consts[:, C_PI] = math.pi
    consts[:, C_NPI] = -math.pi
    consts[:, C_HPI] = math.pi / 2
    consts[:, C_ISQ2] = 1.0 / math.sqrt(2.0)
    consts[:, C_ALPHA] = ALPHA
    consts[:, C_NEG1] = -1.0
    consts[:, C_ONE] = 1.0
    consts[:, C_HALF] = 0.5
    consts[:, C_NEGH] = -0.5

    sel = np.zeros((NE, NE * 128), np.float32)
    for e in range(NE):
        sel[e, e * 128:(e + 1) * 128] = 1.0

    brow = np.zeros((1, 5664), np.float32)

    def put(name, v):
        o, n = BOFF[name]
        brow[0, o:o + n] = np.asarray(v, np.float32).reshape(-1)

    put("bpv", inputs["L0_attn_bpv"]); put("bk", inputs["L0_attn_bk"])
    put("bkv", inputs["L0_attn_bkv"]); put("bg1", inputs["L0_attn_bg1"])
    put("bo", inputs["L0_attn_bo"]); put("bb1", inputs["L0_attn_bb1"])
    put("gate_b", inputs["L0_ffn_gate_b"]); put("bb2", inputs["L0_attn_bb2"])
    put("bg2", inputs["L0_attn_bg2"])

    shared = {
        "Wpv": W("L0_attn_Wpv"), "Wk": W("L0_attn_Wk"),
        "Wkv": W("L0_attn_Wkv"), "Wg1": W("L0_attn_Wg1"),
        "Wg2": W("L0_attn_Wg2"), "Wb1": W("L0_attn_Wb1"),
        "Wb2": W("L0_attn_Wb2"), "Wo": W("L0_attn_Wo"),
        "biasrow": brow,
        "ln_g": colsfmt(inputs["L0_attn_ln_g"]),
        "ln_b": colsfmt(inputs["L0_attn_ln_b"]),
        "nsc": colsfmt(inputs["L0_ffn_norm_scale"]),
        "gate_w": W("L0_ffn_gate_w"),
        "m1w": W("L0_ffn_mlp1_w"),
        "m1bT": np.ascontiguousarray(
            W("L0_ffn_mlp1_b").reshape(NE, JC, 128, 2)
            .transpose(0, 3, 2, 1)),
        "m2w": W("L0_ffn_mlp2_w"),
        "m2bT": np.ascontiguousarray(
            W("L0_ffn_mlp2_b").reshape(NE, C, 128).transpose(0, 2, 1)),
        "selw": sel, "cst": consts,
        "identw": np.eye(128, dtype=np.float32),
    }

    in_maps = []
    for c in range(NCORES):
        order = [k for k in range(NCORES) if k != c] + [c]
        xT_sl = np.zeros((D, NS, TC + 1), np.float32)
        ppT_sl = np.zeros((D, NS, TC), np.float32)
        cm = np.zeros((128, NS), np.float32)
        pm = np.zeros((128, NS), np.float32)
        for jslot, k in enumerate(order):
            t0 = k * TC
            if t0 > 0:
                xT_sl[:, jslot, 0] = xT_full[:, t0 - 1]
                pm[:, jslot] = 1.0
            xT_sl[:, jslot, 1:] = xT_full[:, t0:t0 + TC]
            ppT_sl[:, jslot, :] = ppT_full[:, t0:t0 + TC]
            if k < c:
                cm[:, jslot] = 1.0
        m = dict(shared)
        m["xT"] = xT_sl
        m["ppT"] = ppT_sl
        m["cmask"] = cm
        m["pmask"] = pm
        in_maps.append(m)

    nc = build_program()
    res = run_bass_kernel_spmd(nc, in_maps, core_ids=list(range(NCORES)))
    global LAST_RES
    LAST_RES = res
    outs = []
    for c in range(NCORES):
        o = np.asarray(res.results[c]["out"])
        outs.append(np.transpose(o, (2, 1, 0)).reshape(TC, D))
    return np.concatenate(outs, axis=0).reshape(1, L, D).astype(np.float32)


# version 07
# speedup vs baseline: 1.8144x; 1.0110x over previous
import math
import numpy as np
import concourse.bass as bass
import concourse.mybir as mybir
from concourse.bass_utils import run_bass_kernel_spmd

F32 = mybir.dt.float32
BF16 = mybir.dt.bfloat16
I32 = mybir.dt.int32
AF = mybir.ActivationFunctionType
ALU = mybir.AluOpType
AX = mybir.AxisListType.X

NCORES = 8
D = 1024
C = 8
TC = 2048
NS = 8
SUB = 512
NSUB = TC // SUB
NE = 8
JC = 16
OWN = NS - 1
TWO_PI = 2.0 * math.pi
EPS = 1e-5
ALPHA = 1.702
LIMIT = 7.0
D7 = ALPHA * LIMIT * (1.0 / (1.0 + math.exp(-ALPHA * LIMIT)))

(C_SINB, C_COSB, C_PI, C_NPI, C_HPI, C_ISQ2, C_ALPHA, C_NEG1, C_ONE,
 C_HALF, C_NEGH) = range(11)
NCST = 11

ENGINES = ("sync", "scalar", "vector", "tensor", "gpsimd")

TAPS = set()
LAST_RES = None

BOFF = {"bpv": (0, D), "bk": (D, D), "bkv": (2 * D, D), "bg1": (3 * D, D),
        "bo": (4 * D, D), "bb1": (5 * D, D // 2), "gate_b": (5 * D + 512, NE),
        "bb2": (5 * D + 520, 2), "bg2": (5 * D + 522, 1)}


class Build:
    def __init__(self, nc):
        self.nc = nc
        self.ops = {e: [] for e in ENGINES}
        self.cnt = {}
        self.waited = {}
        self.buf_w = {}
        self.buf_r = {}
        self.alias = {}
        for e in ENGINES:
            self.cnt["p_" + e] = 0
        for e in ("sync", "gpsimd", "scalar"):
            self.cnt["d_" + e] = 0

    def _wait(self, eng, semname, val):
        if val <= 0:
            return
        if semname == "p_" + eng:
            return
        if self.waited.get((eng, semname), 0) >= val:
            return
        self.waited[(eng, semname)] = val
        self.ops[eng].append(("w", semname, val))

    def set_alias(self, ivmap):
        items = list(ivmap.items())
        ov = {}
        for n1, (l1, h1) in items:
            for n2, (l2, h2) in items:
                if n2 != n1 and l2 < h1 and l1 < h2:
                    ov.setdefault(n1, []).append(n2)
        self.alias = ov

    def emit(self, eng, fn, reads=(), writes=(), dma=False, waw=True):
        for bn in reads:
            for sn, v in (self.buf_w.get(bn) or {}).items():
                self._wait(eng, sn, v)
            for an in self.alias.get(bn, ()):
                for sn, v in (self.buf_w.get(an) or {}).items():
                    self._wait(eng, sn, v)
        for bn in writes:
            if waw:
                for sn, v in (self.buf_w.get(bn) or {}).items():
                    self._wait(eng, sn, v)
            for an in self.alias.get(bn, ()):
                for sn, v in (self.buf_w.get(an) or {}).items():
                    self._wait(eng, sn, v)
                for sn, v in (self.buf_r.get(an) or {}).items():
                    self._wait(eng, sn, v)
            for sn, v in (self.buf_r.get(bn) or {}).items():
                self._wait(eng, sn, v)
        sem = ("d_" + eng) if dma else ("p_" + eng)
        inc = 16 if dma else 1
        self.cnt[sem] += inc
        marker = (sem, self.cnt[sem])
        self.ops[eng].append(("i", fn, sem, inc))
        for bn in writes:
            if waw:
                self.buf_w[bn] = {marker[0]: marker[1]}
                self.buf_r[bn] = {}
            else:
                d = self.buf_w.setdefault(bn, {})
                if d.get(marker[0], 0) < marker[1]:
                    d[marker[0]] = marker[1]
        for bn in reads:
            d = self.buf_r.setdefault(bn, {})
            if d.get(marker[0], 0) < marker[1]:
                d[marker[0]] = marker[1]
        return marker

    def barrier(self):
        vals = dict(self.cnt)
        for e in ENGINES:
            for sn, v in vals.items():
                self._wait(e, sn, v)

    def finalize_and_emit(self):
        nc = self.nc
        import contextlib
        with contextlib.ExitStack() as st:
            handles = {}
            for name in self.cnt.keys():
                handles[name] = st.enter_context(nc.semaphore(name))
            block = st.enter_context(nc.Block())
            for eng_name in ENGINES:
                ops = self.ops[eng_name]

                def make(ops=ops):
                    def body(E):
                        for op in ops:
                            if op[0] == "w":
                                E.wait_ge(handles[op[1]], op[2])
                            else:
                                op[1](E).then_inc(handles[op[2]], op[3])
                    return body
                getattr(block, eng_name)(make())


def build_program():
    nc = bass.Bass("TRN2", target_bir_lowering=False)

    def din(name, shape, dt=F32):
        return nc.dram_tensor(name, shape, dt, kind="ExternalInput").ap()

    xT = din("xT", [D, NS, TC + 1])
    ppT = din("ppT", [D, NS, TC])
    Wpv = din("Wpv", [D, D]); Wk = din("Wk", [D, D]); Wkv = din("Wkv", [D, D])
    Wg1 = din("Wg1", [2 * D, D]); Wg2 = din("Wg2", [D, 1])
    Wb1 = din("Wb1", [D, D // 2]); Wb2 = din("Wb2", [D // 2, 2])
    Wo = din("Wo", [D, D])
    biasrow = din("biasrow", [1, 5664])
    ln_g = din("ln_g", [128, C]); ln_b = din("ln_b", [128, C])
    nsc = din("nsc", [128, C])
    gate_w = din("gate_w", [D, NE])
    m1w = din("m1w", [NE, D, 2 * JC * 128])
    m1bT = din("m1bT", [NE, 2, 128, JC])
    m2w = din("m2w", [NE, JC * 128, D])
    m2bT = din("m2bT", [NE, 128, C])
    selw = din("selw", [NE, NE * 128])
    cmask = din("cmask", [128, NS])
    pmask = din("pmask", [128, NS])
    cst = din("cst", [128, NCST])
    identw = din("identw", [128, 128])

    out = nc.dram_tensor("out", [128, C, TC], F32, kind="ExternalOutput").ap()

    sp_xn = nc.dram_tensor("sp_xn", [128, C, TC], BF16, kind="Internal").ap()
    sp_s = nc.dram_tensor("sp_s", [128, C, TC], BF16, kind="Internal").ap()
    sp_pos = nc.dram_tensor("sp_pos", [128, C, TC], F32, kind="Internal").ap()
    sp_kv = nc.dram_tensor("sp_kv", [128, C, TC], F32, kind="Internal").ap()
    sp_bl = nc.dram_tensor("sp_bl", [128, C, TC], F32, kind="Internal").ap()
    m1wb = nc.dram_tensor("m1wb", [NE, D, 2 * JC * 128], BF16,
                          kind="Internal").ap()
    m2wb = nc.dram_tensor("m2wb", [NE, JC * 128, D], BF16,
                          kind="Internal").ap()

    FREE0 = ((nc.sbuf_base + 31) // 32) * 32
    TOP = nc.sbuf_top
    off = [FREE0]

    ALLOC_IV = {}

    def alloc(name, shape, dt, at=None):
        nbytes = int(np.prod(shape[1:])) * mybir.dt.size(dt)
        if at is None:
            at = off[0]
            off[0] = at + ((nbytes + 31) // 32) * 32
        t = nc.alloc_sbuf_tensor_at(name, list(shape), dt, offset=at)
        assert at + nbytes <= TOP, f"SBUF overflow {name}: {at + nbytes} > {TOP}"
        ALLOC_IV[name] = (at, at + nbytes)
        return t

    p_cst = alloc("p_cst", [128, NCST], F32)
    p_cmask = alloc("p_cmask", [128, NS], F32)
    p_pmask = alloc("p_pmask", [128, NS], F32)
    p_lng = alloc("p_lng", [128, C], F32)
    p_lnb = alloc("p_lnb", [128, C], F32)
    p_nsc = alloc("p_nsc", [128, C], F32)
    p_gw = alloc("p_gw", [128, C, NE], BF16)
    p_wg2 = alloc("p_wg2", [128, C, 1], BF16)
    p_wb2 = alloc("p_wb2", [128, 4, 2], BF16)
    p_identf = alloc("p_identf", [128, 128], F32)
    p_ones_r = alloc("p_ones_r", [1, SUB + 128], BF16)
    p_ones_rf = alloc("p_ones_rf", [1, 256], F32)
    p_ones_c = alloc("p_ones_c", [128, 1], F32)
    p_ones_sc = alloc("p_ones_sc", [128, SUB], F32)
    p_sum_pc = alloc("p_sum_pc", [128, C, NS], F32)
    p_sum_ps = alloc("p_sum_ps", [128, C, NS], F32)
    p_sum_kc = alloc("p_sum_kc", [128, C, NS], F32)
    p_sum_ks = alloc("p_sum_ks", [128, C, NS], F32)
    p_sum_vg = alloc("p_sum_vg", [128, NS * NSUB], F32)
    p_vgm = alloc("p_vgm", [128, NS * NSUB], F32)
    p_car_pc = alloc("p_car_pc", [128, C], F32)
    p_car_ps = alloc("p_car_ps", [128, C], F32)
    p_car_kc = alloc("p_car_kc", [128, C], F32)
    p_car_ks = alloc("p_car_ks", [128, C], F32)
    p_car_vg = alloc("p_car_vg", [128, 1], F32)
    p_scarry = alloc("p_scarry", [128, C, 4], F32)
    p_bnd = alloc("p_bnd", [128, C, 4], F32)
    p_acol = alloc("p_acol", [128, C], F32)
    p_acol2 = alloc("p_acol2", [128, C], F32)
    p_bias = alloc("p_bias", [1, 5664], BF16)
    NTT = TC // 128
    p_gl = alloc("p_gl", [128, NTT * NE], F32)
    p_gw1 = alloc("p_gw1", [128, NTT * NE], F32)
    p_t2a = alloc("p_t2a", [128, NTT * NE], F32)
    p_t2b = alloc("p_t2b", [128, NTT * NE], F32)
    p_m1c = alloc("p_m1c", [128, NTT], F32)
    p_m2c = alloc("p_m2c", [128, NTT], F32)
    p_nm1f = alloc("p_nm1f", [128, NTT], F32)
    p_nm2f = alloc("p_nm2f", [128, NTT], F32)
    p_nmh1 = alloc("p_nmh1", [128, NTT], F32)
    p_nmh2 = alloc("p_nmh2", [128, NTT], F32)
    p_th1 = alloc("p_th1", [128, NTT], F32)
    p_th2 = alloc("p_th2", [128, NTT], F32)
    p_p1k = alloc("p_p1k", [128, NTT], F32)
    p_p2k = alloc("p_p2k", [128, NTT], F32)
    f_be = alloc("f_be", [128, 2, JC], F32)
    f_beE = alloc("f_beE", [128, JC], F32)
    f_b2c = alloc("f_b2c", [128, C], F32)
    p_wrow = alloc("p_wrow", [NE, TC], BF16)
    p_sel = alloc("p_sel", [NE, NE * 128], BF16)
    r_gcs = alloc("r_gcs", [1, TC], BF16)
    r_bw0 = alloc("r_bw0", [1, TC], BF16)
    r_bw1 = alloc("r_bw1", [1, TC], BF16)
    r_mu2 = alloc("r_mu2", [1, TC], BF16)
    r_r2 = alloc("r_r2", [1, TC], BF16)
    r_rho = alloc("r_rho", [1, TC], BF16)
    r_vg = alloc("r_vg", [1, TC], F32)
    rf = [alloc(f"rf{i}", [1, SUB], F32) for i in range(4)]
    rwf = alloc("rwf", [1, TC], F32)
    r_ones = alloc("r_ones", [1, TC], F32)
    SP16_OFF = off[0]
    p_sp16a = alloc("p_sp16a", [128, SUB], BF16)
    p_sp16b = alloc("p_sp16b", [128, SUB], BF16)
    p_junk = alloc("p_junk", [128, SUB], F32, at=SP16_OFF)
    p_spfa = alloc("p_spfa", [128, SUB], F32)
    p_spfb = alloc("p_spfb", [128, SUB], F32)
    p_xn = alloc("p_xn", [128, C, TC + 1], BF16)

    A0 = off[0]
    ARENA = TOP - A0
    assert ARENA >= 88000, f"arena too small: {ARENA}"

    a_wpv = alloc("a_wpv", [128, C, D], BF16, at=A0)
    a_wk = alloc("a_wk", [128, C, D], BF16, at=A0 + 16384)
    a_wkv = alloc("a_wkv", [128, C, D], BF16, at=A0 + 32768)
    a_wg1 = alloc("a_wg1", [128, 2 * C, D], BF16, at=A0)
    a_wo = alloc("a_wo", [128, C, D], BF16, at=A0)
    a_wb1 = alloc("a_wb1", [128, C, D // 2], BF16, at=A0 + 32768)
    a_xsub = alloc("a_xsub", [128, C, SUB], F32, at=A0 + 32768)
    a_pc = alloc("a_pc", [128, C, SUB], BF16, at=A0 + 49152)
    a_ps = alloc("a_ps", [128, C, SUB], BF16, at=A0 + 57344)
    a_gh = alloc("a_gh", [128, C, SUB], BF16, at=A0 + 49152)
    a_bh = alloc("a_bh", [128, 4, SUB], BF16, at=A0 + 57344)
    a_ln = alloc("a_ln", [128, C, SUB], BF16, at=A0 + 49152)
    ZB = A0 + 65536
    z_tg1 = alloc("z_tg1", [128, SUB], I32, at=ZB)
    z_tg2 = alloc("z_tg2", [128, SUB], F32, at=ZB + 2048)
    z_kt = alloc("z_kt", [128, SUB], F32, at=ZB + 4096)
    z_ka = alloc("z_ka", [128, SUB], F32, at=ZB + 6144)
    z_kc = alloc("z_kc", [128, SUB + 1], F32, at=ZB + 8192)
    z_ks = alloc("z_ks", [128, SUB + 1], F32, at=ZB + 10304)
    z_p1 = alloc("z_p1", [128, SUB], F32, at=ZB + 12416)
    z_p2 = alloc("z_p2", [128, SUB], F32, at=ZB + 14464)
    z_m1 = alloc("z_m1", [128, SUB], F32, at=ZB + 16512)
    z_m2 = alloc("z_m2", [128, SUB], F32, at=ZB + 18560)
    z_sq = alloc("z_sq", [128, SUB], F32, at=ZB + 20608)
    z_x1 = alloc("z_x1", [128, SUB], F32, at=ZB + 20608)
    assert ZB + 22656 <= TOP

    f_h = alloc("f_h", [128, JC, TC // 2], BF16, at=A0)
    f_acc = alloc("f_acc", [128, C, TC], BF16, at=A0 + 32768)
    f_w1a = alloc("f_w1a", [128, 4, 256], BF16, at=A0 + 65536)
    f_w1b = alloc("f_w1b", [128, 4, 256], BF16, at=A0 + 67584)
    f_w2ma = alloc("f_w2ma", [128, 8, 128], BF16, at=A0 + 69632)
    f_w2mb = alloc("f_w2mb", [128, 8, 128], BF16, at=A0 + 71680)
    f_b1 = alloc("f_b1", [1, 256], BF16, at=A0 + 73728)
    f_b2 = alloc("f_b2", [1, D], BF16, at=A0 + 74240)
    f_bwe = alloc("f_bwe", [128, 2, SUB], F32, at=A0 + 76288)
    f_swz = alloc("f_swz", [128, SUB], F32, at=A0 + 80384)
    f_swz2 = alloc("f_swz2", [128, SUB], F32, at=A0 + 82432)
    f_swz3 = alloc("f_swz3", [128, SUB], BF16, at=A0 + 84480)
    f_swz4 = alloc("f_swz4", [128, SUB], BF16, at=A0 + 85504)
    assert A0 + 86528 <= TOP

    g_x = alloc("g_x", [128, C, SUB], F32, at=A0)
    g_g = alloc("g_g", [128, C, SUB], BF16, at=A0 + 16384)
    g_s = alloc("g_s", [128, C, SUB], BF16, at=A0 + 24576)
    g_o = alloc("g_o", [128, C, SUB], F32, at=A0 + 65536)
    assert A0 + 65536 + 16384 <= TOP

    pA = nc.alloc_psum_tensor("pA", [128, SUB], F32)
    pB = nc.alloc_psum_tensor("pB", [128, SUB], F32)
    pC_ = nc.alloc_psum_tensor("pC", [128, SUB], F32)
    pD = nc.alloc_psum_tensor("pD", [128, SUB], F32)
    pE = nc.alloc_psum_tensor("pE", [128, SUB], F32)
    pF = nc.alloc_psum_tensor("pF", [128, SUB], F32)
    pR0 = nc.alloc_psum_tensor("pR0", [1, SUB], F32)
    pR1 = nc.alloc_psum_tensor("pR1", [1, SUB], F32)

    b = Build(nc)
    b.set_alias(ALLOC_IV)

    def dma(dst_ap, src_ap, reads, writes, eng="gpsimd", slow=False,
            waw=True):
        b.emit(eng,
               lambda E, d=dst_ap, s=src_ap, sl=slow: E.dma_start(
                   out=d, in_=s, allow_slow_non_contiguous=sl),
               reads=reads, writes=writes, dma=True, waw=waw)

    def V(fn, reads, writes):
        b.emit("vector", fn, reads=reads, writes=writes)

    def S(fn, reads, writes):
        b.emit("scalar", fn, reads=reads, writes=writes)

    def T(fn, reads, writes):
        b.emit("tensor", fn, reads=reads, writes=writes)

    def col(i):
        return p_cst[:, i:i + 1]

    def tap(name, sbuf_ap, rtags):
        if name not in TAPS:
            return
        shp = list(sbuf_ap.shape)
        dt = sbuf_ap.dtype
        t_out = nc.dram_tensor("tap_" + name, shp, dt,
                               kind="ExternalOutput").ap()
        dma(t_out, sbuf_ap, rtags, [f"tap_{name}"], eng="sync")

    def tap_dram(name, dram_ap, shp, dt, rtags):
        if name not in TAPS:
            return
        t_out = nc.dram_tensor("tap_" + name, list(shp), dt,
                               kind="ExternalOutput").ap()
        dma(t_out, dram_ap, rtags, [f"tap_{name}"], eng="sync")

    def bias_lhsT(name, mchunk):
        o, n = BOFF[name]
        return p_bias[:, o + mchunk * 128:o + min((mchunk + 1) * 128, o + n)
                      if False else o + mchunk * 128 + 128] \
            if False else p_bias[:, o + mchunk * 128:o + mchunk * 128 + 128]

    dma(p_cst[:], cst, [], ["p_cst"], eng="sync")
    dma(p_cmask[:], cmask, [], ["p_cmask"], eng="sync")
    dma(p_pmask[:], pmask, [], ["p_pmask"], eng="sync")
    dma(p_lng[:], ln_g, [], ["p_lng"], eng="sync")
    dma(p_lnb[:], ln_b, [], ["p_lnb"], eng="sync")
    dma(p_nsc[:], nsc, [], ["p_nsc"], eng="sync")
    dma(p_identf[:], identw, [], ["p_identf"], eng="sync")
    dma(p_identf[:], identw, [], ["p_identf"])
    dma(p_bias[:], biasrow, [], ["p_bias"])
    dma(p_sel[:], selw, [], ["p_sel"])
    dma(p_wg2[:, :, 0], Wg2.rearrange("(c p) o -> p (c o)", p=128)
        if False else Wg2[0:D, 0:1], [], ["p_wg2"]) if False else None
    for c in range(C):
        dma(p_wg2[:, c, 0:1], Wg2[c * 128:(c + 1) * 128, 0:1], [], ["p_wg2"],
            slow=True)
    for c in range(4):
        dma(p_wb2[:, c, :], Wb2[c * 128:(c + 1) * 128, :], [], ["p_wb2"])
    for c in range(C):
        dma(p_gw[:, c, :], gate_w[c * 128:(c + 1) * 128, :], [], ["p_gw"])
    V(lambda E: E.memset(p_ones_r[0:1, 0:SUB], 1.0), [], ["ones"])
    V(lambda E: E.memset(p_ones_r[0:1, SUB:SUB + 128], 1.0 / 32.0), [],
      ["ones"])
    V(lambda E: E.memset(p_ones_rf[0:1, 0:128], 1.0), [], ["ones"])
    V(lambda E: E.memset(p_ones_rf[0:1, 128:256], 1.0 / 32.0), [], ["ones"])
    V(lambda E: E.memset(p_ones_c[:], 1.0), [], ["ones"])
    V(lambda E: E.memset(p_ones_sc[:], 1.0), [], ["ones"])

    def ln_sums(x_chunks_fn, rtags, nn=SUB):
        for kc in range(C):
            T(lambda E, kc=kc, f=(kc == 0), l=(kc == C - 1), nn=nn: E.matmul(
                pR0[0:1, 0:nn], p_ones_c[:, 0:1], x_chunks_fn(kc),
                start=f, stop=l), ["ones"] + rtags, ["pR0"])
        for kc in range(C):
            S(lambda E, kc=kc, nn=nn: E.activation(
                out=z_sq[:, 0:nn], in_=x_chunks_fn(kc), func=AF.Square),
              rtags, ["z_sq"])
            T(lambda E, kc=kc, f=(kc == 0), l=(kc == C - 1), nn=nn: E.matmul(
                pR1[0:1, 0:nn], p_ones_c[:, 0:1], z_sq[:, 0:nn],
                start=f, stop=l), ["ones", "z_sq"], ["pR1"])

    def ln_rows(nn=SUB):
        V(lambda E, nn=nn: E.tensor_scalar(rf[0][0:1, 0:nn], pR0[0:1, 0:nn],
                                           1.0 / D, None, ALU.mult),
          ["pR0"], ["rf0"])
        V(lambda E, nn=nn: E.tensor_scalar(rf[3][0:1, 0:nn], pR1[0:1, 0:nn],
                                           1.0 / D, None, ALU.mult),
          ["pR1"], ["rf3"])
        V(lambda E, nn=nn: E.tensor_mul(rf[1][0:1, 0:nn], rf[0][0:1, 0:nn],
                                        rf[0][0:1, 0:nn]), ["rf0"], ["rf1"])
        V(lambda E, nn=nn: E.tensor_sub(rf[2][0:1, 0:nn], rf[3][0:1, 0:nn],
                                        rf[1][0:1, 0:nn]),
          ["rf3", "rf1"], ["rf2"])
        V(lambda E, nn=nn: E.tensor_scalar(rf[2][0:1, 0:nn], rf[2][0:1, 0:nn],
                                           EPS, None, ALU.add),
          ["rf2"], ["rf2"])
        V(lambda E, nn=nn: E.reciprocal(rf[3][0:1, 0:nn], rf[2][0:1, 0:nn]),
          ["rf2"], ["rf3"])
        S(lambda E, nn=nn: E.activation(out=rf[1][0:1, 0:nn],
                                        in_=rf[3][0:1, 0:nn], func=AF.Sqrt),
          ["rf3"], ["rf1"])

    def bcast(psum, pname, row_ap, rtags, nn=SUB):
        T(lambda E, nn=nn: E.matmul(psum[:, 0:nn], p_ones_rf[0:1, 0:128],
                                    row_ap, start=True, stop=True),
          ["ones"] + rtags, [pname])

    def bcast16(psum, pname, row_ap, rtags, nn=SUB):
        T(lambda E, nn=nn: E.matmul(psum[:, 0:nn], p_ones_r[0:1, 0:128],
                                    row_ap, start=True, stop=True),
          ["ones"] + rtags, [pname])

    def trig(dst_ap, wtag, src_ap, rtag, kind, nn=SUB):
        add_c = 32.0 if kind == "sin" else 32.25
        bias_i = C_SINB if kind == "sin" else C_COSB
        V(lambda E: E.tensor_scalar(z_tg1[:, 0:nn], src_ap, 1.0 / TWO_PI,
                                    add_c, ALU.mult, ALU.add),
          [rtag], ["z_tg1"])
        V(lambda E: E.tensor_copy(z_tg2[:, 0:nn], z_tg1[:, 0:nn]),
          ["z_tg1"], ["z_tg2"])
        V(lambda E: E.scalar_tensor_tensor(z_tg2[:, 0:nn], z_tg2[:, 0:nn],
                                           -TWO_PI, src_ap, ALU.mult, ALU.add),
          ["z_tg2", rtag], ["z_tg2"])
        S(lambda E: E.activation(out=dst_ap, in_=z_tg2[:, 0:nn], func=AF.Sin,
                                 bias=col(bias_i)),
          ["z_tg2", "p_cst"], [wtag])

    def wmm(psum, pname, w_sb, wtag, mchunk, x_fn, xtags, kchunks, bias,
            extra_k=()):
        first = True
        for kc in kchunks:
            T(lambda E, kc=kc, f=first: E.matmul(
                psum[:, 0:SUB], w_sb[:, kc, mchunk * 128:(mchunk + 1) * 128],
                x_fn(kc), start=f, stop=False), [wtag] + xtags, [pname])
            first = False
        for (w2_sb, w2tag, kc2, x2_fn) in extra_k:
            T(lambda E, kc2=kc2, w2_sb=w2_sb, x2_fn=x2_fn: E.matmul(
                psum[:, 0:SUB], w2_sb[:, kc2, mchunk * 128:(mchunk + 1) * 128],
                x2_fn(kc2), start=False, stop=False),
              [w2tag] + xtags, [pname])
        o, _n = BOFF[bias]
        T(lambda E, o=o, m=mchunk: E.matmul(
            psum[:, 0:SUB], p_bias[:, o + m * 128:o + (m + 1) * 128],
            p_ones_r[:, 0:SUB], start=False, stop=True),
          ["p_bias", "ones"], [pname])

    def process_slot(j, own):
        S(lambda E: E.preload_act_table(AF.Sqrt), [], [])
        for n in range(NSUB):
            t0 = 1 + n * SUB
            for c in range(C):
                dma(a_xsub[:, c, :], xT[c * 128:(c + 1) * 128, j, t0:t0 + SUB],
                    ["a_xsub"], ["a_xsub"], eng="sync")
            ln_sums(lambda kc: a_xsub[:, kc, :], ["a_xsub"])
            ln_rows()
            bcast(pA, "pA", rf[0][0:1, 0:SUB], ["rf0"])
            bcast(pB, "pB", rf[1][0:1, 0:SUB], ["rf1"])
            for c in range(C):
                V(lambda E, c=c: E.tensor_sub(z_p1[:, :], a_xsub[:, c, :],
                                              pA[:, :]),
                  ["a_xsub", "pA"], ["z_p1"])
                V(lambda E, c=c, t0=t0: E.tensor_mul(
                    p_xn[:, c, t0:t0 + SUB], z_p1[:, :], pB[:, :]),
                  ["z_p1", "pB"], ["p_xn"])
            if own:
                V(lambda E: E.tensor_scalar(rf[3][0:1, :], rf[2][0:1, :], EPS,
                                            None, ALU.subtract),
                  ["rf2"], ["rf3"])
                V(lambda E: E.reciprocal(rf[0][0:1, :], rf[2][0:1, :]),
                  ["rf2"], ["rf0"])
                V(lambda E: E.tensor_mul(rf[2][0:1, :], rf[3][0:1, :],
                                         rf[0][0:1, :]), ["rf3", "rf0"],
                  ["rf2"])
                V(lambda E: E.tensor_scalar(rf[3][0:1, :], rf[2][0:1, :], EPS,
                                            None, ALU.add), ["rf2"], ["rf3"])
                V(lambda E: E.reciprocal(rf[0][0:1, :], rf[3][0:1, :]),
                  ["rf3"], ["rf0"])
                S(lambda E, n=n: E.activation(
                    out=r_rho[0:1, n * SUB:(n + 1) * SUB], in_=rf[0][0:1, :],
                    func=AF.Sqrt), ["rf0"], ["r_rho"])
        for c in range(C):
            dma(a_xsub[:, c, :], xT[c * 128:(c + 1) * 128, j, 0:SUB],
                ["a_xsub"], ["a_xsub"], eng="sync")
        ln_sums(lambda kc: a_xsub[:, kc, :], ["a_xsub"])
        ln_rows()
        bcast(pA, "pA", rf[0][0:1, 0:SUB], ["rf0"])
        bcast(pB, "pB", rf[1][0:1, 0:SUB], ["rf1"])
        for c in range(C):
            V(lambda E, c=c: E.tensor_sub(z_p1[:, 0:1], a_xsub[:, c, 0:1],
                                          pA[:, 0:1]), ["a_xsub", "pA"],
              ["z_p1"])
            V(lambda E, c=c: E.tensor_mul(p_xn[:, c, 0:1], z_p1[:, 0:1],
                                          pB[:, 0:1]), ["z_p1", "pB"],
              ["p_xn"])
        pass

        S(lambda E: E.preload_act_table(AF.Gelu), [], [])
        for c2 in range(2 * C):
            dma(a_wg1[:, c2, :], Wg1[c2 * 128:(c2 + 1) * 128, :],
                ["a_wg1"] if c2 == 0 else [], ["a_wg1"])
        if own:
            for c in range(C):
                dma(a_wb1[:, c, :], Wb1[c * 128:(c + 1) * 128, :],
                    ["a_wb1"] if c == 0 else [], ["a_wb1"])
        rot4 = [(pA, "pA"), (pB, "pB"), (pC_, "pC"), (pD, "pD")]
        for n in range(NSUB):
            t0 = 1 + n * SUB
            for m in range(C):
                ps1, pn1 = rot4[m % (2 if own else 4)]
                wmm(ps1, pn1, a_wg1, "a_wg1", m,
                    lambda kc, t0=t0: p_xn[:, kc, t0:t0 + SUB], ["p_xn"],
                    range(C), "bg1",
                    extra_k=[(a_wg1, "a_wg1", C + kc2,
                              (lambda kc2_, t0=t0: p_xn[:, kc2_ - C,
                                                        t0 - 1:t0 - 1 + SUB]))
                             for kc2 in range(C)])
                S(lambda E, m=m, ps1=ps1: E.activation(
                    out=a_gh[:, m, :], in_=ps1[:, :], func=AF.Gelu),
                  [pn1], ["a_gh"])
            if own and n == 0:
                tap("gh0", a_gh[:, :, :], ["a_gh"])
            for kc in range(C):
                T(lambda E, kc=kc, f=(kc == 0): E.matmul(
                    pR0[0:1, 0:SUB], p_wg2[:, kc, 0:1], a_gh[:, kc, :],
                    start=f, stop=False), ["p_wg2", "a_gh"], ["pR0"])
            o_bg2, _n2 = BOFF["bg2"]
            T(lambda E, o=o_bg2: E.matmul(pR0[0:1, 0:SUB],
                                          p_bias[:, o:o + 1],
                                          p_ones_r[:, 0:SUB],
                                          start=False, stop=True),
              ["p_bias", "ones"], ["pR0"])
            S(lambda E: E.activation(out=rf[2][0:1, :], in_=pR0[0:1, :],
                                     func=AF.Tanh,
                                     scale=p_cst[0:1, C_HALF:C_HALF + 1]),
              ["pR0", "p_cst"], ["rf2"])
            S(lambda E, n=n: E.activation(
                out=r_vg[0:1, n * SUB:(n + 1) * SUB], in_=rf[2][0:1, :],
                func=AF.Identity, scale=p_cst[0:1, C_HALF:C_HALF + 1],
                bias=p_cst[0:1, C_HALF:C_HALF + 1]),
              ["rf2", "p_cst"], ["r_vg"])
            if own:
                for m in range(4):
                    ps1, pn1 = (pC_, "pC") if m % 2 == 0 else (pD, "pD")
                    wmm(ps1, pn1, a_wb1, "a_wb1", m,
                        lambda kc, t0=t0: p_xn[:, kc, t0:t0 + SUB], ["p_xn"],
                        range(C), "bb1")
                    S(lambda E, m=m, ps1=ps1: E.activation(
                        out=a_bh[:, m, :], in_=ps1[:, :], func=AF.Gelu),
                      [pn1], ["a_bh"])
                o_bb2, _n2 = BOFF["bb2"]
                for zi in range(2):
                    for kc in range(4):
                        T(lambda E, kc=kc, zi=zi, f=(kc == 0):
                          E.matmul(pR1[0:1, 0:SUB], p_wb2[:, kc, zi:zi + 1],
                                   a_bh[:, kc, :], start=f, stop=False),
                          ["p_wb2", "a_bh"], ["pR1"])
                    T(lambda E, zi=zi, o=o_bb2: E.matmul(
                        pR1[0:1, 0:SUB], p_bias[:, o + zi:o + zi + 1],
                        p_ones_r[:, 0:SUB], start=False, stop=True),
                      ["p_bias", "ones"], ["pR1"])
                    V(lambda E, zi=zi: E.tensor_copy(
                        rf[0 if zi == 0 else 3][0:1, :], pR1[0:1, :]),
                      ["pR1"], ["rf0" if zi == 0 else "rf3"])
                V(lambda E: E.tensor_sub(rf[1][0:1, :], rf[0][0:1, :],
                                         rf[3][0:1, :]),
                  ["rf0", "rf3"], ["rf1"])
                S(lambda E: E.activation(out=rf[2][0:1, :],
                                         in_=rf[1][0:1, :], func=AF.Tanh,
                                         scale=p_cst[0:1,
                                                     C_HALF:C_HALF + 1]),
                  ["rf1", "p_cst"], ["rf2"])
                S(lambda E, n=n: E.activation(
                    out=r_bw0[0:1, n * SUB:(n + 1) * SUB],
                    in_=rf[2][0:1, :], func=AF.Identity,
                    scale=p_cst[0:1, C_HALF:C_HALF + 1],
                    bias=p_cst[0:1, C_HALF:C_HALF + 1]),
                  ["rf2", "p_cst"], ["r_bw0"])
                S(lambda E, n=n: E.activation(
                    out=r_bw1[0:1, n * SUB:(n + 1) * SUB],
                    in_=rf[2][0:1, :], func=AF.Identity,
                    scale=p_cst[0:1, C_NEGH:C_NEGH + 1],
                    bias=p_cst[0:1, C_HALF:C_HALF + 1]),
                  ["rf2", "p_cst"], ["r_bw1"])
        pass

        S(lambda E: E.preload_act_table(AF.Silu), [], [])
        for c in range(C):
            dma(a_wpv[:, c, :], Wpv[c * 128:(c + 1) * 128, :],
                ["a_wpv"] if c == 0 else [], ["a_wpv"])
        for c in range(C):
            dma(a_wk[:, c, :], Wk[c * 128:(c + 1) * 128, :],
                ["a_wk"] if c == 0 else [], ["a_wk"])
        for c in range(C):
            dma(a_wkv[:, c, :], Wkv[c * 128:(c + 1) * 128, :],
                ["a_wkv"] if c == 0 else [], ["a_wkv"])
        rot4b = [(pA, "pA"), (pB, "pB"), (pC_, "pC"), (pD, "pD")]
        for m in range(C):
            ps1, pn1 = rot4b[m % 4]
            wmm(ps1, pn1, a_wk, "a_wk", m, lambda kc: p_xn[:, kc, 0:SUB],
                ["p_xn"], range(C), "bk")
            S(lambda E, ps1=ps1: E.activation(out=z_kt[:, :], in_=ps1[:, :],
                                              func=AF.Tanh), [pn1], ["z_kt"])
            V(lambda E, m=m: E.tensor_copy(p_bnd[:, m, 0:1], z_kt[:, 0:1]),
              ["z_kt"], ["p_bnd"])
        S(lambda E: E.activation(out=p_bnd[:, :, 3:4], in_=p_bnd[:, :, 0:1],
                                 func=AF.Abs), ["p_bnd"], ["p_bnd"])
        S(lambda E: E.activation(out=p_bnd[:, :, 2:3], in_=p_bnd[:, :, 0:1],
                                 func=AF.Sin, scale=col(C_PI)),
          ["p_bnd", "p_cst"], ["p_bnd"])
        S(lambda E: E.activation(out=p_bnd[:, :, 3:4], in_=p_bnd[:, :, 3:4],
                                 func=AF.Sin, scale=col(C_NPI),
                                 bias=col(C_HPI)), ["p_bnd", "p_cst"],
          ["p_bnd"]) if False else None
        S(lambda E: E.activation(out=z_ka[:, 0:C], in_=p_bnd[:, :, 3],
                                 func=AF.Sin, scale=col(C_NPI),
                                 bias=col(C_HPI)),
          ["p_bnd", "p_cst"], ["z_ka"])
        for m in range(C):
            V(lambda E, m=m, j=j: E.tensor_scalar(
                p_bnd[:, m, 0:1], p_bnd[:, m, 2:3], p_pmask[:, j:j + 1],
                None, ALU.mult), ["p_bnd", "p_pmask"], ["p_bnd"])
            V(lambda E, m=m, j=j: E.tensor_scalar(
                p_bnd[:, m, 1:2], z_ka[:, m:m + 1], p_pmask[:, j:j + 1],
                None, ALU.mult), ["z_ka", "p_pmask"], ["p_bnd"])
        if own:
            for dst, dtag, src, stag in (
                    (p_car_pc, "p_car_pc", p_sum_pc, "p_sum_pc"),
                    (p_car_ps, "p_car_ps", p_sum_ps, "p_sum_ps"),
                    (p_car_kc, "p_car_kc", p_sum_kc, "p_sum_kc"),
                    (p_car_ks, "p_car_ks", p_sum_ks, "p_sum_ks")):
                V(lambda E, dst=dst: E.memset(dst[:, :], 0.0), [], [dtag])
                for jj in range(NS - 1):
                    V(lambda E, dst=dst, src=src, jj=jj:
                      E.scalar_tensor_tensor(dst[:, :], src[:, :, jj],
                                             p_cmask[:, jj:jj + 1],
                                             dst[:, :], ALU.mult, ALU.add),
                      [stag, "p_cmask", dtag], [dtag])
            for jj in range(NS - 1):
                V(lambda E, jj=jj: E.tensor_scalar(
                    p_vgm[:, jj * NSUB:(jj + 1) * NSUB],
                    p_sum_vg[:, jj * NSUB:(jj + 1) * NSUB],
                    p_cmask[:, jj:jj + 1], None,
                    ALU.mult), ["p_sum_vg", "p_cmask"], ["p_vgm"])
            V(lambda E: E.memset(p_vgm[:, (NS - 1) * NSUB:NS * NSUB], 0.0),
              [], ["p_vgm"])
            S(lambda E: E.nop(), ["p_vgm"], ["rt_vg"])
            V(lambda E: E.reduce_sum(p_car_vg[:, 0:1], p_vgm[:, :], axis=AX),
              ["p_vgm", "rt_vg"], ["p_car_vg"])
            V(lambda E: E.memset(r_ones[0:1, :], 1.0), [], ["r_ones"])
            S(lambda E: E.nop(), ["r_ones"], ["rt_ro"])
            V(lambda E: E.tensor_tensor_scan(
                rwf[0:1, :], r_ones[0:1, :], r_vg[0:1, :], 0.0,
                ALU.mult, ALU.add), ["r_ones", "r_vg", "rt_ro"], ["rwf"])
            V(lambda E: E.tensor_scalar(r_gcs[0:1, :], rwf[0:1, :],
                                        p_car_vg[0:1, 0:1], 1.0, ALU.add,
                                        ALU.max),
              ["rwf", "p_car_vg"], ["r_gcs"])
            for i, (src, stag) in enumerate(((p_car_pc, "p_car_pc"),
                                             (p_car_ps, "p_car_ps"),
                                             (p_car_kc, "p_car_kc"),
                                             (p_car_ks, "p_car_ks"))):
                for c in range(C):
                    V(lambda E, i=i, c=c, src=src: E.tensor_copy(
                        p_scarry[:, c, i:i + 1], src[:, c:c + 1]),
                      [stag], ["p_scarry"])

        for n in range(NSUB):
            t0 = 1 + n * SUB
            for c in range(C):
                dma(z_x1[:, :], ppT[c * 128:(c + 1) * 128, j,
                                    n * SUB:(n + 1) * SUB],
                    ["z_x1"], ["z_x1"], eng="sync")
                trig(a_ps[:, c, :], "a_ps", z_x1[:, :], "z_x1", "sin")
                trig(a_pc[:, c, :], "a_pc", z_x1[:, :], "z_x1", "cos")
            for m in range(C):
                ps1, pn1 = rot4b[m % 4]
                wmm(ps1, pn1, a_wpv, "a_wpv", m,
                    lambda kc, t0=t0: p_xn[:, kc, t0:t0 + SUB], ["p_xn"],
                    range(C), "bpv")
                V(lambda E, m=m, ps1=ps1: E.tensor_mul(
                    z_p1[:, :], a_pc[:, m, :], ps1[:, :]),
                  ["a_pc", pn1], ["z_p1"])
                V(lambda E, m=m, ps1=ps1: E.tensor_mul(
                    z_p2[:, :], a_ps[:, m, :], ps1[:, :]),
                  ["a_ps", pn1], ["z_p2"])
                if own:
                    S(lambda E: E.nop(), ["z_p1", "z_p2", "p_scarry"],
                      ["rt_s2"])
                    V(lambda E, m=m: E.tensor_tensor_scan(
                        z_m1[:, :], p_ones_sc[:, :], z_p1[:, :],
                        p_scarry[:, m, 0:1], ALU.mult, ALU.add),
                      ["ones", "z_p1", "p_scarry", "rt_s2"], ["z_m1"])
                    V(lambda E, m=m: E.tensor_tensor_scan(
                        z_m2[:, :], p_ones_sc[:, :], z_p2[:, :],
                        p_scarry[:, m, 1:2], ALU.mult, ALU.add),
                      ["ones", "z_p2", "p_scarry"], ["z_m2"])
                    V(lambda E, m=m: E.tensor_copy(p_scarry[:, m, 0:1],
                                                   z_m1[:, SUB - 1:SUB]),
                      ["z_m1"], ["p_scarry"])
                    V(lambda E, m=m: E.tensor_copy(p_scarry[:, m, 1:2],
                                                   z_m2[:, SUB - 1:SUB]),
                      ["z_m2"], ["p_scarry"])
                    spb, sptag = (p_spfa, "p_spfa") if m % 2 == 0 \
                        else (p_spfb, "p_spfb")
                    V(lambda E, m=m: E.tensor_mul(z_p1[:, :], a_pc[:, m, :],
                                                  z_m1[:, :]),
                      ["a_pc", "z_m1"], ["z_p1"])
                    V(lambda E, m=m: E.tensor_mul(z_p2[:, :], a_ps[:, m, :],
                                                  z_m2[:, :]),
                      ["a_ps", "z_m2"], ["z_p2"])
                    V(lambda E, spb=spb: E.tensor_add(spb[:, :], z_p1[:, :],
                                                      z_p2[:, :]),
                      ["z_p1", "z_p2"], [sptag])
                    dma(sp_pos[:, m, n * SUB:(n + 1) * SUB], spb[:, :],
                        [sptag], [f"sp_pos_{m}_{n}"])
                else:
                    V(lambda E, m=m: E.scalar_tensor_tensor(
                        p_junk[:, :], z_p1[:, :], 1.0, p_ones_sc[:, :],
                        ALU.mult, ALU.mult, accum_out=p_acol[:, m:m + 1]),
                      ["z_p1", "ones"], ["p_junk", "p_acol"])
                    V(lambda E, m=m: E.scalar_tensor_tensor(
                        p_junk[:, :], z_p2[:, :], 1.0, p_ones_sc[:, :],
                        ALU.mult, ALU.mult, accum_out=p_acol2[:, m:m + 1]),
                      ["z_p2", "ones"], ["p_junk", "p_acol2"])
                if not own and m == C - 1:
                    if n == 0:
                        V(lambda E, j=j: E.tensor_copy(p_sum_pc[:, :, j],
                                                       p_acol[:, :]),
                          ["p_acol"], ["p_sum_pc"])
                        V(lambda E, j=j: E.tensor_copy(p_sum_ps[:, :, j],
                                                       p_acol2[:, :]),
                          ["p_acol2"], ["p_sum_ps"])
                    else:
                        V(lambda E, j=j: E.tensor_add(p_sum_pc[:, :, j],
                                                      p_sum_pc[:, :, j],
                                                      p_acol[:, :]),
                          ["p_acol", "p_sum_pc"], ["p_sum_pc"])
                        V(lambda E, j=j: E.tensor_add(p_sum_ps[:, :, j],
                                                      p_sum_ps[:, :, j],
                                                      p_acol2[:, :]),
                          ["p_acol2", "p_sum_ps"], ["p_sum_ps"])
            bcast(pE, "pE", r_vg[0:1, n * SUB:(n + 1) * SUB], ["r_vg"])
            if not own:
                V(lambda E, j=j, n=n: E.scalar_tensor_tensor(
                    p_junk[:, :], p_ones_sc[:, :], 1.0, pE[:, :],
                    ALU.bypass, ALU.mult,
                    accum_out=p_sum_vg[:, j * NSUB + n:j * NSUB + n + 1]),
                  ["ones", "pE"], ["p_junk", "p_sum_vg"])
            for m in range(C):
                ps1, pn1 = (pC_, "pC") if m % 2 == 0 else (pD, "pD")
                wmm(ps1, pn1, a_wk, "a_wk", m,
                    lambda kc, t0=t0: p_xn[:, kc, t0:t0 + SUB], ["p_xn"],
                    range(C), "bk")
                S(lambda E, ps1=ps1: E.activation(out=z_kt[:, :],
                                                  in_=ps1[:, :],
                                                  func=AF.Tanh),
                  [pn1], ["z_kt"])
                S(lambda E: E.activation(out=z_ka[:, :], in_=z_kt[:, :],
                                         func=AF.Abs), ["z_kt"], ["z_ka"])
                S(lambda E: E.activation(out=z_ks[:, 1:SUB + 1],
                                         in_=z_kt[:, :], func=AF.Sin,
                                         scale=col(C_PI)),
                  ["z_kt", "p_cst"], ["z_ks"])
                S(lambda E: E.activation(out=z_kc[:, 1:SUB + 1],
                                         in_=z_ka[:, :], func=AF.Sin,
                                         scale=col(C_NPI), bias=col(C_HPI)),
                  ["z_ka", "p_cst"], ["z_kc"])
                V(lambda E, m=m: E.tensor_copy(z_kc[:, 0:1],
                                               p_bnd[:, m, 1:2]),
                  ["p_bnd"], ["z_kc"])
                V(lambda E, m=m: E.tensor_copy(z_ks[:, 0:1],
                                               p_bnd[:, m, 0:1]),
                  ["p_bnd"], ["z_ks"])
                V(lambda E, m=m: E.tensor_copy(p_bnd[:, m, 1:2],
                                               z_kc[:, SUB:SUB + 1]),
                  ["z_kc"], ["p_bnd"])
                V(lambda E, m=m: E.tensor_copy(p_bnd[:, m, 0:1],
                                               z_ks[:, SUB:SUB + 1]),
                  ["z_ks"], ["p_bnd"])
                ps2, pn2 = [(pA, "pA"), (pB, "pB"),
                            (pF, "pF")][m % 3]
                wmm(ps2, pn2, a_wkv, "a_wkv", m,
                    lambda kc, t0=t0: p_xn[:, kc, t0:t0 + SUB], ["p_xn"],
                    range(C), "bkv")
                V(lambda E, ps2=ps2: E.tensor_mul(z_p1[:, :], z_kc[:, 0:SUB],
                                                  ps2[:, :]),
                  ["z_kc", pn2], ["z_p1"])
                V(lambda E, ps2=ps2: E.tensor_mul(z_p2[:, :], z_ks[:, 0:SUB],
                                                  ps2[:, :]),
                  ["z_ks", pn2], ["z_p2"])
                if own:
                    V(lambda E: E.scalar_tensor_tensor(
                        z_p1[:, :], z_p1[:, :], 1.0, pE[:, :], ALU.bypass,
                        ALU.mult), ["z_p1", "pE"], ["z_p1"])
                    V(lambda E: E.scalar_tensor_tensor(
                        z_p2[:, :], z_p2[:, :], 1.0, pE[:, :], ALU.bypass,
                        ALU.mult), ["z_p2", "pE"], ["z_p2"])
                    S(lambda E: E.nop(), ["z_p1", "z_p2", "p_scarry"],
                      ["rt_s3"])
                    V(lambda E, m=m: E.tensor_tensor_scan(
                        z_m1[:, :], p_ones_sc[:, :], z_p1[:, :],
                        p_scarry[:, m, 2:3], ALU.mult, ALU.add),
                      ["ones", "z_p1", "p_scarry", "rt_s3"], ["z_m1"])
                    V(lambda E, m=m: E.tensor_tensor_scan(
                        z_m2[:, :], p_ones_sc[:, :], z_p2[:, :],
                        p_scarry[:, m, 3:4], ALU.mult, ALU.add),
                      ["ones", "z_p2", "p_scarry"], ["z_m2"])
                    V(lambda E, m=m: E.tensor_copy(p_scarry[:, m, 2:3],
                                                   z_m1[:, SUB - 1:SUB]),
                      ["z_m1"], ["p_scarry"])
                    V(lambda E, m=m: E.tensor_copy(p_scarry[:, m, 3:4],
                                                   z_m2[:, SUB - 1:SUB]),
                      ["z_m2"], ["p_scarry"])
                    spb, sptag = (p_spfa, "p_spfa") if m % 2 == 0 \
                        else (p_spfb, "p_spfb")
                    V(lambda E: E.tensor_mul(z_p1[:, :], z_kc[:, 1:SUB + 1],
                                             z_m1[:, :]),
                      ["z_kc", "z_m1"], ["z_p1"])
                    V(lambda E: E.tensor_mul(z_p2[:, :], z_ks[:, 1:SUB + 1],
                                             z_m2[:, :]),
                      ["z_ks", "z_m2"], ["z_p2"])
                    V(lambda E, spb=spb: E.tensor_add(spb[:, :], z_p1[:, :],
                                                      z_p2[:, :]),
                      ["z_p1", "z_p2"], [sptag])
                    dma(sp_kv[:, m, n * SUB:(n + 1) * SUB], spb[:, :],
                        [sptag], [f"sp_kv_{m}_{n}"])
                else:
                    V(lambda E, m=m: E.scalar_tensor_tensor(
                        p_junk[:, :], z_p1[:, :], 1.0, pE[:, :], ALU.bypass,
                        ALU.mult, accum_out=p_acol[:, m:m + 1]),
                      ["z_p1", "pE"], ["p_junk", "p_acol"])
                    V(lambda E, m=m: E.scalar_tensor_tensor(
                        p_junk[:, :], z_p2[:, :], 1.0, pE[:, :], ALU.bypass,
                        ALU.mult, accum_out=p_acol2[:, m:m + 1]),
                      ["z_p2", "pE"], ["p_junk", "p_acol2"])
            if not own:
                if n == 0:
                    V(lambda E, j=j: E.tensor_copy(p_sum_kc[:, :, j],
                                                   p_acol[:, :]),
                      ["p_acol"], ["p_sum_kc"])
                    V(lambda E, j=j: E.tensor_copy(p_sum_ks[:, :, j],
                                                   p_acol2[:, :]),
                      ["p_acol2"], ["p_sum_ks"])
                else:
                    V(lambda E, j=j: E.tensor_add(p_sum_kc[:, :, j],
                                                  p_sum_kc[:, :, j],
                                                  p_acol[:, :]),
                      ["p_acol", "p_sum_kc"], ["p_sum_kc"])
                    V(lambda E, j=j: E.tensor_add(p_sum_ks[:, :, j],
                                                  p_sum_ks[:, :, j],
                                                  p_acol2[:, :]),
                      ["p_acol2", "p_sum_ks"], ["p_sum_ks"])
        pass

    def precast(e):
        for r in range(0, D, 256):
            dma(m1wb[e, r:r + 256, :], m1w[e, r:r + 256, :], [], ["m1wb"],
                waw=False)
        dma(m2wb[e, :, :], m2w[e, :, :], [], ["m2wb"], waw=False)

    for j in range(NS - 1):
        process_slot(j, own=False)
        precast(j)
    process_slot(OWN, own=True)
    precast(OWN)

    tap("xn", p_xn[:, :, :], ["p_xn"])
    tap("vg", r_vg[0:1, :], ["r_vg"])
    tap("bw0", r_bw0[0:1, :], ["r_bw0"])
    tap("bw1", r_bw1[0:1, :], ["r_bw1"])
    tap("rho", r_rho[0:1, :], ["r_rho"])
    tap("gcs", r_gcs[0:1, :], ["r_gcs"])
    tap("car_pc", p_car_pc[:, :], ["p_car_pc"])
    tap("car_ps", p_car_ps[:, :], ["p_car_ps"])
    tap("car_kc", p_car_kc[:, :], ["p_car_kc"])
    tap("car_ks", p_car_ks[:, :], ["p_car_ks"])
    tap("sum_pc", p_sum_pc[:, :, :], ["p_sum_pc"])
    tap("sum_kc", p_sum_kc[:, :, :], ["p_sum_kc"])
    tap("sum_vg", p_sum_vg[0:1, :], ["p_sum_vg"])

    S(lambda E: E.preload_act_table(AF.Sqrt), [], [])
    for c in range(C):
        dma(a_wo[:, c, :], Wo[c * 128:(c + 1) * 128, :],
            ["a_wo"] if c == 0 else [], ["a_wo"])
    V(lambda E: E.reciprocal(rwf[0:1, :], r_gcs[0:1, :]), ["r_gcs"], ["rwf"])
    S(lambda E: E.activation(out=r_gcs[0:1, :], in_=rwf[0:1, :],
                             func=AF.Sqrt), ["rwf"], ["r_gcs"])
    V(lambda E: E.tensor_mul(rwf[0:1, :], r_bw1[0:1, :], r_gcs[0:1, :]),
      ["r_bw1", "r_gcs"], ["rwf"])
    for n in range(NSUB):
        T(lambda E, n=n: E.matmul(pA[:, 0:SUB],
                                  p_ones_r[0:1, SUB:SUB + 128],
                                  r_bw0[0:1, n * SUB:(n + 1) * SUB],
                                  start=True, stop=True),
          ["ones", "r_bw0"], ["pA"])
        T(lambda E, n=n: E.matmul(pB[:, 0:SUB],
                                  p_ones_rf[0:1, 128:256],
                                  rwf[0:1, n * SUB:(n + 1) * SUB],
                                  start=True, stop=True),
          ["ones", "rwf"], ["pB"])
        for c in range(C):
            dma(z_kt[:, :], sp_pos[:, c, n * SUB:(n + 1) * SUB],
                [f"sp_pos_{c}_{n}", "z_kt"], ["z_kt"], eng="sync")
            dma(z_ka[:, :], sp_kv[:, c, n * SUB:(n + 1) * SUB],
                [f"sp_kv_{c}_{n}", "z_ka"], ["z_ka"], eng="sync")
            V(lambda E: E.tensor_mul(z_p1[:, :], z_kt[:, :], pA[:, :]),
              ["z_kt", "pA"], ["z_p1"])
            V(lambda E: E.tensor_mul(z_p2[:, :], z_ka[:, :], pB[:, :]),
              ["z_ka", "pB"], ["z_p2"])
            spb, sptag = (p_spfa, "p_spfa") if c % 2 == 0 \
                else (p_spfb, "p_spfb")
            V(lambda E, spb=spb: E.tensor_add(spb[:, :], z_p1[:, :],
                                              z_p2[:, :]),
              ["z_p1", "z_p2"], [sptag])
            T(lambda E, c=c, f=(c == 0), l=(c == C - 1), spb=spb: E.matmul(
                pR0[0:1, 0:SUB], p_ones_c[:, 0:1], spb[:, :],
                start=f, stop=l), ["ones", sptag], ["pR0"])
            S(lambda E, spb=spb: E.activation(out=z_sq[:, :], in_=spb[:, :],
                                              func=AF.Square),
              [sptag], ["z_sq"])
            T(lambda E, c=c, f=(c == 0), l=(c == C - 1): E.matmul(
                pR1[0:1, 0:SUB], p_ones_c[:, 0:1], z_sq[:, :],
                start=f, stop=l), ["ones", "z_sq"], ["pR1"])
            dma(sp_bl[:, c, n * SUB:(n + 1) * SUB], spb[:, :], [sptag],
                [f"sp_bl_{c}_{n}"])
        ln_rows()
        V(lambda E, n=n: E.tensor_copy(r_mu2[0:1, n * SUB:(n + 1) * SUB],
                                       rf[0][0:1, :]), ["rf0"], ["r_mu2"])
        V(lambda E, n=n: E.tensor_copy(r_r2[0:1, n * SUB:(n + 1) * SUB],
                                       rf[1][0:1, :]), ["rf1"], ["r_r2"])
    for n in range(NSUB):
        t0 = 1 + n * SUB
        bcast16(pA, "pA", r_mu2[0:1, n * SUB:(n + 1) * SUB], ["r_mu2"])
        bcast16(pB, "pB", r_r2[0:1, n * SUB:(n + 1) * SUB], ["r_r2"])
        for c in range(C):
            dma(z_kt[:, :], sp_bl[:, c, n * SUB:(n + 1) * SUB],
                [f"sp_bl_{c}_{n}", "z_kt"], ["z_kt"], eng="sync")
            V(lambda E: E.tensor_sub(z_p1[:, :], z_kt[:, :], pA[:, :]),
              ["z_kt", "pA"], ["z_p1"])
            V(lambda E: E.tensor_mul(z_p2[:, :], z_p1[:, :], pB[:, :]),
              ["z_p1", "pB"], ["z_p2"])
            S(lambda E, c=c: E.activation(out=a_ln[:, c, :], in_=z_p2[:, :],
                                          func=AF.Identity,
                                          scale=p_lng[:, c:c + 1],
                                          bias=p_lnb[:, c:c + 1]),
              ["z_p2", "p_lng", "p_lnb"], ["a_ln"])
        for m in range(C):
            ps1, pn1 = (pC_, "pC") if m % 2 == 0 else (pD, "pD")
            wmm(ps1, pn1, a_wo, "a_wo", m, lambda kc: a_ln[:, kc, :],
                ["a_ln"], range(C), "bo")
            spb, sptag = (p_sp16a, "p_sp16a") if m % 2 == 0 \
                else (p_sp16b, "p_sp16b")
            V(lambda E, m=m, ps1=ps1, t0=t0, spb=spb: E.tensor_add(
                spb[:, :], ps1[:, :], p_xn[:, m, t0:t0 + SUB]),
              [pn1, "p_xn"], [sptag])
            dma(sp_s[:, m, n * SUB:(n + 1) * SUB], spb[:, :], [sptag],
                [f"sp_s_{m}_{n}"])
    pass
    tap_dram("pos", sp_pos, [128, C, TC], F32,
             [f"sp_pos_{c}_{n}" for c in range(C) for n in range(NSUB)])
    tap_dram("kv", sp_kv, [128, C, TC], F32,
             [f"sp_kv_{c}_{n}" for c in range(C) for n in range(NSUB)])
    tap_dram("bl", sp_bl, [128, C, TC], F32,
             [f"sp_bl_{c}_{n}" for c in range(C) for n in range(NSUB)])
    tap_dram("s", sp_s, [128, C, TC], BF16,
             [f"sp_s_{c}_{n}" for c in range(C) for n in range(NSUB)])
    tap("mu2", r_mu2[0:1, :], ["r_mu2"])
    tap("r2", r_r2[0:1, :], ["r_r2"])
    tap("bw1b", r_bw1[0:1, :], ["r_bw1"])

    S(lambda E: E.preload_act_table(AF.Silu), [], [])
    for c in range(C):
        dma(sp_xn[:, c, :], p_xn[:, c, 1:TC + 1], ["p_xn"], [f"sp_xn_{c}"])
    for n in range(NSUB):
        t0 = 1 + n * SUB
        bcast16(pA, "pA", r_rho[0:1, n * SUB:(n + 1) * SUB], ["r_rho"])
        for c in range(C):
            V(lambda E, c=c, t0=t0: E.tensor_mul(
                z_p1[:, :], p_xn[:, c, t0:t0 + SUB], pA[:, :]),
              ["p_xn", "pA", f"sp_xn_{c}"], ["z_p1"])
            V(lambda E, c=c, t0=t0: E.tensor_scalar(
                p_xn[:, c, t0:t0 + SUB], z_p1[:, :], p_nsc[:, c:c + 1], None,
                ALU.mult), ["z_p1", "p_nsc"], ["p_xn"])
    for nt in range(NTT):
        t0 = 1 + nt * 128
        psg, psgn = (pE, "pE") if nt % 2 == 0 else (pF, "pF")
        for kc in range(C):
            T(lambda E, kc=kc, t0=t0, f=(kc == 0), psg=psg: E.matmul(
                psg[0:128, 0:NE], p_xn[:, kc, t0:t0 + 128],
                p_gw[:, kc, :], start=f, stop=False),
              ["p_xn", "p_gw"], [psgn])
        o_gb, _n2 = BOFF["gate_b"]
        T(lambda E, o=o_gb, psg=psg: E.matmul(
            psg[0:128, 0:NE], p_ones_r[0:1, 0:128],
            p_bias[:, o:o + NE], start=False, stop=True),
          ["ones", "p_bias"], [psgn])
        S(lambda E, nt=nt, psg=psg: E.activation(
            out=p_gl[:, nt * NE:(nt + 1) * NE], in_=psg[0:128, 0:NE],
            func=AF.Copy), [psgn], ["p_gl"])
    pass

    def sl_(nt):
        return slice(nt * NE, (nt + 1) * NE)
    for nt in range(NTT):
        V(lambda E, nt=nt: E.reduce_max(p_m1c[:, nt:nt + 1],
                                        p_gl[:, sl_(nt)], axis=AX),
          ["p_gl"], ["p_m1c"])
    for nt in range(NTT):
        S(lambda E, nt=nt: E.activation(out=p_nm1f[:, nt:nt + 1],
                                        in_=p_m1c[:, nt:nt + 1],
                                        func=AF.Identity, scale=col(C_NEG1)),
          ["p_m1c", "p_cst"], ["p_nm1f"])
    for nt in range(NTT):
        S(lambda E, nt=nt: E.activation(out=p_t2a[:, sl_(nt)],
                                        in_=p_gl[:, sl_(nt)],
                                        func=AF.Identity,
                                        bias=p_nm1f[:, nt:nt + 1]),
          ["p_gl", "p_nm1f"], ["p_t2a"])
    for nt in range(NTT):
        S(lambda E, nt=nt: E.activation(out=p_t2b[:, sl_(nt)],
                                        in_=p_t2a[:, sl_(nt)],
                                        func=AF.Sign), ["p_t2a"], ["p_t2b"])
    for nt in range(NTT):
        S(lambda E, nt=nt: E.activation(out=p_t2a[:, sl_(nt)],
                                        in_=p_t2b[:, sl_(nt)],
                                        func=AF.Square), ["p_t2b"], ["p_t2a"])
    for nt in range(NTT):
        S(lambda E, nt=nt: E.activation(out=p_gw1[:, sl_(nt)],
                                        in_=p_t2a[:, sl_(nt)],
                                        func=AF.Identity, scale=col(C_NEG1),
                                        bias=col(C_ONE)),
          ["p_t2a", "p_cst"], ["p_gw1"])
    for nt in range(NTT):
        V(lambda E, nt=nt: E.scalar_tensor_tensor(
            p_t2a[:, sl_(nt)], p_gw1[:, sl_(nt)], -30000.0,
            p_gl[:, sl_(nt)], ALU.mult, ALU.add),
          ["p_gw1", "p_gl"], ["p_t2a"])
    S(lambda E: E.nop(), ["p_t2a"], ["rt_t2"])
    for nt in range(NTT):
        V(lambda E, nt=nt: E.reduce_max(p_m2c[:, nt:nt + 1],
                                        p_t2a[:, sl_(nt)], axis=AX),
          ["p_t2a", "rt_t2"], ["p_m2c"])
    for nt in range(NTT):
        S(lambda E, nt=nt: E.activation(out=p_nm2f[:, nt:nt + 1],
                                        in_=p_m2c[:, nt:nt + 1],
                                        func=AF.Identity, scale=col(C_NEG1)),
          ["p_m2c", "p_cst"], ["p_nm2f"])
    for nt in range(NTT):
        S(lambda E, nt=nt: E.activation(out=p_t2a[:, sl_(nt)],
                                        in_=p_gl[:, sl_(nt)],
                                        func=AF.Identity,
                                        bias=p_nm2f[:, nt:nt + 1]),
          ["p_gl", "p_nm2f"], ["p_t2a"])
    for nt in range(NTT):
        S(lambda E, nt=nt: E.activation(out=p_t2b[:, sl_(nt)],
                                        in_=p_t2a[:, sl_(nt)],
                                        func=AF.Sign), ["p_t2a"], ["p_t2b"])
    for nt in range(NTT):
        S(lambda E, nt=nt: E.activation(out=p_t2a[:, sl_(nt)],
                                        in_=p_t2b[:, sl_(nt)],
                                        func=AF.Square), ["p_t2b"], ["p_t2a"])
    for nt in range(NTT):
        S(lambda E, nt=nt: E.activation(out=p_t2b[:, sl_(nt)],
                                        in_=p_t2a[:, sl_(nt)],
                                        func=AF.Identity, scale=col(C_NEG1),
                                        bias=col(C_ONE)),
          ["p_t2a", "p_cst"], ["p_t2b"])
    for nt in range(NTT):
        S(lambda E, nt=nt: E.activation(out=p_nmh2[:, nt:nt + 1],
                                        in_=p_m2c[:, nt:nt + 1],
                                        func=AF.Identity, scale=col(C_NEGH)),
          ["p_m2c", "p_cst"], ["p_nmh2"])
    for nt in range(NTT):
        S(lambda E, nt=nt: E.activation(out=p_nmh1[:, nt:nt + 1],
                                        in_=p_m1c[:, nt:nt + 1],
                                        func=AF.Identity, scale=col(C_NEGH)),
          ["p_m1c", "p_cst"], ["p_nmh1"])
    for nt in range(NTT):
        S(lambda E, nt=nt: E.activation(out=p_th1[:, nt:nt + 1],
                                        in_=p_m1c[:, nt:nt + 1],
                                        func=AF.Tanh, scale=col(C_HALF),
                                        bias=p_nmh2[:, nt:nt + 1]),
          ["p_m1c", "p_nmh2", "p_cst"], ["p_th1"])
    for nt in range(NTT):
        S(lambda E, nt=nt: E.activation(out=p_th2[:, nt:nt + 1],
                                        in_=p_m2c[:, nt:nt + 1],
                                        func=AF.Tanh, scale=col(C_HALF),
                                        bias=p_nmh1[:, nt:nt + 1]),
          ["p_m2c", "p_nmh1", "p_cst"], ["p_th2"])
    for nt in range(NTT):
        S(lambda E, nt=nt: E.activation(out=p_p1k[:, nt:nt + 1],
                                        in_=p_th1[:, nt:nt + 1],
                                        func=AF.Identity, scale=col(C_HALF),
                                        bias=col(C_HALF)),
          ["p_th1", "p_cst"], ["p_p1k"])
    for nt in range(NTT):
        S(lambda E, nt=nt: E.activation(out=p_p2k[:, nt:nt + 1],
                                        in_=p_th2[:, nt:nt + 1],
                                        func=AF.Identity, scale=col(C_HALF),
                                        bias=col(C_HALF)),
          ["p_th2", "p_cst"], ["p_p2k"])
    for nt in range(NTT):
        V(lambda E, nt=nt: E.tensor_scalar(p_t2a[:, sl_(nt)],
                                           p_gw1[:, sl_(nt)],
                                           p_p1k[:, nt:nt + 1], None,
                                           ALU.mult),
          ["p_gw1", "p_p1k"], ["p_t2a"])
    for nt in range(NTT):
        V(lambda E, nt=nt: E.tensor_scalar(p_gw1[:, sl_(nt)],
                                           p_t2b[:, sl_(nt)],
                                           p_p2k[:, nt:nt + 1], None,
                                           ALU.mult),
          ["p_t2b", "p_p2k"], ["p_gw1"])
    V(lambda E: E.tensor_add(p_gw1[:, :], p_gw1[:, :], p_t2a[:, :]),
      ["p_gw1", "p_t2a"], ["p_gw1"])
    tap("gl0", p_gl[:, 0:NE], ["p_gl"])
    tap("m1c", p_m1c[:, :], ["p_m1c"])
    tap("m2c", p_m2c[:, :], ["p_m2c"])
    tap("p1k", p_p1k[:, :], ["p_p1k"])
    tap("gw10", p_gw1[:, 0:NE], ["p_gw1"])
    pass
    for nt in range(NTT):
        psg, psgn = (pE, "pE") if nt % 2 == 0 else (pF, "pF")
        T(lambda E, nt=nt, psg=psg: E.transpose(
            psg[0:NE, 0:128], p_gw1[:, nt * NE:(nt + 1) * NE],
            p_identf[:, :]),
          ["p_gw1", "p_identf"], [psgn])
        S(lambda E, nt=nt, psg=psg: E.activation(
            out=p_wrow[:, nt * 128:(nt + 1) * 128], in_=psg[0:NE, 0:128],
            func=AF.Copy), [psgn], ["p_wrow"])
    tap("wrow", p_wrow[:, :], ["p_wrow"])
    pass
    V(lambda E: E.memset(f_acc[:, :, :], 0.0), [], ["f_acc"])
    tap_dram("m2wb0", m2wb[0, 0:128, 0:128], [128, 128], BF16, ["m2wb"])
    tap_dram("m2wb1", m2wb[0, 1024:1152, 0:128], [128, 128], BF16, ["m2wb"])
    tap_dram("m2wb2", m2wb[0, 1536:1664, 0:128], [128, 128], BF16, ["m2wb"])
    tap_dram("m1wb0", m1wb[0, 512:640, 0:128], [128, 128], BF16, ["m1wb"])
    qrr = [0]

    def wdma(dst, src_ap, rtags, wtags, waw=True):
        eng = ("gpsimd", "sync")[qrr[0] % 2]
        qrr[0] += 1
        dma(dst, src_ap, rtags, wtags, eng=eng, waw=waw)

    for e in range(NE):
        wdma(f_be[:, 0, :], m1bT[e, 0], ["f_be"], ["f_be"])
        wdma(f_be[:, 1, :], m1bT[e, 1], [], ["f_be"])
        wdma(f_b2c[:, :], m2bT[e], ["f_b2c"], ["f_b2c"])
        S(lambda E: E.activation(out=f_beE[:, :], in_=f_be[:, 0, :],
                                 func=AF.Identity, scale=col(C_ALPHA)),
          ["f_be", "p_cst"], ["f_beE"])
        for half in range(2):
            hbase = half * (TC // 2)
            for jj in range(JC):
                for c in range(4):
                    wdma(f_w1a[:, c, :],
                         m1wb[e, c * 128:(c + 1) * 128,
                              jj * 256:(jj + 1) * 256],
                         [], ["f_w1a"], waw=False)
                for c in range(4):
                    wdma(f_w1b[:, c, :],
                         m1wb[e, (c + 4) * 128:(c + 5) * 128,
                              jj * 256:(jj + 1) * 256],
                         [], ["f_w1b"], waw=False)
                for n2 in range(2):
                    t0 = 1 + hbase + n2 * SUB
                    q = jj * 2 + n2
                    psE, pnE, psO, pnO = (
                        (pA, "pA", pB, "pB") if q % 2 == 0
                        else (pC_, "pC", pD, "pD"))
                    for kc in range(C):
                        wt, wtag = (f_w1a, "f_w1a") if kc < 4 \
                            else (f_w1b, "f_w1b")
                        T(lambda E, kc=kc, t0=t0, f=(kc == 0), wt=wt,
                          psE=psE, l=(kc == C - 1): E.matmul(
                            psE[:, 0:SUB], wt[:, kc % 4, 0:256:2],
                            p_xn[:, kc, t0:t0 + SUB], start=f, stop=l),
                          [wtag, "p_xn"], [pnE])
                    for kc in range(C):
                        wt, wtag = (f_w1a, "f_w1a") if kc < 4 \
                            else (f_w1b, "f_w1b")
                        T(lambda E, kc=kc, t0=t0, f=(kc == 0), wt=wt,
                          psO=psO, l=(kc == C - 1): E.matmul(
                            psO[:, 0:SUB], wt[:, kc % 4, 1:256:2],
                            p_xn[:, kc, t0:t0 + SUB], start=f, stop=l),
                          [wtag, "p_xn"], [pnO])
                    S(lambda E, jj=jj, psE=psE: E.activation(
                        out=f_swz3[:, :], in_=psE[:, :], func=AF.Silu,
                        scale=col(C_ALPHA), bias=f_beE[:, jj:jj + 1]),
                      [pnE, "p_cst", "f_beE"], ["f_swz3"])
                    V(lambda E: E.tensor_scalar(f_swz4[:, :], f_swz3[:, :],
                                                D7, None, ALU.min),
                      ["f_swz3"], ["f_swz4"])
                    S(lambda E, jj=jj, psO=psO: E.activation(
                        out=f_swz[:, :], in_=psO[:, :], func=AF.Identity,
                        bias=f_be[:, 1, jj:jj + 1]),
                      [pnO, "f_be"], ["f_swz"])
                    V(lambda E: E.tensor_scalar(f_swz2[:, :], f_swz[:, :],
                                                LIMIT, -LIMIT, ALU.min,
                                                ALU.max),
                      ["f_swz"], ["f_swz2"])
                    V(lambda E: E.tensor_scalar(f_swz[:, :], f_swz2[:, :],
                                                1.0, 1.0 / ALPHA, ALU.add,
                                                ALU.mult),
                      ["f_swz2"], ["f_swz"])
                    V(lambda E, jj=jj, n2=n2: E.tensor_mul(
                        f_h[:, jj, n2 * SUB:(n2 + 1) * SUB], f_swz4[:, :],
                        f_swz[:, :]), ["f_swz4", "f_swz"], ["f_h"])
                    if e == 0 and half == 0 and jj == 0 and n2 == 0:
                        tap("fh0", f_h[:, 0, 0:SUB], ["f_h"])
                        tap("fbeE", f_beE[:, :], ["f_beE"])
                        tap("fbeO", f_be[:, 1, :], ["f_be"])
            if e == 0 and half == 0:
                tap("fhall", f_h[:, :, :], ["f_h"])
            for n2 in range(2):
                n = half * 2 + n2
                T(lambda E, e=e, n=n: E.matmul(
                    pE[:, 0:SUB], p_sel[0:NE, e * 128:(e + 1) * 128],
                    p_wrow[0:NE, n * SUB:(n + 1) * SUB],
                    start=True, stop=True), ["p_sel", "p_wrow"], ["pE"])
                V(lambda E, n2=n2: E.tensor_copy(f_bwe[:, n2, :], pE[:, :]),
                  ["pE"], ["f_bwe"])
            for m in range(C):
                for k in range(8):
                    wdma(f_w2ma[:, k, :],
                         m2wb[e, k * 128:(k + 1) * 128,
                              m * 128:(m + 1) * 128],
                         [], ["f_w2ma"], waw=False)
                for k in range(8):
                    wdma(f_w2mb[:, k, :],
                         m2wb[e, (k + 8) * 128:(k + 9) * 128,
                              m * 128:(m + 1) * 128],
                         [], ["f_w2mb"], waw=False)
                if e == 0 and half == 0 and m == 0:
                    tap("fw2a", f_w2ma[:, :, :], ["f_w2ma"])
                    tap("fw2b", f_w2mb[:, :, :], ["f_w2mb"])
                for n2 in range(2):
                    n = half * 2 + n2
                    ps1, pn1 = ((pC_, "pC") if n2 == 0 else (pD, "pD")) \
                        if m % 2 == 0 else ((pF, "pF") if n2 == 0
                                            else (pA, "pA"))
                    for kc in range(JC):
                        wt, wtag = (f_w2ma, "f_w2ma") if kc < 8 \
                            else (f_w2mb, "f_w2mb")
                        T(lambda E, kc=kc, n2=n2, ps1=ps1, wt=wt,
                          f=(kc == 0), l=(kc == JC - 1):
                          E.matmul(ps1[:, 0:SUB], wt[:, kc % 8, :],
                                   f_h[:, kc, n2 * SUB:(n2 + 1) * SUB],
                                   start=f, stop=l),
                          [wtag, "f_h"], [pn1])
                    S(lambda E, m=m, ps1=ps1: E.activation(
                        out=f_swz3[:, :], in_=ps1[:, :], func=AF.Identity,
                        bias=f_b2c[:, m:m + 1]),
                      [pn1, "f_b2c"], ["f_swz3"])
                    if e == 0 and half == 0 and m == 0 and n2 == 0:
                        tap("fy00", f_swz3[:, :], ["f_swz3"])
                    V(lambda E, n2=n2: E.tensor_mul(
                        f_swz[:, :], f_swz3[:, :], f_bwe[:, n2, :]),
                      ["f_swz3", "f_bwe"], ["f_swz"])
                    V(lambda E, m=m, n=n: E.tensor_add(
                        f_acc[:, m, n * SUB:(n + 1) * SUB],
                        f_acc[:, m, n * SUB:(n + 1) * SUB], f_swz[:, :]),
                      ["f_swz", "f_acc"], ["f_acc"])
                    if e == 0 and m == 0 and n == 3:
                        tap("facc00", f_acc[:, 0, :], ["f_acc"])
                        tap("fbwe0", f_bwe[:, :, :], ["f_bwe"])
    b.barrier()
    tap("acc", f_acc[:, :, :], ["f_acc"])
    tap("t_fm", p_xn[:, :, :], ["p_xn"])

    for c in range(C):
        dma(p_xn[:, c, 1:TC + 1], sp_xn[:, c, :], [f"sp_xn_{c}", "p_xn"],
            ["p_xn"])
    for n in range(NSUB):
        t0 = 1 + n * SUB
        for c in range(C):
            dma(g_x[:, c, :], xT[c * 128:(c + 1) * 128, OWN, t0:t0 + SUB],
                ["g_x"], ["g_x"], eng="sync")
            dma(g_s[:, c, :], sp_s[:, c, n * SUB:(n + 1) * SUB],
                [f"sp_s_{c}_{n}", "g_s"], ["g_s"], eng="sync")
        V(lambda E, t0=t0, n=n: E.tensor_add(
            g_o[:, :, :], p_xn[:, :, t0:t0 + SUB],
            f_acc[:, :, n * SUB:(n + 1) * SUB]),
          ["p_xn", "f_acc"], ["g_o"])
        S(lambda E: E.activation(out=g_g[:, :, :], in_=g_o[:, :, :],
                                 func=AF.Silu), ["g_o"], ["g_g"])
        V(lambda E: E.tensor_mul(g_o[:, :, :], g_s[:, :, :], g_g[:, :, :]),
          ["g_s", "g_g"], ["g_o"])
        V(lambda E: E.tensor_add(g_x[:, :, :], g_x[:, :, :], g_o[:, :, :]),
          ["g_x", "g_o"], ["g_x"])
        dma(out[:, :, n * SUB:(n + 1) * SUB], g_x[:, :, :], ["g_x"],
            [f"out_{n}"], eng="sync")

    for semname in ("d_sync", "d_gpsimd", "d_scalar"):
        if b.cnt.get(semname, 0) > 0:
            b.ops["sync"].append(("w", semname, b.cnt[semname]))
    b.finalize_and_emit()
    return nc


def kernel(**inputs):
    x = np.asarray(inputs["L0_attn_x"])
    pp = np.asarray(inputs["L0_attn_pos_phases"])
    L = x.shape[1]
    xT_full = np.ascontiguousarray(x[0].T)
    ppT_full = np.ascontiguousarray(pp.T)

    def W(name):
        return np.ascontiguousarray(np.asarray(inputs[name], np.float32))

    def colsfmt(v):
        return np.ascontiguousarray(np.asarray(v, np.float32)
                                    .reshape(C, 128).T)

    consts = np.zeros((128, NCST), np.float32)
    consts[:, C_SINB] = 64 * math.pi
    consts[:, C_COSB] = 64 * math.pi + math.pi / 2
    consts[:, C_PI] = math.pi
    consts[:, C_NPI] = -math.pi
    consts[:, C_HPI] = math.pi / 2
    consts[:, C_ISQ2] = 1.0 / math.sqrt(2.0)
    consts[:, C_ALPHA] = ALPHA
    consts[:, C_NEG1] = -1.0
    consts[:, C_ONE] = 1.0
    consts[:, C_HALF] = 0.5
    consts[:, C_NEGH] = -0.5

    sel = np.zeros((NE, NE * 128), np.float32)
    for e in range(NE):
        sel[e, e * 128:(e + 1) * 128] = 1.0

    brow = np.zeros((1, 5664), np.float32)

    def put(name, v):
        o, n = BOFF[name]
        brow[0, o:o + n] = np.asarray(v, np.float32).reshape(-1)

    put("bpv", inputs["L0_attn_bpv"]); put("bk", inputs["L0_attn_bk"])
    put("bkv", inputs["L0_attn_bkv"]); put("bg1", inputs["L0_attn_bg1"])
    put("bo", inputs["L0_attn_bo"]); put("bb1", inputs["L0_attn_bb1"])
    put("gate_b", inputs["L0_ffn_gate_b"]); put("bb2", inputs["L0_attn_bb2"])
    put("bg2", inputs["L0_attn_bg2"])

    shared = {
        "Wpv": W("L0_attn_Wpv"), "Wk": W("L0_attn_Wk"),
        "Wkv": W("L0_attn_Wkv"), "Wg1": W("L0_attn_Wg1"),
        "Wg2": W("L0_attn_Wg2"), "Wb1": W("L0_attn_Wb1"),
        "Wb2": W("L0_attn_Wb2"), "Wo": W("L0_attn_Wo"),
        "biasrow": brow,
        "ln_g": colsfmt(inputs["L0_attn_ln_g"]),
        "ln_b": colsfmt(inputs["L0_attn_ln_b"]),
        "nsc": colsfmt(inputs["L0_ffn_norm_scale"]),
        "gate_w": W("L0_ffn_gate_w"),
        "m1w": W("L0_ffn_mlp1_w"),
        "m1bT": np.ascontiguousarray(
            W("L0_ffn_mlp1_b").reshape(NE, JC, 128, 2)
            .transpose(0, 3, 2, 1)),
        "m2w": W("L0_ffn_mlp2_w"),
        "m2bT": np.ascontiguousarray(
            W("L0_ffn_mlp2_b").reshape(NE, C, 128).transpose(0, 2, 1)),
        "selw": sel, "cst": consts,
        "identw": np.eye(128, dtype=np.float32),
    }

    in_maps = []
    for c in range(NCORES):
        order = [k for k in range(NCORES) if k != c] + [c]
        xT_sl = np.zeros((D, NS, TC + 1), np.float32)
        ppT_sl = np.zeros((D, NS, TC), np.float32)
        cm = np.zeros((128, NS), np.float32)
        pm = np.zeros((128, NS), np.float32)
        for jslot, k in enumerate(order):
            t0 = k * TC
            if t0 > 0:
                xT_sl[:, jslot, 0] = xT_full[:, t0 - 1]
                pm[:, jslot] = 1.0
            xT_sl[:, jslot, 1:] = xT_full[:, t0:t0 + TC]
            ppT_sl[:, jslot, :] = ppT_full[:, t0:t0 + TC]
            if k < c:
                cm[:, jslot] = 1.0
        m = dict(shared)
        m["xT"] = xT_sl
        m["ppT"] = ppT_sl
        m["cmask"] = cm
        m["pmask"] = pm
        in_maps.append(m)

    nc = build_program()
    res = run_bass_kernel_spmd(nc, in_maps, core_ids=list(range(NCORES)))
    global LAST_RES
    LAST_RES = res
    outs = []
    for c in range(NCORES):
        o = np.asarray(res.results[c]["out"])
        outs.append(np.transpose(o, (2, 1, 0)).reshape(TC, D))
    return np.concatenate(outs, axis=0).reshape(1, L, D).astype(np.float32)
